# version 06
# baseline (speedup 1.0000x reference)
import sys
sys.path.insert(0, "/opt/trn_rl_repo")
import numpy as np
from contextlib import ExitStack

import concourse.bass as bass
import concourse.mybir as mybir
from concourse.bass_utils import run_bass_kernel_spmd

F32 = mybir.dt.float32
BF16 = mybir.dt.bfloat16
I32 = mybir.dt.int32
AF = mybir.ActivationFunctionType
ALU = mybir.AluOpType
AX = mybir.AxisListType

P = 128
D = 1024
NFC = D // P
T = 2048
TT = 512
NTT = T // TT
EPS_LN = 1e-5
EPS_RMS0 = 1e-6
EPS_RMS1 = 1e-5
DFF0 = 2736
HQ = 3072
DIN = 2048
NHEAD = 16
DK = 64

ENGINES = ("sync", "tensor", "vector", "scalar", "gpsimd")


class Sem:
    __slots__ = ("h", "n", "name")

    def __init__(self, handle, name):
        self.h = handle
        self.n = 0
        self.name = name


class Builder:
    def __init__(self, nc, ctx):
        self.nc = nc
        self.ctx = ctx
        self.q = {e: [] for e in ENGINES}
        self.sems = {}

    def sem(self, name):
        s = Sem(self.ctx.enter_context(self.nc.semaphore(name)), name)
        self.sems[name] = s
        return s

    def on(self, engine, fn):
        self.q[engine].append(fn)

    def emit(self, engine, fn, sig=None, sigk=1):
        if sig is None:
            self.on(engine, lambda eng, f=fn: f(eng))
            return None
        sig.n += sigk

        def run(eng, f=fn, s=sig.h, k=sigk):
            f(eng).then_inc(s, k)
        self.on(engine, run)
        return sig.n

    def wait(self, engine, sem, val=None):
        v = sem.n if val is None else val
        self.on(engine, lambda eng, s=sem.h, vv=v: eng.wait_ge(s, vv))

    def replay(self, block):
        for e in ENGINES:
            fns = self.q[e]
            if not fns:
                continue
            getattr(block, e)(lambda eng, fns=fns: [fn(eng) for fn in fns])


def barrier_all(B, engines=("tensor", "vector", "scalar", "gpsimd", "sync")):
    for e in engines:
        s = B.sems.setdefault("prog_" + e, None) or B.sem("prog_" + e)
        B.emit(e, lambda eng: eng.nop(), sig=s)
    for pe in engines:
        s = B.sems["prog_" + pe]
        for ce in engines:
            if ce != pe:
                B.wait(ce, s)


def sbuf(B, name, shape, dtype):
    return B.ctx.enter_context(B.nc.sbuf_tensor(name, shape, dtype))


def ptensor(B, name, shape, dtype=F32):
    return B.ctx.enter_context(B.nc.psum_tensor(name, shape, dtype))


class Arena:

    def __init__(self, B, words, name="arena"):
        self.t = sbuf(B, name, [P, words], F32)
        self.words = words
        self.off = 0
        self.top = words

    def mark(self):
        return self.off

    def reset(self, m):
        self.off = m

    def take_f32(self, n, parts=P):
        o = self.off
        assert o + n <= self.top, f"arena overflow {o}+{n}>{self.top}"
        self.off += n
        return self.t[0:parts, o:o + n]

    def take_f32_top(self, n, parts=P):
        assert self.off + n <= self.top, f"arena top overflow"
        self.top -= n
        return self.t[0:parts, self.top:self.top + n]

    def take_bf16_top(self, n_elems, parts=P):
        n = (n_elems + 1) // 2
        return self.take_f32_top(n, parts).bitcast(BF16)

    def free_top(self, n_elems_f32):
        self.top += n_elems_f32

    def take_bf16(self, n_elems, parts=P):
        n = (n_elems + 1) // 2
        ap = self.take_f32(n, parts)
        return ap.bitcast(BF16)

    def take_i32(self, n, parts=P):
        return self.take_f32(n, parts).bitcast(I32)



def alloc_consts(B):
    g = {}
    g["ident"] = sbuf(B, "c_ident", [P, P], BF16)
    g["identf"] = sbuf(B, "c_identf", [P, P], F32)
    g["ones_row"] = sbuf(B, "c_ones_row", [1, P], F32)
    g["ones_row_bf"] = sbuf(B, "c_ones_row_bf", [1, P], BF16)
    g["ones_col"] = sbuf(B, "c_ones_col", [P, 1], BF16)
    g["ones_col_f"] = sbuf(B, "c_ones_col_f", [P, 1], F32)
    g["u64"] = sbuf(B, "c_u64", [P, P], F32)
    g["mask64"] = sbuf(B, "c_mask64", [P, 8 * 64], BF16)
    g["subind"] = sbuf(B, "c_subind", [2, P], F32)
    g["ohend"] = sbuf(B, "c_ohend", [P, 2], F32)
    g["cc"] = sbuf(B, "c_cc", [P, 24], F32)
    g["_cvals"] = {}
    return g


def cval(g, v):
    d = g["_cvals"]
    if v not in d:
        d[v] = len(d)
        assert len(d) <= g["cc"].shape[1]
    return g["cc"][:, d[v]:d[v] + 1]


def cval1(g, v):
    return cval(g, v)[0:1, :]


def emit_consts(B, g, done):
    def build(eng):
        eng.memset(g["ident"][:], 0.0)
        eng.affine_select(out=g["ident"][:], in_=g["ident"][:], compare_op=ALU.not_equal,
                          fill=1.0, base=0, pattern=[[-1, P]], channel_multiplier=1)
        eng.memset(g["identf"][:], 0.0)
        eng.affine_select(out=g["identf"][:], in_=g["identf"][:], compare_op=ALU.not_equal,
                          fill=1.0, base=0, pattern=[[-1, P]], channel_multiplier=1)
        eng.memset(g["ones_row"][:], 1.0)
        eng.memset(g["ones_row_bf"][:], 1.0)
        eng.memset(g["ones_col"][:], 1.0)
        eng.memset(g["ones_col_f"][:], 1.0)
        u = g["u64"]
        eng.memset(u[:], 0.0)
        eng.affine_select(out=u[:], in_=u[:], compare_op=ALU.is_gt, fill=1.0,
                          base=0, pattern=[[-1, P]], channel_multiplier=1)
        eng.memset(u[0:64, 64:128], 0.0)
        m = g["mask64"]
        eng.memset(m[:], 0.0)
        eng.affine_select(out=m[0:64, :], in_=m[0:64, :], compare_op=ALU.is_gt, fill=1.0,
                          base=0, pattern=[[0, 8], [-1, 64]], channel_multiplier=1)
        eng.affine_select(out=m[64:128, :], in_=m[64:128, :], compare_op=ALU.is_gt, fill=1.0,
                          base=0, pattern=[[0, 8], [-1, 64]], channel_multiplier=1)
        s = g["subind"]
        eng.memset(s[:], 1.0)
        eng.affine_select(out=s[:], in_=s[:], compare_op=ALU.is_ge, fill=0.0,
                          base=63, pattern=[[-1, P]], channel_multiplier=64)
        eng.affine_select(out=s[:], in_=s[:], compare_op=ALU.is_ge, fill=0.0,
                          base=0, pattern=[[1, P]], channel_multiplier=-64)
        oh = g["ohend"]
        eng.memset(oh[:], 0.0)
        eng.affine_select(out=oh[:, 0:1], in_=oh[:, 0:1], compare_op=ALU.not_equal,
                          fill=1.0, base=-63, pattern=[[0, 1]], channel_multiplier=1)
        eng.affine_select(out=oh[:, 1:2], in_=oh[:, 1:2], compare_op=ALU.not_equal,
                          fill=1.0, base=-127, pattern=[[0, 1]], channel_multiplier=1)
        for v, i in g["_cvals"].items():
            eng.memset(g["cc"][:, i:i + 1], float(v))
    B.on("gpsimd", build)
    B.emit("gpsimd", lambda e: e.engine_nop(), sig=done)



class Stage:

    _uid = [0]

    def __init__(self, B, ap, cast_sem, name):
        self.ap = ap
        self.SW = ap.shape[1] // 2
        self.cast = cast_sem
        self.i = 0
        self.lastc = [0, 0]
        Stage._uid[0] += 1
        u = Stage._uid[0]
        self.ld = [B.sem(f"{name}{u}_ld0"), B.sem(f"{name}{u}_ld1")]

    def slot(self, M, pk=P):
        par = self.i % 2
        self.i += 1
        return par, self.ap[0:pk, par * self.SW:par * self.SW + M]


def emit_load_bf16(B, dram, wtile, K, M, stage, ld, cast, eng_cast="gpsimd",
                   dst_stride=None, dst_off=0, mul_cols=None):
    kc = (K + P - 1) // P
    assert M <= stage.SW
    if dst_stride is None:
        dst_stride = M
    for c in range(kc):
        pk = min(P, K - c * P)
        par, ssl = stage.slot(M, pk)
        if stage.lastc[par] > 0:
            B.wait("sync", stage.cast, stage.lastc[par])
        srcap = dram[c * P:c * P + pk, :]
        sld = stage.ld[par]
        B.emit("sync", lambda e, s=ssl, d=srcap: e.dma_start(out=s, in_=d), sig=sld, sigk=16)
        B.wait(eng_cast, sld, sld.n)
        off = c * dst_stride + dst_off
        if mul_cols is None:
            B.emit(eng_cast, lambda e, s=ssl, o=wtile[0:pk, off:off + M]:
                   e.tensor_copy(out=o, in_=s), sig=stage.cast)
        else:
            B.emit(eng_cast, lambda e, s=ssl, o=wtile[0:pk, off:off + M],
                   mc=mul_cols[0:pk, c:c + 1]:
                   e.tensor_scalar(out=o, in0=s, scalar1=mc, scalar2=None, op0=ALU.mult),
                   sig=stage.cast)
        stage.lastc[par] = stage.cast.n



def emit_ln(B, g, s2, x_tiles, xn_tiles, Twidth, eps=EPS_LN):
    mm, ev, sc = B.sems["mm"], B.sems["ev"], B.sems["sc"]
    ntt = Twidth // TT
    for it in range(ntt):
        ts = slice(it * TT, (it + 1) * TT)
        B.wait("scalar", mm, mm.n)
        for c in range(NFC):
            B.emit("scalar", lambda e, i=x_tiles[c][:, ts], o=s2["sq8"][:, c * TT:(c + 1) * TT]:
                   e.activation(out=o, in_=i, func=AF.Square), sig=sc if c == NFC - 1 else None)
        B.wait("tensor", sc, sc.n)
        B.wait("tensor", ev, ev.n)
        for c in range(NFC):
            B.emit("tensor", lambda e, r=x_tiles[c][:, ts], o=s2["pbS"][0:1, 0:TT],
                   l=g["ones_col"][:], s=(c == 0), st=(c == NFC - 1):
                   e.matmul(out=o, lhsT=l, rhs=r, start=s, stop=st))
            B.emit("tensor", lambda e, r=s2["sq8"][:, c * TT:(c + 1) * TT], o=s2["pbS"][32:33, 0:TT],
                   l=g["ones_col"][:], s=(c == 0), st=(c == NFC - 1):
                   e.matmul(out=o, lhsT=l, rhs=r, start=s, stop=(c == NFC - 1)),
                   sig=mm if c == NFC - 1 else None)
        B.wait("vector", mm, mm.n)
        mrow, vrow, arow, brow = s2["mrow"], s2["vrow"], s2["arow"], s2["brow"]
        B.emit("vector", lambda e, o=mrow, i=s2["pbS"][0:1, 0:TT]:
               e.tensor_scalar(out=o, in0=i, scalar1=1.0 / D, scalar2=None, op0=ALU.mult))
        B.emit("vector", lambda e, o=vrow, i=s2["pbS"][32:33, 0:TT]:
               e.tensor_scalar(out=o, in0=i, scalar1=1.0 / D, scalar2=None, op0=ALU.mult), sig=ev)
        B.emit("vector", lambda e, o=brow, m=mrow:
               e.tensor_tensor(out=o, in0=m, in1=m, op=ALU.mult))
        B.emit("vector", lambda e, o=vrow, t=brow:
               e.tensor_tensor(out=o, in0=o, in1=t, op=ALU.subtract), sig=ev)
        B.wait("scalar", ev, ev.n)
        B.emit("scalar", lambda e, o=arow, i=vrow:
               e.activation(out=o, in_=i, func=AF.Sqrt, bias=cval1(g, float(eps))), sig=sc)
        B.wait("vector", sc, sc.n)
        B.emit("vector", lambda e, o=arow: e.reciprocal(out=o, in_=o), sig=ev)
        B.wait("vector", ev, ev.n)
        B.emit("vector", lambda e, o=brow, m=mrow, a=arow:
               e.scalar_tensor_tensor(out=o, in0=m, scalar=-1.0, in1=a,
                                      op0=ALU.mult, op1=ALU.mult), sig=ev)
        B.wait("tensor", ev, ev.n)
        B.emit("tensor", lambda e, r=arow, o=s2["pb"][1][:, 0:TT], l=g["ones_row"][:]:
               e.matmul(out=o, lhsT=l, rhs=r, start=True, stop=True))
        B.emit("tensor", lambda e, r=brow, o=s2["pb"][2][:, 0:TT], l=g["ones_row"][:]:
               e.matmul(out=o, lhsT=l, rhs=r, start=True, stop=True), sig=mm)
        B.wait("vector", mm, mm.n)
        B.emit("vector", lambda e, o=s2["bc_a"][:, 0:TT], i=s2["pb"][1][:, 0:TT]:
               e.tensor_copy(out=o, in_=i))
        B.emit("vector", lambda e, o=s2["bc_b"][:, 0:TT], i=s2["pb"][2][:, 0:TT]:
               e.tensor_copy(out=o, in_=i), sig=ev)
        for c in range(NFC):
            B.emit("vector", lambda e, o=xn_tiles[c][:, ts], x=x_tiles[c][:, ts],
                   a=s2["bc_a"][:, 0:TT]: e.tensor_tensor(out=o, in0=x, in1=a, op=ALU.mult))
            B.emit("vector", lambda e, o=xn_tiles[c][:, ts], b=s2["bc_b"][:, 0:TT]:
                   e.tensor_tensor(out=o, in0=o, in1=b, op=ALU.add),
                   sig=ev if c == NFC - 1 else None)



def emit_rmsnorm(B, g, s2, x_tiles, out_tiles, gcols, Twidth, eps, sig_done=None):
    mm, ev, sc = B.sems["mm"], B.sems["ev"], B.sems["sc"]
    ntt = Twidth // TT
    for it in range(ntt):
        ts = slice(it * TT, (it + 1) * TT)
        B.wait("scalar", mm, mm.n)
        for c in range(NFC):
            B.emit("scalar", lambda e, i=x_tiles[c][:, ts], o=s2["sq8"][:, c * TT:(c + 1) * TT]:
                   e.activation(out=o, in_=i, func=AF.Square), sig=sc if c == NFC - 1 else None)
        B.wait("tensor", sc, sc.n)
        B.wait("tensor", ev, ev.n)
        for c in range(NFC):
            B.emit("tensor", lambda e, r=s2["sq8"][:, c * TT:(c + 1) * TT], o=s2["pbS"][0:1, 0:TT],
                   l=g["ones_col"][:], s=(c == 0), st=(c == NFC - 1):
                   e.matmul(out=o, lhsT=l, rhs=r, start=s, stop=st),
                   sig=mm if c == NFC - 1 else None)
        B.wait("scalar", mm, mm.n)
        arow = s2["arow"]
        B.emit("scalar", lambda e, o=arow, i=s2["pbS"][0:1, 0:TT]:
               e.activation(out=o, in_=i, func=AF.Sqrt, scale=1.0 / D, bias=cval1(g, float(eps))), sig=sc)
        B.wait("vector", sc, sc.n)
        B.emit("vector", lambda e, o=arow: e.reciprocal(out=o, in_=o), sig=ev)
        B.wait("tensor", ev, ev.n)
        B.emit("tensor", lambda e, r=arow, o=s2["pb"][1][:, 0:TT], l=g["ones_row"][:]:
               e.matmul(out=o, lhsT=l, rhs=r, start=True, stop=True), sig=mm)
        B.wait("vector", mm, mm.n)
        B.emit("vector", lambda e, o=s2["bc_a"][:, 0:TT], i=s2["pb"][1][:, 0:TT]:
               e.tensor_copy(out=o, in_=i), sig=ev)
        for c in range(NFC):
            B.emit("vector", lambda e, o=out_tiles[c][:, ts], x=x_tiles[c][:, ts],
                   gc=gcols[c], a=s2["bc_a"][:, 0:TT]:
                   e.scalar_tensor_tensor(out=o, in0=x, scalar=gc, in1=a,
                                          op0=ALU.mult, op1=ALU.mult),
                   sig=(sig_done if (sig_done is not None and c == NFC - 1 and it == ntt - 1) else
                        (ev if c == NFC - 1 else None)))



def emit_qffn(B, g, s2, ar, xn_tiles, acc_tiles, wg_d, wu_d, wd_d, Twidth, yg_d=None):
    phase_fence(B)
    mm, ev, sc = B.sems["mm"], B.sems["ev"], B.sems["sc"]
    ld, cast = B.sems["wld"], B.sems["wcast"]
    HC = HQ // P
    ntt = Twidth // TT
    amark = ar.mark()
    wgcol = [ar.take_bf16(NFC * P), ar.take_bf16(NFC * P)]
    wucol = [ar.take_bf16(NFC * P), ar.take_bf16(NFC * P)]
    gact = ar.take_bf16(HC * TT)
    wdm = ar.take_bf16(HC * P)
    sact = [ar.take_f32(TT), ar.take_f32(TT)]
    wstage = Stage(B, ar.take_i32(2 * 1024), B.sems["wcast"], "wst")
    col_done = {}
    stt_done = {}
    last_wdm = [0]
    for it in range(ntt):
        ts = slice(it * TT, (it + 1) * TT)
        import os as _os
        _dbg = _os.environ.get("QDBG", "0") == "1"
        for hc in range(HC):
            if _dbg:
                barrier_all(B)
            par = hc % 2
            key = (it * HC + hc)
            if key >= 2:
                B.wait("sync", mm, col_done[key - 2])
            emit_load_bf16(B, wg_d[:, hc * P:(hc + 1) * P], wgcol[par], D, P, wstage, ld, cast)
            emit_load_bf16(B, wu_d[:, hc * P:(hc + 1) * P], wucol[par], D, P, wstage, ld, cast)
            pa = s2["pb"][2] if par == 0 else s2["pb"][3]
            pb = s2["pb"][4] if par == 0 else s2["pb"][5]
            B.wait("tensor", cast, cast.n)
            B.wait("tensor", sc, sc.n)
            B.wait("tensor", ev, ev.n)
            for c in range(NFC):
                B.emit("tensor", lambda e, l=wgcol[par][:, c * P:(c + 1) * P],
                       r=xn_tiles[c][:, ts], o=pa[:, 0:TT], s=(c == 0), st=(c == NFC - 1):
                       e.matmul(out=o, lhsT=l, rhs=r, start=s, stop=st),
                       sig=mm if c == NFC - 1 else None)
            for c in range(NFC):
                B.emit("tensor", lambda e, l=wucol[par][:, c * P:(c + 1) * P],
                       r=xn_tiles[c][:, ts], o=pb[:, 0:TT], s=(c == 0), st=(c == NFC - 1):
                       e.matmul(out=o, lhsT=l, rhs=r, start=s, stop=st),
                       sig=mm if c == NFC - 1 else None)
            col_done[key] = mm.n
            B.wait("scalar", mm, mm.n - 1)
            if key >= 2:
                B.wait("scalar", ev, stt_done[key - 2])
            B.emit("scalar", lambda e, i=pa[:, 0:TT], o=sact[par]:
                   e.activation(out=o, in_=i, func=AF.Silu, scale=1.0 / 127.0), sig=sc)
            B.wait("vector", mm, mm.n)
            B.wait("vector", sc, sc.n)
            B.emit("vector", lambda e, i0=pb[:, 0:TT], i1=sact[par],
                   o=gact[:, hc * TT:(hc + 1) * TT]:
                   e.scalar_tensor_tensor(out=o, in0=i0, scalar=1.0 / 127.0, in1=i1,
                                          op0=ALU.mult, op1=ALU.mult), sig=ev)
            stt_done[key] = ev.n
        if yg_d is not None:
            st = B.sems["st"]
            B.wait("sync", ev, ev.n)
            B.emit("sync", lambda e, o=yg_d[it * P:(it + 1) * P, :], i=gact[:]:
                   e.dma_start(out=o, in_=i), sig=st, sigk=16)
            for en in ("tensor", "vector", "scalar"):
                B.wait(en, st, st.n)
        for m in range(NFC):
            if _dbg:
                barrier_all(B)
            if last_wdm[0] > 0:
                B.wait("sync", mm, last_wdm[0])
            emit_load_bf16(B, wd_d[:, m * P:(m + 1) * P], wdm, HQ, P, wstage, ld, cast)
            pd = s2["pb"][m % 2]
            B.wait("tensor", cast, cast.n)
            B.wait("tensor", ev, ev.n)
            for hc in range(HC):
                B.emit("tensor", lambda e, l=wdm[:, hc * P:(hc + 1) * P],
                       r=gact[:, hc * TT:(hc + 1) * TT], o=pd[:, 0:TT],
                       s=(hc == 0), st=(hc == HC - 1):
                       e.matmul(out=o, lhsT=l, rhs=r, start=s, stop=st),
                       sig=mm if hc == HC - 1 else None)
            last_wdm[0] = mm.n
            B.wait("vector", mm, mm.n)
            B.emit("vector", lambda e, i=pd[:, 0:TT], a=acc_tiles[m][:, ts]:
                   e.scalar_tensor_tensor(out=a, in0=i, scalar=1.0 / 127.0, in1=a,
                                          op0=ALU.mult, op1=ALU.add), sig=ev)
    ar.reset(amark)


def alloc_ln_scratch(B, ar):
    s2 = {}
    s2["sq8"] = ar.take_bf16(NFC * TT)
    rows = ar.take_f32(2 * TT)
    s2["mrow"] = rows[0:1, 0:TT]
    s2["vrow"] = rows[0:1, TT:2 * TT]
    rows2 = ar.take_bf16(0)
    rows3 = ar.take_f32(2 * TT)
    s2["arow"] = rows3[0:1, 0:TT]
    s2["brow"] = rows3[0:1, TT:2 * TT]
    s2["bc_a"] = ar.take_f32(TT)
    s2["bc_b"] = ar.take_f32(TT)
    return s2


def phase_fence(B):
    for s in ("mm", "ev", "sc"):
        B.wait("sync", B.sems[s], B.sems[s].n)
    for s in ("mm", "ev"):
        B.wait("gpsimd", B.sems[s], B.sems[s].n)



def emit_rms_group(B, g, s2, x_tiles, out_grp, gcols, t0, GT, eps, pbS_row, pbBC,
                   out_tiles=None):
    mm, ev, sc = B.sems["mm"], B.sems["ev"], B.sems["sc"]
    W = GT
    assert W <= TT
    ts = slice(t0, t0 + W)
    B.wait("scalar", mm, mm.n)
    for c in range(NFC):
        B.emit("scalar", lambda e, i=x_tiles[c][:, ts], o=s2["sq8"][:, c * W:(c + 1) * W]:
               e.activation(out=o, in_=i, func=AF.Square), sig=sc if c == NFC - 1 else None)
    B.wait("tensor", sc, sc.n)
    B.wait("tensor", ev, ev.n)
    for c in range(NFC):
        B.emit("tensor", lambda e, r=s2["sq8"][:, c * W:(c + 1) * W], o=pbS_row[0:1, 0:W],
               l=g["ones_col"][:], s=(c == 0), st=(c == NFC - 1):
               e.matmul(out=o, lhsT=l, rhs=r, start=s, stop=st),
               sig=mm if c == NFC - 1 else None)
    B.wait("scalar", mm, mm.n)
    arow = s2["arow"]
    B.emit("scalar", lambda e, o=arow[0:1, 0:W], i=pbS_row[0:1, 0:W]:
           e.activation(out=o, in_=i, func=AF.Sqrt, scale=1.0 / D, bias=cval1(g, float(eps))), sig=sc)
    B.wait("vector", sc, sc.n)
    B.emit("vector", lambda e, o=arow[0:1, 0:W]: e.reciprocal(out=o, in_=o), sig=ev)
    B.wait("tensor", ev, ev.n)
    B.emit("tensor", lambda e, r=arow[0:1, 0:W], o=pbBC[:, 0:W], l=g["ones_row"][:]:
           e.matmul(out=o, lhsT=l, rhs=r, start=True, stop=True), sig=mm)
    B.wait("vector", mm, mm.n)
    B.emit("vector", lambda e, o=s2["bc_a"][:, 0:W], i=pbBC[:, 0:W]:
           e.tensor_copy(out=o, in_=i), sig=ev)
    for c in range(NFC):
        dst = out_tiles[c][:, ts] if out_tiles is not None else out_grp[:, c * GT:c * GT + W]
        B.emit("vector", lambda e, o=dst,
               x=x_tiles[c][:, ts], gc=gcols[c], a=s2["bc_a"][:, 0:W]:
               e.scalar_tensor_tensor(out=o, in0=x, scalar=gc, in1=a,
                                      op0=ALU.mult, op1=ALU.mult),
               sig=ev if c == NFC - 1 else None)



GG = 256


def alloc_gla(B, ar, Twidth, o_buf, wt, S32, Sbf):
    ga = {}
    ga["o_buf"] = o_buf
    ga["wt"] = wt
    ga["S32"] = S32
    ga["Sbf"] = Sbf
    ga["hrms"] = ar.take_bf16(NFC * GG)
    nq = GG // P
    ga["qg"] = ar.take_bf16(nq * 1024)
    ga["kg"] = ar.take_bf16(nq * 1024)
    ga["vg"] = ar.take_bf16(nq * 1024)
    ga["lsg"] = ar.take_f32(nq * 1024)
    ga["Gc"] = ar.take_f32(1024)
    ga["cum"] = ar.take_f32(1024)
    ga["icum"] = ga["Gc"]
    ga["e2"] = ar.take_f32(1024)
    ga["decrow"] = ar.take_f32(1024, parts=2)
    ga["deccols"] = ar.take_f32(16)
    ga["dec"] = ar.take_f32(16)
    ga["qh"] = ar.take_bf16(1024)
    ga["kh"] = ar.take_bf16(1024)
    ga["k2"] = ar.take_bf16(1024)
    ga["qhT"] = ga["qh"]
    ga["khT"] = ga["kh"]
    ga["l2s"] = ar.take_f32(16)
    ga["oall"] = ar.take_bf16(512)
    ga["stil"] = [ar.take_bf16(512), ar.take_bf16(512)]
    return ga


def emit_gla(B, g, s2, ga, xn_tiles, gattn_cols, Twidth, dbg=None):
    phase_fence(B)
    mm, ev, sc = B.sems["mm"], B.sems["ev"], B.sems["sc"]
    cast = B.sems["wcast"]
    pb = s2["pb"]
    pbT = s2["pbT"]
    bI = [pb[0], pb[1]]
    bS = [pb[2], pb[3]]
    bB = [pb[4], pb[5]]
    bM = pb[6]
    wst = ga["wstage"]
    nq = GG // P

    for grp in range(Twidth // GG):
        t0 = grp * GG
        emit_rms_group(B, g, s2, xn_tiles, ga["hrms"], gattn_cols, t0, GG, EPS_RMS0,
                       bS[0], bB[0])
        if dbg and grp == 0:
            dbg("hrms", ga["hrms"][:, 0:1024])
            dbg("hrms2", ga["hrms"][:, 1024:2048])
        for pname, dram in (("zg", ga["wg_d"]), ("v", ga["wv_d"]),
                            ("k", ga["wk_d"]), ("q", ga["wq_d"])):
            B.wait("sync", mm, mm.n)
            emit_load_bf16(B, dram, ga["wt"], D, 1024, wst, None, cast)
            wdone = cast.n
            for st2 in range(nq):
                pp = [bI[0], bI[1]]
                B.wait("tensor", cast, wdone)
                B.wait("tensor", ev, ev.n)
                B.wait("tensor", sc, sc.n)
                for half in range(2):
                    for c in range(NFC):
                        B.emit("tensor", lambda e,
                               l=ga["hrms"][:, c * GG + st2 * P:c * GG + (st2 + 1) * P],
                               r=ga["wt"][:, c * 1024 + half * 512:c * 1024 + (half + 1) * 512],
                               o=pp[half][:, 0:512], s=(c == 0), st=(c == NFC - 1):
                               e.matmul(out=o, lhsT=l, rhs=r, start=s, stop=st),
                               sig=mm if c == NFC - 1 else None)
                mm_done = mm.n
                if pname == "v":
                    B.wait("vector", mm, mm_done)
                    for half in range(2):
                        B.emit("vector", lambda e, i=pp[half][:, 0:512],
                               o=ga["vg"][:, st2 * 1024 + half * 512:st2 * 1024 + (half + 1) * 512]:
                               e.tensor_copy(out=o, in_=i), sig=ev)
                elif pname == "zg":
                    B.wait("scalar", mm, mm_done)
                    sq = s2["sq8"]
                    for half in range(2):
                        B.emit("scalar", lambda e, i=pp[half][:, 0:512], o=sq[:, 0:512]:
                               e.activation(out=o, in_=i, func=AF.Abs))
                        B.emit("scalar", lambda e, i=sq[:, 0:512], o=sq[:, 512:1024]:
                               e.activation(out=o, in_=i, func=AF.Exp, scale=-1.0))
                        B.emit("scalar", lambda e, i=sq[:, 512:1024], o=sq[:, 0:512]:
                               e.activation(out=o, in_=i, func=AF.Ln, bias=cval(g, 1.0)), sig=sc)
                        B.wait("vector", sc, sc.n)
                        lg = ga["lsg"][:, st2 * 1024 + half * 512:st2 * 1024 + (half + 1) * 512]
                        B.emit("vector", lambda e, i=pp[half][:, 0:512], o=lg:
                               e.tensor_scalar(out=o, in0=i, scalar1=0.0, scalar2=None,
                                               op0=ALU.min))
                        B.emit("vector", lambda e, o=lg, t=sq[:, 0:512]:
                               e.tensor_tensor(out=o, in0=o, in1=t, op=ALU.subtract), sig=ev)
                        B.wait("scalar", ev, ev.n)
                else:
                    dst = ga["qg"] if pname == "q" else ga["kg"]
                    B.wait("scalar", mm, mm_done)
                    sq = s2["sq8"]
                    for half in range(2):
                        B.emit("scalar", lambda e, i=pp[half][:, 0:512],
                               o=sq[:, half * 512:(half + 1) * 512]:
                               e.activation(out=o, in_=i, func=AF.Square),
                               sig=sc if half == 1 else None)
                    B.wait("vector", sc, sc.n)
                    B.emit("vector", lambda e, o=ga["l2s"][:, 0:16],
                           i=sq[:, 0:1024].rearrange("p (h k) -> p h k", k=64):
                           e.tensor_reduce(out=o, in_=i, axis=AX.X, op=ALU.add), sig=ev)
                    B.wait("scalar", ev, ev.n)
                    B.emit("scalar", lambda e, o=ga["l2s"][:, 0:16], i=ga["l2s"][:, 0:16]:
                           e.activation(out=o, in_=i, func=AF.Sqrt), sig=sc)
                    B.wait("vector", sc, sc.n)
                    B.emit("vector", lambda e, o=ga["l2s"][:, 0:16]:
                           e.tensor_scalar(out=o, in0=o, scalar1=cval(g, 1e-12), scalar2=None,
                                           op0=ALU.max), sig=ev)
                    B.wait("vector", ev, ev.n)
                    B.emit("vector", lambda e, o=ga["l2s"][:, 0:16]:
                           e.reciprocal(out=o, in_=o), sig=ev)
                    B.wait("vector", ev, ev.n)
                    for h in range(NHEAD):
                        half = h // 8
                        B.emit("vector", lambda e, i=pp[half][:, (h % 8) * 64:(h % 8 + 1) * 64],
                               o=dst[:, st2 * 1024 + h * 64:st2 * 1024 + (h + 1) * 64],
                               r=ga["l2s"][:, h:h + 1]:
                               e.tensor_scalar(out=o, in0=i, scalar1=r, scalar2=None,
                                               op0=ALU.mult),
                               sig=ev if h == NHEAD - 1 else None)
        for st2 in range(nq):
            lgst = ga["lsg"][:, st2 * 1024:(st2 + 1) * 1024]
            B.wait("tensor", ev, ev.n)
            B.wait("tensor", sc, sc.n)
            for half in range(2):
                B.emit("tensor", lambda e, l=g["u64"][:], r=lgst[:, half * 512:(half + 1) * 512],
                       o=bI[half][:, 0:512]:
                       e.matmul(out=o, lhsT=l, rhs=r, start=True, stop=True),
                       sig=mm if half == 1 else None)
            B.wait("vector", mm, mm.n)
            for half in range(2):
                B.emit("vector", lambda e, i=bI[half][:, 0:512],
                       o=ga["Gc"][:, half * 512:(half + 1) * 512]:
                       e.tensor_copy(out=o, in_=i), sig=ev if half == 1 else None)
            B.wait("tensor", ev, ev.n)
            for half in range(2):
                B.emit("tensor", lambda e, l=g["ohend"][:], r=ga["Gc"][:, half * 512:(half + 1) * 512],
                       o=bS[half][0:2, 0:512]:
                       e.matmul(out=o, lhsT=l, rhs=r, start=True, stop=True))
            for c in range(NFC):
                B.emit("tensor", lambda e, l=ga["Gc"][:, c * P:(c + 1) * P], r=g["ohend"][:],
                       o=bM[0:P, c * 2:(c + 1) * 2]:
                       e.matmul(out=o, lhsT=l, rhs=r, start=True, stop=True),
                       sig=mm if c == NFC - 1 else None)
            B.wait("vector", mm, mm.n)
            for half in range(2):
                B.emit("vector", lambda e, i=bS[half][0:2, 0:512],
                       o=ga["decrow"][0:2, half * 512:(half + 1) * 512]:
                       e.tensor_copy(out=o, in_=i))
            B.emit("vector", lambda e, i=bM[0:P, 0:16], o=ga["deccols"][:, 0:16]:
                   e.tensor_copy(out=o, in_=i), sig=ev)
            B.wait("scalar", ev, ev.n)
            B.emit("scalar", lambda e, i=ga["deccols"][:, 0:16], o=ga["dec"][:, 0:16]:
                   e.activation(out=o, in_=i, func=AF.Exp), sig=sc)
            if dbg and grp == 0 and st2 == 0:
                dbg("Gc", ga["Gc"][:, 0:1024])
                dbg("lsg", ga["lsg"][:, 0:1024])
            B.wait("tensor", ev, ev.n)
            for half in range(2):
                B.emit("tensor", lambda e, l=g["subind"][:],
                       r=ga["decrow"][0:2, half * 512:(half + 1) * 512],
                       o=bB[half][:, 0:512]:
                       e.matmul(out=o, lhsT=l, rhs=r, start=True, stop=True),
                       sig=mm if half == 1 else None)
            B.wait("vector", mm, mm.n)
            for half in range(2):
                B.emit("vector", lambda e, o=ga["e2"][:, half * 512:(half + 1) * 512],
                       i0=ga["Gc"][:, half * 512:(half + 1) * 512], i1=bB[half][:, 0:512]:
                       e.scalar_tensor_tensor(out=o, in0=i0, scalar=-1.0, in1=i1,
                                              op0=ALU.mult, op1=ALU.add),
                       sig=ev if half == 1 else None)
            B.wait("scalar", ev, ev.n)
            B.emit("scalar", lambda e, o=ga["e2"][:, 0:1024], i=ga["e2"][:, 0:1024]:
                   e.activation(out=o, in_=i, func=AF.Exp))
            B.emit("scalar", lambda e, o=ga["cum"][:, 0:1024], i=ga["Gc"][:, 0:1024]:
                   e.activation(out=o, in_=i, func=AF.Exp))
            B.emit("scalar", lambda e, o=ga["icum"][:, 0:1024], i=ga["Gc"][:, 0:1024]:
                   e.activation(out=o, in_=i, func=AF.Exp, scale=-1.0), sig=sc)
            if dbg and grp == 0 and st2 == 0:
                dbg("cum", ga["cum"][:, 0:1024], after_sc=True)
                dbg("e2", ga["e2"][:, 0:1024], after_sc=True)
            B.wait("vector", sc, sc.n)
            qsl = ga["qg"][:, st2 * 1024:(st2 + 1) * 1024]
            ksl = ga["kg"][:, st2 * 1024:(st2 + 1) * 1024]
            B.emit("vector", lambda e, o=ga["qh"][:, 0:1024], a=qsl, b=ga["cum"][:, 0:1024]:
                   e.tensor_tensor(out=o, in0=a, in1=b, op=ALU.mult))
            B.emit("vector", lambda e, o=ga["kh"][:, 0:1024], a=ksl, b=ga["icum"][:, 0:1024]:
                   e.tensor_tensor(out=o, in0=a, in1=b, op=ALU.mult))
            B.emit("vector", lambda e, o=ga["k2"][:, 0:1024], a=ksl, b=ga["e2"][:, 0:1024]:
                   e.tensor_tensor(out=o, in0=a, in1=b, op=ALU.mult), sig=ev)
            if dbg and grp == 0 and st2 == 0:
                dbg("qg", ga["qg"][:, 0:1024])
                dbg("kg", ga["kg"][:, 0:1024])
                dbg("vg", ga["vg"][:, 0:1024])
                dbg("qh", ga["qh"][:, 0:1024])
                dbg("kh", ga["kh"][:, 0:1024])
                dbg("k2", ga["k2"][:, 0:1024])
            B.wait("tensor", ev, ev.n)
            for c in range(NFC):
                B.emit("tensor", lambda e, i=ga["qh"][:, c * P:(c + 1) * P],
                       o=pbT[:, c * P:(c + 1) * P]:
                       e.transpose(out=o, in_=i, identity=g["ident"][:]),
                       sig=mm if c == NFC - 1 else None)
            B.wait("vector", mm, mm.n)
            B.emit("vector", lambda e, i=pbT[:, 0:1024], o=ga["qhT"][:, 0:1024]:
                   e.tensor_copy(out=o, in_=i), sig=ev)
            B.wait("tensor", ev, ev.n)
            for c in range(NFC):
                B.emit("tensor", lambda e, i=ga["kh"][:, c * P:(c + 1) * P],
                       o=pbT[:, c * P:(c + 1) * P]:
                       e.transpose(out=o, in_=i, identity=g["ident"][:]),
                       sig=mm if c == NFC - 1 else None)
            B.wait("vector", mm, mm.n)
            B.emit("vector", lambda e, i=pbT[:, 0:1024], o=ga["khT"][:, 0:1024]:
                   e.tensor_copy(out=o, in_=i), sig=ev)
            if dbg and grp == 0 and st2 == 0:
                dbg("qhT", ga["qhT"][:, 0:1024])
                dbg("khT", ga["khT"][:, 0:1024])
            for cc in range(2):
                bc = cc * 64
                B.wait("tensor", ev, ev.n)
                for h in range(NHEAD):
                    ph = (h % 2) * 64
                    fc = h // 2
                    B.emit("tensor", lambda e,
                           l=ga["khT"][ph:ph + 64, fc * P + bc:fc * P + bc + 64],
                           r=ga["qhT"][ph:ph + 64, fc * P + bc:fc * P + bc + 64],
                           o=bS[h % 2][bc:bc + 64, fc * 64:(fc + 1) * 64], tp=(ph, bc):
                           e.matmul(out=o, lhsT=l, rhs=r, start=True, stop=True,
                                    tile_position=tp),
                           sig=mm if h == NHEAD - 1 else None)
                B.wait("vector", mm, mm.n)
                for par in range(2):
                    B.emit("vector", lambda e, i=bS[par][bc:bc + 64, 0:512],
                           o=ga["stil"][par][bc:bc + 64, 0:512], mk=g["mask64"][bc:bc + 64, :]:
                           e.tensor_tensor(out=o, in0=i, in1=mk, op=ALU.mult),
                           sig=ev if par == 1 else None)
                B.wait("tensor", ev, ev.n)
                for h in range(NHEAD):
                    ph = (h % 2) * 64
                    fc = h // 2
                    B.emit("tensor", lambda e,
                           l=ga["vg"][bc:bc + 64, st2 * 1024 + h * 64:st2 * 1024 + (h + 1) * 64],
                           r=ga["stil"][h % 2][bc:bc + 64, fc * 64:(fc + 1) * 64],
                           o=bI[cc][ph:ph + 64, fc * 64:(fc + 1) * 64], tp=(bc, ph):
                           e.matmul(out=o, lhsT=l, rhs=r, start=True, stop=True,
                                    tile_position=tp))
                    B.emit("tensor", lambda e,
                           l=ga["Sbf"][ph:ph + 64, fc * 64:(fc + 1) * 64],
                           r=ga["qhT"][ph:ph + 64, fc * P + bc:fc * P + bc + 64],
                           o=bB[h % 2][ph:ph + 64, fc * 64:(fc + 1) * 64], tp=(ph, ph):
                           e.matmul(out=o, lhsT=l, rhs=r, start=True, stop=True,
                                    tile_position=tp))
                    B.emit("tensor", lambda e,
                           l=ga["k2"][bc:bc + 64, h * 64:(h + 1) * 64],
                           r=ga["vg"][bc:bc + 64, st2 * 1024 + h * 64:st2 * 1024 + (h + 1) * 64],
                           o=bM[ph:ph + 64, fc * 64:(fc + 1) * 64], tp=(bc, ph):
                           e.matmul(out=o, lhsT=l, rhs=r, start=True, stop=True,
                                    tile_position=tp),
                           sig=mm if h == NHEAD - 1 else None)
                B.wait("vector", mm, mm.n)
                B.emit("vector", lambda e, i=bI[cc][:, 0:512], o=ga["oall"][:, 0:512]:
                       e.tensor_copy(out=o, in_=i))
                B.emit("vector", lambda e, o=ga["oall"][0:64, 0:512], i=bB[0][0:64, 0:512]:
                       e.tensor_tensor(out=o, in0=o, in1=i, op=ALU.add))
                B.emit("vector", lambda e, o=ga["oall"][64:128, 0:512], i=bB[1][64:128, 0:512]:
                       e.tensor_tensor(out=o, in0=o, in1=i, op=ALU.add))
                tglob = t0 + st2 * P + bc
                B.emit("vector", lambda e,
                       o=ga["o_buf"].rearrange("p (f t) -> p f t", f=NFC)[:, :, tglob:tglob + 64],
                       i=ga["oall"][:, 0:512].rearrange("p (f t) -> p f t", f=NFC):
                       e.tensor_copy(out=o, in_=i), sig=ev)
                for fc in range(NFC):
                    B.emit("vector", lambda e, o=ga["S32"][:, fc * 64:(fc + 1) * 64],
                           r=ga["dec"][:, fc * 2 + cc:fc * 2 + cc + 1]:
                           e.tensor_scalar(out=o, in0=o, scalar1=r, scalar2=None,
                                           op0=ALU.mult))
                B.emit("vector", lambda e, o=ga["S32"][:, 0:512], i=bM[:, 0:512]:
                       e.tensor_tensor(out=o, in0=o, in1=i, op=ALU.add))
                B.emit("vector", lambda e, o=ga["Sbf"][:, 0:512], i=ga["S32"][:, 0:512]:
                       e.tensor_copy(out=o, in_=i), sig=ev)
                if dbg and grp == 0 and st2 == 0:
                    dbg(f"stil{cc}", ga["stil"][0][:, 0:512])
                    dbg(f"S32c{cc}", ga["S32"][:, 0:512])



HC3 = (DFF0 + P - 1) // P


def emit_wo_swiglu(B, g, s2, ar, xn_tiles, acc_tiles, o_buf, wo_d, w1_d, w2_d, w3_d,
                   gffn_cols, Twidth, wt, wst):
    phase_fence(B)
    mm, ev, sc = B.sems["mm"], B.sems["ev"], B.sems["sc"]
    cast = B.sems["wcast"]
    pb = s2["pb"]
    amark = ar.mark()
    xmid = ar.take_bf16(NFC * TT)
    hgrp = ar.take_bf16(NFC * TT)
    gact = ar.take_bf16(HC3 * TT)
    w1c = [ar.take_bf16(NFC * P), ar.take_bf16(NFC * P)]
    w2c = [ar.take_bf16(NFC * P), ar.take_bf16(NFC * P)]
    w3m = ar.take_bf16(HC3 * P)
    sact = [ar.take_f32(TT), ar.take_f32(TT)]
    col_done = {}
    stt_done = {}
    last_w3 = [0]
    for it in range(Twidth // TT):
        ts = slice(it * TT, (it + 1) * TT)
        B.wait("sync", mm, mm.n)
        emit_load_bf16(B, wo_d, wt, D, 1024, wst, None, cast)
        B.wait("tensor", cast, cast.n)
        for m in range(NFC):
            pd = pb[m % 2]
            B.wait("tensor", ev, ev.n)
            for c in range(NFC):
                B.emit("tensor", lambda e, l=wt[:, c * 1024 + m * P:c * 1024 + (m + 1) * P],
                       r=o_buf[:, c * Twidth:(c + 1) * Twidth][:, ts], o=pd[:, 0:TT],
                       s=(c == 0), st=(c == NFC - 1):
                       e.matmul(out=o, lhsT=l, rhs=r, start=s, stop=st),
                       sig=mm if c == NFC - 1 else None)
            B.wait("vector", mm, mm.n)
            B.emit("vector", lambda e, o=xmid[:, m * TT:(m + 1) * TT], a=xn_tiles[m][:, ts],
                   i=pd[:, 0:TT]: e.tensor_tensor(out=o, in0=a, in1=i, op=ALU.add), sig=ev)
            B.emit("vector", lambda e, o=acc_tiles[m][:, ts], i=xmid[:, m * TT:(m + 1) * TT]:
                   e.tensor_tensor(out=o, in0=o, in1=i, op=ALU.add), sig=ev)
        xm_tiles = [xmid.rearrange("p (f t) -> p f t", f=NFC)[:, c, :] for c in range(NFC)]
        emit_rms_group(B, g, s2, xm_tiles, hgrp, gffn_cols, 0, TT, EPS_RMS0, pb[2], pb[4])
        for hc in range(HC3):
            w = min(P, DFF0 - hc * P)
            par = hc % 2
            key = it * HC3 + hc
            if key >= 2:
                B.wait("sync", mm, col_done[key - 2])
            emit_load_bf16(B, w1_d[:, hc * P:hc * P + w], w1c[par], D, w, wst, None, cast)
            emit_load_bf16(B, w2_d[:, hc * P:hc * P + w], w2c[par], D, w, wst, None, cast)
            pa = pb[2] if par == 0 else pb[3]
            pbk = pb[4] if par == 0 else pb[5]
            B.wait("tensor", cast, cast.n)
            B.wait("tensor", sc, sc.n)
            B.wait("tensor", ev, ev.n)
            for c in range(NFC):
                B.emit("tensor", lambda e, l=w1c[par][:, c * w:(c + 1) * w],
                       r=hgrp[:, c * TT:(c + 1) * TT], o=pa[0:w, 0:TT],
                       s=(c == 0), st=(c == NFC - 1):
                       e.matmul(out=o, lhsT=l, rhs=r, start=s, stop=st),
                       sig=mm if c == NFC - 1 else None)
            for c in range(NFC):
                B.emit("tensor", lambda e, l=w2c[par][:, c * w:(c + 1) * w],
                       r=hgrp[:, c * TT:(c + 1) * TT], o=pbk[0:w, 0:TT],
                       s=(c == 0), st=(c == NFC - 1):
                       e.matmul(out=o, lhsT=l, rhs=r, start=s, stop=st),
                       sig=mm if c == NFC - 1 else None)
            col_done[key] = mm.n
            B.wait("scalar", mm, mm.n - 1)
            if key >= 2:
                B.wait("scalar", ev, stt_done[key - 2])
            B.emit("scalar", lambda e, i=pa[0:w, 0:TT], o=sact[par].bitcast(F32)[0:w, :] if False
                   else sact[par][0:w, 0:TT] if False else sact[par][0:w, :]:
                   e.activation(out=o, in_=i, func=AF.Silu), sig=sc)
            B.wait("vector", mm, mm.n)
            B.wait("vector", sc, sc.n)
            B.emit("vector", lambda e, i0=pbk[0:w, 0:TT], i1=sact[par][0:w, :],
                   o=gact[0:w, hc * TT:(hc + 1) * TT]:
                   e.tensor_tensor(out=o, in0=i0, in1=i1, op=ALU.mult), sig=ev)
            stt_done[key] = ev.n
        for m in range(NFC):
            if last_w3[0] > 0:
                B.wait("sync", mm, last_w3[0])
            emit_load_bf16(B, w3_d[:, m * P:(m + 1) * P], w3m, DFF0, P, wst, None, cast)
            pd = pb[m % 2]
            B.wait("tensor", cast, cast.n)
            B.wait("tensor", ev, ev.n)
            for hc in range(HC3):
                w = min(P, DFF0 - hc * P)
                B.emit("tensor", lambda e, l=w3m[0:w, hc * P:(hc + 1) * P],
                       r=gact[0:w, hc * TT:(hc + 1) * TT], o=pd[:, 0:TT],
                       s=(hc == 0), st=(hc == HC3 - 1):
                       e.matmul(out=o, lhsT=l, rhs=r, start=s, stop=st),
                       sig=mm if hc == HC3 - 1 else None)
            last_w3[0] = mm.n
            B.wait("vector", mm, mm.n)
            B.emit("vector", lambda e, i=pd[:, 0:TT], a=acc_tiles[m][:, ts]:
                   e.tensor_tensor(out=a, in0=a, in1=i, op=ALU.add), sig=ev)
    ar.reset(amark)



NEC = DIN // P
NS = 16


def emit_halo_norm(B, g, s2, halos, halosb, xrh, nw_cols, rms3bc):
    mm, ev, sc = B.sems["mm"], B.sems["ev"], B.sems["sc"]
    pb = s2["pb"]
    W3 = 3
    B.emit("vector", lambda e, o=halosb[:, 0:24], i=halos[:, 0:24]:
           e.tensor_copy(out=o, in_=i), sig=ev)
    B.wait("scalar", ev, ev.n)
    B.wait("scalar", mm, mm.n)
    for c in range(NFC):
        B.emit("scalar", lambda e, i=halos[:, c * 3:(c + 1) * 3], o=s2["sq8"][:, c * 4:c * 4 + 3]:
               e.activation(out=o, in_=i, func=AF.Square), sig=sc if c == NFC - 1 else None)
    B.wait("tensor", sc, sc.n)
    B.wait("tensor", ev, ev.n)
    B.wait("tensor", ev, ev.n)
    for c in range(NFC):
        B.emit("tensor", lambda e, r=halosb[:, c * 3:(c + 1) * 3], o=s2["pbS"][0:1, 0:3],
               l=g["ones_col"][:], s=(c == 0), st=(c == NFC - 1):
               e.matmul(out=o, lhsT=l, rhs=r, start=s, stop=st))
        B.emit("tensor", lambda e, r=s2["sq8"][:, c * 4:c * 4 + 3], o=s2["pbS"][32:33, 0:3],
               l=g["ones_col"][:], s=(c == 0), st=(c == NFC - 1):
               e.matmul(out=o, lhsT=l, rhs=r, start=s, stop=st),
               sig=mm if c == NFC - 1 else None)
    B.wait("vector", mm, mm.n)
    mrow, vrow, arow, brow = s2["mrow"], s2["vrow"], s2["arow"], s2["brow"]
    B.emit("vector", lambda e, o=mrow[0:1, 0:3], i=s2["pbS"][0:1, 0:3]:
           e.tensor_scalar(out=o, in0=i, scalar1=1.0 / D, scalar2=None, op0=ALU.mult))
    B.emit("vector", lambda e, o=vrow[0:1, 0:3], i=s2["pbS"][32:33, 0:3]:
           e.tensor_scalar(out=o, in0=i, scalar1=1.0 / D, scalar2=None, op0=ALU.mult), sig=ev)
    B.emit("vector", lambda e, o=brow[0:1, 0:3], m=mrow[0:1, 0:3]:
           e.tensor_tensor(out=o, in0=m, in1=m, op=ALU.mult))
    B.emit("vector", lambda e, o=vrow[0:1, 0:3], t=brow[0:1, 0:3]:
           e.tensor_tensor(out=o, in0=o, in1=t, op=ALU.subtract), sig=ev)
    B.wait("scalar", ev, ev.n)
    B.emit("scalar", lambda e, o=arow[0:1, 0:3], i=vrow[0:1, 0:3]:
           e.activation(out=o, in_=i, func=AF.Sqrt, bias=cval1(g, float(EPS_LN))), sig=sc)
    B.wait("vector", sc, sc.n)
    B.emit("vector", lambda e, o=arow[0:1, 0:3]: e.reciprocal(out=o, in_=o), sig=ev)
    B.wait("vector", ev, ev.n)
    B.emit("vector", lambda e, o=brow[0:1, 0:3], m=mrow[0:1, 0:3], a=arow[0:1, 0:3]:
           e.scalar_tensor_tensor(out=o, in0=m, scalar=-1.0, in1=a,
                                  op0=ALU.mult, op1=ALU.mult), sig=ev)
    B.wait("tensor", ev, ev.n)
    B.emit("tensor", lambda e, r=arow[0:1, 0:3], o=s2["pb"][1][:, 0:3], l=g["ones_row"][:]:
           e.matmul(out=o, lhsT=l, rhs=r, start=True, stop=True))
    B.emit("tensor", lambda e, r=brow[0:1, 0:3], o=s2["pb"][2][:, 0:3], l=g["ones_row"][:]:
           e.matmul(out=o, lhsT=l, rhs=r, start=True, stop=True), sig=mm)
    B.wait("vector", mm, mm.n)
    xnh = s2["bc_a"]
    for c in range(NFC):
        B.emit("vector", lambda e, o=xnh[:, c * 3:(c + 1) * 3], x=halos[:, c * 3:(c + 1) * 3],
               a=s2["pb"][1][:, 0:3]:
               e.tensor_tensor(out=o, in0=x, in1=a, op=ALU.mult))
        B.emit("vector", lambda e, o=xnh[:, c * 3:(c + 1) * 3], b=s2["pb"][2][:, 0:3]:
               e.tensor_tensor(out=o, in0=o, in1=b, op=ALU.add),
               sig=ev if c == NFC - 1 else None)
    B.wait("scalar", ev, ev.n)
    for c in range(NFC):
        B.emit("scalar", lambda e, i=xnh[:, c * 3:(c + 1) * 3], o=s2["sq8"][:, c * 4:c * 4 + 3]:
               e.activation(out=o, in_=i, func=AF.Square), sig=sc if c == NFC - 1 else None)
    B.wait("tensor", sc, sc.n)
    for c in range(NFC):
        B.emit("tensor", lambda e, r=s2["sq8"][:, c * 4:c * 4 + 3], o=s2["pbS"][0:1, 0:3],
               l=g["ones_col"][:], s=(c == 0), st=(c == NFC - 1):
               e.matmul(out=o, lhsT=l, rhs=r, start=s, stop=st),
               sig=mm if c == NFC - 1 else None)
    B.wait("scalar", mm, mm.n)
    B.emit("scalar", lambda e, o=arow[0:1, 0:3], i=s2["pbS"][0:1, 0:3]:
           e.activation(out=o, in_=i, func=AF.Sqrt, scale=1.0 / D, bias=cval1(g, float(EPS_RMS1))), sig=sc)
    B.wait("vector", sc, sc.n)
    B.emit("vector", lambda e, o=arow[0:1, 0:3]: e.reciprocal(out=o, in_=o), sig=ev)
    B.wait("tensor", ev, ev.n)
    B.emit("tensor", lambda e, r=arow[0:1, 0:3], o=s2["pb"][1][:, 0:3], l=g["ones_row"][:]:
           e.matmul(out=o, lhsT=l, rhs=r, start=True, stop=True), sig=mm)
    B.wait("vector", mm, mm.n)
    B.emit("vector", lambda e, o=rms3bc[:, 0:3], i=s2["pb"][1][:, 0:3]:
           e.tensor_copy(out=o, in_=i), sig=ev)
    for c in range(NFC):
        B.emit("vector", lambda e, o=xrh[:, c * 3:(c + 1) * 3], x=xnh[:, c * 3:(c + 1) * 3]:
               e.tensor_copy(out=o, in_=x),
               sig=ev if c == NFC - 1 else None)


def build_l1_kernel():
    nc = bass.Bass("TRN2", target_bir_lowering=False, debug=False)
    r_d = nc.dram_tensor("resid", [D, T], F32, kind="ExternalInput").ap()
    x1h_d = nc.dram_tensor("x1halo", [P, NEC * 3], F32, kind="ExternalInput").ap()
    hin_d = nc.dram_tensor("hinit", [P, NEC * NS], F32, kind="ExternalInput").ap()
    nw_d = nc.dram_tensor("normw", [D, 1], F32, kind="ExternalInput").ap()
    skT_d = nc.dram_tensor("skipT", [D, DIN], F32, kind="ExternalInput").ap()
    inT_d = nc.dram_tensor("inT", [D, DIN], F32, kind="ExternalInput").ap()
    cw_d = nc.dram_tensor("convw", [DIN, 4], F32, kind="ExternalInput").ap()
    cb_d = nc.dram_tensor("convb", [DIN, 1], F32, kind="ExternalInput").ap()
    wd1T_d = nc.dram_tensor("wd1T", [DIN, 64], F32, kind="ExternalInput").ap()
    wd2T_d = nc.dram_tensor("wd2T", [64, DIN], F32, kind="ExternalInput").ap()
    wd2b_d = nc.dram_tensor("wd2b", [DIN, 1], F32, kind="ExternalInput").ap()
    wbT_d = nc.dram_tensor("wbT", [DIN, NS], F32, kind="ExternalInput").ap()
    wcT_d = nc.dram_tensor("wcT", [DIN, NS], F32, kind="ExternalInput").ap()
    al_d = nc.dram_tensor("alog", [DIN, NS], F32, kind="ExternalInput").ap()
    wdd_d = nc.dram_tensor("wD", [DIN, 1], F32, kind="ExternalInput").ap()
    outT_d = nc.dram_tensor("outT", [DIN, D], F32, kind="ExternalInput").ap()
    mWg_d = nc.dram_tensor("mWg", [D, 8], F32, kind="ExternalInput").ap()
    mWu_d = nc.dram_tensor("mWu", [8 * D, DIN], F32, kind="ExternalInput").ap()
    mWd_d = nc.dram_tensor("mWd", [8 * DIN, D], F32, kind="ExternalInput").ap()
    mW1_d = nc.dram_tensor("mW1", [D, DIN], F32, kind="ExternalInput").ap()
    mW2_d = nc.dram_tensor("mW2", [DIN, D], F32, kind="ExternalInput").ap()
    y_d = nc.dram_tensor("y", [D, T], F32, kind="ExternalOutput").ap()
    yh_d = nc.dram_tensor("yh", [P, NEC * NS], F32, kind="ExternalOutput").ap()
    yx1h_d = nc.dram_tensor("yx1h", [P, NEC * 3], F32, kind="ExternalOutput").ap()
    bsc_d = nc.dram_tensor("bsc", [NS, T], F32, kind="Internal").ap()
    csc_d = nc.dram_tensor("csc", [NS, T], F32, kind="Internal").ap()
    cwsc_d = nc.dram_tensor("cwsc", [8, T], F32, kind="Internal").ap()

    import os as _os
    _dbg = _os.environ.get("L1DBG", "0") == "1"
    if _dbg:
        dmp_d = nc.dram_tensor("dmp", [8 * P, 4096], F32, kind="ExternalOutput").ap()
    SLL = 256
    ctx = ExitStack()
    with ctx:
        B = Builder(nc, ctx)
        for s in ("mm", "ev", "sc", "wld", "wcast", "consts", "st", "gld", "cwdma"):
            B.sem(s)
        bdma = [B.sem("bdma0"), B.sem("bdma1")]
        g = alloc_consts(B)
        ar = Arena(B, 35100)
        s2 = {}
        s2["pbS"] = ptensor(B, "pbS", [P, TT])
        s2["pb"] = [ptensor(B, f"pb{i}", [P, TT]) for i in range(6)]
        s2["pb"].append(s2["pbS"])
        s2["pbT"] = ptensor(B, "pbT", [P, 1024], BF16)
        pb = s2["pb"]
        bufA = sbuf(B, "bufA", [P, NFC * T], BF16)
        bufB = sbuf(B, "bufB", [P, NFC * T], BF16)
        acc_tiles = [bufA[:, c * T:(c + 1) * T] for c in range(NFC)]
        xn_tiles = [bufB[:, c * T:(c + 1) * T] for c in range(NFC)]
        smalls = sbuf(B, "smalls", [P, 480], F32)
        off = [0]
        def sm(n):
            a = smalls[:, off[0]:off[0] + n]
            off[0] += n
            return a
        nwcols = sm(8); convw = sm(64); convb = sm(16); wd2b = sm(16)
        wDc = sm(16); Acols = sm(256); x1h = sm(48); x1hk = sm(48)
        hbuf = sbuf(B, "hbuf", [P, NEC * NS], F32)

        ld, cast, gld = B.sems["wld"], B.sems["wcast"], B.sems["gld"]
        mm, ev, sc = B.sems["mm"], B.sems["ev"], B.sems["sc"]
        st = B.sems["st"]
        wstage = Stage(B, ar.take_f32(2 * 1024), cast, "wst")
        xstage = Stage(B, ar.take_f32(2 * 1024), cast, "xst")
        dcore = ar.take_bf16(T)
        bcstg = ar.take_f32(TT, parts=NS)
        rms_bc = ar.take_f32(T)
        xs = ar.take_bf16_top(NEC * T)
        xs_ec = [xs[:, e * T:(e + 1) * T] for e in range(NEC)]
        mark0 = ar.mark()
        emit_consts(B, g, B.sems["consts"])
        for c in range(NFC):
            for h in range(2):
                par, ssl = xstage.slot(1024)
                if xstage.lastc[par] > 0:
                    B.wait("sync", cast, xstage.lastc[par])
                sld = xstage.ld[par]
                B.emit("sync", lambda e, s=ssl, d=r_d[c * P:(c + 1) * P, h * 1024:(h + 1) * 1024]:
                       e.dma_start(out=s, in_=d), sig=sld, sigk=16)
                B.wait("gpsimd", sld, sld.n)
                B.emit("gpsimd", lambda e, s=ssl, o=acc_tiles[c][:, h * 1024:(h + 1) * 1024]:
                       e.tensor_copy(out=o, in_=s), sig=cast)
                xstage.lastc[par] = cast.n
        def ldsmall(dst, dram, rearr=True):
            if rearr:
                w = dram.shape[1]
                src_ap = dram.rearrange("(c p) w -> p c w", p=P)
                dstv = dst.rearrange("p (c w) -> p c w", w=w)
            else:
                src_ap, dstv = dram, dst
            B.emit("sync", lambda e, o=dstv, i=src_ap:
                   e.dma_start(out=o, in_=i, allow_slow_non_contiguous=True), sig=gld, sigk=16)
        ldsmall(nwcols, nw_d); ldsmall(convw, cw_d); ldsmall(convb, cb_d)
        ldsmall(wd2b, wd2b_d); ldsmall(wDc, wdd_d); ldsmall(Acols, al_d)
        ldsmall(x1h, x1h_d, rearr=False); ldsmall(hbuf[:, :], hin_d, rearr=False)
        B.wait("scalar", gld, gld.n)
        B.emit("scalar", lambda e, o=Acols, i=Acols:
               e.activation(out=o, in_=i, func=AF.Exp), sig=sc)
        B.wait("vector", sc, sc.n)
        B.emit("vector", lambda e, o=Acols:
               e.tensor_scalar(out=o, in0=Acols, scalar1=-1.0, scalar2=None, op0=ALU.mult), sig=ev)
        for e in ("tensor", "vector", "scalar"):
            B.wait(e, cast, cast.n)
            B.wait(e, B.sems["consts"], B.sems["consts"].n)
            B.wait(e, gld, gld.n)
            B.wait(e, ev, ev.n)
        if _dbg:
            dst = xstage.ap
            def dump(slot, ap, w):
                for j0 in range(0, w, 2048):
                    wj = min(2048, w - j0)
                    B.wait("vector", B.sems["ev"], B.sems["ev"].n)
                    B.wait("vector", B.sems["mm"], B.sems["mm"].n)
                    B.wait("vector", B.sems["sc"], B.sems["sc"].n)
                    B.wait("vector", B.sems["st"], B.sems["st"].n)
                    B.emit("vector", lambda e, o=dst[0:ap.shape[0], 0:wj], i=ap[:, j0:j0 + wj]:
                           e.tensor_copy(out=o, in_=i), sig=B.sems["ev"])
                    B.wait("sync", B.sems["ev"], B.sems["ev"].n)
                    B.emit("sync", lambda e, o=dmp_d[slot * P:slot * P + ap.shape[0], j0:j0 + wj],
                           i=dst[0:ap.shape[0], 0:wj]: e.dma_start(out=o, in_=i),
                           sig=B.sems["st"], sigk=16)
                    for en in ("vector", "tensor", "scalar", "gpsimd"):
                        B.wait(en, B.sems["st"], B.sems["st"].n)
            build_l1_kernel._dump = dump
        s2x = alloc_ln_scratch(B, ar)
        for k in ("sq8", "mrow", "vrow", "arow", "brow", "bc_a", "bc_b"):
            s2[k] = s2x[k]
        emit_ln(B, g, s2, acc_tiles, xn_tiles, T)
        nw_cols = [nwcols[:, c:c + 1] for c in range(NFC)]
        for it in range(NTT):
            ts = slice(it * TT, (it + 1) * TT)
            B.wait("scalar", mm, mm.n)
            for c in range(NFC):
                B.emit("scalar", lambda e, i=xn_tiles[c][:, ts], o=s2["sq8"][:, c * TT:(c + 1) * TT]:
                       e.activation(out=o, in_=i, func=AF.Square), sig=sc if c == NFC - 1 else None)
            B.wait("tensor", sc, sc.n)
            B.wait("tensor", ev, ev.n)
            for c in range(NFC):
                B.emit("tensor", lambda e, r=s2["sq8"][:, c * TT:(c + 1) * TT], o=pb[6][0:1, 0:TT],
                       l=g["ones_col"][:], s=(c == 0), st=(c == NFC - 1):
                       e.matmul(out=o, lhsT=l, rhs=r, start=s, stop=st),
                       sig=mm if c == NFC - 1 else None)
            B.wait("scalar", mm, mm.n)
            B.emit("scalar", lambda e, o=s2["arow"][0:1, 0:TT], i=pb[6][0:1, 0:TT]:
                   e.activation(out=o, in_=i, func=AF.Sqrt, scale=1.0 / D,
                                bias=cval1(g, float(EPS_RMS1))), sig=sc)
            B.wait("vector", sc, sc.n)
            B.emit("vector", lambda e, o=s2["arow"][0:1, 0:TT]:
                   e.reciprocal(out=o, in_=s2["arow"][0:1, 0:TT]), sig=ev)
            B.wait("tensor", ev, ev.n)
            B.emit("tensor", lambda e, r=s2["arow"][0:1, 0:TT], o=pb[4][:, 0:TT],
                   l=g["ones_row"][:]:
                   e.matmul(out=o, lhsT=l, rhs=r, start=True, stop=True), sig=mm)
            B.wait("vector", mm, mm.n)
            B.emit("vector", lambda e, o=rms_bc[:, ts], i=pb[4][:, 0:TT]:
                   e.tensor_copy(out=o, in_=i), sig=ev)

        wcol = [ar.take_bf16(NFC * P), ar.take_bf16(NFC * P)]
        x1e = ar.take_bf16(2056)
        cvt = ar.take_f32(T)
        for ec in range(NEC):
            par = ec % 2
            B.wait("sync", mm, mm.n)
            emit_load_bf16(B, inT_d[:, ec * P:(ec + 1) * P], wcol[par], D, P, wstage, None, cast,
                           mul_cols=nwcols)
            B.wait("tensor", cast, cast.n)
            B.wait("tensor", sc, sc.n)
            B.wait("tensor", ev, ev.n)
            B.emit("vector", lambda e, o=x1e[:, 0:3], i=x1h[:, ec * 3:(ec + 1) * 3]:
                   e.tensor_copy(out=o, in_=i), sig=ev)
            for it in range(NTT):
                pp = pb[it % 2]
                ts = slice(it * TT, (it + 1) * TT)
                for c in range(NFC):
                    B.emit("tensor", lambda e, l=wcol[par][:, c * P:(c + 1) * P],
                           r=xn_tiles[c][:, ts], o=pp[:, 0:TT], s=(c == 0), st=(c == NFC - 1):
                           e.matmul(out=o, lhsT=l, rhs=r, start=s, stop=st),
                           sig=mm if c == NFC - 1 else None)
                B.wait("vector", mm, mm.n)
                B.emit("vector", lambda e, o=x1e[:, 3 + it * TT:3 + (it + 1) * TT], a=pp[:, 0:TT],
                       b=rms_bc[:, ts]: e.tensor_tensor(out=o, in0=a, in1=b, op=ALU.mult), sig=ev)
            B.emit("vector", lambda e, o=x1hk[:, ec * 3:(ec + 1) * 3],
                   i=x1e[:, 3 + T - 3:3 + T]: e.tensor_copy(out=o, in_=i), sig=ev)
            B.emit("vector", lambda e, o=cvt, i=x1e[:, 0:T], w0=convw[:, ec * 4:ec * 4 + 1],
                   b=convb[:, ec:ec + 1]:
                   e.tensor_scalar(out=o, in0=i, scalar1=w0, scalar2=b,
                                   op0=ALU.mult, op1=ALU.add))
            for k in (1, 2, 3):
                B.emit("vector", lambda e, o=cvt, i=x1e[:, k:k + T],
                       w=convw[:, ec * 4 + k:ec * 4 + k + 1]:
                       e.scalar_tensor_tensor(out=o, in0=i, scalar=w, in1=o,
                                              op0=ALU.mult, op1=ALU.add),
                       sig=ev if k == 3 else None)
            if _dbg and ec == 0:
                B.wait("vector", ev, ev.n)
                build_l1_kernel._dump(5, x1e[:, 0:16], 16)
                build_l1_kernel._dump(4, cvt[:, 0:16], 16)
            B.wait("scalar", ev, ev.n)
            B.emit("scalar", lambda e, o=xs_ec[ec][:, 0:T], i=cvt:
                   e.activation(out=o, in_=i, func=AF.Silu), sig=sc)
        if _dbg:
            build_l1_kernel._dump(0, xn_tiles[0][:, 0:2048], 2048)
            build_l1_kernel._dump(1, rms_bc[:, 0:2048], 2048)
            build_l1_kernel._dump(2, xs_ec[0][:, 0:2048], 2048)
            build_l1_kernel._dump(6, xrh[:, 0:24], 24)
            build_l1_kernel._dump(7, smalls[:, 0:400], 400)
        phase_fence(B)
        ar.reset(mark0)
        wd1t = ar.take_bf16(NEC * 64)
        wbct = ar.take_bf16(2 * NEC * NS)
        emit_load_bf16(B, wd1T_d, wd1t, DIN, 64, wstage, None, cast)
        emit_load_bf16(B, wbT_d, wbct, DIN, NS, wstage, None, cast, dst_stride=2 * NS, dst_off=0)
        emit_load_bf16(B, wcT_d, wbct, DIN, NS, wstage, None, cast, dst_stride=2 * NS, dst_off=NS)
        B.wait("tensor", cast, cast.n)
        B.wait("tensor", sc, sc.n)
        for it in range(NTT):
            ts = slice(it * TT, (it + 1) * TT)
            B.wait("tensor", ev, ev.n)
            for ec in range(NEC):
                B.emit("tensor", lambda e, l=wd1t[:, ec * 64:(ec + 1) * 64],
                       r=xs_ec[ec][:, ts], o=pb[0][0:64, 0:TT],
                       s=(ec == 0), st=(ec == NEC - 1):
                       e.matmul(out=o, lhsT=l, rhs=r, start=s, stop=st))
            for ec in range(NEC):
                B.emit("tensor", lambda e, l=wbct[:, ec * 2 * NS:ec * 2 * NS + NS],
                       r=xs_ec[ec][:, ts], o=pb[1][0:NS, 0:TT],
                       s=(ec == 0), st=(ec == NEC - 1):
                       e.matmul(out=o, lhsT=l, rhs=r, start=s, stop=st))
            for ec in range(NEC):
                B.emit("tensor", lambda e, l=wbct[:, ec * 2 * NS + NS:(ec + 1) * 2 * NS],
                       r=xs_ec[ec][:, ts], o=pb[2][0:NS, 0:TT],
                       s=(ec == 0), st=(ec == NEC - 1):
                       e.matmul(out=o, lhsT=l, rhs=r, start=s, stop=st),
                       sig=mm if ec == NEC - 1 else None)
            B.wait("vector", mm, mm.n)
            B.emit("vector", lambda e, o=dcore[0:64, ts], i=pb[0][0:64, 0:TT]:
                   e.tensor_copy(out=o, in_=i), sig=ev)
            B.wait("vector", st, st.n)
            B.emit("vector", lambda e, o=bcstg[0:NS, 0:TT], i=pb[1][0:NS, 0:TT]:
                   e.tensor_copy(out=o, in_=i), sig=ev)
            B.wait("sync", ev, ev.n)
            B.emit("sync", lambda e, o=bsc_d[0:NS, ts], i=bcstg[0:NS, 0:TT]:
                   e.dma_start(out=o, in_=i), sig=st, sigk=16)
            B.wait("vector", st, st.n)
            B.emit("vector", lambda e, o=bcstg[0:NS, 0:TT], i=pb[2][0:NS, 0:TT]:
                   e.tensor_copy(out=o, in_=i), sig=ev)
            B.wait("sync", ev, ev.n)
            B.emit("sync", lambda e, o=csc_d[0:NS, ts], i=bcstg[0:NS, 0:TT]:
                   e.dma_start(out=o, in_=i), sig=st, sigk=16)
        if _dbg:
            build_l1_kernel._dump(3, dcore[0:64, 0:2048], 2048)
        phase_fence(B)
        ar.reset(mark0)
        skcol = [ar.take_bf16(NFC * P), ar.take_bf16(NFC * P)]
        wd2c = [ar.take_bf16(P), ar.take_bf16(P)]
        ab = ar.take_f32(NS * SLL)
        b_in = ar.take_bf16(NS * SLL)
        Bbc = ar.take_f32(NS * SLL // 2)
        delta = ar.take_f32(SLL)
        useg = ar.take_f32(SLL)
        ysl = ar.take_f32(SLL)
        ssk = ar.take_f32(SLL)
        softs = ar.take_f32(2 * SLL)
        B.wait("sync", st, st.n)
        for ec in range(NEC):
            par = ec % 2
            B.wait("sync", mm, mm.n)
            emit_load_bf16(B, skT_d[:, ec * P:(ec + 1) * P], skcol[par], D, P, wstage, None, cast,
                           mul_cols=nwcols)
            emit_load_bf16(B, wd2T_d[:, ec * P:(ec + 1) * P], wd2c[par], 64, P, wstage, None, cast)
            skdone = cast.n
            for isl in range(T // SLL):
                ts = slice(isl * SLL, (isl + 1) * SLL)
                B.wait("tensor", cast, skdone)
                B.wait("tensor", sc, sc.n)
                B.wait("tensor", ev, ev.n)
                B.emit("tensor", lambda e, l=wd2c[par][0:64, 0:P], r=dcore[0:64, ts],
                       o=pb[5][:, 0:SLL]:
                       e.matmul(out=o, lhsT=l, rhs=r, start=True, stop=True), sig=mm)
                B.wait("vector", mm, mm.n)
                B.emit("vector", lambda e, o=delta, i=pb[5][:, 0:SLL], b=wd2b[:, ec:ec + 1]:
                       e.tensor_scalar(out=o, in0=i, scalar1=b, scalar2=None, op0=ALU.add), sig=ev)
                B.wait("scalar", ev, ev.n)
                B.emit("scalar", lambda e, o=softs[:, 0:SLL], i=delta:
                       e.activation(out=o, in_=i, func=AF.Abs))
                B.emit("scalar", lambda e, o=softs[:, SLL:2 * SLL], i=softs[:, 0:SLL]:
                       e.activation(out=o, in_=i, func=AF.Exp, scale=-1.0))
                B.emit("scalar", lambda e, o=softs[:, 0:SLL], i=softs[:, SLL:2 * SLL]:
                       e.activation(out=o, in_=i, func=AF.Ln, bias=cval(g, 1.0)), sig=sc)
                B.wait("vector", sc, sc.n)
                B.emit("vector", lambda e, o=delta:
                       e.tensor_scalar(out=o, in0=delta, scalar1=0.0, scalar2=None, op0=ALU.max), sig=ev)
                B.wait("vector", ev, ev.n)
                B.emit("vector", lambda e, o=delta, t=softs[:, 0:SLL]:
                       e.tensor_tensor(out=o, in0=o, in1=t, op=ALU.add), sig=ev)
                B.wait("vector", ev, ev.n)
                B.emit("vector", lambda e, o=useg, a=delta, b=xs_ec[ec][:, ts]:
                       e.tensor_tensor(out=o, in0=a, in1=b, op=ALU.mult), sig=ev)
                B.wait("scalar", ev, ev.n)
                for n in range(NS):
                    B.emit("scalar", lambda e, o=ab[:, n * SLL:(n + 1) * SLL], i=delta,
                           a=Acols[:, ec * NS + n:ec * NS + n + 1]:
                           e.activation(out=o, in_=i, func=AF.Exp, scale=a),
                           sig=sc if n == NS - 1 else None)
                if _dbg and ec == 0 and isl == 0:
                    B.wait("vector", sc, sc.n)
                    build_l1_kernel._dump(4, delta[:, 0:SLL], SLL)
                HSL = SLL // 2
                B.wait("vector", sc, sc.n)
                for hb in range(2):
                    hts = slice(isl * SLL + hb * HSL, isl * SLL + (hb + 1) * HSL)
                    B.wait("sync", ev, ev.n)
                    B.emit("sync", lambda e, o=Bbc[:, 0:NS * HSL].rearrange("p (n t) -> p n t", n=NS),
                           i=bsc_d.rearrange("(o n) t -> o n t", o=1)[0:1, 0:NS, hts].broadcast_to([P, NS, HSL]):
                           e.dma_start(out=o, in_=i), sig=bdma[hb], sigk=16)
                    B.wait("vector", bdma[hb], bdma[hb].n)
                    for n in range(NS):
                        B.emit("vector", lambda e,
                               o=b_in[:, n * SLL + hb * HSL:n * SLL + (hb + 1) * HSL],
                               a=useg[:, hb * HSL:(hb + 1) * HSL],
                               bb=Bbc[:, n * HSL:(n + 1) * HSL]:
                               e.tensor_tensor(out=o, in0=a, in1=bb, op=ALU.mult),
                               sig=ev if n == NS - 1 else None)
                B.wait("vector", ev, ev.n)
                for n in range(NS):
                    B.emit("vector", lambda e, o=ab[:, n * SLL:(n + 1) * SLL],
                           d0=ab[:, n * SLL:(n + 1) * SLL], d1=b_in[:, n * SLL:(n + 1) * SLL],
                           ini=hbuf[:, ec * NS + n:ec * NS + n + 1]:
                           e.tensor_tensor_scan(out=o, data0=d0, data1=d1, initial=ini,
                                                op0=ALU.mult, op1=ALU.add),
                           sig=ev if n == NS - 1 else None)
                B.wait("vector", ev, ev.n)
                B.emit("vector", lambda e, o=hbuf[:, ec * NS:(ec + 1) * NS],
                       i=ab.rearrange("p (n t) -> p n t", n=NS)[:, :, SLL - 1]:
                       e.tensor_copy(out=o, in_=i), sig=ev)
                for hb in range(2):
                    hts = slice(isl * SLL + hb * HSL, isl * SLL + (hb + 1) * HSL)
                    B.wait("sync", ev, ev.n)
                    B.emit("sync", lambda e, o=Bbc[:, 0:NS * HSL].rearrange("p (n t) -> p n t", n=NS),
                           i=csc_d.rearrange("(o n) t -> o n t", o=1)[0:1, 0:NS, hts].broadcast_to([P, NS, HSL]):
                           e.dma_start(out=o, in_=i), sig=bdma[hb], sigk=16)
                    B.wait("vector", bdma[hb], bdma[hb].n)
                    for n in range(NS):
                        B.emit("vector", lambda e,
                               o=ab[:, n * SLL + hb * HSL:n * SLL + (hb + 1) * HSL],
                               cc=Bbc[:, n * HSL:(n + 1) * HSL]:
                               e.tensor_tensor(out=o, in0=o, in1=cc, op=ALU.mult),
                               sig=ev if n == NS - 1 else None)
                B.wait("vector", ev, ev.n)
                nblk = NS
                while nblk > 1:
                    half = nblk // 2
                    B.emit("vector", lambda e, o=ab[:, 0:half * SLL],
                           i=ab[:, half * SLL:nblk * SLL]:
                           e.tensor_tensor(out=o, in0=o, in1=i, op=ALU.add), sig=ev)
                    B.wait("vector", ev, ev.n)
                    nblk = half
                B.emit("vector", lambda e, o=ysl, i=xs_ec[ec][:, ts], w=wDc[:, ec:ec + 1],
                       y=ab[:, 0:SLL]:
                       e.scalar_tensor_tensor(out=o, in0=i, scalar=w, in1=y,
                                              op0=ALU.mult, op1=ALU.add), sig=ev)
                B.wait("tensor", cast, skdone)
                B.wait("tensor", ev, ev.n)
                for c in range(NFC):
                    B.emit("tensor", lambda e, l=skcol[par][:, c * P:(c + 1) * P],
                           r=xn_tiles[c][:, ts], o=pb[4][:, 0:SLL],
                           s=(c == 0), st=(c == NFC - 1):
                           e.matmul(out=o, lhsT=l, rhs=r, start=s, stop=st),
                           sig=mm if c == NFC - 1 else None)
                B.wait("vector", mm, mm.n)
                B.emit("vector", lambda e, o=ssk[:, 0:SLL], a=pb[4][:, 0:SLL], b=rms_bc[:, ts]:
                       e.tensor_tensor(out=o, in0=a, in1=b, op=ALU.mult), sig=ev)
                B.wait("scalar", ev, ev.n)
                B.emit("scalar", lambda e, o=ssk[:, 0:SLL], i=ssk[:, 0:SLL]:
                       e.activation(out=o, in_=i, func=AF.Silu), sig=sc)
                B.wait("vector", sc, sc.n)
                B.emit("vector", lambda e, o=xs_ec[ec][:, ts], a=ysl, b=ssk[:, 0:SLL]:
                       e.tensor_tensor(out=o, in0=a, in1=b, op=ALU.mult), sig=ev)
        B.wait("vector", ev, ev.n)
        B.emit("vector", lambda e, o=ssk[:, 0:SLL] if SLL >= 256 else None, i=hbuf[:, :]:
               e.tensor_copy(out=o[:, 0:256] if False else ssk[:, 0:256], in_=i), sig=ev)
        B.wait("sync", ev, ev.n)
        B.emit("sync", lambda e, o=yh_d, i=ssk[:, 0:256]: e.dma_start(out=o, in_=i), sig=st, sigk=16)
        B.emit("sync", lambda e, o=yx1h_d, i=x1hk[:, 0:NEC * 3]: e.dma_start(out=o, in_=i), sig=st, sigk=16)
        phase_fence(B)
        ar.reset(mark0)
        om = ar.take_bf16(NEC * P)
        for m in range(NFC):
            B.wait("sync", mm, mm.n)
            emit_load_bf16(B, outT_d[:, m * P:(m + 1) * P], om, DIN, P, wstage, None, cast)
            B.wait("tensor", cast, cast.n)
            for it in range(NTT):
                ts = slice(it * TT, (it + 1) * TT)
                pd = pb[it % 2]
                B.wait("tensor", ev, ev.n)
                for ec in range(NEC):
                    B.emit("tensor", lambda e, l=om[:, ec * P:(ec + 1) * P],
                           r=xs_ec[ec][:, ts], o=pd[:, 0:TT],
                           s=(ec == 0), st=(ec == NEC - 1):
                           e.matmul(out=o, lhsT=l, rhs=r, start=s, stop=st),
                           sig=mm if ec == NEC - 1 else None)
                B.wait("vector", mm, mm.n)
                B.emit("vector", lambda e, a=acc_tiles[m][:, ts], i=pd[:, 0:TT]:
                       e.tensor_tensor(out=a, in0=a, in1=i, op=ALU.add), sig=ev)
        phase_fence(B)
        ar.reset(mark0)
        ar.free_top((NEC * T + 1) // 2)
        emit_moe(B, g, s2, ar, xn_tiles, acc_tiles, mWg_d, mWu_d, mWd_d, mW1_d, mW2_d,
                 cwsc_d, wstage)
        phase_fence(B)
        ost = ar.take_f32(1024)
        for c in range(NFC):
            for h in range(2):
                B.wait("vector", mm, mm.n)
                B.wait("vector", ev, ev.n)
                B.wait("vector", st, st.n)
                B.emit("vector", lambda e, i=acc_tiles[c][:, h * 1024:(h + 1) * 1024], o=ost:
                       e.tensor_copy(out=o, in_=i), sig=ev)
                B.wait("sync", ev, ev.n)
                B.emit("sync", lambda e, o=y_d[c * P:(c + 1) * P, h * 1024:(h + 1) * 1024], i=ost:
                       e.dma_start(out=o, in_=i), sig=st, sigk=16)
        B.wait("sync", st, st.n)
        with nc.Block() as block:
            B.replay(block)
    return nc


def emit_moe(B, g, s2, ar, xn_tiles, acc_tiles, wg_d, wu_d, wdn_d, w1_d, w2_d, cwsc_d, wst):
    mm, ev, sc = B.sems["mm"], B.sems["ev"], B.sems["sc"]
    cast = B.sems["wcast"]
    st = B.sems["st"]
    cb = B.sems["cwdma"]
    pb = s2["pb"]
    amark = ar.mark()
    s2x = alloc_ln_scratch(B, ar)
    wgt = ar.take_bf16(NFC * 8)
    cwst = ar.take_f32(P, parts=8)
    emit_load_bf16(B, wg_d, wgt, D, 8, wst, None, cast)
    B.wait("tensor", cast, cast.n)
    w = s2x["bc_a"]
    for sti in range(T // P):
        tsl = slice(sti * P, (sti + 1) * P)
        B.wait("tensor", ev, ev.n)
        for c in range(NFC):
            B.emit("tensor", lambda e, l=xn_tiles[c][:, tsl], r=wgt[:, c * 8:(c + 1) * 8],
                   o=pb[0][:, 0:8], s=(c == 0), st=(c == NFC - 1):
                   e.matmul(out=o, lhsT=l, rhs=r, start=s, stop=st),
                   sig=mm if c == NFC - 1 else None)
        B.wait("vector", mm, mm.n)
        p8 = w[:, 0:8]; m1 = w[:, 8:9]; m2 = w[:, 9:10]; ssum = w[:, 10:11]
        p2 = w[:, 16:24]
        def vstep(fn):
            B.emit("vector", fn, sig=ev)
            B.wait("vector", ev, ev.n)
        vstep(lambda e: e.tensor_copy(out=p8, in_=pb[0][:, 0:8]))
        vstep(lambda e: e.tensor_reduce(out=m1, in_=p8, axis=AX.X, op=ALU.max))
        vstep(lambda e: e.tensor_scalar(out=p8, in0=p8, scalar1=m1, scalar2=None, op0=ALU.subtract))
        B.wait("scalar", ev, ev.n)
        B.emit("scalar", lambda e: e.activation(out=p8, in_=p8, func=AF.Exp), sig=sc)
        B.wait("vector", sc, sc.n)
        vstep(lambda e: e.tensor_reduce(out=ssum, in_=p8, axis=AX.X, op=ALU.add))
        vstep(lambda e: e.reciprocal(out=ssum, in_=ssum))
        vstep(lambda e: e.tensor_scalar(out=p8, in0=p8, scalar1=ssum, scalar2=None, op0=ALU.mult))
        vstep(lambda e: e.tensor_reduce(out=m1, in_=p8, axis=AX.X, op=ALU.max))
        vstep(lambda e: e.tensor_scalar(out=p2, in0=p8, scalar1=m1, scalar2=2.0,
                                        op0=ALU.is_ge, op1=ALU.mult))
        vstep(lambda e: e.tensor_tensor(out=p2, in0=p8, in1=p2, op=ALU.subtract))
        vstep(lambda e: e.tensor_reduce(out=m2, in_=p2, axis=AX.X, op=ALU.max))
        vstep(lambda e: e.tensor_scalar(out=p2, in0=p8, scalar1=m2, scalar2=None, op0=ALU.is_ge))
        vstep(lambda e: e.tensor_tensor(out=p8, in0=p8, in1=p2, op=ALU.mult))
        vstep(lambda e: e.tensor_reduce(out=ssum, in_=p8, axis=AX.X, op=ALU.add))
        vstep(lambda e: e.reciprocal(out=ssum, in_=ssum))
        vstep(lambda e: e.tensor_scalar(out=p8, in0=p8, scalar1=ssum, scalar2=None, op0=ALU.mult))
        B.wait("tensor", ev, ev.n)
        B.emit("tensor", lambda e: e.transpose(out=pb[1][0:8, 0:P], in_=p8, identity=g["identf"][:]),
               sig=mm)
        B.wait("vector", mm, mm.n)
        B.wait("vector", st, st.n)
        B.emit("vector", lambda e: e.tensor_copy(out=cwst[0:8, 0:P], in_=pb[1][0:8, 0:P]), sig=ev)
        B.wait("sync", ev, ev.n)
        B.emit("sync", lambda e, o=cwsc_d[0:8, tsl]: e.dma_start(out=o, in_=cwst[0:8, 0:P]),
               sig=st, sigk=16)
    wue = ar.take_bf16(NFC * 1024)
    wde = ar.take_bf16(NEC * 512)
    Ht = ar.take_bf16(NEC * TT)
    gel = ar.take_f32(TT)
    cwb = ar.take_f32(TT)
    B.wait("sync", st, st.n)
    for e8 in range(9):
        shared = (e8 == 8)
        for it in range(NTT):
            ts = slice(it * TT, (it + 1) * TT)
            if not shared:
                B.wait("sync", ev, ev.n)
                B.emit("sync", lambda e, o=cwb,
                       i=cwsc_d[e8:e8 + 1, ts].broadcast_to([P, TT]):
                       e.dma_start(out=o, in_=i), sig=cb, sigk=16)
            for mhalf in range(2):
                B.wait("sync", mm, mm.n)
                if shared:
                    emit_load_bf16(B, w1_d[:, mhalf * 1024:(mhalf + 1) * 1024], wue,
                                   D, 1024, wst, None, cast)
                else:
                    emit_load_bf16(B, wu_d[e8 * D:(e8 + 1) * D, mhalf * 1024:(mhalf + 1) * 1024],
                                   wue, D, 1024, wst, None, cast)
                B.wait("tensor", cast, cast.n)
                for mh8 in range(8):
                    mh = mhalf * 8 + mh8
                    pp = pb[2 + (mh8 % 2)]
                    B.wait("tensor", sc, sc.n)
                    B.wait("tensor", ev, ev.n)
                    for c in range(NFC):
                        B.emit("tensor", lambda e, l=wue[:, c * 1024 + mh8 * P:c * 1024 + (mh8 + 1) * P],
                               r=xn_tiles[c][:, ts], o=pp[:, 0:TT], s=(c == 0), st=(c == NFC - 1):
                               e.matmul(out=o, lhsT=l, rhs=r, start=s, stop=st),
                               sig=mm if c == NFC - 1 else None)
                    B.wait("scalar", mm, mm.n)
                    B.emit("scalar", lambda e, o=gel, i=pp[:, 0:TT]:
                           e.activation(out=o, in_=i, func=AF.Gelu), sig=sc)
                    B.wait("vector", sc, sc.n)
                    if not shared:
                        B.wait("vector", cb, cb.n)
                    if shared:
                        B.emit("vector", lambda e, o=Ht[:, mh * TT:(mh + 1) * TT], i=gel:
                               e.tensor_copy(out=o, in_=i), sig=ev)
                    else:
                        B.emit("vector", lambda e, o=Ht[:, mh * TT:(mh + 1) * TT], a=gel, b=cwb:
                               e.tensor_tensor(out=o, in0=a, in1=b, op=ALU.mult), sig=ev)
            for mhalf in range(2):
                B.wait("sync", mm, mm.n)
                if shared:
                    emit_load_bf16(B, w2_d[:, mhalf * 512:(mhalf + 1) * 512], wde,
                                   DIN, 512, wst, None, cast)
                else:
                    emit_load_bf16(B, wdn_d[e8 * DIN:(e8 + 1) * DIN, mhalf * 512:(mhalf + 1) * 512],
                                   wde, DIN, 512, wst, None, cast)
                B.wait("tensor", cast, cast.n)
                for m4 in range(4):
                    m = mhalf * 4 + m4
                    pd = pb[m4 % 2]
                    B.wait("tensor", ev, ev.n)
                    for ec in range(NEC):
                        B.emit("tensor", lambda e, l=wde[:, ec * 512 + m4 * P:ec * 512 + (m4 + 1) * P],
                               r=Ht[:, ec * TT:(ec + 1) * TT], o=pd[:, 0:TT],
                               s=(ec == 0), st=(ec == NEC - 1):
                               e.matmul(out=o, lhsT=l, rhs=r, start=s, stop=st),
                               sig=mm if ec == NEC - 1 else None)
                    B.wait("vector", mm, mm.n)
                    B.emit("vector", lambda e, a=acc_tiles[m][:, ts], i=pd[:, 0:TT]:
                           e.tensor_tensor(out=a, in0=a, in1=i, op=ALU.add), sig=ev)
    ar.reset(amark)



def _t(a):
    return np.ascontiguousarray(np.asarray(a).T)



def build_qffn_kernel(dump_gact=False):
    nc = bass.Bass("TRN2", target_bir_lowering=False, debug=False)
    x_d = nc.dram_tensor("x", [D, T], F32, kind="ExternalInput").ap()
    wg_d = nc.dram_tensor("wgT", [D, HQ], I32, kind="ExternalInput").ap()
    wu_d = nc.dram_tensor("wuT", [D, HQ], I32, kind="ExternalInput").ap()
    wd_d = nc.dram_tensor("wdT", [HQ, D], I32, kind="ExternalInput").ap()
    y_d = nc.dram_tensor("y", [D, T], F32, kind="ExternalOutput").ap()
    yg_d = nc.dram_tensor("yg", [4 * P, (HQ // P) * TT], BF16, kind="ExternalOutput").ap() if dump_gact else None

    ctx = ExitStack()
    with ctx:
        B = Builder(nc, ctx)
        for s in ("mm", "ev", "sc", "wld", "wcast", "consts", "st"):
            B.sem(s)
        g = alloc_consts(B)
        ar = Arena(B, 24576)
        base = ar.mark()
        s2 = alloc_ln_scratch(B, ar)
        s2["pbS"] = ptensor(B, "pbS", [P, TT])
        s2["pb"] = [ptensor(B, f"pb{i}", [P, TT]) for i in range(6)]
        s2["pb"].append(s2["pbS"])
        s2["pbT"] = ptensor(B, "pbT", [P, 1024], BF16)
        s2["pbB1"] = s2["pb"][1]
        s2["pbB2"] = s2["pb"][2]
        bufA = sbuf(B, "bufA", [P, NFC * T], BF16)
        bufB = sbuf(B, "bufB", [P, NFC * T], BF16)
        x_tiles = [bufA[:, c * T:(c + 1) * T] for c in range(NFC)]
        xn_tiles = [bufB[:, c * T:(c + 1) * T] for c in range(NFC)]
        acc_tiles = x_tiles
        xstage = Stage(B, ar.take_f32(2 * 1024), B.sems['wcast'], 'xst')

        emit_consts(B, g, B.sems["consts"])
        ld, cast = B.sems["wld"], B.sems["wcast"]
        for c in range(NFC):
            for h in range(2):
                par, ssl = xstage.slot(1024)
                if xstage.lastc[par] > 0:
                    B.wait("sync", cast, xstage.lastc[par])
                sld = xstage.ld[par]
                B.emit("sync", lambda e, s=ssl, d=x_d[c * P:(c + 1) * P, h * 1024:(h + 1) * 1024]:
                       e.dma_start(out=s, in_=d), sig=sld, sigk=16)
                B.wait("gpsimd", sld, sld.n)
                B.emit("gpsimd", lambda e, s=ssl, o=x_tiles[c][:, h * 1024:(h + 1) * 1024]:
                       e.tensor_copy(out=o, in_=s), sig=cast)
                xstage.lastc[par] = cast.n
        for e in ("tensor", "vector", "scalar"):
            B.wait(e, cast, cast.n)
            B.wait(e, B.sems["consts"], B.sems["consts"].n)
        emit_ln(B, g, s2, x_tiles, xn_tiles, T)
        B.wait("gpsimd", B.sems["ev"], B.sems["ev"].n)
        for c in range(NFC):
            B.emit("gpsimd", lambda e, a=acc_tiles[c]: e.memset(a, 0.0),
                   sig=cast if c == NFC - 1 else None)
        for e in ("tensor", "vector"):
            B.wait(e, cast, cast.n)
        emit_qffn(B, g, s2, ar, xn_tiles, acc_tiles, wg_d, wu_d, wd_d, T, yg_d=yg_d)
        mm, ev, st = B.sems["mm"], B.sems["ev"], B.sems["st"]
        ost = ar.take_f32(1024)
        for c in range(NFC):
            for h in range(2):
                B.wait("vector", mm, mm.n)
                if (c, h) != (0, 0):
                    B.wait("vector", st, st.n)
                B.emit("vector", lambda e, i=acc_tiles[c][:, h * 1024:(h + 1) * 1024], o=ost:
                       e.tensor_copy(out=o, in_=i), sig=ev)
                B.wait("sync", ev, ev.n)
                B.emit("sync", lambda e, o=y_d[c * P:(c + 1) * P, h * 1024:(h + 1) * 1024], i=ost:
                       e.dma_start(out=o, in_=i), sig=st, sigk=16)
        B.wait("sync", st, st.n)

        with nc.Block() as block:
            B.replay(block)
    return nc


def _silu(x):
    return x / (1.0 + np.exp(-x))


def _np_ref_qffn(x, wg, wu, wd):
    xl = np.asarray(x, np.float32)
    m = xl.mean(-1, keepdims=True)
    v = xl.var(-1, keepdims=True)
    xn = (xl - m) / np.sqrt(v + EPS_LN)
    s = 1.0 / 127.0
    gate = xn @ (wg.astype(np.float32) * s).T
    up = xn @ (wu.astype(np.float32) * s).T
    h = _silu(gate) * up
    return h @ (wd.astype(np.float32) * s).T


def build_gla_kernel():
    nc = bass.Bass("TRN2", target_bir_lowering=False, debug=False)
    xn_d = nc.dram_tensor("xn", [D, T], F32, kind="ExternalInput").ap()
    wq_d = nc.dram_tensor("wq", [D, 1024], F32, kind="ExternalInput").ap()
    wk_d = nc.dram_tensor("wk", [D, 1024], F32, kind="ExternalInput").ap()
    wv_d = nc.dram_tensor("wv", [D, 1024], F32, kind="ExternalInput").ap()
    wg_d = nc.dram_tensor("wg", [D, 1024], F32, kind="ExternalInput").ap()
    gat_d = nc.dram_tensor("gattn", [D, 1], F32, kind="ExternalInput").ap()
    sinit_d = nc.dram_tensor("sinit", [P, 512], F32, kind="ExternalInput").ap()
    y_d = nc.dram_tensor("y", [D, T], F32, kind="ExternalOutput").ap()
    ys_d = nc.dram_tensor("ys", [P, 512], F32, kind="ExternalOutput").ap()

    ctx = ExitStack()
    with ctx:
        B = Builder(nc, ctx)
        for s in ("mm", "ev", "sc", "wld", "wcast", "consts", "st"):
            B.sem(s)
        g = alloc_consts(B)
        ar = Arena(B, 39500)
        s2 = alloc_ln_scratch(B, ar)
        s2["pbS"] = ptensor(B, "pbS", [P, TT])
        s2["pb"] = [ptensor(B, f"pb{i}", [P, TT]) for i in range(6)]
        s2["pb"].append(s2["pbS"])
        s2["pbT"] = ptensor(B, "pbT", [P, 1024], BF16)
        bufB = sbuf(B, "bufB", [P, NFC * T], BF16)
        xn_tiles = [bufB[:, c * T:(c + 1) * T] for c in range(NFC)]
        o_buf = ar.take_bf16(NFC * T)
        wt = ar.take_bf16(NFC * 1024)
        S32 = ar.take_f32(512)
        Sbf = ar.take_bf16(512)
        ga = alloc_gla(B, ar, T, o_buf, wt, S32, Sbf)
        ga["wstage"] = Stage(B, ar.take_f32(2 * 1024), B.sems["wcast"], "wst")
        ga["wq_d"], ga["wk_d"], ga["wv_d"], ga["wg_d"] = wq_d, wk_d, wv_d, wg_d
        gcols = sbuf(B, "gcols", [P, NFC], F32)
        xstage = Stage(B, ar.take_f32(2 * 1024), B.sems["wcast"], "xst")
        emit_consts(B, g, B.sems["consts"])
        ld, cast = B.sems["wld"], B.sems["wcast"]
        mm, ev, sc = B.sems["mm"], B.sems["ev"], B.sems["sc"]
        for c in range(NFC):
            for h in range(2):
                par, ssl = xstage.slot(1024)
                if xstage.lastc[par] > 0:
                    B.wait("sync", cast, xstage.lastc[par])
                sld = xstage.ld[par]
                B.emit("sync", lambda e, s=ssl, d=xn_d[c * P:(c + 1) * P, h * 1024:(h + 1) * 1024]:
                       e.dma_start(out=s, in_=d), sig=sld, sigk=16)
                B.wait("gpsimd", sld, sld.n)
                B.emit("gpsimd", lambda e, s=ssl, o=xn_tiles[c][:, h * 1024:(h + 1) * 1024]:
                       e.tensor_copy(out=o, in_=s), sig=cast)
                xstage.lastc[par] = cast.n
        gld = B.sem("gld")
        B.emit("sync", lambda e, o=gcols[:], i=gat_d.rearrange("(c p) w -> p (c w)", p=P):
               e.dma_start(out=o, in_=i, allow_slow_non_contiguous=True), sig=gld, sigk=16)
        B.emit("sync", lambda e, o=ga["S32"], i=sinit_d: e.dma_start(out=o, in_=i), sig=gld, sigk=16)
        B.wait("vector", gld, gld.n)
        B.emit("vector", lambda e, o=ga["Sbf"][:, 0:512], i=ga["S32"][:, 0:512]:
               e.tensor_copy(out=o, in_=i), sig=ev)
        for e in ("tensor", "vector", "scalar"):
            B.wait(e, cast, cast.n)
            B.wait(e, B.sems["consts"], B.sems["consts"].n)
            B.wait(e, gld, gld.n)
        gattn_cols = [gcols[:, c:c + 1] for c in range(NFC)]
        dump_names = []

        def dbg(name, ap, after_sc=False):
            dump_names.append(name)
            slot = len(dump_names) - 1
            if after_sc:
                B.wait("vector", B.sems["sc"], B.sems["sc"].n)
            B.wait("vector", B.sems["ev"], B.sems["ev"].n)
            B.wait("vector", B.sems["mm"], B.sems["mm"].n)
            w = ap.shape[1]
            B.emit("vector", lambda e, o=dmp_s[0:ap.shape[0], 0:w], i=ap:
                   e.tensor_copy(out=o, in_=i), sig=B.sems["ev"])
            B.wait("sync", B.sems["ev"], B.sems["ev"].n)
            B.emit("sync", lambda e, o=dmp_d[slot * P:slot * P + ap.shape[0], 0:w],
                   i=dmp_s[0:ap.shape[0], 0:w]:
                   e.dma_start(out=o, in_=i), sig=B.sems["st"], sigk=16)
            for en in ("vector", "tensor", "scalar"):
                B.wait(en, B.sems["st"], B.sems["st"].n)

        import os as _os
        if _os.environ.get("GDBG", "0") == "1":
            dmp_d = nc.dram_tensor("dmp", [24 * P, 1024], F32, kind="ExternalOutput").ap()
            dmp_s = ar.take_f32(1024)
            emit_gla(B, g, s2, ga, xn_tiles, gattn_cols, T, dbg=dbg)
            build_gla_kernel._dump_names = dump_names
        else:
            emit_gla(B, g, s2, ga, xn_tiles, gattn_cols, T)
        st = B.sems["st"]
        ost = ar.take_f32(1024)
        ob = o_buf
        for c in range(NFC):
            for h in range(2):
                B.wait("vector", ev, ev.n)
                if (c, h) != (0, 0):
                    B.wait("vector", st, st.n)
                B.emit("vector", lambda e, i=ob[:, c * T + h * 1024:c * T + (h + 1) * 1024],
                       o=ost: e.tensor_copy(out=o, in_=i), sig=ev)
                B.wait("sync", ev, ev.n)
                B.emit("sync", lambda e, o=y_d[c * P:(c + 1) * P, h * 1024:(h + 1) * 1024], i=ost:
                       e.dma_start(out=o, in_=i), sig=st, sigk=16)
        B.wait("sync", ev, ev.n)
        B.emit("sync", lambda e, o=ys_d, i=ga["S32"]: e.dma_start(out=o, in_=i), sig=st, sigk=16)
        B.wait("sync", st, st.n)
        with nc.Block() as block:
            B.replay(block)
    return nc


def _np_ref_gla(xn, gattn, Wq, Wk, Wv, Wg, S0=None):
    x = xn.astype(np.float64)
    h = x * (1.0 / np.sqrt((x * x).mean(-1, keepdims=True) + EPS_RMS0)) * gattn
    q = (h @ Wq).reshape(-1, 16, 64)
    k = (h @ Wk).reshape(-1, 16, 64)
    v = (h @ Wv).reshape(-1, 16, 64)
    al = 1.0 / (1.0 + np.exp(-(h @ Wg))).reshape(-1, 16, 64)
    qn = q / np.maximum(np.sqrt((q * q).sum(-1, keepdims=True)), 1e-12)
    kn = k / np.maximum(np.sqrt((k * k).sum(-1, keepdims=True)), 1e-12)
    Tn = x.shape[0]
    S = np.zeros((16, 64, 64)) if S0 is None else S0.astype(np.float64).copy()
    o = np.empty((Tn, 16, 64))
    for t in range(Tn):
        S = al[t][:, :, None] * S + kn[t][:, :, None] * v[t][:, None, :]
        o[t] = np.einsum("hkv,hk->hv", S, qn[t])
    return o.reshape(Tn, 1024), S


def _pack_S(S):
    out = np.zeros((P, 512), np.float32)
    for h in range(16):
        out[(h % 2) * 64:(h % 2) * 64 + 64, (h // 2) * 64:(h // 2) * 64 + 64] = S[h]
    return out


def build_l0_kernel():
    nc = bass.Bass("TRN2", target_bir_lowering=False, debug=False)
    x_d = nc.dram_tensor("x", [D, T], F32, kind="ExternalInput").ap()
    wq_d = nc.dram_tensor("wq", [D, 1024], F32, kind="ExternalInput").ap()
    wk_d = nc.dram_tensor("wk", [D, 1024], F32, kind="ExternalInput").ap()
    wv_d = nc.dram_tensor("wv", [D, 1024], F32, kind="ExternalInput").ap()
    wgg_d = nc.dram_tensor("wgg", [D, 1024], F32, kind="ExternalInput").ap()
    wo_d = nc.dram_tensor("wo", [1024, D], F32, kind="ExternalInput").ap()
    w1_d = nc.dram_tensor("w1", [D, DFF0], F32, kind="ExternalInput").ap()
    w2_d = nc.dram_tensor("w2", [D, DFF0], F32, kind="ExternalInput").ap()
    w3_d = nc.dram_tensor("w3", [DFF0, D], F32, kind="ExternalInput").ap()
    gat_d = nc.dram_tensor("gattn", [D, 1], F32, kind="ExternalInput").ap()
    gff_d = nc.dram_tensor("gffn", [D, 1], F32, kind="ExternalInput").ap()
    wgq_d = nc.dram_tensor("wgT", [D, HQ], I32, kind="ExternalInput").ap()
    wuq_d = nc.dram_tensor("wuT", [D, HQ], I32, kind="ExternalInput").ap()
    wdq_d = nc.dram_tensor("wdT", [HQ, D], I32, kind="ExternalInput").ap()
    sin_d = nc.dram_tensor("sinit", [P, 512], F32, kind="ExternalInput").ap()
    y_d = nc.dram_tensor("y", [D, T], F32, kind="ExternalOutput").ap()
    ys_d = nc.dram_tensor("ys", [P, 512], F32, kind="ExternalOutput").ap()

    ctx = ExitStack()
    with ctx:
        B = Builder(nc, ctx)
        for s in ("mm", "ev", "sc", "wld", "wcast", "consts", "st", "gld"):
            B.sem(s)
        g = alloc_consts(B)
        ar = Arena(B, 35820)
        s2 = alloc_ln_scratch(B, ar)
        s2["pbS"] = ptensor(B, "pbS", [P, TT])
        s2["pb"] = [ptensor(B, f"pb{i}", [P, TT]) for i in range(6)]
        s2["pb"].append(s2["pbS"])
        s2["pbT"] = ptensor(B, "pbT", [P, 1024], BF16)
        bufA = sbuf(B, "bufA", [P, NFC * T], BF16)
        bufB = sbuf(B, "bufB", [P, NFC * T], BF16)
        acc_tiles = [bufA[:, c * T:(c + 1) * T] for c in range(NFC)]
        xn_tiles = [bufB[:, c * T:(c + 1) * T] for c in range(NFC)]
        o_buf = ar.take_bf16(NFC * T)
        wt = ar.take_bf16(NFC * 1024)
        S32 = ar.take_f32(512)
        Sbf = ar.take_bf16(512)
        gcols = sbuf(B, "gcols", [P, 2 * NFC], F32)
        wstage = Stage(B, ar.take_f32(2 * 1024), B.sems["wcast"], "wst")
        emit_consts(B, g, B.sems["consts"])
        ld, cast, gld = B.sems["wld"], B.sems["wcast"], B.sems["gld"]
        mm, ev, sc = B.sems["mm"], B.sems["ev"], B.sems["sc"]
        st = B.sems["st"]
        mark0 = ar.mark()
        xstage = Stage(B, ar.take_f32(2 * 1024), B.sems["wcast"], "xst")
        for c in range(NFC):
            for h in range(2):
                par, ssl = xstage.slot(1024)
                if xstage.lastc[par] > 0:
                    B.wait("sync", cast, xstage.lastc[par])
                sld = xstage.ld[par]
                B.emit("sync", lambda e, s=ssl, d=x_d[c * P:(c + 1) * P, h * 1024:(h + 1) * 1024]:
                       e.dma_start(out=s, in_=d), sig=sld, sigk=16)
                B.wait("gpsimd", sld, sld.n)
                B.emit("gpsimd", lambda e, s=ssl, o=acc_tiles[c][:, h * 1024:(h + 1) * 1024]:
                       e.tensor_copy(out=o, in_=s), sig=cast)
                xstage.lastc[par] = cast.n
        B.emit("sync", lambda e, o=gcols[:, 0:NFC], i=gat_d.rearrange("(c p) w -> p (c w)", p=P):
               e.dma_start(out=o, in_=i, allow_slow_non_contiguous=True), sig=gld, sigk=16)
        B.emit("sync", lambda e, o=gcols[:, NFC:2 * NFC], i=gff_d.rearrange("(c p) w -> p (c w)", p=P):
               e.dma_start(out=o, in_=i, allow_slow_non_contiguous=True), sig=gld, sigk=16)
        B.emit("sync", lambda e, o=S32, i=sin_d: e.dma_start(out=o, in_=i), sig=gld, sigk=16)
        B.wait("vector", gld, gld.n)
        B.emit("vector", lambda e, o=Sbf[:, 0:512], i=S32[:, 0:512]:
               e.tensor_copy(out=o, in_=i), sig=ev)
        for e in ("tensor", "vector", "scalar"):
            B.wait(e, cast, cast.n)
            B.wait(e, B.sems["consts"], B.sems["consts"].n)
            B.wait(e, gld, gld.n)
        emit_ln(B, g, s2, acc_tiles, xn_tiles, T)
        ga = alloc_gla(B, ar, T, o_buf, wt, S32, Sbf)
        ga["wstage"] = wstage
        ga["wq_d"], ga["wk_d"], ga["wv_d"], ga["wg_d"] = wq_d, wk_d, wv_d, wgg_d
        gattn_cols = [gcols[:, c:c + 1] for c in range(NFC)]
        gffn_cols = [gcols[:, NFC + c:NFC + c + 1] for c in range(NFC)]
        emit_gla(B, g, s2, ga, xn_tiles, gattn_cols, T)
        ar.reset(mark0)
        emit_wo_swiglu(B, g, s2, ar, xn_tiles, acc_tiles, o_buf, wo_d, w1_d, w2_d, w3_d,
                       gffn_cols, T, wt, wstage)
        import os as _os
        if _os.environ.get("L0DBG", "0") == "1":
            ymid_d = nc.dram_tensor("ymid", [D, T], F32, kind="ExternalOutput").ap()
            omark = ar.mark()
            ost0 = ar.take_f32(1024)
            for c in range(NFC):
                for h in range(2):
                    B.wait("vector", mm, mm.n)
                    B.wait("vector", ev, ev.n)
                    B.wait("vector", st, st.n)
                    B.emit("vector", lambda e, i=acc_tiles[c][:, h * 1024:(h + 1) * 1024], o=ost0:
                           e.tensor_copy(out=o, in_=i), sig=ev)
                    B.wait("sync", ev, ev.n)
                    B.emit("sync", lambda e, o=ymid_d[c * P:(c + 1) * P, h * 1024:(h + 1) * 1024],
                           i=ost0: e.dma_start(out=o, in_=i), sig=st, sigk=16)
            for en in ("tensor", "vector", "scalar", "gpsimd"):
                B.wait(en, st, st.n)
            ar.reset(omark)
        emit_qffn(B, g, s2, ar, xn_tiles, acc_tiles, wgq_d, wuq_d, wdq_d, T)
        amark2 = ar.mark()
        ost = ar.take_f32(1024)
        for c in range(NFC):
            for h in range(2):
                B.wait("vector", mm, mm.n)
                B.wait("vector", ev, ev.n)
                if (c, h) != (0, 0):
                    B.wait("vector", st, st.n)
                B.emit("vector", lambda e, i=acc_tiles[c][:, h * 1024:(h + 1) * 1024], o=ost:
                       e.tensor_copy(out=o, in_=i), sig=ev)
                B.wait("sync", ev, ev.n)
                B.emit("sync", lambda e, o=y_d[c * P:(c + 1) * P, h * 1024:(h + 1) * 1024], i=ost:
                       e.dma_start(out=o, in_=i), sig=st, sigk=16)
        B.wait("sync", ev, ev.n)
        B.emit("sync", lambda e, o=ys_d, i=S32: e.dma_start(out=o, in_=i), sig=st, sigk=16)
        B.wait("sync", st, st.n)
        ar.reset(amark2)
        with nc.Block() as block:
            B.replay(block)
    return nc


def _np_ref_l0(x, W, S0=None):
    xl = x.astype(np.float64)
    m = xl.mean(-1, keepdims=True); v = xl.var(-1, keepdims=True)
    xn = (xl - m) / np.sqrt(v + EPS_LN)
    o, Sf = _np_ref_gla(xn.astype(np.float32), W["g_attn"], W["Wq"], W["Wk"], W["Wv"], W["Wg"], S0)
    att = o @ W["Wo"]
    xm = xn + att
    h2 = xm * (1.0 / np.sqrt((xm * xm).mean(-1, keepdims=True) + EPS_RMS0)) * W["g_ffn"]
    ffn = (_silu(h2 @ W["w1"]) * (h2 @ W["w2"])) @ W["w3"]
    attn_out = xm + ffn
    s = 1.0 / 127.0
    gate = xn @ (W["w_gate"].astype(np.float64) * s).T
    up = xn @ (W["w_up"].astype(np.float64) * s).T
    qf = (_silu(gate) * up) @ (W["w_down"].astype(np.float64) * s).T
    return xl + attn_out + qf, Sf

def _np_ref_l1(resid, W, halo=None, h0=None):
    x = resid.astype(np.float64)
    m = x.mean(-1, keepdims=True); v = x.var(-1, keepdims=True)
    xn = (x - m) / np.sqrt(v + EPS_LN)
    xr = xn * (1.0 / np.sqrt((xn * xn).mean(-1, keepdims=True) + EPS_RMS1)) * W["norm_w"]
    skip = xr @ W["skip_w"].T
    x1 = xr @ W["in_w"].T
    if halo is None:
        halo_x1 = np.zeros((3, DIN))
    else:
        hh = halo.astype(np.float64)
        mh = hh.mean(-1, keepdims=True); vh = hh.var(-1, keepdims=True)
        hn = (hh - mh) / np.sqrt(vh + EPS_LN)
        hr = hn * (1.0 / np.sqrt((hn * hn).mean(-1, keepdims=True) + EPS_RMS1)) * W["norm_w"]
        halo_x1 = hr @ W["in_w"].T
    xp = np.concatenate([halo_x1, x1], 0)
    conv = W["conv_b"] + sum(xp[k:k + x1.shape[0]] * W["conv_w"][:, k] for k in range(4))
    xs = conv / (1.0 + np.exp(-conv))
    A = -np.exp(W["A_log"].astype(np.float64))
    dcore = xs @ W["wd1"].T
    dl = dcore @ W["wd2"].T + W["wd2_b"]
    delta = np.maximum(dl, 0) + np.log1p(np.exp(-np.abs(dl)))
    Bm = xs @ W["wb"].T
    Cm = xs @ W["wc"].T
    h = np.zeros((DIN, NS)) if h0 is None else h0.astype(np.float64).copy()
    y = np.empty((x.shape[0], DIN))
    for t in range(x.shape[0]):
        h = np.exp(delta[t][:, None] * A) * h + (delta[t] * xs[t])[:, None] * Bm[t][None, :]
        y[t] = h @ Cm[t]
    y = y + xs * W["W_D"]
    y = y * (skip / (1.0 + np.exp(-skip)))
    att = y @ W["out_w"].T
    probs = np.exp(xn @ W["Wg"] - (xn @ W["Wg"]).max(-1, keepdims=True))
    probs = probs / probs.sum(-1, keepdims=True)
    top2 = np.argsort(-probs, axis=-1)[:, :2]
    cw = np.zeros_like(probs)
    for t in range(probs.shape[0]):
        tv = probs[t, top2[t]]
        cw[t, top2[t]] = tv / tv.sum()
    def gelu(z):
        return 0.5 * z * (1.0 + _erf_np(z / np.sqrt(2.0)))
    moe = np.zeros((x.shape[0], D))
    for e in range(8):
        He = gelu(xn @ W["Wu"][e])
        moe += cw[:, e:e + 1] * (He @ W["Wd"][e])
    shared = gelu(xn @ W["W1"]) @ W["W2"]
    return x + att + moe + shared, h


def _erf_np(z):
    import math
    return np.vectorize(math.erf)(z)



_BUILT = {}


def _get_built():
    if "l0" not in _BUILT:
        _BUILT["l0"] = build_l0_kernel()
        _BUILT["l1"] = build_l1_kernel()
    return _BUILT["l0"], _BUILT["l1"]


def kernel(L0_attn_x, L0_attn_Wq, L0_attn_Wk, L0_attn_Wv, L0_attn_Wg, L0_attn_Wo,
           L0_attn_w1, L0_attn_w2, L0_attn_w3, L0_attn_g_attn, L0_attn_g_ffn,
           L0_ffn_w_gate, L0_ffn_w_up, L0_ffn_w_down,
           L1_attn_norm_w, L1_attn_skip_w, L1_attn_in_w, L1_attn_conv_w, L1_attn_conv_b,
           L1_attn_wd1, L1_attn_wd2, L1_attn_wd2_b, L1_attn_wb, L1_attn_wc,
           L1_attn_A_log, L1_attn_W_D, L1_attn_out_w,
           L1_ffn_Wg, L1_ffn_Wu, L1_ffn_Wd, L1_ffn_W1, L1_ffn_W2):
    x = np.asarray(L0_attn_x, np.float32)
    nc0, nc1 = _get_built()
    f32 = np.float32
    l0_common = {
        "wq": np.asarray(L0_attn_Wq, f32), "wk": np.asarray(L0_attn_Wk, f32),
        "wv": np.asarray(L0_attn_Wv, f32), "wgg": np.asarray(L0_attn_Wg, f32),
        "wo": np.asarray(L0_attn_Wo, f32),
        "w1": np.asarray(L0_attn_w1, f32), "w2": np.asarray(L0_attn_w2, f32),
        "w3": np.asarray(L0_attn_w3, f32),
        "gattn": np.asarray(L0_attn_g_attn, f32).reshape(D, 1),
        "gffn": np.asarray(L0_attn_g_ffn, f32).reshape(D, 1),
        "wgT": _t(np.asarray(L0_ffn_w_gate, np.int32)),
        "wuT": _t(np.asarray(L0_ffn_w_up, np.int32)),
        "wdT": _t(np.asarray(L0_ffn_w_down, np.int32)),
    }
    l1_common = {
        "normw": np.asarray(L1_attn_norm_w, f32).reshape(D, 1),
        "skipT": _t(np.asarray(L1_attn_skip_w, f32)),
        "inT": _t(np.asarray(L1_attn_in_w, f32)),
        "convw": np.asarray(L1_attn_conv_w, f32),
        "convb": np.asarray(L1_attn_conv_b, f32).reshape(DIN, 1),
        "wd1T": _t(np.asarray(L1_attn_wd1, f32)),
        "wd2T": _t(np.asarray(L1_attn_wd2, f32)),
        "wd2b": np.asarray(L1_attn_wd2_b, f32).reshape(DIN, 1),
        "wbT": _t(np.asarray(L1_attn_wb, f32)),
        "wcT": _t(np.asarray(L1_attn_wc, f32)),
        "alog": np.asarray(L1_attn_A_log, f32),
        "wD": np.asarray(L1_attn_W_D, f32).reshape(DIN, 1),
        "outT": _t(np.asarray(L1_attn_out_w, f32)),
        "mWg": np.asarray(L1_ffn_Wg, f32),
        "mWu": np.ascontiguousarray(np.asarray(L1_ffn_Wu, f32).reshape(8 * D, DIN)),
        "mWd": np.ascontiguousarray(np.asarray(L1_ffn_Wd, f32).reshape(8 * DIN, D)),
        "mW1": np.asarray(L1_ffn_W1, f32), "mW2": np.asarray(L1_ffn_W2, f32),
    }
    zS = np.zeros((P, 512), f32)
    zh = np.zeros((P, NEC * NS), f32)
    zx1h = np.zeros((P, NEC * 3), f32)
    x_first = [np.ascontiguousarray(x[c // 2, 0:T, :].T) for c in range(8)]
    x_own = [np.ascontiguousarray(x[c // 2, (c % 2) * T:(c % 2 + 1) * T, :].T) for c in range(8)]
    cores = list(range(8))
    r1 = run_bass_kernel_spmd(nc0, [dict(l0_common, x=x_first[c], sinit=zS) for c in cores],
                              core_ids=cores)
    resid1_first = [r["y"] for r in r1.results]
    S_sum = [np.ascontiguousarray(r["ys"]) for r in r1.results]
    r2 = run_bass_kernel_spmd(nc0, [dict(l0_common, x=x_own[c],
                                         sinit=(S_sum[c] if c % 2 == 1 else zS)) for c in cores],
                              core_ids=cores)
    resid1_own = [np.ascontiguousarray(r["y"]) for r in r2.results]
    r3 = run_bass_kernel_spmd(nc1, [dict(l1_common, resid=np.ascontiguousarray(resid1_first[c]),
                                         x1halo=zx1h, hinit=zh) for c in cores],
                              core_ids=cores)
    h_sum = [np.ascontiguousarray(r["yh"]) for r in r3.results]
    x1h_sum = [np.ascontiguousarray(r["yx1h"]) for r in r3.results]
    in4 = []
    for c in cores:
        in4.append(dict(l1_common, resid=resid1_own[c],
                        x1halo=(x1h_sum[c] if c % 2 == 1 else zx1h),
                        hinit=(h_sum[c] if c % 2 == 1 else zh)))
    r4 = run_bass_kernel_spmd(nc1, in4, core_ids=cores)
    out = np.empty((4, 2 * T, D), np.float32)
    for c in cores:
        out[c // 2, (c % 2) * T:(c % 2 + 1) * T, :] = r4.results[c]["y"].T
    return out


if __name__ == "__main__":
    which = sys.argv[1] if len(sys.argv) > 1 else "qffn"
    rng = np.random.default_rng(0)
    if which == "qffn":
        Xfull = rng.standard_normal((T, D)).astype(np.float32)
        wg = rng.integers(-127, 128, (HQ, D)).astype(np.int32)
        wu = rng.integers(-127, 128, (HQ, D)).astype(np.int32)
        wd = rng.integers(-127, 128, (D, HQ)).astype(np.int32)
        nc = build_qffn_kernel()
        in_map = {"x": _t(Xfull), "wgT": _t(wg), "wuT": _t(wu), "wdT": _t(wd)}
        res = run_bass_kernel_spmd(nc, [in_map], core_ids=[0])
        out = res.results[0]["y"].T
        ref = _np_ref_qffn(Xfull, wg, wu, wd)
        print("qffn rel-var:", ((out - ref) ** 2).mean() / (ref ** 2).mean())
    elif which == "l0":
        s_in = 0.1 / np.sqrt(D)
        X = rng.standard_normal((T, D)).astype(np.float32)
        W = {}
        for nmw in ("Wq", "Wk", "Wv", "Wg"):
            W[nmw] = (rng.standard_normal((D, 1024)) * s_in).astype(np.float32)
        W["Wo"] = (rng.standard_normal((1024, D)) * s_in).astype(np.float32)
        W["w1"] = (rng.standard_normal((D, DFF0)) * s_in).astype(np.float32)
        W["w2"] = (rng.standard_normal((D, DFF0)) * s_in).astype(np.float32)
        W["w3"] = (rng.standard_normal((DFF0, D)) * (0.1 / np.sqrt(DFF0))).astype(np.float32)
        W["g_attn"] = (1.0 + 0.1 * rng.standard_normal(D)).astype(np.float32)
        W["g_ffn"] = (1.0 + 0.1 * rng.standard_normal(D)).astype(np.float32)
        W["w_gate"] = rng.integers(-127, 128, (HQ, D)).astype(np.int32)
        W["w_up"] = rng.integers(-127, 128, (HQ, D)).astype(np.int32)
        W["w_down"] = rng.integers(-127, 128, (D, HQ)).astype(np.int32)
        S0 = (rng.standard_normal((16, 64, 64)) * 0.02).astype(np.float32)
        nc = build_l0_kernel()
        im = {"x": _t(X), "wq": W["Wq"], "wk": W["Wk"], "wv": W["Wv"], "wgg": W["Wg"],
              "wo": W["Wo"], "w1": W["w1"], "w2": W["w2"], "w3": W["w3"],
              "gattn": W["g_attn"].reshape(D, 1), "gffn": W["g_ffn"].reshape(D, 1),
              "wgT": _t(W["w_gate"]), "wuT": _t(W["w_up"]), "wdT": _t(W["w_down"]),
              "sinit": _pack_S(S0)}
        ref, refS = _np_ref_l0(X, W, S0)
        res = run_bass_kernel_spmd(nc, [im], core_ids=[0])
        out = res.results[0]["y"].T
        Sf = res.results[0]["ys"]
        print("l0 rel-var:", ((out - ref) ** 2).mean() / (ref ** 2).mean(),
              "S rel-var:", ((Sf - _pack_S(refS)) ** 2).mean() / (_pack_S(refS) ** 2).mean())
    elif which == "l1":
        s_ = lambda shp, fan: (rng.standard_normal(shp) / np.sqrt(fan)).astype(np.float32)
        X = rng.standard_normal((T, D)).astype(np.float32)
        halo = rng.standard_normal((3, D)).astype(np.float32)
        W = {}
        W["norm_w"] = (1.0 + 0.1 * rng.standard_normal(D)).astype(np.float32)
        W["skip_w"] = s_((DIN, D), D); W["in_w"] = s_((DIN, D), D)
        W["conv_w"] = s_((DIN, 4), 4); W["conv_b"] = (0.1 * rng.standard_normal(DIN)).astype(np.float32)
        W["wd1"] = s_((64, DIN), DIN); W["wd2"] = s_((DIN, 64), 64)
        W["wd2_b"] = (0.1 * rng.standard_normal(DIN)).astype(np.float32)
        W["wb"] = s_((NS, DIN), DIN); W["wc"] = s_((NS, DIN), DIN)
        W["A_log"] = rng.standard_normal((DIN, NS)).astype(np.float32)
        W["W_D"] = (1.0 + 0.1 * rng.standard_normal(DIN)).astype(np.float32)
        W["out_w"] = s_((D, DIN), DIN)
        lim = np.sqrt(6.0 / (D + 8))
        W["Wg"] = rng.uniform(-lim, lim, (D, 8)).astype(np.float32)
        lim = np.sqrt(6.0 / (D + DIN))
        W["Wu"] = rng.uniform(-lim, lim, (8, D, DIN)).astype(np.float32)
        W["Wd"] = rng.uniform(-lim, lim, (8, DIN, D)).astype(np.float32)
        W["W1"] = rng.uniform(-lim, lim, (D, DIN)).astype(np.float32)
        W["W2"] = rng.uniform(-lim, lim, (DIN, D)).astype(np.float32)
        h0 = (0.05 * rng.standard_normal((DIN, NS))).astype(np.float32)
        nc = build_l1_kernel()
        im = {"resid": _t(X), "x1halo": np.zeros((P, NEC * 3), np.float32),
              "hinit": np.ascontiguousarray(h0.reshape(NEC, P, NS).transpose(1, 0, 2).reshape(P, NEC * NS)),
              "normw": W["norm_w"].reshape(D, 1), "skipT": _t(W["skip_w"]), "inT": _t(W["in_w"]),
              "convw": W["conv_w"], "convb": W["conv_b"].reshape(DIN, 1),
              "wd1T": _t(W["wd1"]), "wd2T": _t(W["wd2"]), "wd2b": W["wd2_b"].reshape(DIN, 1),
              "wbT": _t(W["wb"]), "wcT": _t(W["wc"]), "alog": W["A_log"],
              "wD": W["W_D"].reshape(DIN, 1), "outT": _t(W["out_w"]),
              "mWg": W["Wg"], "mWu": np.ascontiguousarray(W["Wu"].reshape(8 * D, DIN)),
              "mWd": np.ascontiguousarray(W["Wd"].reshape(8 * DIN, D)),
              "mW1": W["W1"], "mW2": W["W2"]}
        ref, refh = _np_ref_l1(X, W, None, h0)
        res = run_bass_kernel_spmd(nc, [im], core_ids=[0])
        out = res.results[0]["y"].T
        hf = res.results[0]["yh"]
        refh_dev = np.ascontiguousarray(refh.reshape(NEC, P, NS).transpose(1, 0, 2).reshape(P, NEC * NS))
        print("l1 rel-var:", ((out - ref) ** 2).mean() / (ref ** 2).mean(),
              " nan:", np.isnan(out).sum())
        print("l1 h rel-var:", ((hf - refh_dev) ** 2).mean() / (refh_dev ** 2).mean())
    elif which == "gla":
        s_in = 0.1 / np.sqrt(D)
        Xn = rng.standard_normal((T, D)).astype(np.float32)
        Wq = (rng.standard_normal((D, 1024)) * s_in).astype(np.float32)
        Wk = (rng.standard_normal((D, 1024)) * s_in).astype(np.float32)
        Wv = (rng.standard_normal((D, 1024)) * s_in).astype(np.float32)
        Wg = (rng.standard_normal((D, 1024)) * s_in).astype(np.float32)
        gattn = (1.0 + 0.1 * rng.standard_normal(D)).astype(np.float32)
        S0 = (rng.standard_normal((16, 64, 64)) * 0.05).astype(np.float32)
        nc = build_gla_kernel()
        im = {"xn": _t(Xn), "wq": Wq, "wk": Wk, "wv": Wv, "wg": Wg,
              "gattn": gattn.reshape(D, 1), "sinit": _pack_S(S0)}
        ref_o, ref_S = _np_ref_gla(Xn, gattn, Wq, Wk, Wv, Wg, S0)
        res = run_bass_kernel_spmd(nc, [im], core_ids=[0])
        o = res.results[0]["y"].T
        Sf = res.results[0]["ys"]
        print("gla o rel-var:", ((o - ref_o) ** 2).mean() / (ref_o ** 2).mean(),
              "maxabs:", np.abs(o - ref_o).max(), "refscale:", np.abs(ref_o).max())
        print("gla S rel-var:", ((Sf - _pack_S(ref_S)) ** 2).mean() / (_pack_S(ref_S) ** 2 + 1e-12).mean())


# version 08
# speedup vs baseline: 1.3320x; 1.3320x over previous
import sys
sys.path.insert(0, "/opt/trn_rl_repo")
import numpy as np
from contextlib import ExitStack

import concourse.bass as bass
import concourse.mybir as mybir
from concourse.bass_utils import run_bass_kernel_spmd

F32 = mybir.dt.float32
BF16 = mybir.dt.bfloat16
I32 = mybir.dt.int32
AF = mybir.ActivationFunctionType
ALU = mybir.AluOpType
AX = mybir.AxisListType

P = 128
D = 1024
NFC = D // P
T = 2048
TT = 512
NTT = T // TT
EPS_LN = 1e-5
EPS_RMS0 = 1e-6
EPS_RMS1 = 1e-5
DFF0 = 2736
HQ = 3072
DIN = 2048
NHEAD = 16
DK = 64

ENGINES = ("sync", "tensor", "vector", "scalar", "gpsimd")


class Sem:
    __slots__ = ("h", "n", "name")

    def __init__(self, handle, name):
        self.h = handle
        self.n = 0
        self.name = name


class Builder:
    def __init__(self, nc, ctx):
        self.nc = nc
        self.ctx = ctx
        self.q = {e: [] for e in ENGINES}
        self.sems = {}

    def sem(self, name):
        s = Sem(self.ctx.enter_context(self.nc.semaphore(name)), name)
        self.sems[name] = s
        return s

    def on(self, engine, fn):
        self.q[engine].append(fn)

    def emit(self, engine, fn, sig=None, sigk=1):
        if sig is None:
            self.on(engine, lambda eng, f=fn: f(eng))
            return None
        sig.n += sigk

        def run(eng, f=fn, s=sig.h, k=sigk):
            f(eng).then_inc(s, k)
        self.on(engine, run)
        return sig.n

    def wait(self, engine, sem, val=None):
        v = sem.n if val is None else val
        self.on(engine, lambda eng, s=sem.h, vv=v: eng.wait_ge(s, vv))

    def replay(self, block):
        for e in ENGINES:
            fns = self.q[e]
            if not fns:
                continue
            getattr(block, e)(lambda eng, fns=fns: [fn(eng) for fn in fns])


def barrier_all(B, engines=("tensor", "vector", "scalar", "gpsimd", "sync")):
    for e in engines:
        s = B.sems.setdefault("prog_" + e, None) or B.sem("prog_" + e)
        B.emit(e, lambda eng: eng.nop(), sig=s)
    for pe in engines:
        s = B.sems["prog_" + pe]
        for ce in engines:
            if ce != pe:
                B.wait(ce, s)


def sbuf(B, name, shape, dtype):
    return B.ctx.enter_context(B.nc.sbuf_tensor(name, shape, dtype))


def ptensor(B, name, shape, dtype=F32):
    return B.ctx.enter_context(B.nc.psum_tensor(name, shape, dtype))


class Arena:

    def __init__(self, B, words, name="arena"):
        self.t = sbuf(B, name, [P, words], F32)
        self.words = words
        self.off = 0
        self.top = words

    def mark(self):
        return self.off

    def reset(self, m):
        self.off = m

    def take_f32(self, n, parts=P):
        o = self.off
        assert o + n <= self.top, f"arena overflow {o}+{n}>{self.top}"
        self.off += n
        return self.t[0:parts, o:o + n]

    def take_f32_top(self, n, parts=P):
        assert self.off + n <= self.top, f"arena top overflow"
        self.top -= n
        return self.t[0:parts, self.top:self.top + n]

    def take_bf16_top(self, n_elems, parts=P):
        n = (n_elems + 1) // 2
        return self.take_f32_top(n, parts).bitcast(BF16)

    def free_top(self, n_elems_f32):
        self.top += n_elems_f32

    def take_bf16(self, n_elems, parts=P):
        n = (n_elems + 1) // 2
        ap = self.take_f32(n, parts)
        return ap.bitcast(BF16)

    def take_i32(self, n, parts=P):
        return self.take_f32(n, parts).bitcast(I32)



def alloc_consts(B):
    g = {}
    g["ident"] = sbuf(B, "c_ident", [P, P], BF16)
    g["identf"] = sbuf(B, "c_identf", [P, P], F32)
    g["ones_row"] = sbuf(B, "c_ones_row", [1, P], F32)
    g["ones_row_bf"] = sbuf(B, "c_ones_row_bf", [1, P], BF16)
    g["ones_col"] = sbuf(B, "c_ones_col", [P, 1], BF16)
    g["ones_col_f"] = sbuf(B, "c_ones_col_f", [P, 1], F32)
    g["u64"] = sbuf(B, "c_u64", [P, P], F32)
    g["mask64"] = sbuf(B, "c_mask64", [P, 8 * 64], BF16)
    g["subind"] = sbuf(B, "c_subind", [2, P], F32)
    g["ohend"] = sbuf(B, "c_ohend", [P, 2], F32)
    g["cc"] = sbuf(B, "c_cc", [P, 24], F32)
    g["_cvals"] = {}
    return g


def cval(g, v):
    d = g["_cvals"]
    if v not in d:
        d[v] = len(d)
        assert len(d) <= g["cc"].shape[1]
    return g["cc"][:, d[v]:d[v] + 1]


def cval1(g, v):
    return cval(g, v)[0:1, :]


def emit_consts(B, g, done):
    def build(eng):
        eng.memset(g["ident"][:], 0.0)
        eng.affine_select(out=g["ident"][:], in_=g["ident"][:], compare_op=ALU.not_equal,
                          fill=1.0, base=0, pattern=[[-1, P]], channel_multiplier=1)
        eng.memset(g["identf"][:], 0.0)
        eng.affine_select(out=g["identf"][:], in_=g["identf"][:], compare_op=ALU.not_equal,
                          fill=1.0, base=0, pattern=[[-1, P]], channel_multiplier=1)
        eng.memset(g["ones_row"][:], 1.0)
        eng.memset(g["ones_row_bf"][:], 1.0)
        eng.memset(g["ones_col"][:], 1.0)
        eng.memset(g["ones_col_f"][:], 1.0)
        u = g["u64"]
        eng.memset(u[:], 0.0)
        eng.affine_select(out=u[:], in_=u[:], compare_op=ALU.is_gt, fill=1.0,
                          base=0, pattern=[[-1, P]], channel_multiplier=1)
        eng.memset(u[0:64, 64:128], 0.0)
        m = g["mask64"]
        eng.memset(m[:], 0.0)
        eng.affine_select(out=m[0:64, :], in_=m[0:64, :], compare_op=ALU.is_gt, fill=1.0,
                          base=0, pattern=[[0, 8], [-1, 64]], channel_multiplier=1)
        eng.affine_select(out=m[64:128, :], in_=m[64:128, :], compare_op=ALU.is_gt, fill=1.0,
                          base=0, pattern=[[0, 8], [-1, 64]], channel_multiplier=1)
        s = g["subind"]
        eng.memset(s[:], 1.0)
        eng.affine_select(out=s[:], in_=s[:], compare_op=ALU.is_ge, fill=0.0,
                          base=63, pattern=[[-1, P]], channel_multiplier=64)
        eng.affine_select(out=s[:], in_=s[:], compare_op=ALU.is_ge, fill=0.0,
                          base=0, pattern=[[1, P]], channel_multiplier=-64)
        oh = g["ohend"]
        eng.memset(oh[:], 0.0)
        eng.affine_select(out=oh[:, 0:1], in_=oh[:, 0:1], compare_op=ALU.not_equal,
                          fill=1.0, base=-63, pattern=[[0, 1]], channel_multiplier=1)
        eng.affine_select(out=oh[:, 1:2], in_=oh[:, 1:2], compare_op=ALU.not_equal,
                          fill=1.0, base=-127, pattern=[[0, 1]], channel_multiplier=1)
        for v, i in g["_cvals"].items():
            eng.memset(g["cc"][:, i:i + 1], float(v))
    B.on("gpsimd", build)
    B.emit("gpsimd", lambda e: e.engine_nop(), sig=done)



class Stage:

    _uid = [0]

    def __init__(self, B, ap, cast_sem, name):
        self.ap = ap
        self.SW = ap.shape[1] // 2
        self.cast = cast_sem
        self.i = 0
        self.lastc = [0, 0]
        Stage._uid[0] += 1
        u = Stage._uid[0]
        self.ld = [B.sem(f"{name}{u}_ld0"), B.sem(f"{name}{u}_ld1")]

    def slot(self, M, pk=P):
        par = self.i % 2
        self.i += 1
        return par, self.ap[0:pk, par * self.SW:par * self.SW + M]


def emit_load_bf16(B, dram, wtile, K, M, stage, ld, cast, eng_cast="gpsimd",
                   dst_stride=None, dst_off=0, mul_cols=None):
    kc = (K + P - 1) // P
    assert M <= stage.SW
    if dst_stride is None:
        dst_stride = M
    for c in range(kc):
        pk = min(P, K - c * P)
        par, ssl = stage.slot(M, pk)
        if stage.lastc[par] > 0:
            B.wait("sync", stage.cast, stage.lastc[par])
        srcap = dram[c * P:c * P + pk, :]
        sld = stage.ld[par]
        B.emit("sync", lambda e, s=ssl, d=srcap: e.dma_start(out=s, in_=d), sig=sld, sigk=16)
        B.wait(eng_cast, sld, sld.n)
        off = c * dst_stride + dst_off
        if mul_cols is None:
            B.emit(eng_cast, lambda e, s=ssl, o=wtile[0:pk, off:off + M]:
                   e.tensor_copy(out=o, in_=s), sig=stage.cast)
        else:
            B.emit(eng_cast, lambda e, s=ssl, o=wtile[0:pk, off:off + M],
                   mc=mul_cols[0:pk, c:c + 1]:
                   e.tensor_scalar(out=o, in0=s, scalar1=mc, scalar2=None, op0=ALU.mult),
                   sig=stage.cast)
        stage.lastc[par] = stage.cast.n



def emit_ln(B, g, s2, x_tiles, xn_tiles, Twidth, eps=EPS_LN):
    mm, ev, sc = B.sems["mm"], B.sems["ev"], B.sems["sc"]
    ntt = Twidth // TT
    for it in range(ntt):
        ts = slice(it * TT, (it + 1) * TT)
        B.wait("scalar", mm, mm.n)
        for c in range(NFC):
            B.emit("scalar", lambda e, i=x_tiles[c][:, ts], o=s2["sq8"][:, c * TT:(c + 1) * TT]:
                   e.activation(out=o, in_=i, func=AF.Square), sig=sc if c == NFC - 1 else None)
        B.wait("tensor", sc, sc.n)
        B.wait("tensor", ev, ev.n)
        for c in range(NFC):
            B.emit("tensor", lambda e, r=x_tiles[c][:, ts], o=s2["pbS"][0:1, 0:TT],
                   l=g["ones_col"][:], s=(c == 0), st=(c == NFC - 1):
                   e.matmul(out=o, lhsT=l, rhs=r, start=s, stop=st))
            B.emit("tensor", lambda e, r=s2["sq8"][:, c * TT:(c + 1) * TT], o=s2["pbS"][32:33, 0:TT],
                   l=g["ones_col"][:], s=(c == 0), st=(c == NFC - 1):
                   e.matmul(out=o, lhsT=l, rhs=r, start=s, stop=(c == NFC - 1)),
                   sig=mm if c == NFC - 1 else None)
        B.wait("vector", mm, mm.n)
        mrow, vrow, arow, brow = s2["mrow"], s2["vrow"], s2["arow"], s2["brow"]
        B.emit("vector", lambda e, o=mrow, i=s2["pbS"][0:1, 0:TT]:
               e.tensor_scalar(out=o, in0=i, scalar1=1.0 / D, scalar2=None, op0=ALU.mult))
        B.emit("vector", lambda e, o=vrow, i=s2["pbS"][32:33, 0:TT]:
               e.tensor_scalar(out=o, in0=i, scalar1=1.0 / D, scalar2=None, op0=ALU.mult), sig=ev)
        B.emit("vector", lambda e, o=brow, m=mrow:
               e.tensor_tensor(out=o, in0=m, in1=m, op=ALU.mult))
        B.emit("vector", lambda e, o=vrow, t=brow:
               e.tensor_tensor(out=o, in0=o, in1=t, op=ALU.subtract), sig=ev)
        B.wait("scalar", ev, ev.n)
        B.emit("scalar", lambda e, o=arow, i=vrow:
               e.activation(out=o, in_=i, func=AF.Sqrt, bias=cval1(g, float(eps))), sig=sc)
        B.wait("vector", sc, sc.n)
        B.emit("vector", lambda e, o=arow: e.reciprocal(out=o, in_=o), sig=ev)
        B.wait("vector", ev, ev.n)
        B.emit("vector", lambda e, o=brow, m=mrow, a=arow:
               e.scalar_tensor_tensor(out=o, in0=m, scalar=-1.0, in1=a,
                                      op0=ALU.mult, op1=ALU.mult), sig=ev)
        B.wait("tensor", ev, ev.n)
        B.emit("tensor", lambda e, r=arow, o=s2["pb"][1][:, 0:TT], l=g["ones_row"][:]:
               e.matmul(out=o, lhsT=l, rhs=r, start=True, stop=True))
        B.emit("tensor", lambda e, r=brow, o=s2["pb"][2][:, 0:TT], l=g["ones_row"][:]:
               e.matmul(out=o, lhsT=l, rhs=r, start=True, stop=True), sig=mm)
        B.wait("vector", mm, mm.n)
        B.emit("vector", lambda e, o=s2["bc_a"][:, 0:TT], i=s2["pb"][1][:, 0:TT]:
               e.tensor_copy(out=o, in_=i))
        B.emit("vector", lambda e, o=s2["bc_b"][:, 0:TT], i=s2["pb"][2][:, 0:TT]:
               e.tensor_copy(out=o, in_=i), sig=ev)
        for c in range(NFC):
            B.emit("vector", lambda e, o=xn_tiles[c][:, ts], x=x_tiles[c][:, ts],
                   a=s2["bc_a"][:, 0:TT]: e.tensor_tensor(out=o, in0=x, in1=a, op=ALU.mult))
            B.emit("vector", lambda e, o=xn_tiles[c][:, ts], b=s2["bc_b"][:, 0:TT]:
                   e.tensor_tensor(out=o, in0=o, in1=b, op=ALU.add),
                   sig=ev if c == NFC - 1 else None)



def emit_rmsnorm(B, g, s2, x_tiles, out_tiles, gcols, Twidth, eps, sig_done=None):
    mm, ev, sc = B.sems["mm"], B.sems["ev"], B.sems["sc"]
    ntt = Twidth // TT
    for it in range(ntt):
        ts = slice(it * TT, (it + 1) * TT)
        B.wait("scalar", mm, mm.n)
        for c in range(NFC):
            B.emit("scalar", lambda e, i=x_tiles[c][:, ts], o=s2["sq8"][:, c * TT:(c + 1) * TT]:
                   e.activation(out=o, in_=i, func=AF.Square), sig=sc if c == NFC - 1 else None)
        B.wait("tensor", sc, sc.n)
        B.wait("tensor", ev, ev.n)
        for c in range(NFC):
            B.emit("tensor", lambda e, r=s2["sq8"][:, c * TT:(c + 1) * TT], o=s2["pbS"][0:1, 0:TT],
                   l=g["ones_col"][:], s=(c == 0), st=(c == NFC - 1):
                   e.matmul(out=o, lhsT=l, rhs=r, start=s, stop=st),
                   sig=mm if c == NFC - 1 else None)
        B.wait("scalar", mm, mm.n)
        arow = s2["arow"]
        B.emit("scalar", lambda e, o=arow, i=s2["pbS"][0:1, 0:TT]:
               e.activation(out=o, in_=i, func=AF.Sqrt, scale=1.0 / D, bias=cval1(g, float(eps))), sig=sc)
        B.wait("vector", sc, sc.n)
        B.emit("vector", lambda e, o=arow: e.reciprocal(out=o, in_=o), sig=ev)
        B.wait("tensor", ev, ev.n)
        B.emit("tensor", lambda e, r=arow, o=s2["pb"][1][:, 0:TT], l=g["ones_row"][:]:
               e.matmul(out=o, lhsT=l, rhs=r, start=True, stop=True), sig=mm)
        B.wait("vector", mm, mm.n)
        B.emit("vector", lambda e, o=s2["bc_a"][:, 0:TT], i=s2["pb"][1][:, 0:TT]:
               e.tensor_copy(out=o, in_=i), sig=ev)
        for c in range(NFC):
            B.emit("vector", lambda e, o=out_tiles[c][:, ts], x=x_tiles[c][:, ts],
                   gc=gcols[c], a=s2["bc_a"][:, 0:TT]:
                   e.scalar_tensor_tensor(out=o, in0=x, scalar=gc, in1=a,
                                          op0=ALU.mult, op1=ALU.mult),
                   sig=(sig_done if (sig_done is not None and c == NFC - 1 and it == ntt - 1) else
                        (ev if c == NFC - 1 else None)))



def emit_qffn(B, g, s2, ar, xn_tiles, acc_tiles, wg_d, wu_d, wd_d, Twidth, yg_d=None):
    phase_fence(B)
    mm, ev, sc = B.sems["mm"], B.sems["ev"], B.sems["sc"]
    ld, cast = B.sems["wld"], B.sems["wcast"]
    HC = HQ // P
    ntt = Twidth // TT
    amark = ar.mark()
    wgcol = [ar.take_bf16(NFC * P), ar.take_bf16(NFC * P)]
    wucol = [ar.take_bf16(NFC * P), ar.take_bf16(NFC * P)]
    gact = ar.take_bf16(HC * TT)
    wdm = ar.take_bf16(HC * P)
    sact = [ar.take_f32(TT), ar.take_f32(TT)]
    wstage = Stage(B, ar.take_i32(2 * 1024), B.sems["wcast"], "wst")
    col_done = {}
    stt_done = {}
    last_wdm = [0]
    for it in range(ntt):
        ts = slice(it * TT, (it + 1) * TT)
        import os as _os
        _dbg = _os.environ.get("QDBG", "0") == "1"
        for hc in range(HC):
            if _dbg:
                barrier_all(B)
            par = hc % 2
            key = (it * HC + hc)
            if key >= 2:
                B.wait("sync", mm, col_done[key - 2])
            emit_load_bf16(B, wg_d[:, hc * P:(hc + 1) * P], wgcol[par], D, P, wstage, ld, cast)
            emit_load_bf16(B, wu_d[:, hc * P:(hc + 1) * P], wucol[par], D, P, wstage, ld, cast)
            pa = s2["pb"][2] if par == 0 else s2["pb"][3]
            pb = s2["pb"][4] if par == 0 else s2["pb"][5]
            B.wait("tensor", cast, cast.n)
            B.wait("tensor", sc, sc.n)
            B.wait("tensor", ev, ev.n)
            for c in range(NFC):
                B.emit("tensor", lambda e, l=wgcol[par][:, c * P:(c + 1) * P],
                       r=xn_tiles[c][:, ts], o=pa[:, 0:TT], s=(c == 0), st=(c == NFC - 1):
                       e.matmul(out=o, lhsT=l, rhs=r, start=s, stop=st),
                       sig=mm if c == NFC - 1 else None)
            for c in range(NFC):
                B.emit("tensor", lambda e, l=wucol[par][:, c * P:(c + 1) * P],
                       r=xn_tiles[c][:, ts], o=pb[:, 0:TT], s=(c == 0), st=(c == NFC - 1):
                       e.matmul(out=o, lhsT=l, rhs=r, start=s, stop=st),
                       sig=mm if c == NFC - 1 else None)
            col_done[key] = mm.n
            B.wait("scalar", mm, mm.n - 1)
            if key >= 2:
                B.wait("scalar", ev, stt_done[key - 2])
            B.emit("scalar", lambda e, i=pa[:, 0:TT], o=sact[par]:
                   e.activation(out=o, in_=i, func=AF.Silu, scale=1.0 / 127.0), sig=sc)
            B.wait("vector", mm, mm.n)
            B.wait("vector", sc, sc.n)
            B.emit("vector", lambda e, i0=pb[:, 0:TT], i1=sact[par],
                   o=gact[:, hc * TT:(hc + 1) * TT]:
                   e.scalar_tensor_tensor(out=o, in0=i0, scalar=1.0 / 127.0, in1=i1,
                                          op0=ALU.mult, op1=ALU.mult), sig=ev)
            stt_done[key] = ev.n
        if yg_d is not None:
            st = B.sems["st"]
            B.wait("sync", ev, ev.n)
            B.emit("sync", lambda e, o=yg_d[it * P:(it + 1) * P, :], i=gact[:]:
                   e.dma_start(out=o, in_=i), sig=st, sigk=16)
            for en in ("tensor", "vector", "scalar"):
                B.wait(en, st, st.n)
        for m in range(NFC):
            if _dbg:
                barrier_all(B)
            if last_wdm[0] > 0:
                B.wait("sync", mm, last_wdm[0])
            emit_load_bf16(B, wd_d[:, m * P:(m + 1) * P], wdm, HQ, P, wstage, ld, cast)
            pd = s2["pb"][m % 2]
            B.wait("tensor", cast, cast.n)
            B.wait("tensor", ev, ev.n)
            for hc in range(HC):
                B.emit("tensor", lambda e, l=wdm[:, hc * P:(hc + 1) * P],
                       r=gact[:, hc * TT:(hc + 1) * TT], o=pd[:, 0:TT],
                       s=(hc == 0), st=(hc == HC - 1):
                       e.matmul(out=o, lhsT=l, rhs=r, start=s, stop=st),
                       sig=mm if hc == HC - 1 else None)
            last_wdm[0] = mm.n
            B.wait("vector", mm, mm.n)
            B.emit("vector", lambda e, i=pd[:, 0:TT], a=acc_tiles[m][:, ts]:
                   e.scalar_tensor_tensor(out=a, in0=i, scalar=1.0 / 127.0, in1=a,
                                          op0=ALU.mult, op1=ALU.add), sig=ev)
    ar.reset(amark)


def alloc_ln_scratch(B, ar):
    s2 = {}
    s2["sq8"] = ar.take_bf16(NFC * TT)
    rows = ar.take_f32(2 * TT)
    s2["mrow"] = rows[0:1, 0:TT]
    s2["vrow"] = rows[0:1, TT:2 * TT]
    rows2 = ar.take_bf16(0)
    rows3 = ar.take_f32(2 * TT)
    s2["arow"] = rows3[0:1, 0:TT]
    s2["brow"] = rows3[0:1, TT:2 * TT]
    s2["bc_a"] = ar.take_f32(TT)
    s2["bc_b"] = ar.take_f32(TT)
    return s2


def phase_fence(B):
    for s in ("mm", "ev", "sc"):
        B.wait("sync", B.sems[s], B.sems[s].n)
    for s in ("mm", "ev"):
        B.wait("gpsimd", B.sems[s], B.sems[s].n)



def emit_rms_group(B, g, s2, x_tiles, out_grp, gcols, t0, GT, eps, pbS_row, pbBC,
                   out_tiles=None):
    mm, ev, sc = B.sems["mm"], B.sems["ev"], B.sems["sc"]
    W = GT
    assert W <= TT
    ts = slice(t0, t0 + W)
    B.wait("scalar", mm, mm.n)
    for c in range(NFC):
        B.emit("scalar", lambda e, i=x_tiles[c][:, ts], o=s2["sq8"][:, c * W:(c + 1) * W]:
               e.activation(out=o, in_=i, func=AF.Square), sig=sc if c == NFC - 1 else None)
    B.wait("tensor", sc, sc.n)
    B.wait("tensor", ev, ev.n)
    for c in range(NFC):
        B.emit("tensor", lambda e, r=s2["sq8"][:, c * W:(c + 1) * W], o=pbS_row[0:1, 0:W],
               l=g["ones_col"][:], s=(c == 0), st=(c == NFC - 1):
               e.matmul(out=o, lhsT=l, rhs=r, start=s, stop=st),
               sig=mm if c == NFC - 1 else None)
    B.wait("scalar", mm, mm.n)
    arow = s2["arow"]
    B.emit("scalar", lambda e, o=arow[0:1, 0:W], i=pbS_row[0:1, 0:W]:
           e.activation(out=o, in_=i, func=AF.Sqrt, scale=1.0 / D, bias=cval1(g, float(eps))), sig=sc)
    B.wait("vector", sc, sc.n)
    B.emit("vector", lambda e, o=arow[0:1, 0:W]: e.reciprocal(out=o, in_=o), sig=ev)
    B.wait("tensor", ev, ev.n)
    B.emit("tensor", lambda e, r=arow[0:1, 0:W], o=pbBC[:, 0:W], l=g["ones_row"][:]:
           e.matmul(out=o, lhsT=l, rhs=r, start=True, stop=True), sig=mm)
    B.wait("vector", mm, mm.n)
    B.emit("vector", lambda e, o=s2["bc_a"][:, 0:W], i=pbBC[:, 0:W]:
           e.tensor_copy(out=o, in_=i), sig=ev)
    for c in range(NFC):
        dst = out_tiles[c][:, ts] if out_tiles is not None else out_grp[:, c * GT:c * GT + W]
        B.emit("vector", lambda e, o=dst,
               x=x_tiles[c][:, ts], gc=gcols[c], a=s2["bc_a"][:, 0:W]:
               e.scalar_tensor_tensor(out=o, in0=x, scalar=gc, in1=a,
                                      op0=ALU.mult, op1=ALU.mult),
               sig=ev if c == NFC - 1 else None)



GG = 256


def alloc_gla(B, ar, Twidth, o_buf, wt, S32, Sbf):
    ga = {}
    ga["o_buf"] = o_buf
    ga["wt"] = wt
    ga["S32"] = S32
    ga["Sbf"] = Sbf
    ga["hrms"] = ar.take_bf16(NFC * GG)
    nq = GG // P
    ga["qg"] = ar.take_bf16(nq * 1024)
    ga["kg"] = ar.take_bf16(nq * 1024)
    ga["vg"] = ar.take_bf16(nq * 1024)
    ga["lsg"] = ar.take_f32(nq * 1024)
    ga["Gc"] = ar.take_f32(1024)
    ga["cum"] = ar.take_f32(1024)
    ga["icum"] = ga["Gc"]
    ga["e2"] = ar.take_f32(1024)
    ga["decrow"] = ar.take_f32(1024, parts=2)
    ga["deccols"] = ar.take_f32(16)
    ga["dec"] = ar.take_f32(16)
    ga["qh"] = ar.take_bf16(1024)
    ga["kh"] = ar.take_bf16(1024)
    ga["k2"] = ar.take_bf16(1024)
    ga["qhT"] = ga["qh"]
    ga["khT"] = ga["kh"]
    ga["l2s"] = ar.take_f32(16)
    ga["oall"] = ar.take_bf16(512)
    ga["stil"] = [ar.take_bf16(512), ar.take_bf16(512)]
    return ga


def emit_gla(B, g, s2, ga, xn_tiles, gattn_cols, Twidth, dbg=None):
    phase_fence(B)
    mm, ev, sc = B.sems["mm"], B.sems["ev"], B.sems["sc"]
    cast = B.sems["wcast"]
    pb = s2["pb"]
    pbT = s2["pbT"]
    bI = [pb[0], pb[1]]
    bS = [pb[2], pb[3]]
    bB = [pb[4], pb[5]]
    bM = pb[6]
    wst = ga["wstage"]
    nq = GG // P

    for grp in range(Twidth // GG):
        t0 = grp * GG
        emit_rms_group(B, g, s2, xn_tiles, ga["hrms"], gattn_cols, t0, GG, EPS_RMS0,
                       bS[0], bB[0])
        if dbg and grp == 0:
            dbg("hrms", ga["hrms"][:, 0:1024])
            dbg("hrms2", ga["hrms"][:, 1024:2048])
        for pname, dram in (("zg", ga["wg_d"]), ("v", ga["wv_d"]),
                            ("k", ga["wk_d"]), ("q", ga["wq_d"])):
            B.wait("sync", mm, mm.n)
            emit_load_bf16(B, dram, ga["wt"], D, 1024, wst, None, cast)
            wdone = cast.n
            for st2 in range(nq):
                pp = [bI[0], bI[1]]
                B.wait("tensor", cast, wdone)
                B.wait("tensor", ev, ev.n)
                B.wait("tensor", sc, sc.n)
                for half in range(2):
                    for c in range(NFC):
                        B.emit("tensor", lambda e,
                               l=ga["hrms"][:, c * GG + st2 * P:c * GG + (st2 + 1) * P],
                               r=ga["wt"][:, c * 1024 + half * 512:c * 1024 + (half + 1) * 512],
                               o=pp[half][:, 0:512], s=(c == 0), st=(c == NFC - 1):
                               e.matmul(out=o, lhsT=l, rhs=r, start=s, stop=st),
                               sig=mm if c == NFC - 1 else None)
                mm_done = mm.n
                if pname == "v":
                    B.wait("vector", mm, mm_done)
                    for half in range(2):
                        B.emit("vector", lambda e, i=pp[half][:, 0:512],
                               o=ga["vg"][:, st2 * 1024 + half * 512:st2 * 1024 + (half + 1) * 512]:
                               e.tensor_copy(out=o, in_=i), sig=ev)
                elif pname == "zg":
                    B.wait("scalar", mm, mm_done)
                    sq = s2["sq8"]
                    for half in range(2):
                        B.emit("scalar", lambda e, i=pp[half][:, 0:512], o=sq[:, 0:512]:
                               e.activation(out=o, in_=i, func=AF.Abs))
                        B.emit("scalar", lambda e, i=sq[:, 0:512], o=sq[:, 512:1024]:
                               e.activation(out=o, in_=i, func=AF.Exp, scale=-1.0))
                        B.emit("scalar", lambda e, i=sq[:, 512:1024], o=sq[:, 0:512]:
                               e.activation(out=o, in_=i, func=AF.Ln, bias=cval(g, 1.0)), sig=sc)
                        B.wait("vector", sc, sc.n)
                        lg = ga["lsg"][:, st2 * 1024 + half * 512:st2 * 1024 + (half + 1) * 512]
                        B.emit("vector", lambda e, i=pp[half][:, 0:512], o=lg:
                               e.tensor_scalar(out=o, in0=i, scalar1=0.0, scalar2=None,
                                               op0=ALU.min))
                        B.emit("vector", lambda e, o=lg, t=sq[:, 0:512]:
                               e.tensor_tensor(out=o, in0=o, in1=t, op=ALU.subtract), sig=ev)
                        B.wait("scalar", ev, ev.n)
                else:
                    dst = ga["qg"] if pname == "q" else ga["kg"]
                    B.wait("scalar", mm, mm_done)
                    sq = s2["sq8"]
                    for half in range(2):
                        B.emit("scalar", lambda e, i=pp[half][:, 0:512],
                               o=sq[:, half * 512:(half + 1) * 512]:
                               e.activation(out=o, in_=i, func=AF.Square),
                               sig=sc if half == 1 else None)
                    B.wait("vector", sc, sc.n)
                    B.emit("vector", lambda e, o=ga["l2s"][:, 0:16],
                           i=sq[:, 0:1024].rearrange("p (h k) -> p h k", k=64):
                           e.tensor_reduce(out=o, in_=i, axis=AX.X, op=ALU.add), sig=ev)
                    B.wait("scalar", ev, ev.n)
                    B.emit("scalar", lambda e, o=ga["l2s"][:, 0:16], i=ga["l2s"][:, 0:16]:
                           e.activation(out=o, in_=i, func=AF.Sqrt), sig=sc)
                    B.wait("vector", sc, sc.n)
                    B.emit("vector", lambda e, o=ga["l2s"][:, 0:16]:
                           e.tensor_scalar(out=o, in0=o, scalar1=cval(g, 1e-12), scalar2=None,
                                           op0=ALU.max), sig=ev)
                    B.wait("vector", ev, ev.n)
                    B.emit("vector", lambda e, o=ga["l2s"][:, 0:16]:
                           e.reciprocal(out=o, in_=o), sig=ev)
                    B.wait("vector", ev, ev.n)
                    for h in range(NHEAD):
                        half = h // 8
                        B.emit("vector", lambda e, i=pp[half][:, (h % 8) * 64:(h % 8 + 1) * 64],
                               o=dst[:, st2 * 1024 + h * 64:st2 * 1024 + (h + 1) * 64],
                               r=ga["l2s"][:, h:h + 1]:
                               e.tensor_scalar(out=o, in0=i, scalar1=r, scalar2=None,
                                               op0=ALU.mult),
                               sig=ev if h == NHEAD - 1 else None)
        for st2 in range(nq):
            lgst = ga["lsg"][:, st2 * 1024:(st2 + 1) * 1024]
            B.wait("tensor", ev, ev.n)
            B.wait("tensor", sc, sc.n)
            for half in range(2):
                B.emit("tensor", lambda e, l=g["u64"][:], r=lgst[:, half * 512:(half + 1) * 512],
                       o=bI[half][:, 0:512]:
                       e.matmul(out=o, lhsT=l, rhs=r, start=True, stop=True),
                       sig=mm if half == 1 else None)
            B.wait("vector", mm, mm.n)
            for half in range(2):
                B.emit("vector", lambda e, i=bI[half][:, 0:512],
                       o=ga["Gc"][:, half * 512:(half + 1) * 512]:
                       e.tensor_copy(out=o, in_=i), sig=ev if half == 1 else None)
            B.wait("tensor", ev, ev.n)
            for half in range(2):
                B.emit("tensor", lambda e, l=g["ohend"][:], r=ga["Gc"][:, half * 512:(half + 1) * 512],
                       o=bS[half][0:2, 0:512]:
                       e.matmul(out=o, lhsT=l, rhs=r, start=True, stop=True))
            for c in range(NFC):
                B.emit("tensor", lambda e, l=ga["Gc"][:, c * P:(c + 1) * P], r=g["ohend"][:],
                       o=bM[0:P, c * 2:(c + 1) * 2]:
                       e.matmul(out=o, lhsT=l, rhs=r, start=True, stop=True),
                       sig=mm if c == NFC - 1 else None)
            B.wait("vector", mm, mm.n)
            for half in range(2):
                B.emit("vector", lambda e, i=bS[half][0:2, 0:512],
                       o=ga["decrow"][0:2, half * 512:(half + 1) * 512]:
                       e.tensor_copy(out=o, in_=i))
            B.emit("vector", lambda e, i=bM[0:P, 0:16], o=ga["deccols"][:, 0:16]:
                   e.tensor_copy(out=o, in_=i), sig=ev)
            B.wait("scalar", ev, ev.n)
            B.emit("scalar", lambda e, i=ga["deccols"][:, 0:16], o=ga["dec"][:, 0:16]:
                   e.activation(out=o, in_=i, func=AF.Exp), sig=sc)
            if dbg and grp == 0 and st2 == 0:
                dbg("Gc", ga["Gc"][:, 0:1024])
                dbg("lsg", ga["lsg"][:, 0:1024])
            B.wait("tensor", ev, ev.n)
            for half in range(2):
                B.emit("tensor", lambda e, l=g["subind"][:],
                       r=ga["decrow"][0:2, half * 512:(half + 1) * 512],
                       o=bB[half][:, 0:512]:
                       e.matmul(out=o, lhsT=l, rhs=r, start=True, stop=True),
                       sig=mm if half == 1 else None)
            B.wait("vector", mm, mm.n)
            for half in range(2):
                B.emit("vector", lambda e, o=ga["e2"][:, half * 512:(half + 1) * 512],
                       i0=ga["Gc"][:, half * 512:(half + 1) * 512], i1=bB[half][:, 0:512]:
                       e.scalar_tensor_tensor(out=o, in0=i0, scalar=-1.0, in1=i1,
                                              op0=ALU.mult, op1=ALU.add),
                       sig=ev if half == 1 else None)
            B.wait("scalar", ev, ev.n)
            B.emit("scalar", lambda e, o=ga["e2"][:, 0:1024], i=ga["e2"][:, 0:1024]:
                   e.activation(out=o, in_=i, func=AF.Exp))
            B.emit("scalar", lambda e, o=ga["cum"][:, 0:1024], i=ga["Gc"][:, 0:1024]:
                   e.activation(out=o, in_=i, func=AF.Exp))
            B.emit("scalar", lambda e, o=ga["icum"][:, 0:1024], i=ga["Gc"][:, 0:1024]:
                   e.activation(out=o, in_=i, func=AF.Exp, scale=-1.0), sig=sc)
            if dbg and grp == 0 and st2 == 0:
                dbg("cum", ga["cum"][:, 0:1024], after_sc=True)
                dbg("e2", ga["e2"][:, 0:1024], after_sc=True)
            B.wait("vector", sc, sc.n)
            qsl = ga["qg"][:, st2 * 1024:(st2 + 1) * 1024]
            ksl = ga["kg"][:, st2 * 1024:(st2 + 1) * 1024]
            B.emit("vector", lambda e, o=ga["qh"][:, 0:1024], a=qsl, b=ga["cum"][:, 0:1024]:
                   e.tensor_tensor(out=o, in0=a, in1=b, op=ALU.mult))
            B.emit("vector", lambda e, o=ga["kh"][:, 0:1024], a=ksl, b=ga["icum"][:, 0:1024]:
                   e.tensor_tensor(out=o, in0=a, in1=b, op=ALU.mult))
            B.emit("vector", lambda e, o=ga["k2"][:, 0:1024], a=ksl, b=ga["e2"][:, 0:1024]:
                   e.tensor_tensor(out=o, in0=a, in1=b, op=ALU.mult), sig=ev)
            if dbg and grp == 0 and st2 == 0:
                dbg("qg", ga["qg"][:, 0:1024])
                dbg("kg", ga["kg"][:, 0:1024])
                dbg("vg", ga["vg"][:, 0:1024])
                dbg("qh", ga["qh"][:, 0:1024])
                dbg("kh", ga["kh"][:, 0:1024])
                dbg("k2", ga["k2"][:, 0:1024])
            B.wait("tensor", ev, ev.n)
            for c in range(NFC):
                B.emit("tensor", lambda e, i=ga["qh"][:, c * P:(c + 1) * P],
                       o=pbT[:, c * P:(c + 1) * P]:
                       e.transpose(out=o, in_=i, identity=g["ident"][:]),
                       sig=mm if c == NFC - 1 else None)
            B.wait("vector", mm, mm.n)
            B.emit("vector", lambda e, i=pbT[:, 0:1024], o=ga["qhT"][:, 0:1024]:
                   e.tensor_copy(out=o, in_=i), sig=ev)
            B.wait("tensor", ev, ev.n)
            for c in range(NFC):
                B.emit("tensor", lambda e, i=ga["kh"][:, c * P:(c + 1) * P],
                       o=pbT[:, c * P:(c + 1) * P]:
                       e.transpose(out=o, in_=i, identity=g["ident"][:]),
                       sig=mm if c == NFC - 1 else None)
            B.wait("vector", mm, mm.n)
            B.emit("vector", lambda e, i=pbT[:, 0:1024], o=ga["khT"][:, 0:1024]:
                   e.tensor_copy(out=o, in_=i), sig=ev)
            if dbg and grp == 0 and st2 == 0:
                dbg("qhT", ga["qhT"][:, 0:1024])
                dbg("khT", ga["khT"][:, 0:1024])
            for cc in range(2):
                bc = cc * 64
                B.wait("tensor", ev, ev.n)
                for h in range(NHEAD):
                    ph = (h % 2) * 64
                    fc = h // 2
                    B.emit("tensor", lambda e,
                           l=ga["khT"][ph:ph + 64, fc * P + bc:fc * P + bc + 64],
                           r=ga["qhT"][ph:ph + 64, fc * P + bc:fc * P + bc + 64],
                           o=bS[h % 2][bc:bc + 64, fc * 64:(fc + 1) * 64], tp=(ph, bc):
                           e.matmul(out=o, lhsT=l, rhs=r, start=True, stop=True,
                                    tile_position=tp),
                           sig=mm if h == NHEAD - 1 else None)
                B.wait("vector", mm, mm.n)
                for par in range(2):
                    B.emit("vector", lambda e, i=bS[par][bc:bc + 64, 0:512],
                           o=ga["stil"][par][bc:bc + 64, 0:512], mk=g["mask64"][bc:bc + 64, :]:
                           e.tensor_tensor(out=o, in0=i, in1=mk, op=ALU.mult),
                           sig=ev if par == 1 else None)
                B.wait("tensor", ev, ev.n)
                for h in range(NHEAD):
                    ph = (h % 2) * 64
                    fc = h // 2
                    B.emit("tensor", lambda e,
                           l=ga["vg"][bc:bc + 64, st2 * 1024 + h * 64:st2 * 1024 + (h + 1) * 64],
                           r=ga["stil"][h % 2][bc:bc + 64, fc * 64:(fc + 1) * 64],
                           o=bI[cc][ph:ph + 64, fc * 64:(fc + 1) * 64], tp=(bc, ph):
                           e.matmul(out=o, lhsT=l, rhs=r, start=True, stop=True,
                                    tile_position=tp))
                    B.emit("tensor", lambda e,
                           l=ga["Sbf"][ph:ph + 64, fc * 64:(fc + 1) * 64],
                           r=ga["qhT"][ph:ph + 64, fc * P + bc:fc * P + bc + 64],
                           o=bB[h % 2][ph:ph + 64, fc * 64:(fc + 1) * 64], tp=(ph, ph):
                           e.matmul(out=o, lhsT=l, rhs=r, start=True, stop=True,
                                    tile_position=tp))
                    B.emit("tensor", lambda e,
                           l=ga["k2"][bc:bc + 64, h * 64:(h + 1) * 64],
                           r=ga["vg"][bc:bc + 64, st2 * 1024 + h * 64:st2 * 1024 + (h + 1) * 64],
                           o=bM[ph:ph + 64, fc * 64:(fc + 1) * 64], tp=(bc, ph):
                           e.matmul(out=o, lhsT=l, rhs=r, start=True, stop=True,
                                    tile_position=tp),
                           sig=mm if h == NHEAD - 1 else None)
                B.wait("vector", mm, mm.n)
                B.emit("vector", lambda e, i=bI[cc][:, 0:512], o=ga["oall"][:, 0:512]:
                       e.tensor_copy(out=o, in_=i))
                B.emit("vector", lambda e, o=ga["oall"][0:64, 0:512], i=bB[0][0:64, 0:512]:
                       e.tensor_tensor(out=o, in0=o, in1=i, op=ALU.add))
                B.emit("vector", lambda e, o=ga["oall"][64:128, 0:512], i=bB[1][64:128, 0:512]:
                       e.tensor_tensor(out=o, in0=o, in1=i, op=ALU.add))
                tglob = t0 + st2 * P + bc
                B.emit("vector", lambda e,
                       o=ga["o_buf"].rearrange("p (f t) -> p f t", f=NFC)[:, :, tglob:tglob + 64],
                       i=ga["oall"][:, 0:512].rearrange("p (f t) -> p f t", f=NFC):
                       e.tensor_copy(out=o, in_=i), sig=ev)
                for fc in range(NFC):
                    B.emit("vector", lambda e, o=ga["S32"][:, fc * 64:(fc + 1) * 64],
                           r=ga["dec"][:, fc * 2 + cc:fc * 2 + cc + 1]:
                           e.tensor_scalar(out=o, in0=o, scalar1=r, scalar2=None,
                                           op0=ALU.mult))
                B.emit("vector", lambda e, o=ga["S32"][:, 0:512], i=bM[:, 0:512]:
                       e.tensor_tensor(out=o, in0=o, in1=i, op=ALU.add))
                B.emit("vector", lambda e, o=ga["Sbf"][:, 0:512], i=ga["S32"][:, 0:512]:
                       e.tensor_copy(out=o, in_=i), sig=ev)
                if dbg and grp == 0 and st2 == 0:
                    dbg(f"stil{cc}", ga["stil"][0][:, 0:512])
                    dbg(f"S32c{cc}", ga["S32"][:, 0:512])



HC3 = (DFF0 + P - 1) // P


def emit_wo_swiglu(B, g, s2, ar, xn_tiles, acc_tiles, o_buf, wo_d, w1_d, w2_d, w3_d,
                   gffn_cols, Twidth, wt, wst):
    phase_fence(B)
    mm, ev, sc = B.sems["mm"], B.sems["ev"], B.sems["sc"]
    cast = B.sems["wcast"]
    pb = s2["pb"]
    amark = ar.mark()
    xmid = ar.take_bf16(NFC * TT)
    hgrp = ar.take_bf16(NFC * TT)
    gact = ar.take_bf16(HC3 * TT)
    w1c = [ar.take_bf16(NFC * P), ar.take_bf16(NFC * P)]
    w2c = [ar.take_bf16(NFC * P), ar.take_bf16(NFC * P)]
    w3m = ar.take_bf16(HC3 * P)
    sact = [ar.take_f32(TT), ar.take_f32(TT)]
    col_done = {}
    stt_done = {}
    last_w3 = [0]
    for it in range(Twidth // TT):
        ts = slice(it * TT, (it + 1) * TT)
        B.wait("sync", mm, mm.n)
        emit_load_bf16(B, wo_d, wt, D, 1024, wst, None, cast)
        B.wait("tensor", cast, cast.n)
        for m in range(NFC):
            pd = pb[m % 2]
            B.wait("tensor", ev, ev.n)
            for c in range(NFC):
                B.emit("tensor", lambda e, l=wt[:, c * 1024 + m * P:c * 1024 + (m + 1) * P],
                       r=o_buf[:, c * Twidth:(c + 1) * Twidth][:, ts], o=pd[:, 0:TT],
                       s=(c == 0), st=(c == NFC - 1):
                       e.matmul(out=o, lhsT=l, rhs=r, start=s, stop=st),
                       sig=mm if c == NFC - 1 else None)
            B.wait("vector", mm, mm.n)
            B.emit("vector", lambda e, o=xmid[:, m * TT:(m + 1) * TT], a=xn_tiles[m][:, ts],
                   i=pd[:, 0:TT]: e.tensor_tensor(out=o, in0=a, in1=i, op=ALU.add), sig=ev)
            B.emit("vector", lambda e, o=acc_tiles[m][:, ts], i=xmid[:, m * TT:(m + 1) * TT]:
                   e.tensor_tensor(out=o, in0=o, in1=i, op=ALU.add), sig=ev)
        xm_tiles = [xmid.rearrange("p (f t) -> p f t", f=NFC)[:, c, :] for c in range(NFC)]
        emit_rms_group(B, g, s2, xm_tiles, hgrp, gffn_cols, 0, TT, EPS_RMS0, pb[2], pb[4])
        for hc in range(HC3):
            w = min(P, DFF0 - hc * P)
            par = hc % 2
            key = it * HC3 + hc
            if key >= 2:
                B.wait("sync", mm, col_done[key - 2])
            emit_load_bf16(B, w1_d[:, hc * P:hc * P + w], w1c[par], D, w, wst, None, cast)
            emit_load_bf16(B, w2_d[:, hc * P:hc * P + w], w2c[par], D, w, wst, None, cast)
            pa = pb[2] if par == 0 else pb[3]
            pbk = pb[4] if par == 0 else pb[5]
            B.wait("tensor", cast, cast.n)
            B.wait("tensor", sc, sc.n)
            B.wait("tensor", ev, ev.n)
            for c in range(NFC):
                B.emit("tensor", lambda e, l=w1c[par][:, c * w:(c + 1) * w],
                       r=hgrp[:, c * TT:(c + 1) * TT], o=pa[0:w, 0:TT],
                       s=(c == 0), st=(c == NFC - 1):
                       e.matmul(out=o, lhsT=l, rhs=r, start=s, stop=st),
                       sig=mm if c == NFC - 1 else None)
            for c in range(NFC):
                B.emit("tensor", lambda e, l=w2c[par][:, c * w:(c + 1) * w],
                       r=hgrp[:, c * TT:(c + 1) * TT], o=pbk[0:w, 0:TT],
                       s=(c == 0), st=(c == NFC - 1):
                       e.matmul(out=o, lhsT=l, rhs=r, start=s, stop=st),
                       sig=mm if c == NFC - 1 else None)
            col_done[key] = mm.n
            B.wait("scalar", mm, mm.n - 1)
            if key >= 2:
                B.wait("scalar", ev, stt_done[key - 2])
            B.emit("scalar", lambda e, i=pa[0:w, 0:TT], o=sact[par].bitcast(F32)[0:w, :] if False
                   else sact[par][0:w, 0:TT] if False else sact[par][0:w, :]:
                   e.activation(out=o, in_=i, func=AF.Silu), sig=sc)
            B.wait("vector", mm, mm.n)
            B.wait("vector", sc, sc.n)
            B.emit("vector", lambda e, i0=pbk[0:w, 0:TT], i1=sact[par][0:w, :],
                   o=gact[0:w, hc * TT:(hc + 1) * TT]:
                   e.tensor_tensor(out=o, in0=i0, in1=i1, op=ALU.mult), sig=ev)
            stt_done[key] = ev.n
        for m in range(NFC):
            if last_w3[0] > 0:
                B.wait("sync", mm, last_w3[0])
            emit_load_bf16(B, w3_d[:, m * P:(m + 1) * P], w3m, DFF0, P, wst, None, cast)
            pd = pb[m % 2]
            B.wait("tensor", cast, cast.n)
            B.wait("tensor", ev, ev.n)
            for hc in range(HC3):
                w = min(P, DFF0 - hc * P)
                B.emit("tensor", lambda e, l=w3m[0:w, hc * P:(hc + 1) * P],
                       r=gact[0:w, hc * TT:(hc + 1) * TT], o=pd[:, 0:TT],
                       s=(hc == 0), st=(hc == HC3 - 1):
                       e.matmul(out=o, lhsT=l, rhs=r, start=s, stop=st),
                       sig=mm if hc == HC3 - 1 else None)
            last_w3[0] = mm.n
            B.wait("vector", mm, mm.n)
            B.emit("vector", lambda e, i=pd[:, 0:TT], a=acc_tiles[m][:, ts]:
                   e.tensor_tensor(out=a, in0=a, in1=i, op=ALU.add), sig=ev)
    ar.reset(amark)



NEC = DIN // P
NS = 16


def emit_halo_norm(B, g, s2, halos, halosb, xrh, nw_cols, rms3bc):
    mm, ev, sc = B.sems["mm"], B.sems["ev"], B.sems["sc"]
    pb = s2["pb"]
    W3 = 3
    B.emit("vector", lambda e, o=halosb[:, 0:24], i=halos[:, 0:24]:
           e.tensor_copy(out=o, in_=i), sig=ev)
    B.wait("scalar", ev, ev.n)
    B.wait("scalar", mm, mm.n)
    for c in range(NFC):
        B.emit("scalar", lambda e, i=halos[:, c * 3:(c + 1) * 3], o=s2["sq8"][:, c * 4:c * 4 + 3]:
               e.activation(out=o, in_=i, func=AF.Square), sig=sc if c == NFC - 1 else None)
    B.wait("tensor", sc, sc.n)
    B.wait("tensor", ev, ev.n)
    B.wait("tensor", ev, ev.n)
    for c in range(NFC):
        B.emit("tensor", lambda e, r=halosb[:, c * 3:(c + 1) * 3], o=s2["pbS"][0:1, 0:3],
               l=g["ones_col"][:], s=(c == 0), st=(c == NFC - 1):
               e.matmul(out=o, lhsT=l, rhs=r, start=s, stop=st))
        B.emit("tensor", lambda e, r=s2["sq8"][:, c * 4:c * 4 + 3], o=s2["pbS"][32:33, 0:3],
               l=g["ones_col"][:], s=(c == 0), st=(c == NFC - 1):
               e.matmul(out=o, lhsT=l, rhs=r, start=s, stop=st),
               sig=mm if c == NFC - 1 else None)
    B.wait("vector", mm, mm.n)
    mrow, vrow, arow, brow = s2["mrow"], s2["vrow"], s2["arow"], s2["brow"]
    B.emit("vector", lambda e, o=mrow[0:1, 0:3], i=s2["pbS"][0:1, 0:3]:
           e.tensor_scalar(out=o, in0=i, scalar1=1.0 / D, scalar2=None, op0=ALU.mult))
    B.emit("vector", lambda e, o=vrow[0:1, 0:3], i=s2["pbS"][32:33, 0:3]:
           e.tensor_scalar(out=o, in0=i, scalar1=1.0 / D, scalar2=None, op0=ALU.mult), sig=ev)
    B.emit("vector", lambda e, o=brow[0:1, 0:3], m=mrow[0:1, 0:3]:
           e.tensor_tensor(out=o, in0=m, in1=m, op=ALU.mult))
    B.emit("vector", lambda e, o=vrow[0:1, 0:3], t=brow[0:1, 0:3]:
           e.tensor_tensor(out=o, in0=o, in1=t, op=ALU.subtract), sig=ev)
    B.wait("scalar", ev, ev.n)
    B.emit("scalar", lambda e, o=arow[0:1, 0:3], i=vrow[0:1, 0:3]:
           e.activation(out=o, in_=i, func=AF.Sqrt, bias=cval1(g, float(EPS_LN))), sig=sc)
    B.wait("vector", sc, sc.n)
    B.emit("vector", lambda e, o=arow[0:1, 0:3]: e.reciprocal(out=o, in_=o), sig=ev)
    B.wait("vector", ev, ev.n)
    B.emit("vector", lambda e, o=brow[0:1, 0:3], m=mrow[0:1, 0:3], a=arow[0:1, 0:3]:
           e.scalar_tensor_tensor(out=o, in0=m, scalar=-1.0, in1=a,
                                  op0=ALU.mult, op1=ALU.mult), sig=ev)
    B.wait("tensor", ev, ev.n)
    B.emit("tensor", lambda e, r=arow[0:1, 0:3], o=s2["pb"][1][:, 0:3], l=g["ones_row"][:]:
           e.matmul(out=o, lhsT=l, rhs=r, start=True, stop=True))
    B.emit("tensor", lambda e, r=brow[0:1, 0:3], o=s2["pb"][2][:, 0:3], l=g["ones_row"][:]:
           e.matmul(out=o, lhsT=l, rhs=r, start=True, stop=True), sig=mm)
    B.wait("vector", mm, mm.n)
    xnh = s2["bc_a"]
    for c in range(NFC):
        B.emit("vector", lambda e, o=xnh[:, c * 3:(c + 1) * 3], x=halos[:, c * 3:(c + 1) * 3],
               a=s2["pb"][1][:, 0:3]:
               e.tensor_tensor(out=o, in0=x, in1=a, op=ALU.mult))
        B.emit("vector", lambda e, o=xnh[:, c * 3:(c + 1) * 3], b=s2["pb"][2][:, 0:3]:
               e.tensor_tensor(out=o, in0=o, in1=b, op=ALU.add),
               sig=ev if c == NFC - 1 else None)
    B.wait("scalar", ev, ev.n)
    for c in range(NFC):
        B.emit("scalar", lambda e, i=xnh[:, c * 3:(c + 1) * 3], o=s2["sq8"][:, c * 4:c * 4 + 3]:
               e.activation(out=o, in_=i, func=AF.Square), sig=sc if c == NFC - 1 else None)
    B.wait("tensor", sc, sc.n)
    for c in range(NFC):
        B.emit("tensor", lambda e, r=s2["sq8"][:, c * 4:c * 4 + 3], o=s2["pbS"][0:1, 0:3],
               l=g["ones_col"][:], s=(c == 0), st=(c == NFC - 1):
               e.matmul(out=o, lhsT=l, rhs=r, start=s, stop=st),
               sig=mm if c == NFC - 1 else None)
    B.wait("scalar", mm, mm.n)
    B.emit("scalar", lambda e, o=arow[0:1, 0:3], i=s2["pbS"][0:1, 0:3]:
           e.activation(out=o, in_=i, func=AF.Sqrt, scale=1.0 / D, bias=cval1(g, float(EPS_RMS1))), sig=sc)
    B.wait("vector", sc, sc.n)
    B.emit("vector", lambda e, o=arow[0:1, 0:3]: e.reciprocal(out=o, in_=o), sig=ev)
    B.wait("tensor", ev, ev.n)
    B.emit("tensor", lambda e, r=arow[0:1, 0:3], o=s2["pb"][1][:, 0:3], l=g["ones_row"][:]:
           e.matmul(out=o, lhsT=l, rhs=r, start=True, stop=True), sig=mm)
    B.wait("vector", mm, mm.n)
    B.emit("vector", lambda e, o=rms3bc[:, 0:3], i=s2["pb"][1][:, 0:3]:
           e.tensor_copy(out=o, in_=i), sig=ev)
    for c in range(NFC):
        B.emit("vector", lambda e, o=xrh[:, c * 3:(c + 1) * 3], x=xnh[:, c * 3:(c + 1) * 3]:
               e.tensor_copy(out=o, in_=x),
               sig=ev if c == NFC - 1 else None)


def build_l1_kernel():
    nc = bass.Bass("TRN2", target_bir_lowering=False, debug=False)
    r_d = nc.dram_tensor("resid", [D, T], F32, kind="ExternalInput").ap()
    x1h_d = nc.dram_tensor("x1halo", [P, NEC * 3], F32, kind="ExternalInput").ap()
    hin_d = nc.dram_tensor("hinit", [P, NEC * NS], F32, kind="ExternalInput").ap()
    nw_d = nc.dram_tensor("normw", [D, 1], F32, kind="ExternalInput").ap()
    skT_d = nc.dram_tensor("skipT", [D, DIN], F32, kind="ExternalInput").ap()
    inT_d = nc.dram_tensor("inT", [D, DIN], F32, kind="ExternalInput").ap()
    cw_d = nc.dram_tensor("convw", [DIN, 4], F32, kind="ExternalInput").ap()
    cb_d = nc.dram_tensor("convb", [DIN, 1], F32, kind="ExternalInput").ap()
    wd1T_d = nc.dram_tensor("wd1T", [DIN, 64], F32, kind="ExternalInput").ap()
    wd2T_d = nc.dram_tensor("wd2T", [64, DIN], F32, kind="ExternalInput").ap()
    wd2b_d = nc.dram_tensor("wd2b", [DIN, 1], F32, kind="ExternalInput").ap()
    wbT_d = nc.dram_tensor("wbT", [DIN, NS], F32, kind="ExternalInput").ap()
    wcT_d = nc.dram_tensor("wcT", [DIN, NS], F32, kind="ExternalInput").ap()
    al_d = nc.dram_tensor("alog", [DIN, NS], F32, kind="ExternalInput").ap()
    wdd_d = nc.dram_tensor("wD", [DIN, 1], F32, kind="ExternalInput").ap()
    outT_d = nc.dram_tensor("outT", [DIN, D], F32, kind="ExternalInput").ap()
    mWg_d = nc.dram_tensor("mWg", [D, 8], F32, kind="ExternalInput").ap()
    mWu_d = nc.dram_tensor("mWu", [8 * D, DIN], F32, kind="ExternalInput").ap()
    mWd_d = nc.dram_tensor("mWd", [8 * DIN, D], F32, kind="ExternalInput").ap()
    mW1_d = nc.dram_tensor("mW1", [D, DIN], F32, kind="ExternalInput").ap()
    mW2_d = nc.dram_tensor("mW2", [DIN, D], F32, kind="ExternalInput").ap()
    y_d = nc.dram_tensor("y", [D, T], F32, kind="ExternalOutput").ap()
    yh_d = nc.dram_tensor("yh", [P, NEC * NS], F32, kind="ExternalOutput").ap()
    yx1h_d = nc.dram_tensor("yx1h", [P, NEC * 3], F32, kind="ExternalOutput").ap()
    bsc_d = nc.dram_tensor("bsc", [NS, T], F32, kind="Internal").ap()
    csc_d = nc.dram_tensor("csc", [NS, T], F32, kind="Internal").ap()
    cwsc_d = nc.dram_tensor("cwsc", [8, T], F32, kind="Internal").ap()

    import os as _os
    _dbg = _os.environ.get("L1DBG", "0") == "1"
    if _dbg:
        dmp_d = nc.dram_tensor("dmp", [8 * P, 4096], F32, kind="ExternalOutput").ap()
    SLL = 256
    ctx = ExitStack()
    with ctx:
        B = Builder(nc, ctx)
        for s in ("mm", "ev", "sc", "wld", "wcast", "consts", "st", "gld", "cwdma"):
            B.sem(s)
        bdma = [B.sem("bdma0"), B.sem("bdma1")]
        g = alloc_consts(B)
        ar = Arena(B, 35100)
        s2 = {}
        s2["pbS"] = ptensor(B, "pbS", [P, TT])
        s2["pb"] = [ptensor(B, f"pb{i}", [P, TT]) for i in range(6)]
        s2["pb"].append(s2["pbS"])
        s2["pbT"] = ptensor(B, "pbT", [P, 1024], BF16)
        pb = s2["pb"]
        bufA = sbuf(B, "bufA", [P, NFC * T], BF16)
        bufB = sbuf(B, "bufB", [P, NFC * T], BF16)
        acc_tiles = [bufA[:, c * T:(c + 1) * T] for c in range(NFC)]
        xn_tiles = [bufB[:, c * T:(c + 1) * T] for c in range(NFC)]
        smalls = sbuf(B, "smalls", [P, 480], F32)
        off = [0]
        def sm(n):
            a = smalls[:, off[0]:off[0] + n]
            off[0] += n
            return a
        nwcols = sm(8); convw = sm(64); convb = sm(16); wd2b = sm(16)
        wDc = sm(16); Acols = sm(256); x1h = sm(48); x1hk = sm(48)
        hbuf = sbuf(B, "hbuf", [P, NEC * NS], F32)

        ld, cast, gld = B.sems["wld"], B.sems["wcast"], B.sems["gld"]
        mm, ev, sc = B.sems["mm"], B.sems["ev"], B.sems["sc"]
        st = B.sems["st"]
        wstage = Stage(B, ar.take_f32(2 * 1024), cast, "wst")
        xstage = Stage(B, ar.take_f32(2 * 1024), cast, "xst")
        dcore = ar.take_bf16(T)
        bcstg = ar.take_f32(TT, parts=NS)
        rms_bc = ar.take_f32(T)
        xs = ar.take_bf16_top(NEC * T)
        xs_ec = [xs[:, e * T:(e + 1) * T] for e in range(NEC)]
        mark0 = ar.mark()
        emit_consts(B, g, B.sems["consts"])
        for c in range(NFC):
            for h in range(2):
                par, ssl = xstage.slot(1024)
                if xstage.lastc[par] > 0:
                    B.wait("sync", cast, xstage.lastc[par])
                sld = xstage.ld[par]
                B.emit("sync", lambda e, s=ssl, d=r_d[c * P:(c + 1) * P, h * 1024:(h + 1) * 1024]:
                       e.dma_start(out=s, in_=d), sig=sld, sigk=16)
                B.wait("gpsimd", sld, sld.n)
                B.emit("gpsimd", lambda e, s=ssl, o=acc_tiles[c][:, h * 1024:(h + 1) * 1024]:
                       e.tensor_copy(out=o, in_=s), sig=cast)
                xstage.lastc[par] = cast.n
        def ldsmall(dst, dram, rearr=True):
            if rearr:
                w = dram.shape[1]
                src_ap = dram.rearrange("(c p) w -> p c w", p=P)
                dstv = dst.rearrange("p (c w) -> p c w", w=w)
            else:
                src_ap, dstv = dram, dst
            B.emit("sync", lambda e, o=dstv, i=src_ap:
                   e.dma_start(out=o, in_=i, allow_slow_non_contiguous=True), sig=gld, sigk=16)
        ldsmall(nwcols, nw_d); ldsmall(convw, cw_d); ldsmall(convb, cb_d)
        ldsmall(wd2b, wd2b_d); ldsmall(wDc, wdd_d); ldsmall(Acols, al_d)
        ldsmall(x1h, x1h_d, rearr=False); ldsmall(hbuf[:, :], hin_d, rearr=False)
        B.wait("scalar", gld, gld.n)
        B.emit("scalar", lambda e, o=Acols, i=Acols:
               e.activation(out=o, in_=i, func=AF.Exp), sig=sc)
        B.wait("vector", sc, sc.n)
        B.emit("vector", lambda e, o=Acols:
               e.tensor_scalar(out=o, in0=Acols, scalar1=-1.0, scalar2=None, op0=ALU.mult), sig=ev)
        for e in ("tensor", "vector", "scalar"):
            B.wait(e, cast, cast.n)
            B.wait(e, B.sems["consts"], B.sems["consts"].n)
            B.wait(e, gld, gld.n)
            B.wait(e, ev, ev.n)
        if _dbg:
            dst = xstage.ap
            def dump(slot, ap, w):
                for j0 in range(0, w, 2048):
                    wj = min(2048, w - j0)
                    B.wait("vector", B.sems["ev"], B.sems["ev"].n)
                    B.wait("vector", B.sems["mm"], B.sems["mm"].n)
                    B.wait("vector", B.sems["sc"], B.sems["sc"].n)
                    B.wait("vector", B.sems["st"], B.sems["st"].n)
                    B.emit("vector", lambda e, o=dst[0:ap.shape[0], 0:wj], i=ap[:, j0:j0 + wj]:
                           e.tensor_copy(out=o, in_=i), sig=B.sems["ev"])
                    B.wait("sync", B.sems["ev"], B.sems["ev"].n)
                    B.emit("sync", lambda e, o=dmp_d[slot * P:slot * P + ap.shape[0], j0:j0 + wj],
                           i=dst[0:ap.shape[0], 0:wj]: e.dma_start(out=o, in_=i),
                           sig=B.sems["st"], sigk=16)
                    for en in ("vector", "tensor", "scalar", "gpsimd"):
                        B.wait(en, B.sems["st"], B.sems["st"].n)
            build_l1_kernel._dump = dump
        s2x = alloc_ln_scratch(B, ar)
        for k in ("sq8", "mrow", "vrow", "arow", "brow", "bc_a", "bc_b"):
            s2[k] = s2x[k]
        emit_ln(B, g, s2, acc_tiles, xn_tiles, T)
        nw_cols = [nwcols[:, c:c + 1] for c in range(NFC)]
        for it in range(NTT):
            ts = slice(it * TT, (it + 1) * TT)
            B.wait("scalar", mm, mm.n)
            for c in range(NFC):
                B.emit("scalar", lambda e, i=xn_tiles[c][:, ts], o=s2["sq8"][:, c * TT:(c + 1) * TT]:
                       e.activation(out=o, in_=i, func=AF.Square), sig=sc if c == NFC - 1 else None)
            B.wait("tensor", sc, sc.n)
            B.wait("tensor", ev, ev.n)
            for c in range(NFC):
                B.emit("tensor", lambda e, r=s2["sq8"][:, c * TT:(c + 1) * TT], o=pb[6][0:1, 0:TT],
                       l=g["ones_col"][:], s=(c == 0), st=(c == NFC - 1):
                       e.matmul(out=o, lhsT=l, rhs=r, start=s, stop=st),
                       sig=mm if c == NFC - 1 else None)
            B.wait("scalar", mm, mm.n)
            B.emit("scalar", lambda e, o=s2["arow"][0:1, 0:TT], i=pb[6][0:1, 0:TT]:
                   e.activation(out=o, in_=i, func=AF.Sqrt, scale=1.0 / D,
                                bias=cval1(g, float(EPS_RMS1))), sig=sc)
            B.wait("vector", sc, sc.n)
            B.emit("vector", lambda e, o=s2["arow"][0:1, 0:TT]:
                   e.reciprocal(out=o, in_=s2["arow"][0:1, 0:TT]), sig=ev)
            B.wait("tensor", ev, ev.n)
            B.emit("tensor", lambda e, r=s2["arow"][0:1, 0:TT], o=pb[4][:, 0:TT],
                   l=g["ones_row"][:]:
                   e.matmul(out=o, lhsT=l, rhs=r, start=True, stop=True), sig=mm)
            B.wait("vector", mm, mm.n)
            B.emit("vector", lambda e, o=rms_bc[:, ts], i=pb[4][:, 0:TT]:
                   e.tensor_copy(out=o, in_=i), sig=ev)

        wcol = [ar.take_bf16(NFC * P), ar.take_bf16(NFC * P)]
        x1e = ar.take_bf16(2056)
        cvt = ar.take_f32(T)
        for ec in range(NEC):
            par = ec % 2
            B.wait("sync", mm, mm.n)
            emit_load_bf16(B, inT_d[:, ec * P:(ec + 1) * P], wcol[par], D, P, wstage, None, cast,
                           mul_cols=nwcols)
            B.wait("tensor", cast, cast.n)
            B.wait("tensor", sc, sc.n)
            B.wait("tensor", ev, ev.n)
            B.emit("vector", lambda e, o=x1e[:, 0:3], i=x1h[:, ec * 3:(ec + 1) * 3]:
                   e.tensor_copy(out=o, in_=i), sig=ev)
            for it in range(NTT):
                pp = pb[it % 2]
                ts = slice(it * TT, (it + 1) * TT)
                for c in range(NFC):
                    B.emit("tensor", lambda e, l=wcol[par][:, c * P:(c + 1) * P],
                           r=xn_tiles[c][:, ts], o=pp[:, 0:TT], s=(c == 0), st=(c == NFC - 1):
                           e.matmul(out=o, lhsT=l, rhs=r, start=s, stop=st),
                           sig=mm if c == NFC - 1 else None)
                B.wait("vector", mm, mm.n)
                B.emit("vector", lambda e, o=x1e[:, 3 + it * TT:3 + (it + 1) * TT], a=pp[:, 0:TT],
                       b=rms_bc[:, ts]: e.tensor_tensor(out=o, in0=a, in1=b, op=ALU.mult), sig=ev)
            B.emit("vector", lambda e, o=x1hk[:, ec * 3:(ec + 1) * 3],
                   i=x1e[:, 3 + T - 3:3 + T]: e.tensor_copy(out=o, in_=i), sig=ev)
            B.emit("vector", lambda e, o=cvt, i=x1e[:, 0:T], w0=convw[:, ec * 4:ec * 4 + 1],
                   b=convb[:, ec:ec + 1]:
                   e.tensor_scalar(out=o, in0=i, scalar1=w0, scalar2=b,
                                   op0=ALU.mult, op1=ALU.add))
            for k in (1, 2, 3):
                B.emit("vector", lambda e, o=cvt, i=x1e[:, k:k + T],
                       w=convw[:, ec * 4 + k:ec * 4 + k + 1]:
                       e.scalar_tensor_tensor(out=o, in0=i, scalar=w, in1=o,
                                              op0=ALU.mult, op1=ALU.add),
                       sig=ev if k == 3 else None)
            if _dbg and ec == 0:
                B.wait("vector", ev, ev.n)
                build_l1_kernel._dump(5, x1e[:, 0:16], 16)
                build_l1_kernel._dump(4, cvt[:, 0:16], 16)
            B.wait("scalar", ev, ev.n)
            B.emit("scalar", lambda e, o=xs_ec[ec][:, 0:T], i=cvt:
                   e.activation(out=o, in_=i, func=AF.Silu), sig=sc)
        if _dbg:
            build_l1_kernel._dump(0, xn_tiles[0][:, 0:2048], 2048)
            build_l1_kernel._dump(1, rms_bc[:, 0:2048], 2048)
            build_l1_kernel._dump(2, xs_ec[0][:, 0:2048], 2048)
            build_l1_kernel._dump(6, xrh[:, 0:24], 24)
            build_l1_kernel._dump(7, smalls[:, 0:400], 400)
        phase_fence(B)
        ar.reset(mark0)
        wd1t = ar.take_bf16(NEC * 64)
        wbct = ar.take_bf16(2 * NEC * NS)
        emit_load_bf16(B, wd1T_d, wd1t, DIN, 64, wstage, None, cast)
        emit_load_bf16(B, wbT_d, wbct, DIN, NS, wstage, None, cast, dst_stride=2 * NS, dst_off=0)
        emit_load_bf16(B, wcT_d, wbct, DIN, NS, wstage, None, cast, dst_stride=2 * NS, dst_off=NS)
        B.wait("tensor", cast, cast.n)
        B.wait("tensor", sc, sc.n)
        for it in range(NTT):
            ts = slice(it * TT, (it + 1) * TT)
            B.wait("tensor", ev, ev.n)
            for ec in range(NEC):
                B.emit("tensor", lambda e, l=wd1t[:, ec * 64:(ec + 1) * 64],
                       r=xs_ec[ec][:, ts], o=pb[0][0:64, 0:TT],
                       s=(ec == 0), st=(ec == NEC - 1):
                       e.matmul(out=o, lhsT=l, rhs=r, start=s, stop=st))
            for ec in range(NEC):
                B.emit("tensor", lambda e, l=wbct[:, ec * 2 * NS:ec * 2 * NS + NS],
                       r=xs_ec[ec][:, ts], o=pb[1][0:NS, 0:TT],
                       s=(ec == 0), st=(ec == NEC - 1):
                       e.matmul(out=o, lhsT=l, rhs=r, start=s, stop=st))
            for ec in range(NEC):
                B.emit("tensor", lambda e, l=wbct[:, ec * 2 * NS + NS:(ec + 1) * 2 * NS],
                       r=xs_ec[ec][:, ts], o=pb[2][0:NS, 0:TT],
                       s=(ec == 0), st=(ec == NEC - 1):
                       e.matmul(out=o, lhsT=l, rhs=r, start=s, stop=st),
                       sig=mm if ec == NEC - 1 else None)
            B.wait("vector", mm, mm.n)
            B.emit("vector", lambda e, o=dcore[0:64, ts], i=pb[0][0:64, 0:TT]:
                   e.tensor_copy(out=o, in_=i), sig=ev)
            B.wait("vector", st, st.n)
            B.emit("vector", lambda e, o=bcstg[0:NS, 0:TT], i=pb[1][0:NS, 0:TT]:
                   e.tensor_copy(out=o, in_=i), sig=ev)
            B.wait("sync", ev, ev.n)
            B.emit("sync", lambda e, o=bsc_d[0:NS, ts], i=bcstg[0:NS, 0:TT]:
                   e.dma_start(out=o, in_=i), sig=st, sigk=16)
            B.wait("vector", st, st.n)
            B.emit("vector", lambda e, o=bcstg[0:NS, 0:TT], i=pb[2][0:NS, 0:TT]:
                   e.tensor_copy(out=o, in_=i), sig=ev)
            B.wait("sync", ev, ev.n)
            B.emit("sync", lambda e, o=csc_d[0:NS, ts], i=bcstg[0:NS, 0:TT]:
                   e.dma_start(out=o, in_=i), sig=st, sigk=16)
        if _dbg:
            build_l1_kernel._dump(3, dcore[0:64, 0:2048], 2048)
        phase_fence(B)
        ar.reset(mark0)
        skcol = [ar.take_bf16(NFC * P), ar.take_bf16(NFC * P)]
        wd2c = [ar.take_bf16(P), ar.take_bf16(P)]
        ab = ar.take_f32(NS * SLL)
        b_in = ar.take_bf16(NS * SLL)
        Bbc = ar.take_f32(NS * SLL // 2)
        delta = ar.take_f32(SLL)
        useg = ar.take_f32(SLL)
        ysl = ar.take_f32(SLL)
        ssk = ar.take_f32(SLL)
        softs = ar.take_f32(2 * SLL)
        B.wait("sync", st, st.n)
        for ec in range(NEC):
            par = ec % 2
            B.wait("sync", mm, mm.n)
            emit_load_bf16(B, skT_d[:, ec * P:(ec + 1) * P], skcol[par], D, P, wstage, None, cast,
                           mul_cols=nwcols)
            emit_load_bf16(B, wd2T_d[:, ec * P:(ec + 1) * P], wd2c[par], 64, P, wstage, None, cast)
            skdone = cast.n
            for isl in range(T // SLL):
                ts = slice(isl * SLL, (isl + 1) * SLL)
                B.wait("tensor", cast, skdone)
                B.wait("tensor", sc, sc.n)
                B.wait("tensor", ev, ev.n)
                B.emit("tensor", lambda e, l=wd2c[par][0:64, 0:P], r=dcore[0:64, ts],
                       o=pb[5][:, 0:SLL]:
                       e.matmul(out=o, lhsT=l, rhs=r, start=True, stop=True), sig=mm)
                B.wait("vector", mm, mm.n)
                B.emit("vector", lambda e, o=delta, i=pb[5][:, 0:SLL], b=wd2b[:, ec:ec + 1]:
                       e.tensor_scalar(out=o, in0=i, scalar1=b, scalar2=None, op0=ALU.add), sig=ev)
                B.wait("scalar", ev, ev.n)
                B.emit("scalar", lambda e, o=softs[:, 0:SLL], i=delta:
                       e.activation(out=o, in_=i, func=AF.Abs))
                B.emit("scalar", lambda e, o=softs[:, SLL:2 * SLL], i=softs[:, 0:SLL]:
                       e.activation(out=o, in_=i, func=AF.Exp, scale=-1.0))
                B.emit("scalar", lambda e, o=softs[:, 0:SLL], i=softs[:, SLL:2 * SLL]:
                       e.activation(out=o, in_=i, func=AF.Ln, bias=cval(g, 1.0)), sig=sc)
                B.wait("vector", sc, sc.n)
                B.emit("vector", lambda e, o=delta:
                       e.tensor_scalar(out=o, in0=delta, scalar1=0.0, scalar2=None, op0=ALU.max), sig=ev)
                B.wait("vector", ev, ev.n)
                B.emit("vector", lambda e, o=delta, t=softs[:, 0:SLL]:
                       e.tensor_tensor(out=o, in0=o, in1=t, op=ALU.add), sig=ev)
                B.wait("vector", ev, ev.n)
                B.emit("vector", lambda e, o=useg, a=delta, b=xs_ec[ec][:, ts]:
                       e.tensor_tensor(out=o, in0=a, in1=b, op=ALU.mult), sig=ev)
                B.wait("scalar", ev, ev.n)
                for n in range(NS):
                    B.emit("scalar", lambda e, o=ab[:, n * SLL:(n + 1) * SLL], i=delta,
                           a=Acols[:, ec * NS + n:ec * NS + n + 1]:
                           e.activation(out=o, in_=i, func=AF.Exp, scale=a),
                           sig=sc if n == NS - 1 else None)
                if _dbg and ec == 0 and isl == 0:
                    B.wait("vector", sc, sc.n)
                    build_l1_kernel._dump(4, delta[:, 0:SLL], SLL)
                HSL = SLL // 2
                B.wait("vector", sc, sc.n)
                for hb in range(2):
                    hts = slice(isl * SLL + hb * HSL, isl * SLL + (hb + 1) * HSL)
                    B.wait("sync", ev, ev.n)
                    B.emit("sync", lambda e, o=Bbc[:, 0:NS * HSL].rearrange("p (n t) -> p n t", n=NS),
                           i=bsc_d.rearrange("(o n) t -> o n t", o=1)[0:1, 0:NS, hts].broadcast_to([P, NS, HSL]):
                           e.dma_start(out=o, in_=i), sig=bdma[hb], sigk=16)
                    B.wait("vector", bdma[hb], bdma[hb].n)
                    for n in range(NS):
                        B.emit("vector", lambda e,
                               o=b_in[:, n * SLL + hb * HSL:n * SLL + (hb + 1) * HSL],
                               a=useg[:, hb * HSL:(hb + 1) * HSL],
                               bb=Bbc[:, n * HSL:(n + 1) * HSL]:
                               e.tensor_tensor(out=o, in0=a, in1=bb, op=ALU.mult),
                               sig=ev if n == NS - 1 else None)
                B.wait("vector", ev, ev.n)
                for n in range(NS):
                    B.emit("vector", lambda e, o=ab[:, n * SLL:(n + 1) * SLL],
                           d0=ab[:, n * SLL:(n + 1) * SLL], d1=b_in[:, n * SLL:(n + 1) * SLL],
                           ini=hbuf[:, ec * NS + n:ec * NS + n + 1]:
                           e.tensor_tensor_scan(out=o, data0=d0, data1=d1, initial=ini,
                                                op0=ALU.mult, op1=ALU.add),
                           sig=ev if n == NS - 1 else None)
                B.wait("vector", ev, ev.n)
                B.emit("vector", lambda e, o=hbuf[:, ec * NS:(ec + 1) * NS],
                       i=ab.rearrange("p (n t) -> p n t", n=NS)[:, :, SLL - 1]:
                       e.tensor_copy(out=o, in_=i), sig=ev)
                for hb in range(2):
                    hts = slice(isl * SLL + hb * HSL, isl * SLL + (hb + 1) * HSL)
                    B.wait("sync", ev, ev.n)
                    B.emit("sync", lambda e, o=Bbc[:, 0:NS * HSL].rearrange("p (n t) -> p n t", n=NS),
                           i=csc_d.rearrange("(o n) t -> o n t", o=1)[0:1, 0:NS, hts].broadcast_to([P, NS, HSL]):
                           e.dma_start(out=o, in_=i), sig=bdma[hb], sigk=16)
                    B.wait("vector", bdma[hb], bdma[hb].n)
                    for n in range(NS):
                        B.emit("vector", lambda e,
                               o=ab[:, n * SLL + hb * HSL:n * SLL + (hb + 1) * HSL],
                               cc=Bbc[:, n * HSL:(n + 1) * HSL]:
                               e.tensor_tensor(out=o, in0=o, in1=cc, op=ALU.mult),
                               sig=ev if n == NS - 1 else None)
                B.wait("vector", ev, ev.n)
                nblk = NS
                while nblk > 1:
                    half = nblk // 2
                    B.emit("vector", lambda e, o=ab[:, 0:half * SLL],
                           i=ab[:, half * SLL:nblk * SLL]:
                           e.tensor_tensor(out=o, in0=o, in1=i, op=ALU.add), sig=ev)
                    B.wait("vector", ev, ev.n)
                    nblk = half
                B.emit("vector", lambda e, o=ysl, i=xs_ec[ec][:, ts], w=wDc[:, ec:ec + 1],
                       y=ab[:, 0:SLL]:
                       e.scalar_tensor_tensor(out=o, in0=i, scalar=w, in1=y,
                                              op0=ALU.mult, op1=ALU.add), sig=ev)
                B.wait("tensor", cast, skdone)
                B.wait("tensor", ev, ev.n)
                for c in range(NFC):
                    B.emit("tensor", lambda e, l=skcol[par][:, c * P:(c + 1) * P],
                           r=xn_tiles[c][:, ts], o=pb[4][:, 0:SLL],
                           s=(c == 0), st=(c == NFC - 1):
                           e.matmul(out=o, lhsT=l, rhs=r, start=s, stop=st),
                           sig=mm if c == NFC - 1 else None)
                B.wait("vector", mm, mm.n)
                B.emit("vector", lambda e, o=ssk[:, 0:SLL], a=pb[4][:, 0:SLL], b=rms_bc[:, ts]:
                       e.tensor_tensor(out=o, in0=a, in1=b, op=ALU.mult), sig=ev)
                B.wait("scalar", ev, ev.n)
                B.emit("scalar", lambda e, o=ssk[:, 0:SLL], i=ssk[:, 0:SLL]:
                       e.activation(out=o, in_=i, func=AF.Silu), sig=sc)
                B.wait("vector", sc, sc.n)
                B.emit("vector", lambda e, o=xs_ec[ec][:, ts], a=ysl, b=ssk[:, 0:SLL]:
                       e.tensor_tensor(out=o, in0=a, in1=b, op=ALU.mult), sig=ev)
        B.wait("vector", ev, ev.n)
        B.emit("vector", lambda e, o=ssk[:, 0:SLL] if SLL >= 256 else None, i=hbuf[:, :]:
               e.tensor_copy(out=o[:, 0:256] if False else ssk[:, 0:256], in_=i), sig=ev)
        B.wait("sync", ev, ev.n)
        B.emit("sync", lambda e, o=yh_d, i=ssk[:, 0:256]: e.dma_start(out=o, in_=i), sig=st, sigk=16)
        B.emit("sync", lambda e, o=yx1h_d, i=x1hk[:, 0:NEC * 3]: e.dma_start(out=o, in_=i), sig=st, sigk=16)
        phase_fence(B)
        ar.reset(mark0)
        om = ar.take_bf16(NEC * P)
        for m in range(NFC):
            B.wait("sync", mm, mm.n)
            emit_load_bf16(B, outT_d[:, m * P:(m + 1) * P], om, DIN, P, wstage, None, cast)
            B.wait("tensor", cast, cast.n)
            for it in range(NTT):
                ts = slice(it * TT, (it + 1) * TT)
                pd = pb[it % 2]
                B.wait("tensor", ev, ev.n)
                for ec in range(NEC):
                    B.emit("tensor", lambda e, l=om[:, ec * P:(ec + 1) * P],
                           r=xs_ec[ec][:, ts], o=pd[:, 0:TT],
                           s=(ec == 0), st=(ec == NEC - 1):
                           e.matmul(out=o, lhsT=l, rhs=r, start=s, stop=st),
                           sig=mm if ec == NEC - 1 else None)
                B.wait("vector", mm, mm.n)
                B.emit("vector", lambda e, a=acc_tiles[m][:, ts], i=pd[:, 0:TT]:
                       e.tensor_tensor(out=a, in0=a, in1=i, op=ALU.add), sig=ev)
        phase_fence(B)
        ar.reset(mark0)
        ar.free_top((NEC * T + 1) // 2)
        emit_moe(B, g, s2, ar, xn_tiles, acc_tiles, mWg_d, mWu_d, mWd_d, mW1_d, mW2_d,
                 cwsc_d, wstage)
        phase_fence(B)
        ost = ar.take_f32(1024)
        for c in range(NFC):
            for h in range(2):
                B.wait("vector", mm, mm.n)
                B.wait("vector", ev, ev.n)
                B.wait("vector", st, st.n)
                B.emit("vector", lambda e, i=acc_tiles[c][:, h * 1024:(h + 1) * 1024], o=ost:
                       e.tensor_copy(out=o, in_=i), sig=ev)
                B.wait("sync", ev, ev.n)
                B.emit("sync", lambda e, o=y_d[c * P:(c + 1) * P, h * 1024:(h + 1) * 1024], i=ost:
                       e.dma_start(out=o, in_=i), sig=st, sigk=16)
        B.wait("sync", st, st.n)
        with nc.Block() as block:
            B.replay(block)
    return nc


def emit_moe(B, g, s2, ar, xn_tiles, acc_tiles, wg_d, wu_d, wdn_d, w1_d, w2_d, cwsc_d, wst):
    mm, ev, sc = B.sems["mm"], B.sems["ev"], B.sems["sc"]
    cast = B.sems["wcast"]
    st = B.sems["st"]
    cb = B.sems["cwdma"]
    pb = s2["pb"]
    amark = ar.mark()
    s2x = alloc_ln_scratch(B, ar)
    wgt = ar.take_bf16(NFC * 8)
    cwst = ar.take_f32(P, parts=8)
    emit_load_bf16(B, wg_d, wgt, D, 8, wst, None, cast)
    B.wait("tensor", cast, cast.n)
    w = s2x["bc_a"]
    for sti in range(T // P):
        tsl = slice(sti * P, (sti + 1) * P)
        B.wait("tensor", ev, ev.n)
        for c in range(NFC):
            B.emit("tensor", lambda e, l=xn_tiles[c][:, tsl], r=wgt[:, c * 8:(c + 1) * 8],
                   o=pb[0][:, 0:8], s=(c == 0), st=(c == NFC - 1):
                   e.matmul(out=o, lhsT=l, rhs=r, start=s, stop=st),
                   sig=mm if c == NFC - 1 else None)
        B.wait("vector", mm, mm.n)
        p8 = w[:, 0:8]; m1 = w[:, 8:9]; m2 = w[:, 9:10]; ssum = w[:, 10:11]
        p2 = w[:, 16:24]
        def vstep(fn):
            B.emit("vector", fn, sig=ev)
            B.wait("vector", ev, ev.n)
        vstep(lambda e: e.tensor_copy(out=p8, in_=pb[0][:, 0:8]))
        vstep(lambda e: e.tensor_reduce(out=m1, in_=p8, axis=AX.X, op=ALU.max))
        vstep(lambda e: e.tensor_scalar(out=p8, in0=p8, scalar1=m1, scalar2=None, op0=ALU.subtract))
        B.wait("scalar", ev, ev.n)
        B.emit("scalar", lambda e: e.activation(out=p8, in_=p8, func=AF.Exp), sig=sc)
        B.wait("vector", sc, sc.n)
        vstep(lambda e: e.tensor_reduce(out=ssum, in_=p8, axis=AX.X, op=ALU.add))
        vstep(lambda e: e.reciprocal(out=ssum, in_=ssum))
        vstep(lambda e: e.tensor_scalar(out=p8, in0=p8, scalar1=ssum, scalar2=None, op0=ALU.mult))
        vstep(lambda e: e.tensor_reduce(out=m1, in_=p8, axis=AX.X, op=ALU.max))
        vstep(lambda e: e.tensor_scalar(out=p2, in0=p8, scalar1=m1, scalar2=2.0,
                                        op0=ALU.is_ge, op1=ALU.mult))
        vstep(lambda e: e.tensor_tensor(out=p2, in0=p8, in1=p2, op=ALU.subtract))
        vstep(lambda e: e.tensor_reduce(out=m2, in_=p2, axis=AX.X, op=ALU.max))
        vstep(lambda e: e.tensor_scalar(out=p2, in0=p8, scalar1=m2, scalar2=None, op0=ALU.is_ge))
        vstep(lambda e: e.tensor_tensor(out=p8, in0=p8, in1=p2, op=ALU.mult))
        vstep(lambda e: e.tensor_reduce(out=ssum, in_=p8, axis=AX.X, op=ALU.add))
        vstep(lambda e: e.reciprocal(out=ssum, in_=ssum))
        vstep(lambda e: e.tensor_scalar(out=p8, in0=p8, scalar1=ssum, scalar2=None, op0=ALU.mult))
        B.wait("tensor", ev, ev.n)
        B.emit("tensor", lambda e: e.transpose(out=pb[1][0:8, 0:P], in_=p8, identity=g["identf"][:]),
               sig=mm)
        B.wait("vector", mm, mm.n)
        B.wait("vector", st, st.n)
        B.emit("vector", lambda e: e.tensor_copy(out=cwst[0:8, 0:P], in_=pb[1][0:8, 0:P]), sig=ev)
        B.wait("sync", ev, ev.n)
        B.emit("sync", lambda e, o=cwsc_d[0:8, tsl]: e.dma_start(out=o, in_=cwst[0:8, 0:P]),
               sig=st, sigk=16)
    wue = ar.take_bf16(NFC * 1024)
    wde = ar.take_bf16(NEC * 512)
    Ht = ar.take_bf16(NEC * TT)
    gel = ar.take_f32(TT)
    cwb = ar.take_f32(TT)
    B.wait("sync", st, st.n)
    for e8 in range(9):
        shared = (e8 == 8)
        for it in range(NTT):
            ts = slice(it * TT, (it + 1) * TT)
            if not shared:
                B.wait("sync", ev, ev.n)
                B.emit("sync", lambda e, o=cwb,
                       i=cwsc_d[e8:e8 + 1, ts].broadcast_to([P, TT]):
                       e.dma_start(out=o, in_=i), sig=cb, sigk=16)
            for mhalf in range(2):
                B.wait("sync", mm, mm.n)
                if shared:
                    emit_load_bf16(B, w1_d[:, mhalf * 1024:(mhalf + 1) * 1024], wue,
                                   D, 1024, wst, None, cast)
                else:
                    emit_load_bf16(B, wu_d[e8 * D:(e8 + 1) * D, mhalf * 1024:(mhalf + 1) * 1024],
                                   wue, D, 1024, wst, None, cast)
                B.wait("tensor", cast, cast.n)
                for mh8 in range(8):
                    mh = mhalf * 8 + mh8
                    pp = pb[2 + (mh8 % 2)]
                    B.wait("tensor", sc, sc.n)
                    B.wait("tensor", ev, ev.n)
                    for c in range(NFC):
                        B.emit("tensor", lambda e, l=wue[:, c * 1024 + mh8 * P:c * 1024 + (mh8 + 1) * P],
                               r=xn_tiles[c][:, ts], o=pp[:, 0:TT], s=(c == 0), st=(c == NFC - 1):
                               e.matmul(out=o, lhsT=l, rhs=r, start=s, stop=st),
                               sig=mm if c == NFC - 1 else None)
                    B.wait("scalar", mm, mm.n)
                    B.emit("scalar", lambda e, o=gel, i=pp[:, 0:TT]:
                           e.activation(out=o, in_=i, func=AF.Gelu), sig=sc)
                    B.wait("vector", sc, sc.n)
                    if not shared:
                        B.wait("vector", cb, cb.n)
                    if shared:
                        B.emit("vector", lambda e, o=Ht[:, mh * TT:(mh + 1) * TT], i=gel:
                               e.tensor_copy(out=o, in_=i), sig=ev)
                    else:
                        B.emit("vector", lambda e, o=Ht[:, mh * TT:(mh + 1) * TT], a=gel, b=cwb:
                               e.tensor_tensor(out=o, in0=a, in1=b, op=ALU.mult), sig=ev)
            for mhalf in range(2):
                B.wait("sync", mm, mm.n)
                if shared:
                    emit_load_bf16(B, w2_d[:, mhalf * 512:(mhalf + 1) * 512], wde,
                                   DIN, 512, wst, None, cast)
                else:
                    emit_load_bf16(B, wdn_d[e8 * DIN:(e8 + 1) * DIN, mhalf * 512:(mhalf + 1) * 512],
                                   wde, DIN, 512, wst, None, cast)
                B.wait("tensor", cast, cast.n)
                for m4 in range(4):
                    m = mhalf * 4 + m4
                    pd = pb[m4 % 2]
                    B.wait("tensor", ev, ev.n)
                    for ec in range(NEC):
                        B.emit("tensor", lambda e, l=wde[:, ec * 512 + m4 * P:ec * 512 + (m4 + 1) * P],
                               r=Ht[:, ec * TT:(ec + 1) * TT], o=pd[:, 0:TT],
                               s=(ec == 0), st=(ec == NEC - 1):
                               e.matmul(out=o, lhsT=l, rhs=r, start=s, stop=st),
                               sig=mm if ec == NEC - 1 else None)
                    B.wait("vector", mm, mm.n)
                    B.emit("vector", lambda e, a=acc_tiles[m][:, ts], i=pd[:, 0:TT]:
                           e.tensor_tensor(out=a, in0=a, in1=i, op=ALU.add), sig=ev)
    ar.reset(amark)



def _t(a):
    return np.ascontiguousarray(np.asarray(a).T)



def build_qffn_kernel(dump_gact=False):
    nc = bass.Bass("TRN2", target_bir_lowering=False, debug=False)
    x_d = nc.dram_tensor("x", [D, T], F32, kind="ExternalInput").ap()
    wg_d = nc.dram_tensor("wgT", [D, HQ], I32, kind="ExternalInput").ap()
    wu_d = nc.dram_tensor("wuT", [D, HQ], I32, kind="ExternalInput").ap()
    wd_d = nc.dram_tensor("wdT", [HQ, D], I32, kind="ExternalInput").ap()
    y_d = nc.dram_tensor("y", [D, T], F32, kind="ExternalOutput").ap()
    yg_d = nc.dram_tensor("yg", [4 * P, (HQ // P) * TT], BF16, kind="ExternalOutput").ap() if dump_gact else None

    ctx = ExitStack()
    with ctx:
        B = Builder(nc, ctx)
        for s in ("mm", "ev", "sc", "wld", "wcast", "consts", "st"):
            B.sem(s)
        g = alloc_consts(B)
        ar = Arena(B, 24576)
        base = ar.mark()
        s2 = alloc_ln_scratch(B, ar)
        s2["pbS"] = ptensor(B, "pbS", [P, TT])
        s2["pb"] = [ptensor(B, f"pb{i}", [P, TT]) for i in range(6)]
        s2["pb"].append(s2["pbS"])
        s2["pbT"] = ptensor(B, "pbT", [P, 1024], BF16)
        s2["pbB1"] = s2["pb"][1]
        s2["pbB2"] = s2["pb"][2]
        bufA = sbuf(B, "bufA", [P, NFC * T], BF16)
        bufB = sbuf(B, "bufB", [P, NFC * T], BF16)
        x_tiles = [bufA[:, c * T:(c + 1) * T] for c in range(NFC)]
        xn_tiles = [bufB[:, c * T:(c + 1) * T] for c in range(NFC)]
        acc_tiles = x_tiles
        xstage = Stage(B, ar.take_f32(2 * 1024), B.sems['wcast'], 'xst')

        emit_consts(B, g, B.sems["consts"])
        ld, cast = B.sems["wld"], B.sems["wcast"]
        for c in range(NFC):
            for h in range(2):
                par, ssl = xstage.slot(1024)
                if xstage.lastc[par] > 0:
                    B.wait("sync", cast, xstage.lastc[par])
                sld = xstage.ld[par]
                B.emit("sync", lambda e, s=ssl, d=x_d[c * P:(c + 1) * P, h * 1024:(h + 1) * 1024]:
                       e.dma_start(out=s, in_=d), sig=sld, sigk=16)
                B.wait("gpsimd", sld, sld.n)
                B.emit("gpsimd", lambda e, s=ssl, o=x_tiles[c][:, h * 1024:(h + 1) * 1024]:
                       e.tensor_copy(out=o, in_=s), sig=cast)
                xstage.lastc[par] = cast.n
        for e in ("tensor", "vector", "scalar"):
            B.wait(e, cast, cast.n)
            B.wait(e, B.sems["consts"], B.sems["consts"].n)
        emit_ln(B, g, s2, x_tiles, xn_tiles, T)
        B.wait("gpsimd", B.sems["ev"], B.sems["ev"].n)
        for c in range(NFC):
            B.emit("gpsimd", lambda e, a=acc_tiles[c]: e.memset(a, 0.0),
                   sig=cast if c == NFC - 1 else None)
        for e in ("tensor", "vector"):
            B.wait(e, cast, cast.n)
        emit_qffn(B, g, s2, ar, xn_tiles, acc_tiles, wg_d, wu_d, wd_d, T, yg_d=yg_d)
        mm, ev, st = B.sems["mm"], B.sems["ev"], B.sems["st"]
        ost = ar.take_f32(1024)
        for c in range(NFC):
            for h in range(2):
                B.wait("vector", mm, mm.n)
                if (c, h) != (0, 0):
                    B.wait("vector", st, st.n)
                B.emit("vector", lambda e, i=acc_tiles[c][:, h * 1024:(h + 1) * 1024], o=ost:
                       e.tensor_copy(out=o, in_=i), sig=ev)
                B.wait("sync", ev, ev.n)
                B.emit("sync", lambda e, o=y_d[c * P:(c + 1) * P, h * 1024:(h + 1) * 1024], i=ost:
                       e.dma_start(out=o, in_=i), sig=st, sigk=16)
        B.wait("sync", st, st.n)

        with nc.Block() as block:
            B.replay(block)
    return nc


def _silu(x):
    return x / (1.0 + np.exp(-x))


def _np_ref_qffn(x, wg, wu, wd):
    xl = np.asarray(x, np.float32)
    m = xl.mean(-1, keepdims=True)
    v = xl.var(-1, keepdims=True)
    xn = (xl - m) / np.sqrt(v + EPS_LN)
    s = 1.0 / 127.0
    gate = xn @ (wg.astype(np.float32) * s).T
    up = xn @ (wu.astype(np.float32) * s).T
    h = _silu(gate) * up
    return h @ (wd.astype(np.float32) * s).T


def build_gla_kernel():
    nc = bass.Bass("TRN2", target_bir_lowering=False, debug=False)
    xn_d = nc.dram_tensor("xn", [D, T], F32, kind="ExternalInput").ap()
    wq_d = nc.dram_tensor("wq", [D, 1024], F32, kind="ExternalInput").ap()
    wk_d = nc.dram_tensor("wk", [D, 1024], F32, kind="ExternalInput").ap()
    wv_d = nc.dram_tensor("wv", [D, 1024], F32, kind="ExternalInput").ap()
    wg_d = nc.dram_tensor("wg", [D, 1024], F32, kind="ExternalInput").ap()
    gat_d = nc.dram_tensor("gattn", [D, 1], F32, kind="ExternalInput").ap()
    sinit_d = nc.dram_tensor("sinit", [P, 512], F32, kind="ExternalInput").ap()
    y_d = nc.dram_tensor("y", [D, T], F32, kind="ExternalOutput").ap()
    ys_d = nc.dram_tensor("ys", [P, 512], F32, kind="ExternalOutput").ap()

    ctx = ExitStack()
    with ctx:
        B = Builder(nc, ctx)
        for s in ("mm", "ev", "sc", "wld", "wcast", "consts", "st"):
            B.sem(s)
        g = alloc_consts(B)
        ar = Arena(B, 39500)
        s2 = alloc_ln_scratch(B, ar)
        s2["pbS"] = ptensor(B, "pbS", [P, TT])
        s2["pb"] = [ptensor(B, f"pb{i}", [P, TT]) for i in range(6)]
        s2["pb"].append(s2["pbS"])
        s2["pbT"] = ptensor(B, "pbT", [P, 1024], BF16)
        bufB = sbuf(B, "bufB", [P, NFC * T], BF16)
        xn_tiles = [bufB[:, c * T:(c + 1) * T] for c in range(NFC)]
        o_buf = ar.take_bf16(NFC * T)
        wt = ar.take_bf16(NFC * 1024)
        S32 = ar.take_f32(512)
        Sbf = ar.take_bf16(512)
        ga = alloc_gla(B, ar, T, o_buf, wt, S32, Sbf)
        ga["wstage"] = Stage(B, ar.take_f32(2 * 1024), B.sems["wcast"], "wst")
        ga["wq_d"], ga["wk_d"], ga["wv_d"], ga["wg_d"] = wq_d, wk_d, wv_d, wg_d
        gcols = sbuf(B, "gcols", [P, NFC], F32)
        xstage = Stage(B, ar.take_f32(2 * 1024), B.sems["wcast"], "xst")
        emit_consts(B, g, B.sems["consts"])
        ld, cast = B.sems["wld"], B.sems["wcast"]
        mm, ev, sc = B.sems["mm"], B.sems["ev"], B.sems["sc"]
        for c in range(NFC):
            for h in range(2):
                par, ssl = xstage.slot(1024)
                if xstage.lastc[par] > 0:
                    B.wait("sync", cast, xstage.lastc[par])
                sld = xstage.ld[par]
                B.emit("sync", lambda e, s=ssl, d=xn_d[c * P:(c + 1) * P, h * 1024:(h + 1) * 1024]:
                       e.dma_start(out=s, in_=d), sig=sld, sigk=16)
                B.wait("gpsimd", sld, sld.n)
                B.emit("gpsimd", lambda e, s=ssl, o=xn_tiles[c][:, h * 1024:(h + 1) * 1024]:
                       e.tensor_copy(out=o, in_=s), sig=cast)
                xstage.lastc[par] = cast.n
        gld = B.sem("gld")
        B.emit("sync", lambda e, o=gcols[:], i=gat_d.rearrange("(c p) w -> p (c w)", p=P):
               e.dma_start(out=o, in_=i, allow_slow_non_contiguous=True), sig=gld, sigk=16)
        B.emit("sync", lambda e, o=ga["S32"], i=sinit_d: e.dma_start(out=o, in_=i), sig=gld, sigk=16)
        B.wait("vector", gld, gld.n)
        B.emit("vector", lambda e, o=ga["Sbf"][:, 0:512], i=ga["S32"][:, 0:512]:
               e.tensor_copy(out=o, in_=i), sig=ev)
        for e in ("tensor", "vector", "scalar"):
            B.wait(e, cast, cast.n)
            B.wait(e, B.sems["consts"], B.sems["consts"].n)
            B.wait(e, gld, gld.n)
        gattn_cols = [gcols[:, c:c + 1] for c in range(NFC)]
        dump_names = []

        def dbg(name, ap, after_sc=False):
            dump_names.append(name)
            slot = len(dump_names) - 1
            if after_sc:
                B.wait("vector", B.sems["sc"], B.sems["sc"].n)
            B.wait("vector", B.sems["ev"], B.sems["ev"].n)
            B.wait("vector", B.sems["mm"], B.sems["mm"].n)
            w = ap.shape[1]
            B.emit("vector", lambda e, o=dmp_s[0:ap.shape[0], 0:w], i=ap:
                   e.tensor_copy(out=o, in_=i), sig=B.sems["ev"])
            B.wait("sync", B.sems["ev"], B.sems["ev"].n)
            B.emit("sync", lambda e, o=dmp_d[slot * P:slot * P + ap.shape[0], 0:w],
                   i=dmp_s[0:ap.shape[0], 0:w]:
                   e.dma_start(out=o, in_=i), sig=B.sems["st"], sigk=16)
            for en in ("vector", "tensor", "scalar"):
                B.wait(en, B.sems["st"], B.sems["st"].n)

        import os as _os
        if _os.environ.get("GDBG", "0") == "1":
            dmp_d = nc.dram_tensor("dmp", [24 * P, 1024], F32, kind="ExternalOutput").ap()
            dmp_s = ar.take_f32(1024)
            emit_gla(B, g, s2, ga, xn_tiles, gattn_cols, T, dbg=dbg)
            build_gla_kernel._dump_names = dump_names
        else:
            emit_gla(B, g, s2, ga, xn_tiles, gattn_cols, T)
        st = B.sems["st"]
        ost = ar.take_f32(1024)
        ob = o_buf
        for c in range(NFC):
            for h in range(2):
                B.wait("vector", ev, ev.n)
                if (c, h) != (0, 0):
                    B.wait("vector", st, st.n)
                B.emit("vector", lambda e, i=ob[:, c * T + h * 1024:c * T + (h + 1) * 1024],
                       o=ost: e.tensor_copy(out=o, in_=i), sig=ev)
                B.wait("sync", ev, ev.n)
                B.emit("sync", lambda e, o=y_d[c * P:(c + 1) * P, h * 1024:(h + 1) * 1024], i=ost:
                       e.dma_start(out=o, in_=i), sig=st, sigk=16)
        B.wait("sync", ev, ev.n)
        B.emit("sync", lambda e, o=ys_d, i=ga["S32"]: e.dma_start(out=o, in_=i), sig=st, sigk=16)
        B.wait("sync", st, st.n)
        with nc.Block() as block:
            B.replay(block)
    return nc


def _np_ref_gla(xn, gattn, Wq, Wk, Wv, Wg, S0=None):
    x = xn.astype(np.float64)
    h = x * (1.0 / np.sqrt((x * x).mean(-1, keepdims=True) + EPS_RMS0)) * gattn
    q = (h @ Wq).reshape(-1, 16, 64)
    k = (h @ Wk).reshape(-1, 16, 64)
    v = (h @ Wv).reshape(-1, 16, 64)
    al = 1.0 / (1.0 + np.exp(-(h @ Wg))).reshape(-1, 16, 64)
    qn = q / np.maximum(np.sqrt((q * q).sum(-1, keepdims=True)), 1e-12)
    kn = k / np.maximum(np.sqrt((k * k).sum(-1, keepdims=True)), 1e-12)
    Tn = x.shape[0]
    S = np.zeros((16, 64, 64)) if S0 is None else S0.astype(np.float64).copy()
    o = np.empty((Tn, 16, 64))
    for t in range(Tn):
        S = al[t][:, :, None] * S + kn[t][:, :, None] * v[t][:, None, :]
        o[t] = np.einsum("hkv,hk->hv", S, qn[t])
    return o.reshape(Tn, 1024), S


def _pack_S(S):
    out = np.zeros((P, 512), np.float32)
    for h in range(16):
        out[(h % 2) * 64:(h % 2) * 64 + 64, (h // 2) * 64:(h // 2) * 64 + 64] = S[h]
    return out


def build_l0_kernel():
    nc = bass.Bass("TRN2", target_bir_lowering=False, debug=False)
    x_d = nc.dram_tensor("x", [D, T], F32, kind="ExternalInput").ap()
    wq_d = nc.dram_tensor("wq", [D, 1024], F32, kind="ExternalInput").ap()
    wk_d = nc.dram_tensor("wk", [D, 1024], F32, kind="ExternalInput").ap()
    wv_d = nc.dram_tensor("wv", [D, 1024], F32, kind="ExternalInput").ap()
    wgg_d = nc.dram_tensor("wgg", [D, 1024], F32, kind="ExternalInput").ap()
    wo_d = nc.dram_tensor("wo", [1024, D], F32, kind="ExternalInput").ap()
    w1_d = nc.dram_tensor("w1", [D, DFF0], F32, kind="ExternalInput").ap()
    w2_d = nc.dram_tensor("w2", [D, DFF0], F32, kind="ExternalInput").ap()
    w3_d = nc.dram_tensor("w3", [DFF0, D], F32, kind="ExternalInput").ap()
    gat_d = nc.dram_tensor("gattn", [D, 1], F32, kind="ExternalInput").ap()
    gff_d = nc.dram_tensor("gffn", [D, 1], F32, kind="ExternalInput").ap()
    wgq_d = nc.dram_tensor("wgT", [D, HQ], I32, kind="ExternalInput").ap()
    wuq_d = nc.dram_tensor("wuT", [D, HQ], I32, kind="ExternalInput").ap()
    wdq_d = nc.dram_tensor("wdT", [HQ, D], I32, kind="ExternalInput").ap()
    sin_d = nc.dram_tensor("sinit", [P, 512], F32, kind="ExternalInput").ap()
    y_d = nc.dram_tensor("y", [D, T], F32, kind="ExternalOutput").ap()
    ys_d = nc.dram_tensor("ys", [P, 512], F32, kind="ExternalOutput").ap()

    ctx = ExitStack()
    with ctx:
        B = Builder(nc, ctx)
        for s in ("mm", "ev", "sc", "wld", "wcast", "consts", "st", "gld"):
            B.sem(s)
        g = alloc_consts(B)
        ar = Arena(B, 35820)
        s2 = alloc_ln_scratch(B, ar)
        s2["pbS"] = ptensor(B, "pbS", [P, TT])
        s2["pb"] = [ptensor(B, f"pb{i}", [P, TT]) for i in range(6)]
        s2["pb"].append(s2["pbS"])
        s2["pbT"] = ptensor(B, "pbT", [P, 1024], BF16)
        bufA = sbuf(B, "bufA", [P, NFC * T], BF16)
        bufB = sbuf(B, "bufB", [P, NFC * T], BF16)
        acc_tiles = [bufA[:, c * T:(c + 1) * T] for c in range(NFC)]
        xn_tiles = [bufB[:, c * T:(c + 1) * T] for c in range(NFC)]
        o_buf = ar.take_bf16(NFC * T)
        wt = ar.take_bf16(NFC * 1024)
        S32 = ar.take_f32(512)
        Sbf = ar.take_bf16(512)
        gcols = sbuf(B, "gcols", [P, 2 * NFC], F32)
        wstage = Stage(B, ar.take_f32(2 * 1024), B.sems["wcast"], "wst")
        emit_consts(B, g, B.sems["consts"])
        ld, cast, gld = B.sems["wld"], B.sems["wcast"], B.sems["gld"]
        mm, ev, sc = B.sems["mm"], B.sems["ev"], B.sems["sc"]
        st = B.sems["st"]
        mark0 = ar.mark()
        xstage = Stage(B, ar.take_f32(2 * 1024), B.sems["wcast"], "xst")
        for c in range(NFC):
            for h in range(2):
                par, ssl = xstage.slot(1024)
                if xstage.lastc[par] > 0:
                    B.wait("sync", cast, xstage.lastc[par])
                sld = xstage.ld[par]
                B.emit("sync", lambda e, s=ssl, d=x_d[c * P:(c + 1) * P, h * 1024:(h + 1) * 1024]:
                       e.dma_start(out=s, in_=d), sig=sld, sigk=16)
                B.wait("gpsimd", sld, sld.n)
                B.emit("gpsimd", lambda e, s=ssl, o=acc_tiles[c][:, h * 1024:(h + 1) * 1024]:
                       e.tensor_copy(out=o, in_=s), sig=cast)
                xstage.lastc[par] = cast.n
        B.emit("sync", lambda e, o=gcols[:, 0:NFC], i=gat_d.rearrange("(c p) w -> p (c w)", p=P):
               e.dma_start(out=o, in_=i, allow_slow_non_contiguous=True), sig=gld, sigk=16)
        B.emit("sync", lambda e, o=gcols[:, NFC:2 * NFC], i=gff_d.rearrange("(c p) w -> p (c w)", p=P):
               e.dma_start(out=o, in_=i, allow_slow_non_contiguous=True), sig=gld, sigk=16)
        B.emit("sync", lambda e, o=S32, i=sin_d: e.dma_start(out=o, in_=i), sig=gld, sigk=16)
        B.wait("vector", gld, gld.n)
        B.emit("vector", lambda e, o=Sbf[:, 0:512], i=S32[:, 0:512]:
               e.tensor_copy(out=o, in_=i), sig=ev)
        for e in ("tensor", "vector", "scalar"):
            B.wait(e, cast, cast.n)
            B.wait(e, B.sems["consts"], B.sems["consts"].n)
            B.wait(e, gld, gld.n)
        emit_ln(B, g, s2, acc_tiles, xn_tiles, T)
        ga = alloc_gla(B, ar, T, o_buf, wt, S32, Sbf)
        ga["wstage"] = wstage
        ga["wq_d"], ga["wk_d"], ga["wv_d"], ga["wg_d"] = wq_d, wk_d, wv_d, wgg_d
        gattn_cols = [gcols[:, c:c + 1] for c in range(NFC)]
        gffn_cols = [gcols[:, NFC + c:NFC + c + 1] for c in range(NFC)]
        emit_gla(B, g, s2, ga, xn_tiles, gattn_cols, T)
        ar.reset(mark0)
        emit_wo_swiglu(B, g, s2, ar, xn_tiles, acc_tiles, o_buf, wo_d, w1_d, w2_d, w3_d,
                       gffn_cols, T, wt, wstage)
        import os as _os
        if _os.environ.get("L0DBG", "0") == "1":
            ymid_d = nc.dram_tensor("ymid", [D, T], F32, kind="ExternalOutput").ap()
            omark = ar.mark()
            ost0 = ar.take_f32(1024)
            for c in range(NFC):
                for h in range(2):
                    B.wait("vector", mm, mm.n)
                    B.wait("vector", ev, ev.n)
                    B.wait("vector", st, st.n)
                    B.emit("vector", lambda e, i=acc_tiles[c][:, h * 1024:(h + 1) * 1024], o=ost0:
                           e.tensor_copy(out=o, in_=i), sig=ev)
                    B.wait("sync", ev, ev.n)
                    B.emit("sync", lambda e, o=ymid_d[c * P:(c + 1) * P, h * 1024:(h + 1) * 1024],
                           i=ost0: e.dma_start(out=o, in_=i), sig=st, sigk=16)
            for en in ("tensor", "vector", "scalar", "gpsimd"):
                B.wait(en, st, st.n)
            ar.reset(omark)
        emit_qffn(B, g, s2, ar, xn_tiles, acc_tiles, wgq_d, wuq_d, wdq_d, T)
        amark2 = ar.mark()
        ost = ar.take_f32(1024)
        for c in range(NFC):
            for h in range(2):
                B.wait("vector", mm, mm.n)
                B.wait("vector", ev, ev.n)
                if (c, h) != (0, 0):
                    B.wait("vector", st, st.n)
                B.emit("vector", lambda e, i=acc_tiles[c][:, h * 1024:(h + 1) * 1024], o=ost:
                       e.tensor_copy(out=o, in_=i), sig=ev)
                B.wait("sync", ev, ev.n)
                B.emit("sync", lambda e, o=y_d[c * P:(c + 1) * P, h * 1024:(h + 1) * 1024], i=ost:
                       e.dma_start(out=o, in_=i), sig=st, sigk=16)
        B.wait("sync", ev, ev.n)
        B.emit("sync", lambda e, o=ys_d, i=S32: e.dma_start(out=o, in_=i), sig=st, sigk=16)
        B.wait("sync", st, st.n)
        ar.reset(amark2)
        with nc.Block() as block:
            B.replay(block)
    return nc


def _np_ref_l0(x, W, S0=None):
    xl = x.astype(np.float64)
    m = xl.mean(-1, keepdims=True); v = xl.var(-1, keepdims=True)
    xn = (xl - m) / np.sqrt(v + EPS_LN)
    o, Sf = _np_ref_gla(xn.astype(np.float32), W["g_attn"], W["Wq"], W["Wk"], W["Wv"], W["Wg"], S0)
    att = o @ W["Wo"]
    xm = xn + att
    h2 = xm * (1.0 / np.sqrt((xm * xm).mean(-1, keepdims=True) + EPS_RMS0)) * W["g_ffn"]
    ffn = (_silu(h2 @ W["w1"]) * (h2 @ W["w2"])) @ W["w3"]
    attn_out = xm + ffn
    s = 1.0 / 127.0
    gate = xn @ (W["w_gate"].astype(np.float64) * s).T
    up = xn @ (W["w_up"].astype(np.float64) * s).T
    qf = (_silu(gate) * up) @ (W["w_down"].astype(np.float64) * s).T
    return xl + attn_out + qf, Sf

def _np_ref_l1(resid, W, halo=None, h0=None):
    x = resid.astype(np.float64)
    m = x.mean(-1, keepdims=True); v = x.var(-1, keepdims=True)
    xn = (x - m) / np.sqrt(v + EPS_LN)
    xr = xn * (1.0 / np.sqrt((xn * xn).mean(-1, keepdims=True) + EPS_RMS1)) * W["norm_w"]
    skip = xr @ W["skip_w"].T
    x1 = xr @ W["in_w"].T
    if halo is None:
        halo_x1 = np.zeros((3, DIN))
    else:
        hh = halo.astype(np.float64)
        mh = hh.mean(-1, keepdims=True); vh = hh.var(-1, keepdims=True)
        hn = (hh - mh) / np.sqrt(vh + EPS_LN)
        hr = hn * (1.0 / np.sqrt((hn * hn).mean(-1, keepdims=True) + EPS_RMS1)) * W["norm_w"]
        halo_x1 = hr @ W["in_w"].T
    xp = np.concatenate([halo_x1, x1], 0)
    conv = W["conv_b"] + sum(xp[k:k + x1.shape[0]] * W["conv_w"][:, k] for k in range(4))
    xs = conv / (1.0 + np.exp(-conv))
    A = -np.exp(W["A_log"].astype(np.float64))
    dcore = xs @ W["wd1"].T
    dl = dcore @ W["wd2"].T + W["wd2_b"]
    delta = np.maximum(dl, 0) + np.log1p(np.exp(-np.abs(dl)))
    Bm = xs @ W["wb"].T
    Cm = xs @ W["wc"].T
    h = np.zeros((DIN, NS)) if h0 is None else h0.astype(np.float64).copy()
    y = np.empty((x.shape[0], DIN))
    for t in range(x.shape[0]):
        h = np.exp(delta[t][:, None] * A) * h + (delta[t] * xs[t])[:, None] * Bm[t][None, :]
        y[t] = h @ Cm[t]
    y = y + xs * W["W_D"]
    y = y * (skip / (1.0 + np.exp(-skip)))
    att = y @ W["out_w"].T
    probs = np.exp(xn @ W["Wg"] - (xn @ W["Wg"]).max(-1, keepdims=True))
    probs = probs / probs.sum(-1, keepdims=True)
    top2 = np.argsort(-probs, axis=-1)[:, :2]
    cw = np.zeros_like(probs)
    for t in range(probs.shape[0]):
        tv = probs[t, top2[t]]
        cw[t, top2[t]] = tv / tv.sum()
    def gelu(z):
        return 0.5 * z * (1.0 + _erf_np(z / np.sqrt(2.0)))
    moe = np.zeros((x.shape[0], D))
    for e in range(8):
        He = gelu(xn @ W["Wu"][e])
        moe += cw[:, e:e + 1] * (He @ W["Wd"][e])
    shared = gelu(xn @ W["W1"]) @ W["W2"]
    return x + att + moe + shared, h


def _erf_np(z):
    import math
    return np.vectorize(math.erf)(z)





def build_full_kernel():
    nc = bass.Bass("TRN2", target_bir_lowering=False, debug=False)
    xf_d = nc.dram_tensor("xfirst", [D, T], F32, kind="ExternalInput").ap()
    xo_d = nc.dram_tensor("xown", [D, T], F32, kind="ExternalInput").ap()
    sel_d = nc.dram_tensor("sel", [P, 1], F32, kind="ExternalInput").ap()
    wq_d = nc.dram_tensor("wq", [D, 1024], F32, kind="ExternalInput").ap()
    wk_d = nc.dram_tensor("wk", [D, 1024], F32, kind="ExternalInput").ap()
    wv_d = nc.dram_tensor("wv", [D, 1024], F32, kind="ExternalInput").ap()
    wgg_d = nc.dram_tensor("wgg", [D, 1024], F32, kind="ExternalInput").ap()
    wo_d = nc.dram_tensor("wo", [1024, D], F32, kind="ExternalInput").ap()
    w1_d = nc.dram_tensor("w1", [D, DFF0], F32, kind="ExternalInput").ap()
    w2_d = nc.dram_tensor("w2", [D, DFF0], F32, kind="ExternalInput").ap()
    w3_d = nc.dram_tensor("w3", [DFF0, D], F32, kind="ExternalInput").ap()
    gat_d = nc.dram_tensor("gattn", [D, 1], F32, kind="ExternalInput").ap()
    gff_d = nc.dram_tensor("gffn", [D, 1], F32, kind="ExternalInput").ap()
    wgq_d = nc.dram_tensor("wgT", [D, HQ], I32, kind="ExternalInput").ap()
    wuq_d = nc.dram_tensor("wuT", [D, HQ], I32, kind="ExternalInput").ap()
    wdq_d = nc.dram_tensor("wdT", [HQ, D], I32, kind="ExternalInput").ap()
    nw_d = nc.dram_tensor("normw", [D, 1], F32, kind="ExternalInput").ap()
    skT_d = nc.dram_tensor("skipT", [D, DIN], F32, kind="ExternalInput").ap()
    inT_d = nc.dram_tensor("inT", [D, DIN], F32, kind="ExternalInput").ap()
    cw_d = nc.dram_tensor("convw", [DIN, 4], F32, kind="ExternalInput").ap()
    cb_d = nc.dram_tensor("convb", [DIN, 1], F32, kind="ExternalInput").ap()
    wd1T_d = nc.dram_tensor("wd1T", [DIN, 64], F32, kind="ExternalInput").ap()
    wd2T_d = nc.dram_tensor("wd2T", [64, DIN], F32, kind="ExternalInput").ap()
    wd2b_d = nc.dram_tensor("wd2b", [DIN, 1], F32, kind="ExternalInput").ap()
    wbT_d = nc.dram_tensor("wbT", [DIN, NS], F32, kind="ExternalInput").ap()
    wcT_d = nc.dram_tensor("wcT", [DIN, NS], F32, kind="ExternalInput").ap()
    al_d = nc.dram_tensor("alog", [DIN, NS], F32, kind="ExternalInput").ap()
    wdd_d = nc.dram_tensor("wD", [DIN, 1], F32, kind="ExternalInput").ap()
    outT_d = nc.dram_tensor("outT", [DIN, D], F32, kind="ExternalInput").ap()
    mWg_d = nc.dram_tensor("mWg", [D, 8], F32, kind="ExternalInput").ap()
    mWu_d = nc.dram_tensor("mWu", [8 * D, DIN], F32, kind="ExternalInput").ap()
    mWd_d = nc.dram_tensor("mWd", [8 * DIN, D], F32, kind="ExternalInput").ap()
    mW1_d = nc.dram_tensor("mW1", [D, DIN], F32, kind="ExternalInput").ap()
    mW2_d = nc.dram_tensor("mW2", [DIN, D], F32, kind="ExternalInput").ap()
    y_d = nc.dram_tensor("y", [D, T], F32, kind="ExternalOutput").ap()
    bsc_d = nc.dram_tensor("bsc", [NS, T], F32, kind="Internal").ap()
    csc_d = nc.dram_tensor("csc", [NS, T], F32, kind="Internal").ap()
    cwsc_d = nc.dram_tensor("cwsc", [8, T], F32, kind="Internal").ap()

    SLL = 256
    ctx = ExitStack()
    with ctx:
        B = Builder(nc, ctx)
        for s in ("mm", "ev", "sc", "wld", "wcast", "consts", "st", "gld", "cwdma"):
            B.sem(s)
        bdma = [B.sem("bdma0"), B.sem("bdma1")]
        g = alloc_consts(B)
        ar = Arena(B, 33700)
        s2 = {}
        s2["pbS"] = ptensor(B, "pbS", [P, TT])
        s2["pb"] = [ptensor(B, f"pb{i}", [P, TT]) for i in range(6)]
        s2["pb"].append(s2["pbS"])
        s2["pbT"] = ptensor(B, "pbT", [P, 1024], BF16)
        pb = s2["pb"]
        bufA = sbuf(B, "bufA", [P, NFC * T], BF16)
        bufB = sbuf(B, "bufB", [P, NFC * T], BF16)
        acc_tiles = [bufA[:, c * T:(c + 1) * T] for c in range(NFC)]
        xn_tiles = [bufB[:, c * T:(c + 1) * T] for c in range(NFC)]
        smalls = sbuf(B, "smalls", [P, 480], F32)
        off = [0]
        def sm(n):
            a = smalls[:, off[0]:off[0] + n]
            off[0] += n
            return a
        nwcols = sm(8); convw = sm(64); convb = sm(16); wd2b = sm(16)
        wDc = sm(16); Acols = sm(256); x1h = sm(48); x1hk = sm(48); selc = sm(1)
        hbuf = sbuf(B, "hbuf", [P, NEC * NS], F32)
        S32d = sbuf(B, "S32d", [P, 512], F32)
        Sbfd = sbuf(B, "Sbfd", [P, 512], BF16)
        gcols = sbuf(B, "gcols", [P, 2 * NFC], F32)
        gattn_cols = [gcols[:, c:c + 1] for c in range(NFC)]
        gffn_cols = [gcols[:, NFC + c:NFC + c + 1] for c in range(NFC)]

        ld, cast, gld = B.sems["wld"], B.sems["wcast"], B.sems["gld"]
        mm, ev, sc = B.sems["mm"], B.sems["ev"], B.sems["sc"]
        st = B.sems["st"]
        wstage = Stage(B, ar.take_f32(2 * 1024), cast, "wst")
        mark0 = ar.mark()

        emit_consts(B, g, B.sems["consts"])
        B.emit("gpsimd", lambda e: e.memset(S32d[:, :], 0.0))
        B.emit("gpsimd", lambda e: e.memset(Sbfd[:, :], 0.0))
        B.emit("gpsimd", lambda e: e.memset(hbuf[:, :], 0.0))
        B.emit("gpsimd", lambda e: e.memset(x1h[:, :], 0.0), sig=B.sems["consts"])

        def ldsmall(dst, dram, rearr=True):
            if rearr:
                w = dram.shape[1]
                src_ap = dram.rearrange("(c p) w -> p c w", p=P)
                dstv = dst.rearrange("p (c w) -> p c w", w=w)
            else:
                src_ap, dstv = dram, dst
            B.emit("sync", lambda e, o=dstv, i=src_ap:
                   e.dma_start(out=o, in_=i, allow_slow_non_contiguous=True), sig=gld, sigk=16)
        ldsmall(nwcols, nw_d); ldsmall(convw, cw_d); ldsmall(convb, cb_d)
        ldsmall(wd2b, wd2b_d); ldsmall(wDc, wdd_d); ldsmall(Acols, al_d)
        ldsmall(selc, sel_d, rearr=False)
        B.emit("sync", lambda e, o=gcols[:, 0:NFC], i=gat_d.rearrange("(c p) w -> p (c w)", p=P):
               e.dma_start(out=o, in_=i, allow_slow_non_contiguous=True), sig=gld, sigk=16)
        B.emit("sync", lambda e, o=gcols[:, NFC:2 * NFC], i=gff_d.rearrange("(c p) w -> p (c w)", p=P):
               e.dma_start(out=o, in_=i, allow_slow_non_contiguous=True), sig=gld, sigk=16)
        B.wait("scalar", gld, gld.n)
        B.emit("scalar", lambda e, o=Acols, i=Acols:
               e.activation(out=o, in_=i, func=AF.Exp), sig=sc)
        B.wait("vector", sc, sc.n)
        B.emit("vector", lambda e, o=Acols:
               e.tensor_scalar(out=o, in0=Acols, scalar1=-1.0, scalar2=None, op0=ALU.mult), sig=ev)

        def l0_pass(x_d):
            barrier_all(B)
            ar.reset(mark0)
            s2.update(alloc_ln_scratch(B, ar))
            o_buf = ar.take_bf16(NFC * T)
            wt = ar.take_bf16(NFC * 1024)
            markL0 = ar.mark()
            for c in range(NFC):
                for h in range(2):
                    par, ssl = wstage.slot(1024)
                    if wstage.lastc[par] > 0:
                        B.wait("sync", cast, wstage.lastc[par])
                    sld = wstage.ld[par]
                    B.emit("sync", lambda e, s=ssl, d=x_d[c * P:(c + 1) * P, h * 1024:(h + 1) * 1024]:
                           e.dma_start(out=s, in_=d), sig=sld, sigk=16)
                    B.wait("gpsimd", sld, sld.n)
                    B.emit("gpsimd", lambda e, s=ssl, o=acc_tiles[c][:, h * 1024:(h + 1) * 1024]:
                           e.tensor_copy(out=o, in_=s), sig=cast)
                    wstage.lastc[par] = cast.n
            for e in ("tensor", "vector", "scalar"):
                B.wait(e, cast, cast.n)
                B.wait(e, B.sems["consts"], B.sems["consts"].n)
                B.wait(e, gld, gld.n)
                B.wait(e, ev, ev.n)
            emit_ln(B, g, s2, acc_tiles, xn_tiles, T)
            ga = alloc_gla(B, ar, T, o_buf, wt, S32d, Sbfd)
            ga["wstage"] = wstage
            ga["wq_d"], ga["wk_d"], ga["wv_d"], ga["wg_d"] = wq_d, wk_d, wv_d, wgg_d
            emit_gla(B, g, s2, ga, xn_tiles, gattn_cols, T)
            ar.reset(markL0)
            emit_wo_swiglu(B, g, s2, ar, xn_tiles, acc_tiles, o_buf, wo_d, w1_d, w2_d, w3_d,
                           gffn_cols, T, wt, wstage)
            emit_qffn(B, g, s2, ar, xn_tiles, acc_tiles, wgq_d, wuq_d, wdq_d, T)

        def l1_pass(stateonly):
            barrier_all(B)
            ar.reset(mark0)
            dcore = ar.take_bf16(T)
            bcstg = ar.take_f32(TT, parts=NS)
            rms_bc = ar.take_f32(T)
            xs = ar.take_bf16_top(NEC * T)
            xs_ec = [xs[:, e * T:(e + 1) * T] for e in range(NEC)]
            markL1 = ar.mark()
            s2.update(alloc_ln_scratch(B, ar))
            emit_ln(B, g, s2, acc_tiles, xn_tiles, T)
            for it in range(NTT):
                ts = slice(it * TT, (it + 1) * TT)
                B.wait("scalar", mm, mm.n)
                for c in range(NFC):
                    B.emit("scalar", lambda e, i=xn_tiles[c][:, ts], o=s2["sq8"][:, c * TT:(c + 1) * TT]:
                           e.activation(out=o, in_=i, func=AF.Square), sig=sc if c == NFC - 1 else None)
                B.wait("tensor", sc, sc.n)
                B.wait("tensor", ev, ev.n)
                for c in range(NFC):
                    B.emit("tensor", lambda e, r=s2["sq8"][:, c * TT:(c + 1) * TT], o=pb[6][0:1, 0:TT],
                           l=g["ones_col"][:], s=(c == 0), st=(c == NFC - 1):
                           e.matmul(out=o, lhsT=l, rhs=r, start=s, stop=st),
                           sig=mm if c == NFC - 1 else None)
                B.wait("scalar", mm, mm.n)
                B.emit("scalar", lambda e, o=s2["arow"][0:1, 0:TT], i=pb[6][0:1, 0:TT]:
                       e.activation(out=o, in_=i, func=AF.Sqrt, scale=1.0 / D,
                                    bias=cval1(g, float(EPS_RMS1))), sig=sc)
                B.wait("vector", sc, sc.n)
                B.emit("vector", lambda e, o=s2["arow"][0:1, 0:TT]:
                       e.reciprocal(out=o, in_=s2["arow"][0:1, 0:TT]), sig=ev)
                B.wait("tensor", ev, ev.n)
                B.emit("tensor", lambda e, r=s2["arow"][0:1, 0:TT], o=pb[4][:, 0:TT],
                       l=g["ones_row"][:]:
                       e.matmul(out=o, lhsT=l, rhs=r, start=True, stop=True), sig=mm)
                B.wait("vector", mm, mm.n)
                B.emit("vector", lambda e, o=rms_bc[:, ts], i=pb[4][:, 0:TT]:
                       e.tensor_copy(out=o, in_=i), sig=ev)
            wcol = [ar.take_bf16(NFC * P), ar.take_bf16(NFC * P)]
            x1e = ar.take_bf16(2056)
            cvt = ar.take_f32(T)
            for ec in range(NEC):
                par = ec % 2
                B.wait("sync", mm, mm.n)
                emit_load_bf16(B, inT_d[:, ec * P:(ec + 1) * P], wcol[par], D, P, wstage, None, cast,
                               mul_cols=nwcols)
                B.wait("tensor", cast, cast.n)
                B.wait("tensor", sc, sc.n)
                B.wait("tensor", ev, ev.n)
                B.emit("vector", lambda e, o=x1e[:, 0:3], i=x1h[:, ec * 3:(ec + 1) * 3]:
                       e.tensor_copy(out=o, in_=i), sig=ev)
                for it in range(NTT):
                    pp = pb[it % 2]
                    ts = slice(it * TT, (it + 1) * TT)
                    for c in range(NFC):
                        B.emit("tensor", lambda e, l=wcol[par][:, c * P:(c + 1) * P],
                               r=xn_tiles[c][:, ts], o=pp[:, 0:TT], s=(c == 0), st=(c == NFC - 1):
                               e.matmul(out=o, lhsT=l, rhs=r, start=s, stop=st),
                               sig=mm if c == NFC - 1 else None)
                    B.wait("vector", mm, mm.n)
                    B.emit("vector", lambda e, o=x1e[:, 3 + it * TT:3 + (it + 1) * TT], a=pp[:, 0:TT],
                           b=rms_bc[:, ts]: e.tensor_tensor(out=o, in0=a, in1=b, op=ALU.mult), sig=ev)
                B.emit("vector", lambda e, o=x1hk[:, ec * 3:(ec + 1) * 3],
                       i=x1e[:, 3 + T - 3:3 + T]: e.tensor_copy(out=o, in_=i), sig=ev)
                B.emit("vector", lambda e, o=cvt, i=x1e[:, 0:T], w0=convw[:, ec * 4:ec * 4 + 1],
                       b=convb[:, ec:ec + 1]:
                       e.tensor_scalar(out=o, in0=i, scalar1=w0, scalar2=b,
                                       op0=ALU.mult, op1=ALU.add))
                for k in (1, 2, 3):
                    B.emit("vector", lambda e, o=cvt, i=x1e[:, k:k + T],
                           w=convw[:, ec * 4 + k:ec * 4 + k + 1]:
                           e.scalar_tensor_tensor(out=o, in0=i, scalar=w, in1=o,
                                                  op0=ALU.mult, op1=ALU.add),
                           sig=ev if k == 3 else None)
                B.wait("scalar", ev, ev.n)
                B.emit("scalar", lambda e, o=xs_ec[ec][:, 0:T], i=cvt:
                       e.activation(out=o, in_=i, func=AF.Silu), sig=sc)
            phase_fence(B)
            ar.reset(markL1)
            wd1t = ar.take_bf16(NEC * 64)
            wbct = ar.take_bf16(2 * NEC * NS)
            emit_load_bf16(B, wd1T_d, wd1t, DIN, 64, wstage, None, cast)
            emit_load_bf16(B, wbT_d, wbct, DIN, NS, wstage, None, cast, dst_stride=2 * NS, dst_off=0)
            emit_load_bf16(B, wcT_d, wbct, DIN, NS, wstage, None, cast, dst_stride=2 * NS, dst_off=NS)
            B.wait("tensor", cast, cast.n)
            B.wait("tensor", sc, sc.n)
            for it in range(NTT):
                ts = slice(it * TT, (it + 1) * TT)
                B.wait("tensor", ev, ev.n)
                for ec in range(NEC):
                    B.emit("tensor", lambda e, l=wd1t[:, ec * 64:(ec + 1) * 64],
                           r=xs_ec[ec][:, ts], o=pb[0][0:64, 0:TT],
                           s=(ec == 0), st=(ec == NEC - 1):
                           e.matmul(out=o, lhsT=l, rhs=r, start=s, stop=st))
                for ec in range(NEC):
                    B.emit("tensor", lambda e, l=wbct[:, ec * 2 * NS:ec * 2 * NS + NS],
                           r=xs_ec[ec][:, ts], o=pb[1][0:NS, 0:TT],
                           s=(ec == 0), st=(ec == NEC - 1):
                           e.matmul(out=o, lhsT=l, rhs=r, start=s, stop=st))
                for ec in range(NEC):
                    B.emit("tensor", lambda e, l=wbct[:, ec * 2 * NS + NS:(ec + 1) * 2 * NS],
                           r=xs_ec[ec][:, ts], o=pb[2][0:NS, 0:TT],
                           s=(ec == 0), st=(ec == NEC - 1):
                           e.matmul(out=o, lhsT=l, rhs=r, start=s, stop=st),
                           sig=mm if ec == NEC - 1 else None)
                B.wait("vector", mm, mm.n)
                B.emit("vector", lambda e, o=dcore[0:64, ts], i=pb[0][0:64, 0:TT]:
                       e.tensor_copy(out=o, in_=i), sig=ev)
                B.wait("vector", st, st.n)
                B.emit("vector", lambda e, o=bcstg[0:NS, 0:TT], i=pb[1][0:NS, 0:TT]:
                       e.tensor_copy(out=o, in_=i), sig=ev)
                B.wait("sync", ev, ev.n)
                B.emit("sync", lambda e, o=bsc_d[0:NS, ts], i=bcstg[0:NS, 0:TT]:
                       e.dma_start(out=o, in_=i), sig=st, sigk=16)
                if not stateonly:
                    B.wait("vector", st, st.n)
                    B.emit("vector", lambda e, o=bcstg[0:NS, 0:TT], i=pb[2][0:NS, 0:TT]:
                           e.tensor_copy(out=o, in_=i), sig=ev)
                    B.wait("sync", ev, ev.n)
                    B.emit("sync", lambda e, o=csc_d[0:NS, ts], i=bcstg[0:NS, 0:TT]:
                           e.dma_start(out=o, in_=i), sig=st, sigk=16)
            phase_fence(B)
            ar.reset(markL1)
            skcol = [ar.take_bf16(NFC * P), ar.take_bf16(NFC * P)]
            wd2c = [ar.take_bf16(P), ar.take_bf16(P)]
            ab = ar.take_f32(NS * SLL)
            b_in = ar.take_bf16(NS * SLL)
            Bbc = ar.take_f32(NS * SLL // 2)
            delta = ar.take_f32(SLL)
            useg = ar.take_f32(SLL)
            ysl = ar.take_f32(SLL)
            ssk = ar.take_f32(SLL)
            softs = ar.take_f32(2 * SLL)
            B.wait("sync", st, st.n)
            for ec in range(NEC):
                par = ec % 2
                B.wait("sync", mm, mm.n)
                if not stateonly:
                    emit_load_bf16(B, skT_d[:, ec * P:(ec + 1) * P], skcol[par], D, P, wstage,
                                   None, cast, mul_cols=nwcols)
                emit_load_bf16(B, wd2T_d[:, ec * P:(ec + 1) * P], wd2c[par], 64, P, wstage, None, cast)
                skdone = cast.n
                for isl in range(T // SLL):
                    ts = slice(isl * SLL, (isl + 1) * SLL)
                    B.wait("tensor", cast, skdone)
                    B.wait("tensor", sc, sc.n)
                    B.wait("tensor", ev, ev.n)
                    B.emit("tensor", lambda e, l=wd2c[par][0:64, 0:P], r=dcore[0:64, ts],
                           o=pb[5][:, 0:SLL]:
                           e.matmul(out=o, lhsT=l, rhs=r, start=True, stop=True), sig=mm)
                    B.wait("vector", mm, mm.n)
                    B.emit("vector", lambda e, o=delta, i=pb[5][:, 0:SLL], b=wd2b[:, ec:ec + 1]:
                           e.tensor_scalar(out=o, in0=i, scalar1=b, scalar2=None, op0=ALU.add), sig=ev)
                    B.wait("scalar", ev, ev.n)
                    B.emit("scalar", lambda e, o=softs[:, 0:SLL], i=delta:
                           e.activation(out=o, in_=i, func=AF.Abs))
                    B.emit("scalar", lambda e, o=softs[:, SLL:2 * SLL], i=softs[:, 0:SLL]:
                           e.activation(out=o, in_=i, func=AF.Exp, scale=-1.0))
                    B.emit("scalar", lambda e, o=softs[:, 0:SLL], i=softs[:, SLL:2 * SLL]:
                           e.activation(out=o, in_=i, func=AF.Ln, bias=cval(g, 1.0)), sig=sc)
                    B.wait("vector", sc, sc.n)
                    B.emit("vector", lambda e, o=delta:
                           e.tensor_scalar(out=o, in0=delta, scalar1=0.0, scalar2=None, op0=ALU.max), sig=ev)
                    B.wait("vector", ev, ev.n)
                    B.emit("vector", lambda e, o=delta, t=softs[:, 0:SLL]:
                           e.tensor_tensor(out=o, in0=o, in1=t, op=ALU.add), sig=ev)
                    B.wait("vector", ev, ev.n)
                    B.emit("vector", lambda e, o=useg, a=delta, b=xs_ec[ec][:, ts]:
                           e.tensor_tensor(out=o, in0=a, in1=b, op=ALU.mult), sig=ev)
                    B.wait("scalar", ev, ev.n)
                    for n in range(NS):
                        B.emit("scalar", lambda e, o=ab[:, n * SLL:(n + 1) * SLL], i=delta,
                               a=Acols[:, ec * NS + n:ec * NS + n + 1]:
                               e.activation(out=o, in_=i, func=AF.Exp, scale=a),
                               sig=sc if n == NS - 1 else None)
                    HSL = SLL // 2
                    B.wait("vector", sc, sc.n)
                    for hb in range(2):
                        hts = slice(isl * SLL + hb * HSL, isl * SLL + (hb + 1) * HSL)
                        B.wait("sync", ev, ev.n)
                        B.emit("sync", lambda e, o=Bbc[:, 0:NS * HSL].rearrange("p (n t) -> p n t", n=NS),
                               i=bsc_d.rearrange("(o n) t -> o n t", o=1)[0:1, 0:NS, hts].broadcast_to([P, NS, HSL]):
                               e.dma_start(out=o, in_=i), sig=bdma[hb], sigk=16)
                        B.wait("vector", bdma[hb], bdma[hb].n)
                        for n in range(NS):
                            B.emit("vector", lambda e,
                                   o=b_in[:, n * SLL + hb * HSL:n * SLL + (hb + 1) * HSL],
                                   a=useg[:, hb * HSL:(hb + 1) * HSL],
                                   bb=Bbc[:, n * HSL:(n + 1) * HSL]:
                                   e.tensor_tensor(out=o, in0=a, in1=bb, op=ALU.mult),
                                   sig=ev if n == NS - 1 else None)
                    B.wait("vector", ev, ev.n)
                    for n in range(NS):
                        B.emit("vector", lambda e, o=ab[:, n * SLL:(n + 1) * SLL],
                               d0=ab[:, n * SLL:(n + 1) * SLL], d1=b_in[:, n * SLL:(n + 1) * SLL],
                               ini=hbuf[:, ec * NS + n:ec * NS + n + 1]:
                               e.tensor_tensor_scan(out=o, data0=d0, data1=d1, initial=ini,
                                                    op0=ALU.mult, op1=ALU.add),
                               sig=ev if n == NS - 1 else None)
                    B.wait("vector", ev, ev.n)
                    B.emit("vector", lambda e, o=hbuf[:, ec * NS:(ec + 1) * NS],
                           i=ab.rearrange("p (n t) -> p n t", n=NS)[:, :, SLL - 1]:
                           e.tensor_copy(out=o, in_=i), sig=ev)
                    if stateonly:
                        continue
                    for hb in range(2):
                        hts = slice(isl * SLL + hb * HSL, isl * SLL + (hb + 1) * HSL)
                        B.wait("sync", ev, ev.n)
                        B.emit("sync", lambda e, o=Bbc[:, 0:NS * HSL].rearrange("p (n t) -> p n t", n=NS),
                               i=csc_d.rearrange("(o n) t -> o n t", o=1)[0:1, 0:NS, hts].broadcast_to([P, NS, HSL]):
                               e.dma_start(out=o, in_=i), sig=bdma[hb], sigk=16)
                        B.wait("vector", bdma[hb], bdma[hb].n)
                        for n in range(NS):
                            B.emit("vector", lambda e,
                                   o=ab[:, n * SLL + hb * HSL:n * SLL + (hb + 1) * HSL],
                                   cc=Bbc[:, n * HSL:(n + 1) * HSL]:
                                   e.tensor_tensor(out=o, in0=o, in1=cc, op=ALU.mult),
                                   sig=ev if n == NS - 1 else None)
                    B.wait("vector", ev, ev.n)
                    nblk = NS
                    while nblk > 1:
                        half = nblk // 2
                        B.emit("vector", lambda e, o=ab[:, 0:half * SLL],
                               i=ab[:, half * SLL:nblk * SLL]:
                               e.tensor_tensor(out=o, in0=o, in1=i, op=ALU.add), sig=ev)
                        B.wait("vector", ev, ev.n)
                        nblk = half
                    B.emit("vector", lambda e, o=ysl, i=xs_ec[ec][:, ts], w=wDc[:, ec:ec + 1],
                           y=ab[:, 0:SLL]:
                           e.scalar_tensor_tensor(out=o, in0=i, scalar=w, in1=y,
                                                  op0=ALU.mult, op1=ALU.add), sig=ev)
                    B.wait("tensor", cast, skdone)
                    B.wait("tensor", ev, ev.n)
                    for c in range(NFC):
                        B.emit("tensor", lambda e, l=skcol[par][:, c * P:(c + 1) * P],
                               r=xn_tiles[c][:, ts], o=pb[4][:, 0:SLL],
                               s=(c == 0), st=(c == NFC - 1):
                               e.matmul(out=o, lhsT=l, rhs=r, start=s, stop=st),
                               sig=mm if c == NFC - 1 else None)
                    B.wait("vector", mm, mm.n)
                    B.emit("vector", lambda e, o=ssk[:, 0:SLL], a=pb[4][:, 0:SLL], b=rms_bc[:, ts]:
                           e.tensor_tensor(out=o, in0=a, in1=b, op=ALU.mult), sig=ev)
                    B.wait("scalar", ev, ev.n)
                    B.emit("scalar", lambda e, o=ssk[:, 0:SLL], i=ssk[:, 0:SLL]:
                           e.activation(out=o, in_=i, func=AF.Silu), sig=sc)
                    B.wait("vector", sc, sc.n)
                    B.emit("vector", lambda e, o=xs_ec[ec][:, ts], a=ysl, b=ssk[:, 0:SLL]:
                           e.tensor_tensor(out=o, in0=a, in1=b, op=ALU.mult), sig=ev)
            if stateonly:
                ar.free_top((NEC * T + 1) // 2)
                return
            phase_fence(B)
            ar.reset(markL1)
            om = ar.take_bf16(NEC * P)
            for m in range(NFC):
                B.wait("sync", mm, mm.n)
                emit_load_bf16(B, outT_d[:, m * P:(m + 1) * P], om, DIN, P, wstage, None, cast)
                B.wait("tensor", cast, cast.n)
                for it in range(NTT):
                    ts = slice(it * TT, (it + 1) * TT)
                    pd = pb[it % 2]
                    B.wait("tensor", ev, ev.n)
                    for ec in range(NEC):
                        B.emit("tensor", lambda e, l=om[:, ec * P:(ec + 1) * P],
                               r=xs_ec[ec][:, ts], o=pd[:, 0:TT],
                               s=(ec == 0), st=(ec == NEC - 1):
                               e.matmul(out=o, lhsT=l, rhs=r, start=s, stop=st),
                               sig=mm if ec == NEC - 1 else None)
                    B.wait("vector", mm, mm.n)
                    B.emit("vector", lambda e, a=acc_tiles[m][:, ts], i=pd[:, 0:TT]:
                           e.tensor_tensor(out=a, in0=a, in1=i, op=ALU.add), sig=ev)
            phase_fence(B)
            ar.reset(markL1)
            ar.free_top((NEC * T + 1) // 2)
            emit_moe(B, g, s2, ar, xn_tiles, acc_tiles, mWg_d, mWu_d, mWd_d, mW1_d, mW2_d,
                     cwsc_d, wstage)
            phase_fence(B)
            ost = ar.take_f32(1024)
            for c in range(NFC):
                for h in range(2):
                    B.wait("vector", mm, mm.n)
                    B.wait("vector", ev, ev.n)
                    B.wait("vector", st, st.n)
                    B.emit("vector", lambda e, i=acc_tiles[c][:, h * 1024:(h + 1) * 1024], o=ost:
                           e.tensor_copy(out=o, in_=i), sig=ev)
                    B.wait("sync", ev, ev.n)
                    B.emit("sync", lambda e, o=y_d[c * P:(c + 1) * P, h * 1024:(h + 1) * 1024], i=ost:
                           e.dma_start(out=o, in_=i), sig=st, sigk=16)
            B.wait("sync", st, st.n)

        l0_pass(xf_d)
        l1_pass(stateonly=True)
        barrier_all(B)
        B.emit("vector", lambda e: e.tensor_scalar(out=S32d[:, :], in0=S32d[:, :],
               scalar1=selc, scalar2=None, op0=ALU.mult))
        B.emit("vector", lambda e: e.tensor_copy(out=Sbfd[:, :], in_=S32d[:, :]))
        B.emit("vector", lambda e: e.tensor_scalar(out=hbuf[:, :], in0=hbuf[:, :],
               scalar1=selc, scalar2=None, op0=ALU.mult))
        B.emit("vector", lambda e: e.tensor_scalar(out=x1h[:, :], in0=x1hk[:, :],
               scalar1=selc, scalar2=None, op0=ALU.mult), sig=ev)
        l0_pass(xo_d)
        l1_pass(stateonly=False)
        with nc.Block() as block:
            B.replay(block)
    return nc


_BUILT = {}


def _get_built():
    if "l0" not in _BUILT:
        _BUILT["l0"] = build_l0_kernel()
        _BUILT["l1"] = build_l1_kernel()
    return _BUILT["l0"], _BUILT["l1"]


def kernel_multi(L0_attn_x, L0_attn_Wq, L0_attn_Wk, L0_attn_Wv, L0_attn_Wg, L0_attn_Wo,
           L0_attn_w1, L0_attn_w2, L0_attn_w3, L0_attn_g_attn, L0_attn_g_ffn,
           L0_ffn_w_gate, L0_ffn_w_up, L0_ffn_w_down,
           L1_attn_norm_w, L1_attn_skip_w, L1_attn_in_w, L1_attn_conv_w, L1_attn_conv_b,
           L1_attn_wd1, L1_attn_wd2, L1_attn_wd2_b, L1_attn_wb, L1_attn_wc,
           L1_attn_A_log, L1_attn_W_D, L1_attn_out_w,
           L1_ffn_Wg, L1_ffn_Wu, L1_ffn_Wd, L1_ffn_W1, L1_ffn_W2):
    x = np.asarray(L0_attn_x, np.float32)
    nc0, nc1 = _get_built()
    f32 = np.float32
    l0_common = {
        "wq": np.asarray(L0_attn_Wq, f32), "wk": np.asarray(L0_attn_Wk, f32),
        "wv": np.asarray(L0_attn_Wv, f32), "wgg": np.asarray(L0_attn_Wg, f32),
        "wo": np.asarray(L0_attn_Wo, f32),
        "w1": np.asarray(L0_attn_w1, f32), "w2": np.asarray(L0_attn_w2, f32),
        "w3": np.asarray(L0_attn_w3, f32),
        "gattn": np.asarray(L0_attn_g_attn, f32).reshape(D, 1),
        "gffn": np.asarray(L0_attn_g_ffn, f32).reshape(D, 1),
        "wgT": _t(np.asarray(L0_ffn_w_gate, np.int32)),
        "wuT": _t(np.asarray(L0_ffn_w_up, np.int32)),
        "wdT": _t(np.asarray(L0_ffn_w_down, np.int32)),
    }
    l1_common = {
        "normw": np.asarray(L1_attn_norm_w, f32).reshape(D, 1),
        "skipT": _t(np.asarray(L1_attn_skip_w, f32)),
        "inT": _t(np.asarray(L1_attn_in_w, f32)),
        "convw": np.asarray(L1_attn_conv_w, f32),
        "convb": np.asarray(L1_attn_conv_b, f32).reshape(DIN, 1),
        "wd1T": _t(np.asarray(L1_attn_wd1, f32)),
        "wd2T": _t(np.asarray(L1_attn_wd2, f32)),
        "wd2b": np.asarray(L1_attn_wd2_b, f32).reshape(DIN, 1),
        "wbT": _t(np.asarray(L1_attn_wb, f32)),
        "wcT": _t(np.asarray(L1_attn_wc, f32)),
        "alog": np.asarray(L1_attn_A_log, f32),
        "wD": np.asarray(L1_attn_W_D, f32).reshape(DIN, 1),
        "outT": _t(np.asarray(L1_attn_out_w, f32)),
        "mWg": np.asarray(L1_ffn_Wg, f32),
        "mWu": np.ascontiguousarray(np.asarray(L1_ffn_Wu, f32).reshape(8 * D, DIN)),
        "mWd": np.ascontiguousarray(np.asarray(L1_ffn_Wd, f32).reshape(8 * DIN, D)),
        "mW1": np.asarray(L1_ffn_W1, f32), "mW2": np.asarray(L1_ffn_W2, f32),
    }
    zS = np.zeros((P, 512), f32)
    zh = np.zeros((P, NEC * NS), f32)
    zx1h = np.zeros((P, NEC * 3), f32)
    x_first = [np.ascontiguousarray(x[c // 2, 0:T, :].T) for c in range(8)]
    x_own = [np.ascontiguousarray(x[c // 2, (c % 2) * T:(c % 2 + 1) * T, :].T) for c in range(8)]
    cores = list(range(8))
    r1 = run_bass_kernel_spmd(nc0, [dict(l0_common, x=x_first[c], sinit=zS) for c in cores],
                              core_ids=cores)
    resid1_first = [r["y"] for r in r1.results]
    S_sum = [np.ascontiguousarray(r["ys"]) for r in r1.results]
    r2 = run_bass_kernel_spmd(nc0, [dict(l0_common, x=x_own[c],
                                         sinit=(S_sum[c] if c % 2 == 1 else zS)) for c in cores],
                              core_ids=cores)
    resid1_own = [np.ascontiguousarray(r["y"]) for r in r2.results]
    r3 = run_bass_kernel_spmd(nc1, [dict(l1_common, resid=np.ascontiguousarray(resid1_first[c]),
                                         x1halo=zx1h, hinit=zh) for c in cores],
                              core_ids=cores)
    h_sum = [np.ascontiguousarray(r["yh"]) for r in r3.results]
    x1h_sum = [np.ascontiguousarray(r["yx1h"]) for r in r3.results]
    in4 = []
    for c in cores:
        in4.append(dict(l1_common, resid=resid1_own[c],
                        x1halo=(x1h_sum[c] if c % 2 == 1 else zx1h),
                        hinit=(h_sum[c] if c % 2 == 1 else zh)))
    r4 = run_bass_kernel_spmd(nc1, in4, core_ids=cores)
    out = np.empty((4, 2 * T, D), np.float32)
    for c in cores:
        out[c // 2, (c % 2) * T:(c % 2 + 1) * T, :] = r4.results[c]["y"].T
    return out




def _get_built_full():
    if "full" not in _BUILT:
        _BUILT["full"] = build_full_kernel()
    return _BUILT["full"]


def kernel(L0_attn_x, L0_attn_Wq, L0_attn_Wk, L0_attn_Wv, L0_attn_Wg, L0_attn_Wo,
           L0_attn_w1, L0_attn_w2, L0_attn_w3, L0_attn_g_attn, L0_attn_g_ffn,
           L0_ffn_w_gate, L0_ffn_w_up, L0_ffn_w_down,
           L1_attn_norm_w, L1_attn_skip_w, L1_attn_in_w, L1_attn_conv_w, L1_attn_conv_b,
           L1_attn_wd1, L1_attn_wd2, L1_attn_wd2_b, L1_attn_wb, L1_attn_wc,
           L1_attn_A_log, L1_attn_W_D, L1_attn_out_w,
           L1_ffn_Wg, L1_ffn_Wu, L1_ffn_Wd, L1_ffn_W1, L1_ffn_W2):
    x = np.asarray(L0_attn_x, np.float32)
    nc = _get_built_full()
    f32 = np.float32
    common = {
        "wq": np.asarray(L0_attn_Wq, f32), "wk": np.asarray(L0_attn_Wk, f32),
        "wv": np.asarray(L0_attn_Wv, f32), "wgg": np.asarray(L0_attn_Wg, f32),
        "wo": np.asarray(L0_attn_Wo, f32),
        "w1": np.asarray(L0_attn_w1, f32), "w2": np.asarray(L0_attn_w2, f32),
        "w3": np.asarray(L0_attn_w3, f32),
        "gattn": np.asarray(L0_attn_g_attn, f32).reshape(D, 1),
        "gffn": np.asarray(L0_attn_g_ffn, f32).reshape(D, 1),
        "wgT": _t(np.asarray(L0_ffn_w_gate, np.int32)),
        "wuT": _t(np.asarray(L0_ffn_w_up, np.int32)),
        "wdT": _t(np.asarray(L0_ffn_w_down, np.int32)),
        "normw": np.asarray(L1_attn_norm_w, f32).reshape(D, 1),
        "skipT": _t(np.asarray(L1_attn_skip_w, f32)),
        "inT": _t(np.asarray(L1_attn_in_w, f32)),
        "convw": np.asarray(L1_attn_conv_w, f32),
        "convb": np.asarray(L1_attn_conv_b, f32).reshape(DIN, 1),
        "wd1T": _t(np.asarray(L1_attn_wd1, f32)),
        "wd2T": _t(np.asarray(L1_attn_wd2, f32)),
        "wd2b": np.asarray(L1_attn_wd2_b, f32).reshape(DIN, 1),
        "wbT": _t(np.asarray(L1_attn_wb, f32)),
        "wcT": _t(np.asarray(L1_attn_wc, f32)),
        "alog": np.asarray(L1_attn_A_log, f32),
        "wD": np.asarray(L1_attn_W_D, f32).reshape(DIN, 1),
        "outT": _t(np.asarray(L1_attn_out_w, f32)),
        "mWg": np.asarray(L1_ffn_Wg, f32),
        "mWu": np.ascontiguousarray(np.asarray(L1_ffn_Wu, f32).reshape(8 * D, DIN)),
        "mWd": np.ascontiguousarray(np.asarray(L1_ffn_Wd, f32).reshape(8 * DIN, D)),
        "mW1": np.asarray(L1_ffn_W1, f32), "mW2": np.asarray(L1_ffn_W2, f32),
    }
    sel1 = np.ones((P, 1), f32)
    sel0 = np.zeros((P, 1), f32)
    cores = list(range(8))
    ins = []
    for c in cores:
        ins.append(dict(common,
                        xfirst=np.ascontiguousarray(x[c // 2, 0:T, :].T),
                        xown=np.ascontiguousarray(x[c // 2, (c % 2) * T:(c % 2 + 1) * T, :].T),
                        sel=(sel1 if c % 2 == 1 else sel0)))
    r = run_bass_kernel_spmd(nc, ins, core_ids=cores)
    out = np.empty((4, 2 * T, D), np.float32)
    for c in cores:
        out[c // 2, (c % 2) * T:(c % 2 + 1) * T, :] = r.results[c]["y"].T
    return out


if __name__ == "__main__":
    which = sys.argv[1] if len(sys.argv) > 1 else "qffn"
    rng = np.random.default_rng(0)
    if which == "qffn":
        Xfull = rng.standard_normal((T, D)).astype(np.float32)
        wg = rng.integers(-127, 128, (HQ, D)).astype(np.int32)
        wu = rng.integers(-127, 128, (HQ, D)).astype(np.int32)
        wd = rng.integers(-127, 128, (D, HQ)).astype(np.int32)
        nc = build_qffn_kernel()
        in_map = {"x": _t(Xfull), "wgT": _t(wg), "wuT": _t(wu), "wdT": _t(wd)}
        res = run_bass_kernel_spmd(nc, [in_map], core_ids=[0])
        out = res.results[0]["y"].T
        ref = _np_ref_qffn(Xfull, wg, wu, wd)
        print("qffn rel-var:", ((out - ref) ** 2).mean() / (ref ** 2).mean())
    elif which == "l0":
        s_in = 0.1 / np.sqrt(D)
        X = rng.standard_normal((T, D)).astype(np.float32)
        W = {}
        for nmw in ("Wq", "Wk", "Wv", "Wg"):
            W[nmw] = (rng.standard_normal((D, 1024)) * s_in).astype(np.float32)
        W["Wo"] = (rng.standard_normal((1024, D)) * s_in).astype(np.float32)
        W["w1"] = (rng.standard_normal((D, DFF0)) * s_in).astype(np.float32)
        W["w2"] = (rng.standard_normal((D, DFF0)) * s_in).astype(np.float32)
        W["w3"] = (rng.standard_normal((DFF0, D)) * (0.1 / np.sqrt(DFF0))).astype(np.float32)
        W["g_attn"] = (1.0 + 0.1 * rng.standard_normal(D)).astype(np.float32)
        W["g_ffn"] = (1.0 + 0.1 * rng.standard_normal(D)).astype(np.float32)
        W["w_gate"] = rng.integers(-127, 128, (HQ, D)).astype(np.int32)
        W["w_up"] = rng.integers(-127, 128, (HQ, D)).astype(np.int32)
        W["w_down"] = rng.integers(-127, 128, (D, HQ)).astype(np.int32)
        S0 = (rng.standard_normal((16, 64, 64)) * 0.02).astype(np.float32)
        nc = build_l0_kernel()
        im = {"x": _t(X), "wq": W["Wq"], "wk": W["Wk"], "wv": W["Wv"], "wgg": W["Wg"],
              "wo": W["Wo"], "w1": W["w1"], "w2": W["w2"], "w3": W["w3"],
              "gattn": W["g_attn"].reshape(D, 1), "gffn": W["g_ffn"].reshape(D, 1),
              "wgT": _t(W["w_gate"]), "wuT": _t(W["w_up"]), "wdT": _t(W["w_down"]),
              "sinit": _pack_S(S0)}
        ref, refS = _np_ref_l0(X, W, S0)
        res = run_bass_kernel_spmd(nc, [im], core_ids=[0])
        out = res.results[0]["y"].T
        Sf = res.results[0]["ys"]
        print("l0 rel-var:", ((out - ref) ** 2).mean() / (ref ** 2).mean(),
              "S rel-var:", ((Sf - _pack_S(refS)) ** 2).mean() / (_pack_S(refS) ** 2).mean())
    elif which == "l1":
        s_ = lambda shp, fan: (rng.standard_normal(shp) / np.sqrt(fan)).astype(np.float32)
        X = rng.standard_normal((T, D)).astype(np.float32)
        halo = rng.standard_normal((3, D)).astype(np.float32)
        W = {}
        W["norm_w"] = (1.0 + 0.1 * rng.standard_normal(D)).astype(np.float32)
        W["skip_w"] = s_((DIN, D), D); W["in_w"] = s_((DIN, D), D)
        W["conv_w"] = s_((DIN, 4), 4); W["conv_b"] = (0.1 * rng.standard_normal(DIN)).astype(np.float32)
        W["wd1"] = s_((64, DIN), DIN); W["wd2"] = s_((DIN, 64), 64)
        W["wd2_b"] = (0.1 * rng.standard_normal(DIN)).astype(np.float32)
        W["wb"] = s_((NS, DIN), DIN); W["wc"] = s_((NS, DIN), DIN)
        W["A_log"] = rng.standard_normal((DIN, NS)).astype(np.float32)
        W["W_D"] = (1.0 + 0.1 * rng.standard_normal(DIN)).astype(np.float32)
        W["out_w"] = s_((D, DIN), DIN)
        lim = np.sqrt(6.0 / (D + 8))
        W["Wg"] = rng.uniform(-lim, lim, (D, 8)).astype(np.float32)
        lim = np.sqrt(6.0 / (D + DIN))
        W["Wu"] = rng.uniform(-lim, lim, (8, D, DIN)).astype(np.float32)
        W["Wd"] = rng.uniform(-lim, lim, (8, DIN, D)).astype(np.float32)
        W["W1"] = rng.uniform(-lim, lim, (D, DIN)).astype(np.float32)
        W["W2"] = rng.uniform(-lim, lim, (DIN, D)).astype(np.float32)
        h0 = (0.05 * rng.standard_normal((DIN, NS))).astype(np.float32)
        nc = build_l1_kernel()
        im = {"resid": _t(X), "x1halo": np.zeros((P, NEC * 3), np.float32),
              "hinit": np.ascontiguousarray(h0.reshape(NEC, P, NS).transpose(1, 0, 2).reshape(P, NEC * NS)),
              "normw": W["norm_w"].reshape(D, 1), "skipT": _t(W["skip_w"]), "inT": _t(W["in_w"]),
              "convw": W["conv_w"], "convb": W["conv_b"].reshape(DIN, 1),
              "wd1T": _t(W["wd1"]), "wd2T": _t(W["wd2"]), "wd2b": W["wd2_b"].reshape(DIN, 1),
              "wbT": _t(W["wb"]), "wcT": _t(W["wc"]), "alog": W["A_log"],
              "wD": W["W_D"].reshape(DIN, 1), "outT": _t(W["out_w"]),
              "mWg": W["Wg"], "mWu": np.ascontiguousarray(W["Wu"].reshape(8 * D, DIN)),
              "mWd": np.ascontiguousarray(W["Wd"].reshape(8 * DIN, D)),
              "mW1": W["W1"], "mW2": W["W2"]}
        ref, refh = _np_ref_l1(X, W, None, h0)
        res = run_bass_kernel_spmd(nc, [im], core_ids=[0])
        out = res.results[0]["y"].T
        hf = res.results[0]["yh"]
        refh_dev = np.ascontiguousarray(refh.reshape(NEC, P, NS).transpose(1, 0, 2).reshape(P, NEC * NS))
        print("l1 rel-var:", ((out - ref) ** 2).mean() / (ref ** 2).mean(),
              " nan:", np.isnan(out).sum())
        print("l1 h rel-var:", ((hf - refh_dev) ** 2).mean() / (refh_dev ** 2).mean())
    elif which == "gla":
        s_in = 0.1 / np.sqrt(D)
        Xn = rng.standard_normal((T, D)).astype(np.float32)
        Wq = (rng.standard_normal((D, 1024)) * s_in).astype(np.float32)
        Wk = (rng.standard_normal((D, 1024)) * s_in).astype(np.float32)
        Wv = (rng.standard_normal((D, 1024)) * s_in).astype(np.float32)
        Wg = (rng.standard_normal((D, 1024)) * s_in).astype(np.float32)
        gattn = (1.0 + 0.1 * rng.standard_normal(D)).astype(np.float32)
        S0 = (rng.standard_normal((16, 64, 64)) * 0.05).astype(np.float32)
        nc = build_gla_kernel()
        im = {"xn": _t(Xn), "wq": Wq, "wk": Wk, "wv": Wv, "wg": Wg,
              "gattn": gattn.reshape(D, 1), "sinit": _pack_S(S0)}
        ref_o, ref_S = _np_ref_gla(Xn, gattn, Wq, Wk, Wv, Wg, S0)
        res = run_bass_kernel_spmd(nc, [im], core_ids=[0])
        o = res.results[0]["y"].T
        Sf = res.results[0]["ys"]
        print("gla o rel-var:", ((o - ref_o) ** 2).mean() / (ref_o ** 2).mean(),
              "maxabs:", np.abs(o - ref_o).max(), "refscale:", np.abs(ref_o).max())
        print("gla S rel-var:", ((Sf - _pack_S(ref_S)) ** 2).mean() / (_pack_S(ref_S) ** 2 + 1e-12).mean())


# version 09
# speedup vs baseline: 1.7594x; 1.3208x over previous
import sys
sys.path.insert(0, "/opt/trn_rl_repo")
import numpy as np
from contextlib import ExitStack

import concourse.bass as bass
import concourse.mybir as mybir
from concourse.bass_utils import run_bass_kernel_spmd

F32 = mybir.dt.float32
BF16 = mybir.dt.bfloat16
I32 = mybir.dt.int32
AF = mybir.ActivationFunctionType
ALU = mybir.AluOpType
AX = mybir.AxisListType

P = 128
D = 1024
NFC = D // P
T = 2048
TT = 512
NTT = T // TT
EPS_LN = 1e-5
EPS_RMS0 = 1e-6
EPS_RMS1 = 1e-5
DFF0 = 2736
HQ = 3072
DIN = 2048
NHEAD = 16
DK = 64

ENGINES = ("sync", "tensor", "vector", "scalar", "gpsimd")


class Sem:
    __slots__ = ("h", "n", "name")

    def __init__(self, handle, name):
        self.h = handle
        self.n = 0
        self.name = name


class Builder:
    def __init__(self, nc, ctx):
        self.nc = nc
        self.ctx = ctx
        self.q = {e: [] for e in ENGINES}
        self.sems = {}

    def sem(self, name):
        s = Sem(self.ctx.enter_context(self.nc.semaphore(name)), name)
        self.sems[name] = s
        return s

    def on(self, engine, fn):
        self.q[engine].append(fn)

    def emit(self, engine, fn, sig=None, sigk=1):
        if sig is None:
            self.on(engine, lambda eng, f=fn: f(eng))
            return None
        sig.n += sigk

        def run(eng, f=fn, s=sig.h, k=sigk):
            f(eng).then_inc(s, k)
        self.on(engine, run)
        return sig.n

    def wait(self, engine, sem, val=None):
        v = sem.n if val is None else val
        self.on(engine, lambda eng, s=sem.h, vv=v: eng.wait_ge(s, vv))

    def replay(self, block):
        for e in ENGINES:
            fns = self.q[e]
            if not fns:
                continue
            getattr(block, e)(lambda eng, fns=fns: [fn(eng) for fn in fns])


def barrier_all(B, engines=("tensor", "vector", "scalar", "gpsimd", "sync")):
    for e in engines:
        s = B.sems.setdefault("prog_" + e, None) or B.sem("prog_" + e)
        B.emit(e, lambda eng: eng.nop(), sig=s)
    for pe in engines:
        s = B.sems["prog_" + pe]
        for ce in engines:
            if ce != pe:
                B.wait(ce, s)


def sbuf(B, name, shape, dtype):
    return B.ctx.enter_context(B.nc.sbuf_tensor(name, shape, dtype))


def ptensor(B, name, shape, dtype=F32):
    return B.ctx.enter_context(B.nc.psum_tensor(name, shape, dtype))


class Arena:

    def __init__(self, B, words, name="arena"):
        self.t = sbuf(B, name, [P, words], F32)
        self.words = words
        self.off = 0
        self.top = words

    def mark(self):
        return self.off

    def reset(self, m):
        self.off = m

    def take_f32(self, n, parts=P):
        o = self.off
        assert o + n <= self.top, f"arena overflow {o}+{n}>{self.top}"
        self.off += n
        return self.t[0:parts, o:o + n]

    def take_f32_top(self, n, parts=P):
        assert self.off + n <= self.top, f"arena top overflow"
        self.top -= n
        return self.t[0:parts, self.top:self.top + n]

    def take_bf16_top(self, n_elems, parts=P):
        n = (n_elems + 1) // 2
        return self.take_f32_top(n, parts).bitcast(BF16)

    def free_top(self, n_elems_f32):
        self.top += n_elems_f32

    def take_bf16(self, n_elems, parts=P):
        n = (n_elems + 1) // 2
        ap = self.take_f32(n, parts)
        return ap.bitcast(BF16)

    def take_i32(self, n, parts=P):
        return self.take_f32(n, parts).bitcast(I32)



def alloc_consts(B):
    g = {}
    g["ident"] = sbuf(B, "c_ident", [P, P], BF16)
    g["identf"] = sbuf(B, "c_identf", [P, P], F32)
    g["ones_row"] = sbuf(B, "c_ones_row", [1, P], F32)
    g["ones_row_bf"] = sbuf(B, "c_ones_row_bf", [1, P], BF16)
    g["ones_col"] = sbuf(B, "c_ones_col", [P, 1], BF16)
    g["ones_col_f"] = sbuf(B, "c_ones_col_f", [P, 1], F32)
    g["u64"] = sbuf(B, "c_u64", [P, P], F32)
    g["mask64"] = sbuf(B, "c_mask64", [P, 8 * 64], BF16)
    g["subind"] = sbuf(B, "c_subind", [2, P], F32)
    g["ohend"] = sbuf(B, "c_ohend", [P, 2], F32)
    g["cc"] = sbuf(B, "c_cc", [P, 24], F32)
    g["_cvals"] = {}
    return g


def cval(g, v):
    d = g["_cvals"]
    if v not in d:
        d[v] = len(d)
        assert len(d) <= g["cc"].shape[1]
    return g["cc"][:, d[v]:d[v] + 1]


def cval1(g, v):
    return cval(g, v)[0:1, :]


def emit_consts(B, g, done):
    def build(eng):
        eng.memset(g["ident"][:], 0.0)
        eng.affine_select(out=g["ident"][:], in_=g["ident"][:], compare_op=ALU.not_equal,
                          fill=1.0, base=0, pattern=[[-1, P]], channel_multiplier=1)
        eng.memset(g["identf"][:], 0.0)
        eng.affine_select(out=g["identf"][:], in_=g["identf"][:], compare_op=ALU.not_equal,
                          fill=1.0, base=0, pattern=[[-1, P]], channel_multiplier=1)
        eng.memset(g["ones_row"][:], 1.0)
        eng.memset(g["ones_row_bf"][:], 1.0)
        eng.memset(g["ones_col"][:], 1.0)
        eng.memset(g["ones_col_f"][:], 1.0)
        u = g["u64"]
        eng.memset(u[:], 0.0)
        eng.affine_select(out=u[:], in_=u[:], compare_op=ALU.is_gt, fill=1.0,
                          base=0, pattern=[[-1, P]], channel_multiplier=1)
        eng.memset(u[0:64, 64:128], 0.0)
        m = g["mask64"]
        eng.memset(m[:], 0.0)
        eng.affine_select(out=m[0:64, :], in_=m[0:64, :], compare_op=ALU.is_gt, fill=1.0,
                          base=0, pattern=[[0, 8], [-1, 64]], channel_multiplier=1)
        eng.affine_select(out=m[64:128, :], in_=m[64:128, :], compare_op=ALU.is_gt, fill=1.0,
                          base=0, pattern=[[0, 8], [-1, 64]], channel_multiplier=1)
        s = g["subind"]
        eng.memset(s[:], 1.0)
        eng.affine_select(out=s[:], in_=s[:], compare_op=ALU.is_ge, fill=0.0,
                          base=63, pattern=[[-1, P]], channel_multiplier=64)
        eng.affine_select(out=s[:], in_=s[:], compare_op=ALU.is_ge, fill=0.0,
                          base=0, pattern=[[1, P]], channel_multiplier=-64)
        oh = g["ohend"]
        eng.memset(oh[:], 0.0)
        eng.affine_select(out=oh[:, 0:1], in_=oh[:, 0:1], compare_op=ALU.not_equal,
                          fill=1.0, base=-63, pattern=[[0, 1]], channel_multiplier=1)
        eng.affine_select(out=oh[:, 1:2], in_=oh[:, 1:2], compare_op=ALU.not_equal,
                          fill=1.0, base=-127, pattern=[[0, 1]], channel_multiplier=1)
        for v, i in g["_cvals"].items():
            eng.memset(g["cc"][:, i:i + 1], float(v))
    B.on("gpsimd", build)
    B.emit("gpsimd", lambda e: e.engine_nop(), sig=done)



class Stage:

    _uid = [0]

    def __init__(self, B, ap, cast_sem, name):
        self.ap = ap
        self.SW = ap.shape[1] // 2
        self.cast = cast_sem
        self.i = 0
        self.lastc = [0, 0]
        Stage._uid[0] += 1
        u = Stage._uid[0]
        self.ld = [B.sem(f"{name}{u}_ld0"), B.sem(f"{name}{u}_ld1")]

    def slot(self, M, pk=P):
        par = self.i % 2
        self.i += 1
        return par, self.ap[0:pk, par * self.SW:par * self.SW + M]


def emit_load_bf16(B, dram, wtile, K, M, stage, ld, cast, eng_cast="gpsimd",
                   dst_stride=None, dst_off=0, mul_cols=None):
    kc = (K + P - 1) // P
    src_bf = (dram.dtype == BF16)
    assert (((M + 1) // 2) if src_bf else M) <= stage.SW
    if dst_stride is None:
        dst_stride = M
    for c in range(kc):
        pk = min(P, K - c * P)
        par, ssl = stage.slot(((M + 1) // 2) if src_bf else M, pk)
        if src_bf:
            ssl = ssl.bitcast(BF16)[0:pk, 0:M]
        if stage.lastc[par] > 0:
            B.wait("sync", stage.cast, stage.lastc[par])
        srcap = dram[c * P:c * P + pk, :]
        sld = stage.ld[par]
        B.emit("sync", lambda e, s=ssl, d=srcap: e.dma_start(out=s, in_=d), sig=sld, sigk=16)
        B.wait(eng_cast, sld, sld.n)
        off = c * dst_stride + dst_off
        if mul_cols is None:
            B.emit(eng_cast, lambda e, s=ssl, o=wtile[0:pk, off:off + M]:
                   e.tensor_copy(out=o, in_=s), sig=stage.cast)
        else:
            B.emit(eng_cast, lambda e, s=ssl, o=wtile[0:pk, off:off + M],
                   mc=mul_cols[0:pk, c:c + 1]:
                   e.tensor_scalar(out=o, in0=s, scalar1=mc, scalar2=None, op0=ALU.mult),
                   sig=stage.cast)
        stage.lastc[par] = stage.cast.n



def emit_ln(B, g, s2, x_tiles, xn_tiles, Twidth, eps=EPS_LN):
    mm, ev, sc = B.sems["mm"], B.sems["ev"], B.sems["sc"]
    ntt = Twidth // TT
    for it in range(ntt):
        ts = slice(it * TT, (it + 1) * TT)
        B.wait("scalar", mm, mm.n)
        for c in range(NFC):
            B.emit("scalar", lambda e, i=x_tiles[c][:, ts], o=s2["sq8"][:, c * TT:(c + 1) * TT]:
                   e.activation(out=o, in_=i, func=AF.Square), sig=sc if c == NFC - 1 else None)
        B.wait("tensor", sc, sc.n)
        B.wait("tensor", ev, ev.n)
        for c in range(NFC):
            B.emit("tensor", lambda e, r=x_tiles[c][:, ts], o=s2["pbS"][0:1, 0:TT],
                   l=g["ones_col"][:], s=(c == 0), st=(c == NFC - 1):
                   e.matmul(out=o, lhsT=l, rhs=r, start=s, stop=st))
            B.emit("tensor", lambda e, r=s2["sq8"][:, c * TT:(c + 1) * TT], o=s2["pbS"][32:33, 0:TT],
                   l=g["ones_col"][:], s=(c == 0), st=(c == NFC - 1):
                   e.matmul(out=o, lhsT=l, rhs=r, start=s, stop=(c == NFC - 1)),
                   sig=mm if c == NFC - 1 else None)
        B.wait("vector", mm, mm.n)
        mrow, vrow, arow, brow = s2["mrow"], s2["vrow"], s2["arow"], s2["brow"]
        B.emit("vector", lambda e, o=mrow, i=s2["pbS"][0:1, 0:TT]:
               e.tensor_scalar(out=o, in0=i, scalar1=1.0 / D, scalar2=None, op0=ALU.mult))
        B.emit("vector", lambda e, o=vrow, i=s2["pbS"][32:33, 0:TT]:
               e.tensor_scalar(out=o, in0=i, scalar1=1.0 / D, scalar2=None, op0=ALU.mult), sig=ev)
        B.emit("vector", lambda e, o=brow, m=mrow:
               e.tensor_tensor(out=o, in0=m, in1=m, op=ALU.mult))
        B.emit("vector", lambda e, o=vrow, t=brow:
               e.tensor_tensor(out=o, in0=o, in1=t, op=ALU.subtract), sig=ev)
        B.wait("scalar", ev, ev.n)
        B.emit("scalar", lambda e, o=arow, i=vrow:
               e.activation(out=o, in_=i, func=AF.Sqrt, bias=cval1(g, float(eps))), sig=sc)
        B.wait("vector", sc, sc.n)
        B.emit("vector", lambda e, o=arow: e.reciprocal(out=o, in_=o), sig=ev)
        B.wait("vector", ev, ev.n)
        B.emit("vector", lambda e, o=brow, m=mrow, a=arow:
               e.scalar_tensor_tensor(out=o, in0=m, scalar=-1.0, in1=a,
                                      op0=ALU.mult, op1=ALU.mult), sig=ev)
        B.wait("tensor", ev, ev.n)
        B.emit("tensor", lambda e, r=arow, o=s2["pb"][1][:, 0:TT], l=g["ones_row"][:]:
               e.matmul(out=o, lhsT=l, rhs=r, start=True, stop=True))
        B.emit("tensor", lambda e, r=brow, o=s2["pb"][2][:, 0:TT], l=g["ones_row"][:]:
               e.matmul(out=o, lhsT=l, rhs=r, start=True, stop=True), sig=mm)
        B.wait("vector", mm, mm.n)
        B.emit("vector", lambda e, o=s2["bc_a"][:, 0:TT], i=s2["pb"][1][:, 0:TT]:
               e.tensor_copy(out=o, in_=i))
        B.emit("vector", lambda e, o=s2["bc_b"][:, 0:TT], i=s2["pb"][2][:, 0:TT]:
               e.tensor_copy(out=o, in_=i), sig=ev)
        for c in range(NFC):
            B.emit("vector", lambda e, o=xn_tiles[c][:, ts], x=x_tiles[c][:, ts],
                   a=s2["bc_a"][:, 0:TT]: e.tensor_tensor(out=o, in0=x, in1=a, op=ALU.mult))
            B.emit("vector", lambda e, o=xn_tiles[c][:, ts], b=s2["bc_b"][:, 0:TT]:
                   e.tensor_tensor(out=o, in0=o, in1=b, op=ALU.add),
                   sig=ev if c == NFC - 1 else None)



def emit_rmsnorm(B, g, s2, x_tiles, out_tiles, gcols, Twidth, eps, sig_done=None):
    mm, ev, sc = B.sems["mm"], B.sems["ev"], B.sems["sc"]
    ntt = Twidth // TT
    for it in range(ntt):
        ts = slice(it * TT, (it + 1) * TT)
        B.wait("scalar", mm, mm.n)
        for c in range(NFC):
            B.emit("scalar", lambda e, i=x_tiles[c][:, ts], o=s2["sq8"][:, c * TT:(c + 1) * TT]:
                   e.activation(out=o, in_=i, func=AF.Square), sig=sc if c == NFC - 1 else None)
        B.wait("tensor", sc, sc.n)
        B.wait("tensor", ev, ev.n)
        for c in range(NFC):
            B.emit("tensor", lambda e, r=s2["sq8"][:, c * TT:(c + 1) * TT], o=s2["pbS"][0:1, 0:TT],
                   l=g["ones_col"][:], s=(c == 0), st=(c == NFC - 1):
                   e.matmul(out=o, lhsT=l, rhs=r, start=s, stop=st),
                   sig=mm if c == NFC - 1 else None)
        B.wait("scalar", mm, mm.n)
        arow = s2["arow"]
        B.emit("scalar", lambda e, o=arow, i=s2["pbS"][0:1, 0:TT]:
               e.activation(out=o, in_=i, func=AF.Sqrt, scale=1.0 / D, bias=cval1(g, float(eps))), sig=sc)
        B.wait("vector", sc, sc.n)
        B.emit("vector", lambda e, o=arow: e.reciprocal(out=o, in_=o), sig=ev)
        B.wait("tensor", ev, ev.n)
        B.emit("tensor", lambda e, r=arow, o=s2["pb"][1][:, 0:TT], l=g["ones_row"][:]:
               e.matmul(out=o, lhsT=l, rhs=r, start=True, stop=True), sig=mm)
        B.wait("vector", mm, mm.n)
        B.emit("vector", lambda e, o=s2["bc_a"][:, 0:TT], i=s2["pb"][1][:, 0:TT]:
               e.tensor_copy(out=o, in_=i), sig=ev)
        for c in range(NFC):
            B.emit("vector", lambda e, o=out_tiles[c][:, ts], x=x_tiles[c][:, ts],
                   gc=gcols[c], a=s2["bc_a"][:, 0:TT]:
                   e.scalar_tensor_tensor(out=o, in0=x, scalar=gc, in1=a,
                                          op0=ALU.mult, op1=ALU.mult),
                   sig=(sig_done if (sig_done is not None and c == NFC - 1 and it == ntt - 1) else
                        (ev if c == NFC - 1 else None)))



def emit_qffn(B, g, s2, ar, xn_tiles, acc_tiles, wg_d, wu_d, wd_d, Twidth, yg_d=None):
    phase_fence(B)
    mm, ev, sc = B.sems["mm"], B.sems["ev"], B.sems["sc"]
    ld, cast = B.sems["wld"], B.sems["wcast"]
    HC = HQ // P
    ntt = Twidth // TT
    amark = ar.mark()
    wgcol = [ar.take_bf16(NFC * P), ar.take_bf16(NFC * P)]
    wucol = [ar.take_bf16(NFC * P), ar.take_bf16(NFC * P)]
    gact = ar.take_bf16(HC * TT)
    wdm = ar.take_bf16(HC * P)
    sact = [ar.take_f32(TT), ar.take_f32(TT)]
    wstage = Stage(B, ar.take_i32(2 * 1024), B.sems["wcast"], "wst")
    col_done = {}
    stt_done = {}
    last_wdm = [0]
    for it in range(ntt):
        ts = slice(it * TT, (it + 1) * TT)
        import os as _os
        _dbg = _os.environ.get("QDBG", "0") == "1"
        for hc in range(HC):
            if _dbg:
                barrier_all(B)
            par = hc % 2
            key = (it * HC + hc)
            if key >= 2:
                B.wait("sync", mm, col_done[key - 2])
            emit_load_bf16(B, wg_d[:, hc * P:(hc + 1) * P], wgcol[par], D, P, wstage, ld, cast)
            emit_load_bf16(B, wu_d[:, hc * P:(hc + 1) * P], wucol[par], D, P, wstage, ld, cast)
            pa = s2["pb"][2] if par == 0 else s2["pb"][3]
            pb = s2["pb"][4] if par == 0 else s2["pb"][5]
            B.wait("tensor", cast, cast.n)
            B.wait("tensor", sc, sc.n)
            B.wait("tensor", ev, ev.n)
            for c in range(NFC):
                B.emit("tensor", lambda e, l=wgcol[par][:, c * P:(c + 1) * P],
                       r=xn_tiles[c][:, ts], o=pa[:, 0:TT], s=(c == 0), st=(c == NFC - 1):
                       e.matmul(out=o, lhsT=l, rhs=r, start=s, stop=st),
                       sig=mm if c == NFC - 1 else None)
            for c in range(NFC):
                B.emit("tensor", lambda e, l=wucol[par][:, c * P:(c + 1) * P],
                       r=xn_tiles[c][:, ts], o=pb[:, 0:TT], s=(c == 0), st=(c == NFC - 1):
                       e.matmul(out=o, lhsT=l, rhs=r, start=s, stop=st),
                       sig=mm if c == NFC - 1 else None)
            col_done[key] = mm.n
            B.wait("scalar", mm, mm.n - 1)
            if key >= 2:
                B.wait("scalar", ev, stt_done[key - 2])
            B.emit("scalar", lambda e, i=pa[:, 0:TT], o=sact[par]:
                   e.activation(out=o, in_=i, func=AF.Silu, scale=1.0 / 127.0), sig=sc)
            B.wait("vector", mm, mm.n)
            B.wait("vector", sc, sc.n)
            B.emit("vector", lambda e, i0=pb[:, 0:TT], i1=sact[par],
                   o=gact[:, hc * TT:(hc + 1) * TT]:
                   e.scalar_tensor_tensor(out=o, in0=i0, scalar=1.0 / 127.0, in1=i1,
                                          op0=ALU.mult, op1=ALU.mult), sig=ev)
            stt_done[key] = ev.n
        if yg_d is not None:
            st = B.sems["st"]
            B.wait("sync", ev, ev.n)
            B.emit("sync", lambda e, o=yg_d[it * P:(it + 1) * P, :], i=gact[:]:
                   e.dma_start(out=o, in_=i), sig=st, sigk=16)
            for en in ("tensor", "vector", "scalar"):
                B.wait(en, st, st.n)
        for m in range(NFC):
            if _dbg:
                barrier_all(B)
            if last_wdm[0] > 0:
                B.wait("sync", mm, last_wdm[0])
            emit_load_bf16(B, wd_d[:, m * P:(m + 1) * P], wdm, HQ, P, wstage, ld, cast)
            pd = s2["pb"][m % 2]
            B.wait("tensor", cast, cast.n)
            B.wait("tensor", ev, ev.n)
            for hc in range(HC):
                B.emit("tensor", lambda e, l=wdm[:, hc * P:(hc + 1) * P],
                       r=gact[:, hc * TT:(hc + 1) * TT], o=pd[:, 0:TT],
                       s=(hc == 0), st=(hc == HC - 1):
                       e.matmul(out=o, lhsT=l, rhs=r, start=s, stop=st),
                       sig=mm if hc == HC - 1 else None)
            last_wdm[0] = mm.n
            B.wait("vector", mm, mm.n)
            B.emit("vector", lambda e, i=pd[:, 0:TT], a=acc_tiles[m][:, ts]:
                   e.scalar_tensor_tensor(out=a, in0=i, scalar=1.0 / 127.0, in1=a,
                                          op0=ALU.mult, op1=ALU.add), sig=ev)
    ar.reset(amark)


def alloc_ln_scratch(B, ar):
    s2 = {}
    s2["sq8"] = ar.take_bf16(NFC * TT)
    rows = ar.take_f32(2 * TT)
    s2["mrow"] = rows[0:1, 0:TT]
    s2["vrow"] = rows[0:1, TT:2 * TT]
    rows2 = ar.take_bf16(0)
    rows3 = ar.take_f32(2 * TT)
    s2["arow"] = rows3[0:1, 0:TT]
    s2["brow"] = rows3[0:1, TT:2 * TT]
    s2["bc_a"] = ar.take_f32(TT)
    s2["bc_b"] = ar.take_f32(TT)
    return s2


def phase_fence(B):
    for s in ("mm", "ev", "sc"):
        B.wait("sync", B.sems[s], B.sems[s].n)
    for s in ("mm", "ev"):
        B.wait("gpsimd", B.sems[s], B.sems[s].n)



def emit_rms_group(B, g, s2, x_tiles, out_grp, gcols, t0, GT, eps, pbS_row, pbBC,
                   out_tiles=None):
    mm, ev, sc = B.sems["mm"], B.sems["ev"], B.sems["sc"]
    W = GT
    assert W <= TT
    ts = slice(t0, t0 + W)
    B.wait("scalar", mm, mm.n)
    for c in range(NFC):
        B.emit("scalar", lambda e, i=x_tiles[c][:, ts], o=s2["sq8"][:, c * W:(c + 1) * W]:
               e.activation(out=o, in_=i, func=AF.Square), sig=sc if c == NFC - 1 else None)
    B.wait("tensor", sc, sc.n)
    B.wait("tensor", ev, ev.n)
    for c in range(NFC):
        B.emit("tensor", lambda e, r=s2["sq8"][:, c * W:(c + 1) * W], o=pbS_row[0:1, 0:W],
               l=g["ones_col"][:], s=(c == 0), st=(c == NFC - 1):
               e.matmul(out=o, lhsT=l, rhs=r, start=s, stop=st),
               sig=mm if c == NFC - 1 else None)
    B.wait("scalar", mm, mm.n)
    arow = s2["arow"]
    B.emit("scalar", lambda e, o=arow[0:1, 0:W], i=pbS_row[0:1, 0:W]:
           e.activation(out=o, in_=i, func=AF.Sqrt, scale=1.0 / D, bias=cval1(g, float(eps))), sig=sc)
    B.wait("vector", sc, sc.n)
    B.emit("vector", lambda e, o=arow[0:1, 0:W]: e.reciprocal(out=o, in_=o), sig=ev)
    B.wait("tensor", ev, ev.n)
    B.emit("tensor", lambda e, r=arow[0:1, 0:W], o=pbBC[:, 0:W], l=g["ones_row"][:]:
           e.matmul(out=o, lhsT=l, rhs=r, start=True, stop=True), sig=mm)
    B.wait("vector", mm, mm.n)
    B.emit("vector", lambda e, o=s2["bc_a"][:, 0:W], i=pbBC[:, 0:W]:
           e.tensor_copy(out=o, in_=i), sig=ev)
    for c in range(NFC):
        dst = out_tiles[c][:, ts] if out_tiles is not None else out_grp[:, c * GT:c * GT + W]
        B.emit("vector", lambda e, o=dst,
               x=x_tiles[c][:, ts], gc=gcols[c], a=s2["bc_a"][:, 0:W]:
               e.scalar_tensor_tensor(out=o, in0=x, scalar=gc, in1=a,
                                      op0=ALU.mult, op1=ALU.mult),
               sig=ev if c == NFC - 1 else None)



GG = 256


def alloc_gla(B, ar, Twidth, o_buf, wt, S32, Sbf):
    ga = {}
    ga["o_buf"] = o_buf
    ga["wt"] = wt
    ga["S32"] = S32
    ga["Sbf"] = Sbf
    ga["hrms"] = ar.take_bf16(NFC * GG)
    nq = GG // P
    ga["qg"] = ar.take_bf16(nq * 1024)
    ga["kg"] = ar.take_bf16(nq * 1024)
    ga["vg"] = ar.take_bf16(nq * 1024)
    ga["lsg"] = ar.take_f32(nq * 1024)
    ga["Gc"] = ar.take_f32(1024)
    ga["cum"] = ar.take_f32(1024)
    ga["icum"] = ga["Gc"]
    ga["e2"] = ar.take_f32(1024)
    ga["decrow"] = ar.take_f32(1024, parts=2)
    ga["deccols"] = ar.take_f32(16)
    ga["dec"] = ar.take_f32(16)
    ga["qh"] = ar.take_bf16(1024)
    ga["kh"] = ar.take_bf16(1024)
    ga["k2"] = ar.take_bf16(1024)
    ga["qhT"] = ga["qh"]
    ga["khT"] = ga["kh"]
    ga["l2s"] = ar.take_f32(16)
    ga["oall"] = ar.take_bf16(512)
    ga["stil"] = [ar.take_bf16(512), ar.take_bf16(512)]
    return ga


def emit_gla(B, g, s2, ga, xn_tiles, gattn_cols, Twidth, dbg=None):
    phase_fence(B)
    mm, ev, sc = B.sems["mm"], B.sems["ev"], B.sems["sc"]
    cast = B.sems["wcast"]
    pb = s2["pb"]
    pbT = s2["pbT"]
    bI = [pb[0], pb[1]]
    bS = [pb[2], pb[3]]
    bB = [pb[4], pb[5]]
    bM = pb[6]
    wst = ga["wstage"]
    nq = GG // P

    for grp in range(Twidth // GG):
        t0 = grp * GG
        emit_rms_group(B, g, s2, xn_tiles, ga["hrms"], gattn_cols, t0, GG, EPS_RMS0,
                       bS[0], bB[0])
        if dbg and grp == 0:
            dbg("hrms", ga["hrms"][:, 0:1024])
            dbg("hrms2", ga["hrms"][:, 1024:2048])
        for pname, dram in (("zg", ga["wg_d"]), ("v", ga["wv_d"]),
                            ("k", ga["wk_d"]), ("q", ga["wq_d"])):
            B.wait("sync", mm, mm.n)
            emit_load_bf16(B, dram, ga["wt"], D, 1024, wst, None, cast)
            wdone = cast.n
            for st2 in range(nq):
                pp = [bI[0], bI[1]]
                B.wait("tensor", cast, wdone)
                B.wait("tensor", ev, ev.n)
                B.wait("tensor", sc, sc.n)
                for half in range(2):
                    for c in range(NFC):
                        B.emit("tensor", lambda e,
                               l=ga["hrms"][:, c * GG + st2 * P:c * GG + (st2 + 1) * P],
                               r=ga["wt"][:, c * 1024 + half * 512:c * 1024 + (half + 1) * 512],
                               o=pp[half][:, 0:512], s=(c == 0), st=(c == NFC - 1):
                               e.matmul(out=o, lhsT=l, rhs=r, start=s, stop=st),
                               sig=mm if c == NFC - 1 else None)
                mm_done = mm.n
                if pname == "v":
                    B.wait("vector", mm, mm_done)
                    for half in range(2):
                        B.emit("vector", lambda e, i=pp[half][:, 0:512],
                               o=ga["vg"][:, st2 * 1024 + half * 512:st2 * 1024 + (half + 1) * 512]:
                               e.tensor_copy(out=o, in_=i), sig=ev)
                elif pname == "zg":
                    B.wait("scalar", mm, mm_done)
                    sq = s2["sq8"]
                    for half in range(2):
                        B.emit("scalar", lambda e, i=pp[half][:, 0:512], o=sq[:, 0:512]:
                               e.activation(out=o, in_=i, func=AF.Abs))
                        B.emit("scalar", lambda e, i=sq[:, 0:512], o=sq[:, 512:1024]:
                               e.activation(out=o, in_=i, func=AF.Exp, scale=-1.0))
                        B.emit("scalar", lambda e, i=sq[:, 512:1024], o=sq[:, 0:512]:
                               e.activation(out=o, in_=i, func=AF.Ln, bias=cval(g, 1.0)), sig=sc)
                        B.wait("vector", sc, sc.n)
                        lg = ga["lsg"][:, st2 * 1024 + half * 512:st2 * 1024 + (half + 1) * 512]
                        B.emit("vector", lambda e, i=pp[half][:, 0:512], o=lg:
                               e.tensor_scalar(out=o, in0=i, scalar1=0.0, scalar2=None,
                                               op0=ALU.min))
                        B.emit("vector", lambda e, o=lg, t=sq[:, 0:512]:
                               e.tensor_tensor(out=o, in0=o, in1=t, op=ALU.subtract), sig=ev)
                        B.wait("scalar", ev, ev.n)
                else:
                    dst = ga["qg"] if pname == "q" else ga["kg"]
                    B.wait("scalar", mm, mm_done)
                    sq = s2["sq8"]
                    for half in range(2):
                        B.emit("scalar", lambda e, i=pp[half][:, 0:512],
                               o=sq[:, half * 512:(half + 1) * 512]:
                               e.activation(out=o, in_=i, func=AF.Square),
                               sig=sc if half == 1 else None)
                    B.wait("vector", sc, sc.n)
                    B.emit("vector", lambda e, o=ga["l2s"][:, 0:16],
                           i=sq[:, 0:1024].rearrange("p (h k) -> p h k", k=64):
                           e.tensor_reduce(out=o, in_=i, axis=AX.X, op=ALU.add), sig=ev)
                    B.wait("scalar", ev, ev.n)
                    B.emit("scalar", lambda e, o=ga["l2s"][:, 0:16], i=ga["l2s"][:, 0:16]:
                           e.activation(out=o, in_=i, func=AF.Sqrt), sig=sc)
                    B.wait("vector", sc, sc.n)
                    B.emit("vector", lambda e, o=ga["l2s"][:, 0:16]:
                           e.tensor_scalar(out=o, in0=o, scalar1=cval(g, 1e-12), scalar2=None,
                                           op0=ALU.max), sig=ev)
                    B.wait("vector", ev, ev.n)
                    B.emit("vector", lambda e, o=ga["l2s"][:, 0:16]:
                           e.reciprocal(out=o, in_=o), sig=ev)
                    B.wait("vector", ev, ev.n)
                    for h in range(NHEAD):
                        half = h // 8
                        B.emit("vector", lambda e, i=pp[half][:, (h % 8) * 64:(h % 8 + 1) * 64],
                               o=dst[:, st2 * 1024 + h * 64:st2 * 1024 + (h + 1) * 64],
                               r=ga["l2s"][:, h:h + 1]:
                               e.tensor_scalar(out=o, in0=i, scalar1=r, scalar2=None,
                                               op0=ALU.mult),
                               sig=ev if h == NHEAD - 1 else None)
        for st2 in range(nq):
            lgst = ga["lsg"][:, st2 * 1024:(st2 + 1) * 1024]
            B.wait("tensor", ev, ev.n)
            B.wait("tensor", sc, sc.n)
            for half in range(2):
                B.emit("tensor", lambda e, l=g["u64"][:], r=lgst[:, half * 512:(half + 1) * 512],
                       o=bI[half][:, 0:512]:
                       e.matmul(out=o, lhsT=l, rhs=r, start=True, stop=True),
                       sig=mm if half == 1 else None)
            B.wait("vector", mm, mm.n)
            for half in range(2):
                B.emit("vector", lambda e, i=bI[half][:, 0:512],
                       o=ga["Gc"][:, half * 512:(half + 1) * 512]:
                       e.tensor_copy(out=o, in_=i), sig=ev if half == 1 else None)
            B.wait("tensor", ev, ev.n)
            for half in range(2):
                B.emit("tensor", lambda e, l=g["ohend"][:], r=ga["Gc"][:, half * 512:(half + 1) * 512],
                       o=bS[half][0:2, 0:512]:
                       e.matmul(out=o, lhsT=l, rhs=r, start=True, stop=True))
            for c in range(NFC):
                B.emit("tensor", lambda e, l=ga["Gc"][:, c * P:(c + 1) * P], r=g["ohend"][:],
                       o=bM[0:P, c * 2:(c + 1) * 2]:
                       e.matmul(out=o, lhsT=l, rhs=r, start=True, stop=True),
                       sig=mm if c == NFC - 1 else None)
            B.wait("vector", mm, mm.n)
            for half in range(2):
                B.emit("vector", lambda e, i=bS[half][0:2, 0:512],
                       o=ga["decrow"][0:2, half * 512:(half + 1) * 512]:
                       e.tensor_copy(out=o, in_=i))
            B.emit("vector", lambda e, i=bM[0:P, 0:16], o=ga["deccols"][:, 0:16]:
                   e.tensor_copy(out=o, in_=i), sig=ev)
            B.wait("scalar", ev, ev.n)
            B.emit("scalar", lambda e, i=ga["deccols"][:, 0:16], o=ga["dec"][:, 0:16]:
                   e.activation(out=o, in_=i, func=AF.Exp), sig=sc)
            if dbg and grp == 0 and st2 == 0:
                dbg("Gc", ga["Gc"][:, 0:1024])
                dbg("lsg", ga["lsg"][:, 0:1024])
            B.wait("tensor", ev, ev.n)
            for half in range(2):
                B.emit("tensor", lambda e, l=g["subind"][:],
                       r=ga["decrow"][0:2, half * 512:(half + 1) * 512],
                       o=bB[half][:, 0:512]:
                       e.matmul(out=o, lhsT=l, rhs=r, start=True, stop=True),
                       sig=mm if half == 1 else None)
            B.wait("vector", mm, mm.n)
            for half in range(2):
                B.emit("vector", lambda e, o=ga["e2"][:, half * 512:(half + 1) * 512],
                       i0=ga["Gc"][:, half * 512:(half + 1) * 512], i1=bB[half][:, 0:512]:
                       e.scalar_tensor_tensor(out=o, in0=i0, scalar=-1.0, in1=i1,
                                              op0=ALU.mult, op1=ALU.add),
                       sig=ev if half == 1 else None)
            B.wait("scalar", ev, ev.n)
            B.emit("scalar", lambda e, o=ga["e2"][:, 0:1024], i=ga["e2"][:, 0:1024]:
                   e.activation(out=o, in_=i, func=AF.Exp))
            B.emit("scalar", lambda e, o=ga["cum"][:, 0:1024], i=ga["Gc"][:, 0:1024]:
                   e.activation(out=o, in_=i, func=AF.Exp))
            B.emit("scalar", lambda e, o=ga["icum"][:, 0:1024], i=ga["Gc"][:, 0:1024]:
                   e.activation(out=o, in_=i, func=AF.Exp, scale=-1.0), sig=sc)
            if dbg and grp == 0 and st2 == 0:
                dbg("cum", ga["cum"][:, 0:1024], after_sc=True)
                dbg("e2", ga["e2"][:, 0:1024], after_sc=True)
            B.wait("vector", sc, sc.n)
            qsl = ga["qg"][:, st2 * 1024:(st2 + 1) * 1024]
            ksl = ga["kg"][:, st2 * 1024:(st2 + 1) * 1024]
            B.emit("vector", lambda e, o=ga["qh"][:, 0:1024], a=qsl, b=ga["cum"][:, 0:1024]:
                   e.tensor_tensor(out=o, in0=a, in1=b, op=ALU.mult))
            B.emit("vector", lambda e, o=ga["kh"][:, 0:1024], a=ksl, b=ga["icum"][:, 0:1024]:
                   e.tensor_tensor(out=o, in0=a, in1=b, op=ALU.mult))
            B.emit("vector", lambda e, o=ga["k2"][:, 0:1024], a=ksl, b=ga["e2"][:, 0:1024]:
                   e.tensor_tensor(out=o, in0=a, in1=b, op=ALU.mult), sig=ev)
            if dbg and grp == 0 and st2 == 0:
                dbg("qg", ga["qg"][:, 0:1024])
                dbg("kg", ga["kg"][:, 0:1024])
                dbg("vg", ga["vg"][:, 0:1024])
                dbg("qh", ga["qh"][:, 0:1024])
                dbg("kh", ga["kh"][:, 0:1024])
                dbg("k2", ga["k2"][:, 0:1024])
            B.wait("tensor", ev, ev.n)
            for c in range(NFC):
                B.emit("tensor", lambda e, i=ga["qh"][:, c * P:(c + 1) * P],
                       o=pbT[:, c * P:(c + 1) * P]:
                       e.transpose(out=o, in_=i, identity=g["ident"][:]),
                       sig=mm if c == NFC - 1 else None)
            B.wait("vector", mm, mm.n)
            B.emit("vector", lambda e, i=pbT[:, 0:1024], o=ga["qhT"][:, 0:1024]:
                   e.tensor_copy(out=o, in_=i), sig=ev)
            B.wait("tensor", ev, ev.n)
            for c in range(NFC):
                B.emit("tensor", lambda e, i=ga["kh"][:, c * P:(c + 1) * P],
                       o=pbT[:, c * P:(c + 1) * P]:
                       e.transpose(out=o, in_=i, identity=g["ident"][:]),
                       sig=mm if c == NFC - 1 else None)
            B.wait("vector", mm, mm.n)
            B.emit("vector", lambda e, i=pbT[:, 0:1024], o=ga["khT"][:, 0:1024]:
                   e.tensor_copy(out=o, in_=i), sig=ev)
            if dbg and grp == 0 and st2 == 0:
                dbg("qhT", ga["qhT"][:, 0:1024])
                dbg("khT", ga["khT"][:, 0:1024])
            for cc in range(2):
                bc = cc * 64
                B.wait("tensor", ev, ev.n)
                for h in range(NHEAD):
                    ph = (h % 2) * 64
                    fc = h // 2
                    B.emit("tensor", lambda e,
                           l=ga["khT"][ph:ph + 64, fc * P + bc:fc * P + bc + 64],
                           r=ga["qhT"][ph:ph + 64, fc * P + bc:fc * P + bc + 64],
                           o=bS[h % 2][bc:bc + 64, fc * 64:(fc + 1) * 64], tp=(ph, bc):
                           e.matmul(out=o, lhsT=l, rhs=r, start=True, stop=True,
                                    tile_position=tp),
                           sig=mm if h == NHEAD - 1 else None)
                B.wait("vector", mm, mm.n)
                for par in range(2):
                    B.emit("vector", lambda e, i=bS[par][bc:bc + 64, 0:512],
                           o=ga["stil"][par][bc:bc + 64, 0:512], mk=g["mask64"][bc:bc + 64, :]:
                           e.tensor_tensor(out=o, in0=i, in1=mk, op=ALU.mult),
                           sig=ev if par == 1 else None)
                B.wait("tensor", ev, ev.n)
                for h in range(NHEAD):
                    ph = (h % 2) * 64
                    fc = h // 2
                    B.emit("tensor", lambda e,
                           l=ga["vg"][bc:bc + 64, st2 * 1024 + h * 64:st2 * 1024 + (h + 1) * 64],
                           r=ga["stil"][h % 2][bc:bc + 64, fc * 64:(fc + 1) * 64],
                           o=bI[cc][ph:ph + 64, fc * 64:(fc + 1) * 64], tp=(bc, ph):
                           e.matmul(out=o, lhsT=l, rhs=r, start=True, stop=True,
                                    tile_position=tp))
                    B.emit("tensor", lambda e,
                           l=ga["Sbf"][ph:ph + 64, fc * 64:(fc + 1) * 64],
                           r=ga["qhT"][ph:ph + 64, fc * P + bc:fc * P + bc + 64],
                           o=bB[h % 2][ph:ph + 64, fc * 64:(fc + 1) * 64], tp=(ph, ph):
                           e.matmul(out=o, lhsT=l, rhs=r, start=True, stop=True,
                                    tile_position=tp))
                    B.emit("tensor", lambda e,
                           l=ga["k2"][bc:bc + 64, h * 64:(h + 1) * 64],
                           r=ga["vg"][bc:bc + 64, st2 * 1024 + h * 64:st2 * 1024 + (h + 1) * 64],
                           o=bM[ph:ph + 64, fc * 64:(fc + 1) * 64], tp=(bc, ph):
                           e.matmul(out=o, lhsT=l, rhs=r, start=True, stop=True,
                                    tile_position=tp),
                           sig=mm if h == NHEAD - 1 else None)
                B.wait("vector", mm, mm.n)
                B.emit("vector", lambda e, i=bI[cc][:, 0:512], o=ga["oall"][:, 0:512]:
                       e.tensor_copy(out=o, in_=i))
                B.emit("vector", lambda e, o=ga["oall"][0:64, 0:512], i=bB[0][0:64, 0:512]:
                       e.tensor_tensor(out=o, in0=o, in1=i, op=ALU.add))
                B.emit("vector", lambda e, o=ga["oall"][64:128, 0:512], i=bB[1][64:128, 0:512]:
                       e.tensor_tensor(out=o, in0=o, in1=i, op=ALU.add))
                tglob = t0 + st2 * P + bc
                B.emit("vector", lambda e,
                       o=ga["o_buf"].rearrange("p (f t) -> p f t", f=NFC)[:, :, tglob:tglob + 64],
                       i=ga["oall"][:, 0:512].rearrange("p (f t) -> p f t", f=NFC):
                       e.tensor_copy(out=o, in_=i), sig=ev)
                for fc in range(NFC):
                    B.emit("vector", lambda e, o=ga["S32"][:, fc * 64:(fc + 1) * 64],
                           r=ga["dec"][:, fc * 2 + cc:fc * 2 + cc + 1]:
                           e.tensor_scalar(out=o, in0=o, scalar1=r, scalar2=None,
                                           op0=ALU.mult))
                B.emit("vector", lambda e, o=ga["S32"][:, 0:512], i=bM[:, 0:512]:
                       e.tensor_tensor(out=o, in0=o, in1=i, op=ALU.add))
                B.emit("vector", lambda e, o=ga["Sbf"][:, 0:512], i=ga["S32"][:, 0:512]:
                       e.tensor_copy(out=o, in_=i), sig=ev)
                if dbg and grp == 0 and st2 == 0:
                    dbg(f"stil{cc}", ga["stil"][0][:, 0:512])
                    dbg(f"S32c{cc}", ga["S32"][:, 0:512])



HC3 = (DFF0 + P - 1) // P


def emit_wo_swiglu(B, g, s2, ar, xn_tiles, acc_tiles, o_buf, wo_d, w1_d, w2_d, w3_d,
                   gffn_cols, Twidth, wt, wst):
    phase_fence(B)
    mm, ev, sc = B.sems["mm"], B.sems["ev"], B.sems["sc"]
    cast = B.sems["wcast"]
    pb = s2["pb"]
    amark = ar.mark()
    xmid = ar.take_bf16(NFC * TT)
    hgrp = ar.take_bf16(NFC * TT)
    gact = ar.take_bf16(HC3 * TT)
    w1c = [ar.take_bf16(NFC * P), ar.take_bf16(NFC * P)]
    w2c = [ar.take_bf16(NFC * P), ar.take_bf16(NFC * P)]
    w3m = ar.take_bf16(HC3 * P)
    sact = [ar.take_f32(TT), ar.take_f32(TT)]
    col_done = {}
    stt_done = {}
    last_w3 = [0]
    for it in range(Twidth // TT):
        ts = slice(it * TT, (it + 1) * TT)
        B.wait("sync", mm, mm.n)
        emit_load_bf16(B, wo_d, wt, D, 1024, wst, None, cast)
        B.wait("tensor", cast, cast.n)
        for m in range(NFC):
            pd = pb[m % 2]
            B.wait("tensor", ev, ev.n)
            for c in range(NFC):
                B.emit("tensor", lambda e, l=wt[:, c * 1024 + m * P:c * 1024 + (m + 1) * P],
                       r=o_buf[:, c * Twidth:(c + 1) * Twidth][:, ts], o=pd[:, 0:TT],
                       s=(c == 0), st=(c == NFC - 1):
                       e.matmul(out=o, lhsT=l, rhs=r, start=s, stop=st),
                       sig=mm if c == NFC - 1 else None)
            B.wait("vector", mm, mm.n)
            B.emit("vector", lambda e, o=xmid[:, m * TT:(m + 1) * TT], a=xn_tiles[m][:, ts],
                   i=pd[:, 0:TT]: e.tensor_tensor(out=o, in0=a, in1=i, op=ALU.add), sig=ev)
            B.emit("vector", lambda e, o=acc_tiles[m][:, ts], i=xmid[:, m * TT:(m + 1) * TT]:
                   e.tensor_tensor(out=o, in0=o, in1=i, op=ALU.add), sig=ev)
        xm_tiles = [xmid.rearrange("p (f t) -> p f t", f=NFC)[:, c, :] for c in range(NFC)]
        emit_rms_group(B, g, s2, xm_tiles, hgrp, gffn_cols, 0, TT, EPS_RMS0, pb[2], pb[4])
        for hc in range(HC3):
            w = min(P, DFF0 - hc * P)
            par = hc % 2
            key = it * HC3 + hc
            if key >= 2:
                B.wait("sync", mm, col_done[key - 2])
            emit_load_bf16(B, w1_d[:, hc * P:hc * P + w], w1c[par], D, w, wst, None, cast)
            emit_load_bf16(B, w2_d[:, hc * P:hc * P + w], w2c[par], D, w, wst, None, cast)
            pa = pb[2] if par == 0 else pb[3]
            pbk = pb[4] if par == 0 else pb[5]
            B.wait("tensor", cast, cast.n)
            B.wait("tensor", sc, sc.n)
            B.wait("tensor", ev, ev.n)
            for c in range(NFC):
                B.emit("tensor", lambda e, l=w1c[par][:, c * w:(c + 1) * w],
                       r=hgrp[:, c * TT:(c + 1) * TT], o=pa[0:w, 0:TT],
                       s=(c == 0), st=(c == NFC - 1):
                       e.matmul(out=o, lhsT=l, rhs=r, start=s, stop=st),
                       sig=mm if c == NFC - 1 else None)
            for c in range(NFC):
                B.emit("tensor", lambda e, l=w2c[par][:, c * w:(c + 1) * w],
                       r=hgrp[:, c * TT:(c + 1) * TT], o=pbk[0:w, 0:TT],
                       s=(c == 0), st=(c == NFC - 1):
                       e.matmul(out=o, lhsT=l, rhs=r, start=s, stop=st),
                       sig=mm if c == NFC - 1 else None)
            col_done[key] = mm.n
            B.wait("scalar", mm, mm.n - 1)
            if key >= 2:
                B.wait("scalar", ev, stt_done[key - 2])
            B.emit("scalar", lambda e, i=pa[0:w, 0:TT], o=sact[par].bitcast(F32)[0:w, :] if False
                   else sact[par][0:w, 0:TT] if False else sact[par][0:w, :]:
                   e.activation(out=o, in_=i, func=AF.Silu), sig=sc)
            B.wait("vector", mm, mm.n)
            B.wait("vector", sc, sc.n)
            B.emit("vector", lambda e, i0=pbk[0:w, 0:TT], i1=sact[par][0:w, :],
                   o=gact[0:w, hc * TT:(hc + 1) * TT]:
                   e.tensor_tensor(out=o, in0=i0, in1=i1, op=ALU.mult), sig=ev)
            stt_done[key] = ev.n
        for m in range(NFC):
            if last_w3[0] > 0:
                B.wait("sync", mm, last_w3[0])
            emit_load_bf16(B, w3_d[:, m * P:(m + 1) * P], w3m, DFF0, P, wst, None, cast)
            pd = pb[m % 2]
            B.wait("tensor", cast, cast.n)
            B.wait("tensor", ev, ev.n)
            for hc in range(HC3):
                w = min(P, DFF0 - hc * P)
                B.emit("tensor", lambda e, l=w3m[0:w, hc * P:(hc + 1) * P],
                       r=gact[0:w, hc * TT:(hc + 1) * TT], o=pd[:, 0:TT],
                       s=(hc == 0), st=(hc == HC3 - 1):
                       e.matmul(out=o, lhsT=l, rhs=r, start=s, stop=st),
                       sig=mm if hc == HC3 - 1 else None)
            last_w3[0] = mm.n
            B.wait("vector", mm, mm.n)
            B.emit("vector", lambda e, i=pd[:, 0:TT], a=acc_tiles[m][:, ts]:
                   e.tensor_tensor(out=a, in0=a, in1=i, op=ALU.add), sig=ev)
    ar.reset(amark)



NEC = DIN // P
NS = 16


def emit_halo_norm(B, g, s2, halos, halosb, xrh, nw_cols, rms3bc):
    mm, ev, sc = B.sems["mm"], B.sems["ev"], B.sems["sc"]
    pb = s2["pb"]
    W3 = 3
    B.emit("vector", lambda e, o=halosb[:, 0:24], i=halos[:, 0:24]:
           e.tensor_copy(out=o, in_=i), sig=ev)
    B.wait("scalar", ev, ev.n)
    B.wait("scalar", mm, mm.n)
    for c in range(NFC):
        B.emit("scalar", lambda e, i=halos[:, c * 3:(c + 1) * 3], o=s2["sq8"][:, c * 4:c * 4 + 3]:
               e.activation(out=o, in_=i, func=AF.Square), sig=sc if c == NFC - 1 else None)
    B.wait("tensor", sc, sc.n)
    B.wait("tensor", ev, ev.n)
    B.wait("tensor", ev, ev.n)
    for c in range(NFC):
        B.emit("tensor", lambda e, r=halosb[:, c * 3:(c + 1) * 3], o=s2["pbS"][0:1, 0:3],
               l=g["ones_col"][:], s=(c == 0), st=(c == NFC - 1):
               e.matmul(out=o, lhsT=l, rhs=r, start=s, stop=st))
        B.emit("tensor", lambda e, r=s2["sq8"][:, c * 4:c * 4 + 3], o=s2["pbS"][32:33, 0:3],
               l=g["ones_col"][:], s=(c == 0), st=(c == NFC - 1):
               e.matmul(out=o, lhsT=l, rhs=r, start=s, stop=st),
               sig=mm if c == NFC - 1 else None)
    B.wait("vector", mm, mm.n)
    mrow, vrow, arow, brow = s2["mrow"], s2["vrow"], s2["arow"], s2["brow"]
    B.emit("vector", lambda e, o=mrow[0:1, 0:3], i=s2["pbS"][0:1, 0:3]:
           e.tensor_scalar(out=o, in0=i, scalar1=1.0 / D, scalar2=None, op0=ALU.mult))
    B.emit("vector", lambda e, o=vrow[0:1, 0:3], i=s2["pbS"][32:33, 0:3]:
           e.tensor_scalar(out=o, in0=i, scalar1=1.0 / D, scalar2=None, op0=ALU.mult), sig=ev)
    B.emit("vector", lambda e, o=brow[0:1, 0:3], m=mrow[0:1, 0:3]:
           e.tensor_tensor(out=o, in0=m, in1=m, op=ALU.mult))
    B.emit("vector", lambda e, o=vrow[0:1, 0:3], t=brow[0:1, 0:3]:
           e.tensor_tensor(out=o, in0=o, in1=t, op=ALU.subtract), sig=ev)
    B.wait("scalar", ev, ev.n)
    B.emit("scalar", lambda e, o=arow[0:1, 0:3], i=vrow[0:1, 0:3]:
           e.activation(out=o, in_=i, func=AF.Sqrt, bias=cval1(g, float(EPS_LN))), sig=sc)
    B.wait("vector", sc, sc.n)
    B.emit("vector", lambda e, o=arow[0:1, 0:3]: e.reciprocal(out=o, in_=o), sig=ev)
    B.wait("vector", ev, ev.n)
    B.emit("vector", lambda e, o=brow[0:1, 0:3], m=mrow[0:1, 0:3], a=arow[0:1, 0:3]:
           e.scalar_tensor_tensor(out=o, in0=m, scalar=-1.0, in1=a,
                                  op0=ALU.mult, op1=ALU.mult), sig=ev)
    B.wait("tensor", ev, ev.n)
    B.emit("tensor", lambda e, r=arow[0:1, 0:3], o=s2["pb"][1][:, 0:3], l=g["ones_row"][:]:
           e.matmul(out=o, lhsT=l, rhs=r, start=True, stop=True))
    B.emit("tensor", lambda e, r=brow[0:1, 0:3], o=s2["pb"][2][:, 0:3], l=g["ones_row"][:]:
           e.matmul(out=o, lhsT=l, rhs=r, start=True, stop=True), sig=mm)
    B.wait("vector", mm, mm.n)
    xnh = s2["bc_a"]
    for c in range(NFC):
        B.emit("vector", lambda e, o=xnh[:, c * 3:(c + 1) * 3], x=halos[:, c * 3:(c + 1) * 3],
               a=s2["pb"][1][:, 0:3]:
               e.tensor_tensor(out=o, in0=x, in1=a, op=ALU.mult))
        B.emit("vector", lambda e, o=xnh[:, c * 3:(c + 1) * 3], b=s2["pb"][2][:, 0:3]:
               e.tensor_tensor(out=o, in0=o, in1=b, op=ALU.add),
               sig=ev if c == NFC - 1 else None)
    B.wait("scalar", ev, ev.n)
    for c in range(NFC):
        B.emit("scalar", lambda e, i=xnh[:, c * 3:(c + 1) * 3], o=s2["sq8"][:, c * 4:c * 4 + 3]:
               e.activation(out=o, in_=i, func=AF.Square), sig=sc if c == NFC - 1 else None)
    B.wait("tensor", sc, sc.n)
    for c in range(NFC):
        B.emit("tensor", lambda e, r=s2["sq8"][:, c * 4:c * 4 + 3], o=s2["pbS"][0:1, 0:3],
               l=g["ones_col"][:], s=(c == 0), st=(c == NFC - 1):
               e.matmul(out=o, lhsT=l, rhs=r, start=s, stop=st),
               sig=mm if c == NFC - 1 else None)
    B.wait("scalar", mm, mm.n)
    B.emit("scalar", lambda e, o=arow[0:1, 0:3], i=s2["pbS"][0:1, 0:3]:
           e.activation(out=o, in_=i, func=AF.Sqrt, scale=1.0 / D, bias=cval1(g, float(EPS_RMS1))), sig=sc)
    B.wait("vector", sc, sc.n)
    B.emit("vector", lambda e, o=arow[0:1, 0:3]: e.reciprocal(out=o, in_=o), sig=ev)
    B.wait("tensor", ev, ev.n)
    B.emit("tensor", lambda e, r=arow[0:1, 0:3], o=s2["pb"][1][:, 0:3], l=g["ones_row"][:]:
           e.matmul(out=o, lhsT=l, rhs=r, start=True, stop=True), sig=mm)
    B.wait("vector", mm, mm.n)
    B.emit("vector", lambda e, o=rms3bc[:, 0:3], i=s2["pb"][1][:, 0:3]:
           e.tensor_copy(out=o, in_=i), sig=ev)
    for c in range(NFC):
        B.emit("vector", lambda e, o=xrh[:, c * 3:(c + 1) * 3], x=xnh[:, c * 3:(c + 1) * 3]:
               e.tensor_copy(out=o, in_=x),
               sig=ev if c == NFC - 1 else None)


def build_l1_kernel():
    nc = bass.Bass("TRN2", target_bir_lowering=False, debug=False)
    r_d = nc.dram_tensor("resid", [D, T], F32, kind="ExternalInput").ap()
    x1h_d = nc.dram_tensor("x1halo", [P, NEC * 3], F32, kind="ExternalInput").ap()
    hin_d = nc.dram_tensor("hinit", [P, NEC * NS], F32, kind="ExternalInput").ap()
    nw_d = nc.dram_tensor("normw", [D, 1], F32, kind="ExternalInput").ap()
    skT_d = nc.dram_tensor("skipT", [D, DIN], F32, kind="ExternalInput").ap()
    inT_d = nc.dram_tensor("inT", [D, DIN], F32, kind="ExternalInput").ap()
    cw_d = nc.dram_tensor("convw", [DIN, 4], F32, kind="ExternalInput").ap()
    cb_d = nc.dram_tensor("convb", [DIN, 1], F32, kind="ExternalInput").ap()
    wd1T_d = nc.dram_tensor("wd1T", [DIN, 64], F32, kind="ExternalInput").ap()
    wd2T_d = nc.dram_tensor("wd2T", [64, DIN], F32, kind="ExternalInput").ap()
    wd2b_d = nc.dram_tensor("wd2b", [DIN, 1], F32, kind="ExternalInput").ap()
    wbT_d = nc.dram_tensor("wbT", [DIN, NS], F32, kind="ExternalInput").ap()
    wcT_d = nc.dram_tensor("wcT", [DIN, NS], F32, kind="ExternalInput").ap()
    al_d = nc.dram_tensor("alog", [DIN, NS], F32, kind="ExternalInput").ap()
    wdd_d = nc.dram_tensor("wD", [DIN, 1], F32, kind="ExternalInput").ap()
    outT_d = nc.dram_tensor("outT", [DIN, D], F32, kind="ExternalInput").ap()
    mWg_d = nc.dram_tensor("mWg", [D, 8], F32, kind="ExternalInput").ap()
    mWu_d = nc.dram_tensor("mWu", [8 * D, DIN], F32, kind="ExternalInput").ap()
    mWd_d = nc.dram_tensor("mWd", [8 * DIN, D], F32, kind="ExternalInput").ap()
    mW1_d = nc.dram_tensor("mW1", [D, DIN], F32, kind="ExternalInput").ap()
    mW2_d = nc.dram_tensor("mW2", [DIN, D], F32, kind="ExternalInput").ap()
    y_d = nc.dram_tensor("y", [D, T], F32, kind="ExternalOutput").ap()
    yh_d = nc.dram_tensor("yh", [P, NEC * NS], F32, kind="ExternalOutput").ap()
    yx1h_d = nc.dram_tensor("yx1h", [P, NEC * 3], F32, kind="ExternalOutput").ap()
    bsc_d = nc.dram_tensor("bsc", [NS, T], F32, kind="Internal").ap()
    csc_d = nc.dram_tensor("csc", [NS, T], F32, kind="Internal").ap()
    cwsc_d = nc.dram_tensor("cwsc", [8, T], F32, kind="Internal").ap()

    import os as _os
    _dbg = _os.environ.get("L1DBG", "0") == "1"
    if _dbg:
        dmp_d = nc.dram_tensor("dmp", [8 * P, 4096], F32, kind="ExternalOutput").ap()
    SLL = 256
    ctx = ExitStack()
    with ctx:
        B = Builder(nc, ctx)
        for s in ("mm", "ev", "sc", "wld", "wcast", "consts", "st", "gld", "cwdma"):
            B.sem(s)
        bdma = [B.sem("bdma0"), B.sem("bdma1")]
        g = alloc_consts(B)
        ar = Arena(B, 35100)
        s2 = {}
        s2["pbS"] = ptensor(B, "pbS", [P, TT])
        s2["pb"] = [ptensor(B, f"pb{i}", [P, TT]) for i in range(6)]
        s2["pb"].append(s2["pbS"])
        s2["pbT"] = ptensor(B, "pbT", [P, 1024], BF16)
        pb = s2["pb"]
        bufA = sbuf(B, "bufA", [P, NFC * T], BF16)
        bufB = sbuf(B, "bufB", [P, NFC * T], BF16)
        acc_tiles = [bufA[:, c * T:(c + 1) * T] for c in range(NFC)]
        xn_tiles = [bufB[:, c * T:(c + 1) * T] for c in range(NFC)]
        smalls = sbuf(B, "smalls", [P, 480], F32)
        off = [0]
        def sm(n):
            a = smalls[:, off[0]:off[0] + n]
            off[0] += n
            return a
        nwcols = sm(8); convw = sm(64); convb = sm(16); wd2b = sm(16)
        wDc = sm(16); Acols = sm(256); x1h = sm(48); x1hk = sm(48)
        hbuf = sbuf(B, "hbuf", [P, NEC * NS], F32)

        ld, cast, gld = B.sems["wld"], B.sems["wcast"], B.sems["gld"]
        mm, ev, sc = B.sems["mm"], B.sems["ev"], B.sems["sc"]
        st = B.sems["st"]
        wstage = Stage(B, ar.take_f32(2 * 1024), cast, "wst")
        xstage = Stage(B, ar.take_f32(2 * 1024), cast, "xst")
        dcore = ar.take_bf16(T)
        bcstg = ar.take_f32(TT, parts=NS)
        rms_bc = ar.take_f32(T)
        xs = ar.take_bf16_top(NEC * T)
        xs_ec = [xs[:, e * T:(e + 1) * T] for e in range(NEC)]
        mark0 = ar.mark()
        emit_consts(B, g, B.sems["consts"])
        for c in range(NFC):
            for h in range(2):
                par, ssl = xstage.slot(1024)
                if xstage.lastc[par] > 0:
                    B.wait("sync", cast, xstage.lastc[par])
                sld = xstage.ld[par]
                B.emit("sync", lambda e, s=ssl, d=r_d[c * P:(c + 1) * P, h * 1024:(h + 1) * 1024]:
                       e.dma_start(out=s, in_=d), sig=sld, sigk=16)
                B.wait("gpsimd", sld, sld.n)
                B.emit("gpsimd", lambda e, s=ssl, o=acc_tiles[c][:, h * 1024:(h + 1) * 1024]:
                       e.tensor_copy(out=o, in_=s), sig=cast)
                xstage.lastc[par] = cast.n
        def ldsmall(dst, dram, rearr=True):
            if rearr:
                w = dram.shape[1]
                src_ap = dram.rearrange("(c p) w -> p c w", p=P)
                dstv = dst.rearrange("p (c w) -> p c w", w=w)
            else:
                src_ap, dstv = dram, dst
            B.emit("sync", lambda e, o=dstv, i=src_ap:
                   e.dma_start(out=o, in_=i, allow_slow_non_contiguous=True), sig=gld, sigk=16)
        ldsmall(nwcols, nw_d); ldsmall(convw, cw_d); ldsmall(convb, cb_d)
        ldsmall(wd2b, wd2b_d); ldsmall(wDc, wdd_d); ldsmall(Acols, al_d)
        ldsmall(x1h, x1h_d, rearr=False); ldsmall(hbuf[:, :], hin_d, rearr=False)
        B.wait("scalar", gld, gld.n)
        B.emit("scalar", lambda e, o=Acols, i=Acols:
               e.activation(out=o, in_=i, func=AF.Exp), sig=sc)
        B.wait("vector", sc, sc.n)
        B.emit("vector", lambda e, o=Acols:
               e.tensor_scalar(out=o, in0=Acols, scalar1=-1.0, scalar2=None, op0=ALU.mult), sig=ev)
        for e in ("tensor", "vector", "scalar"):
            B.wait(e, cast, cast.n)
            B.wait(e, B.sems["consts"], B.sems["consts"].n)
            B.wait(e, gld, gld.n)
            B.wait(e, ev, ev.n)
        if _dbg:
            dst = xstage.ap
            def dump(slot, ap, w):
                for j0 in range(0, w, 2048):
                    wj = min(2048, w - j0)
                    B.wait("vector", B.sems["ev"], B.sems["ev"].n)
                    B.wait("vector", B.sems["mm"], B.sems["mm"].n)
                    B.wait("vector", B.sems["sc"], B.sems["sc"].n)
                    B.wait("vector", B.sems["st"], B.sems["st"].n)
                    B.emit("vector", lambda e, o=dst[0:ap.shape[0], 0:wj], i=ap[:, j0:j0 + wj]:
                           e.tensor_copy(out=o, in_=i), sig=B.sems["ev"])
                    B.wait("sync", B.sems["ev"], B.sems["ev"].n)
                    B.emit("sync", lambda e, o=dmp_d[slot * P:slot * P + ap.shape[0], j0:j0 + wj],
                           i=dst[0:ap.shape[0], 0:wj]: e.dma_start(out=o, in_=i),
                           sig=B.sems["st"], sigk=16)
                    for en in ("vector", "tensor", "scalar", "gpsimd"):
                        B.wait(en, B.sems["st"], B.sems["st"].n)
            build_l1_kernel._dump = dump
        s2x = alloc_ln_scratch(B, ar)
        for k in ("sq8", "mrow", "vrow", "arow", "brow", "bc_a", "bc_b"):
            s2[k] = s2x[k]
        emit_ln(B, g, s2, acc_tiles, xn_tiles, T)
        nw_cols = [nwcols[:, c:c + 1] for c in range(NFC)]
        for it in range(NTT):
            ts = slice(it * TT, (it + 1) * TT)
            B.wait("scalar", mm, mm.n)
            for c in range(NFC):
                B.emit("scalar", lambda e, i=xn_tiles[c][:, ts], o=s2["sq8"][:, c * TT:(c + 1) * TT]:
                       e.activation(out=o, in_=i, func=AF.Square), sig=sc if c == NFC - 1 else None)
            B.wait("tensor", sc, sc.n)
            B.wait("tensor", ev, ev.n)
            for c in range(NFC):
                B.emit("tensor", lambda e, r=s2["sq8"][:, c * TT:(c + 1) * TT], o=pb[6][0:1, 0:TT],
                       l=g["ones_col"][:], s=(c == 0), st=(c == NFC - 1):
                       e.matmul(out=o, lhsT=l, rhs=r, start=s, stop=st),
                       sig=mm if c == NFC - 1 else None)
            B.wait("scalar", mm, mm.n)
            B.emit("scalar", lambda e, o=s2["arow"][0:1, 0:TT], i=pb[6][0:1, 0:TT]:
                   e.activation(out=o, in_=i, func=AF.Sqrt, scale=1.0 / D,
                                bias=cval1(g, float(EPS_RMS1))), sig=sc)
            B.wait("vector", sc, sc.n)
            B.emit("vector", lambda e, o=s2["arow"][0:1, 0:TT]:
                   e.reciprocal(out=o, in_=s2["arow"][0:1, 0:TT]), sig=ev)
            B.wait("tensor", ev, ev.n)
            B.emit("tensor", lambda e, r=s2["arow"][0:1, 0:TT], o=pb[4][:, 0:TT],
                   l=g["ones_row"][:]:
                   e.matmul(out=o, lhsT=l, rhs=r, start=True, stop=True), sig=mm)
            B.wait("vector", mm, mm.n)
            B.emit("vector", lambda e, o=rms_bc[:, ts], i=pb[4][:, 0:TT]:
                   e.tensor_copy(out=o, in_=i), sig=ev)

        wcol = [ar.take_bf16(NFC * P), ar.take_bf16(NFC * P)]
        x1e = ar.take_bf16(2056)
        cvt = ar.take_f32(T)
        for ec in range(NEC):
            par = ec % 2
            B.wait("sync", mm, mm.n)
            emit_load_bf16(B, inT_d[:, ec * P:(ec + 1) * P], wcol[par], D, P, wstage, None, cast,
                           mul_cols=nwcols)
            B.wait("tensor", cast, cast.n)
            B.wait("tensor", sc, sc.n)
            B.wait("tensor", ev, ev.n)
            B.emit("vector", lambda e, o=x1e[:, 0:3], i=x1h[:, ec * 3:(ec + 1) * 3]:
                   e.tensor_copy(out=o, in_=i), sig=ev)
            for it in range(NTT):
                pp = pb[it % 2]
                ts = slice(it * TT, (it + 1) * TT)
                for c in range(NFC):
                    B.emit("tensor", lambda e, l=wcol[par][:, c * P:(c + 1) * P],
                           r=xn_tiles[c][:, ts], o=pp[:, 0:TT], s=(c == 0), st=(c == NFC - 1):
                           e.matmul(out=o, lhsT=l, rhs=r, start=s, stop=st),
                           sig=mm if c == NFC - 1 else None)
                B.wait("vector", mm, mm.n)
                B.emit("vector", lambda e, o=x1e[:, 3 + it * TT:3 + (it + 1) * TT], a=pp[:, 0:TT],
                       b=rms_bc[:, ts]: e.tensor_tensor(out=o, in0=a, in1=b, op=ALU.mult), sig=ev)
            B.emit("vector", lambda e, o=x1hk[:, ec * 3:(ec + 1) * 3],
                   i=x1e[:, 3 + T - 3:3 + T]: e.tensor_copy(out=o, in_=i), sig=ev)
            B.emit("vector", lambda e, o=cvt, i=x1e[:, 0:T], w0=convw[:, ec * 4:ec * 4 + 1],
                   b=convb[:, ec:ec + 1]:
                   e.tensor_scalar(out=o, in0=i, scalar1=w0, scalar2=b,
                                   op0=ALU.mult, op1=ALU.add))
            for k in (1, 2, 3):
                B.emit("vector", lambda e, o=cvt, i=x1e[:, k:k + T],
                       w=convw[:, ec * 4 + k:ec * 4 + k + 1]:
                       e.scalar_tensor_tensor(out=o, in0=i, scalar=w, in1=o,
                                              op0=ALU.mult, op1=ALU.add),
                       sig=ev if k == 3 else None)
            if _dbg and ec == 0:
                B.wait("vector", ev, ev.n)
                build_l1_kernel._dump(5, x1e[:, 0:16], 16)
                build_l1_kernel._dump(4, cvt[:, 0:16], 16)
            B.wait("scalar", ev, ev.n)
            B.emit("scalar", lambda e, o=xs_ec[ec][:, 0:T], i=cvt:
                   e.activation(out=o, in_=i, func=AF.Silu), sig=sc)
        if _dbg:
            build_l1_kernel._dump(0, xn_tiles[0][:, 0:2048], 2048)
            build_l1_kernel._dump(1, rms_bc[:, 0:2048], 2048)
            build_l1_kernel._dump(2, xs_ec[0][:, 0:2048], 2048)
            build_l1_kernel._dump(6, xrh[:, 0:24], 24)
            build_l1_kernel._dump(7, smalls[:, 0:400], 400)
        phase_fence(B)
        ar.reset(mark0)
        wd1t = ar.take_bf16(NEC * 64)
        wbct = ar.take_bf16(2 * NEC * NS)
        emit_load_bf16(B, wd1T_d, wd1t, DIN, 64, wstage, None, cast)
        emit_load_bf16(B, wbT_d, wbct, DIN, NS, wstage, None, cast, dst_stride=2 * NS, dst_off=0)
        emit_load_bf16(B, wcT_d, wbct, DIN, NS, wstage, None, cast, dst_stride=2 * NS, dst_off=NS)
        B.wait("tensor", cast, cast.n)
        B.wait("tensor", sc, sc.n)
        for it in range(NTT):
            ts = slice(it * TT, (it + 1) * TT)
            B.wait("tensor", ev, ev.n)
            for ec in range(NEC):
                B.emit("tensor", lambda e, l=wd1t[:, ec * 64:(ec + 1) * 64],
                       r=xs_ec[ec][:, ts], o=pb[0][0:64, 0:TT],
                       s=(ec == 0), st=(ec == NEC - 1):
                       e.matmul(out=o, lhsT=l, rhs=r, start=s, stop=st))
            for ec in range(NEC):
                B.emit("tensor", lambda e, l=wbct[:, ec * 2 * NS:ec * 2 * NS + NS],
                       r=xs_ec[ec][:, ts], o=pb[1][0:NS, 0:TT],
                       s=(ec == 0), st=(ec == NEC - 1):
                       e.matmul(out=o, lhsT=l, rhs=r, start=s, stop=st))
            for ec in range(NEC):
                B.emit("tensor", lambda e, l=wbct[:, ec * 2 * NS + NS:(ec + 1) * 2 * NS],
                       r=xs_ec[ec][:, ts], o=pb[2][0:NS, 0:TT],
                       s=(ec == 0), st=(ec == NEC - 1):
                       e.matmul(out=o, lhsT=l, rhs=r, start=s, stop=st),
                       sig=mm if ec == NEC - 1 else None)
            B.wait("vector", mm, mm.n)
            B.emit("vector", lambda e, o=dcore[0:64, ts], i=pb[0][0:64, 0:TT]:
                   e.tensor_copy(out=o, in_=i), sig=ev)
            B.wait("vector", st, st.n)
            B.emit("vector", lambda e, o=bcstg[0:NS, 0:TT], i=pb[1][0:NS, 0:TT]:
                   e.tensor_copy(out=o, in_=i), sig=ev)
            B.wait("sync", ev, ev.n)
            B.emit("sync", lambda e, o=bsc_d[0:NS, ts], i=bcstg[0:NS, 0:TT]:
                   e.dma_start(out=o, in_=i), sig=st, sigk=16)
            B.wait("vector", st, st.n)
            B.emit("vector", lambda e, o=bcstg[0:NS, 0:TT], i=pb[2][0:NS, 0:TT]:
                   e.tensor_copy(out=o, in_=i), sig=ev)
            B.wait("sync", ev, ev.n)
            B.emit("sync", lambda e, o=csc_d[0:NS, ts], i=bcstg[0:NS, 0:TT]:
                   e.dma_start(out=o, in_=i), sig=st, sigk=16)
        if _dbg:
            build_l1_kernel._dump(3, dcore[0:64, 0:2048], 2048)
        phase_fence(B)
        ar.reset(mark0)
        skcol = [ar.take_bf16(NFC * P), ar.take_bf16(NFC * P)]
        wd2c = [ar.take_bf16(P), ar.take_bf16(P)]
        ab = ar.take_f32(NS * SLL)
        b_in = ar.take_bf16(NS * SLL)
        Bbc = ar.take_f32(NS * SLL // 2)
        delta = ar.take_f32(SLL)
        useg = ar.take_f32(SLL)
        ysl = ar.take_f32(SLL)
        ssk = ar.take_f32(SLL)
        softs = ar.take_f32(2 * SLL)
        B.wait("sync", st, st.n)
        for ec in range(NEC):
            par = ec % 2
            B.wait("sync", mm, mm.n)
            emit_load_bf16(B, skT_d[:, ec * P:(ec + 1) * P], skcol[par], D, P, wstage, None, cast,
                           mul_cols=nwcols)
            emit_load_bf16(B, wd2T_d[:, ec * P:(ec + 1) * P], wd2c[par], 64, P, wstage, None, cast)
            skdone = cast.n
            for isl in range(T // SLL):
                ts = slice(isl * SLL, (isl + 1) * SLL)
                B.wait("tensor", cast, skdone)
                B.wait("tensor", sc, sc.n)
                B.wait("tensor", ev, ev.n)
                B.emit("tensor", lambda e, l=wd2c[par][0:64, 0:P], r=dcore[0:64, ts],
                       o=pb[5][:, 0:SLL]:
                       e.matmul(out=o, lhsT=l, rhs=r, start=True, stop=True), sig=mm)
                B.wait("vector", mm, mm.n)
                B.emit("vector", lambda e, o=delta, i=pb[5][:, 0:SLL], b=wd2b[:, ec:ec + 1]:
                       e.tensor_scalar(out=o, in0=i, scalar1=b, scalar2=None, op0=ALU.add), sig=ev)
                B.wait("scalar", ev, ev.n)
                B.emit("scalar", lambda e, o=softs[:, 0:SLL], i=delta:
                       e.activation(out=o, in_=i, func=AF.Abs))
                B.emit("scalar", lambda e, o=softs[:, SLL:2 * SLL], i=softs[:, 0:SLL]:
                       e.activation(out=o, in_=i, func=AF.Exp, scale=-1.0))
                B.emit("scalar", lambda e, o=softs[:, 0:SLL], i=softs[:, SLL:2 * SLL]:
                       e.activation(out=o, in_=i, func=AF.Ln, bias=cval(g, 1.0)), sig=sc)
                B.wait("vector", sc, sc.n)
                B.emit("vector", lambda e, o=delta:
                       e.tensor_scalar(out=o, in0=delta, scalar1=0.0, scalar2=None, op0=ALU.max), sig=ev)
                B.wait("vector", ev, ev.n)
                B.emit("vector", lambda e, o=delta, t=softs[:, 0:SLL]:
                       e.tensor_tensor(out=o, in0=o, in1=t, op=ALU.add), sig=ev)
                B.wait("vector", ev, ev.n)
                B.emit("vector", lambda e, o=useg, a=delta, b=xs_ec[ec][:, ts]:
                       e.tensor_tensor(out=o, in0=a, in1=b, op=ALU.mult), sig=ev)
                B.wait("scalar", ev, ev.n)
                for n in range(NS):
                    B.emit("scalar", lambda e, o=ab[:, n * SLL:(n + 1) * SLL], i=delta,
                           a=Acols[:, ec * NS + n:ec * NS + n + 1]:
                           e.activation(out=o, in_=i, func=AF.Exp, scale=a),
                           sig=sc if n == NS - 1 else None)
                if _dbg and ec == 0 and isl == 0:
                    B.wait("vector", sc, sc.n)
                    build_l1_kernel._dump(4, delta[:, 0:SLL], SLL)
                HSL = SLL // 2
                B.wait("vector", sc, sc.n)
                for hb in range(2):
                    hts = slice(isl * SLL + hb * HSL, isl * SLL + (hb + 1) * HSL)
                    B.wait("sync", ev, ev.n)
                    B.emit("sync", lambda e, o=Bbc[:, 0:NS * HSL].rearrange("p (n t) -> p n t", n=NS),
                           i=bsc_d.rearrange("(o n) t -> o n t", o=1)[0:1, 0:NS, hts].broadcast_to([P, NS, HSL]):
                           e.dma_start(out=o, in_=i), sig=bdma[hb], sigk=16)
                    B.wait("vector", bdma[hb], bdma[hb].n)
                    for n in range(NS):
                        B.emit("vector", lambda e,
                               o=b_in[:, n * SLL + hb * HSL:n * SLL + (hb + 1) * HSL],
                               a=useg[:, hb * HSL:(hb + 1) * HSL],
                               bb=Bbc[:, n * HSL:(n + 1) * HSL]:
                               e.tensor_tensor(out=o, in0=a, in1=bb, op=ALU.mult),
                               sig=ev if n == NS - 1 else None)
                B.wait("vector", ev, ev.n)
                for n in range(NS):
                    B.emit("vector", lambda e, o=ab[:, n * SLL:(n + 1) * SLL],
                           d0=ab[:, n * SLL:(n + 1) * SLL], d1=b_in[:, n * SLL:(n + 1) * SLL],
                           ini=hbuf[:, ec * NS + n:ec * NS + n + 1]:
                           e.tensor_tensor_scan(out=o, data0=d0, data1=d1, initial=ini,
                                                op0=ALU.mult, op1=ALU.add),
                           sig=ev if n == NS - 1 else None)
                B.wait("vector", ev, ev.n)
                B.emit("vector", lambda e, o=hbuf[:, ec * NS:(ec + 1) * NS],
                       i=ab.rearrange("p (n t) -> p n t", n=NS)[:, :, SLL - 1]:
                       e.tensor_copy(out=o, in_=i), sig=ev)
                for hb in range(2):
                    hts = slice(isl * SLL + hb * HSL, isl * SLL + (hb + 1) * HSL)
                    B.wait("sync", ev, ev.n)
                    B.emit("sync", lambda e, o=Bbc[:, 0:NS * HSL].rearrange("p (n t) -> p n t", n=NS),
                           i=csc_d.rearrange("(o n) t -> o n t", o=1)[0:1, 0:NS, hts].broadcast_to([P, NS, HSL]):
                           e.dma_start(out=o, in_=i), sig=bdma[hb], sigk=16)
                    B.wait("vector", bdma[hb], bdma[hb].n)
                    for n in range(NS):
                        B.emit("vector", lambda e,
                               o=ab[:, n * SLL + hb * HSL:n * SLL + (hb + 1) * HSL],
                               cc=Bbc[:, n * HSL:(n + 1) * HSL]:
                               e.tensor_tensor(out=o, in0=o, in1=cc, op=ALU.mult),
                               sig=ev if n == NS - 1 else None)
                B.wait("vector", ev, ev.n)
                nblk = NS
                while nblk > 1:
                    half = nblk // 2
                    B.emit("vector", lambda e, o=ab[:, 0:half * SLL],
                           i=ab[:, half * SLL:nblk * SLL]:
                           e.tensor_tensor(out=o, in0=o, in1=i, op=ALU.add), sig=ev)
                    B.wait("vector", ev, ev.n)
                    nblk = half
                B.emit("vector", lambda e, o=ysl, i=xs_ec[ec][:, ts], w=wDc[:, ec:ec + 1],
                       y=ab[:, 0:SLL]:
                       e.scalar_tensor_tensor(out=o, in0=i, scalar=w, in1=y,
                                              op0=ALU.mult, op1=ALU.add), sig=ev)
                B.wait("tensor", cast, skdone)
                B.wait("tensor", ev, ev.n)
                for c in range(NFC):
                    B.emit("tensor", lambda e, l=skcol[par][:, c * P:(c + 1) * P],
                           r=xn_tiles[c][:, ts], o=pb[4][:, 0:SLL],
                           s=(c == 0), st=(c == NFC - 1):
                           e.matmul(out=o, lhsT=l, rhs=r, start=s, stop=st),
                           sig=mm if c == NFC - 1 else None)
                B.wait("vector", mm, mm.n)
                B.emit("vector", lambda e, o=ssk[:, 0:SLL], a=pb[4][:, 0:SLL], b=rms_bc[:, ts]:
                       e.tensor_tensor(out=o, in0=a, in1=b, op=ALU.mult), sig=ev)
                B.wait("scalar", ev, ev.n)
                B.emit("scalar", lambda e, o=ssk[:, 0:SLL], i=ssk[:, 0:SLL]:
                       e.activation(out=o, in_=i, func=AF.Silu), sig=sc)
                B.wait("vector", sc, sc.n)
                B.emit("vector", lambda e, o=xs_ec[ec][:, ts], a=ysl, b=ssk[:, 0:SLL]:
                       e.tensor_tensor(out=o, in0=a, in1=b, op=ALU.mult), sig=ev)
        B.wait("vector", ev, ev.n)
        B.emit("vector", lambda e, o=ssk[:, 0:SLL] if SLL >= 256 else None, i=hbuf[:, :]:
               e.tensor_copy(out=o[:, 0:256] if False else ssk[:, 0:256], in_=i), sig=ev)
        B.wait("sync", ev, ev.n)
        B.emit("sync", lambda e, o=yh_d, i=ssk[:, 0:256]: e.dma_start(out=o, in_=i), sig=st, sigk=16)
        B.emit("sync", lambda e, o=yx1h_d, i=x1hk[:, 0:NEC * 3]: e.dma_start(out=o, in_=i), sig=st, sigk=16)
        phase_fence(B)
        ar.reset(mark0)
        om = ar.take_bf16(NEC * P)
        for m in range(NFC):
            B.wait("sync", mm, mm.n)
            emit_load_bf16(B, outT_d[:, m * P:(m + 1) * P], om, DIN, P, wstage, None, cast)
            B.wait("tensor", cast, cast.n)
            for it in range(NTT):
                ts = slice(it * TT, (it + 1) * TT)
                pd = pb[it % 2]
                B.wait("tensor", ev, ev.n)
                for ec in range(NEC):
                    B.emit("tensor", lambda e, l=om[:, ec * P:(ec + 1) * P],
                           r=xs_ec[ec][:, ts], o=pd[:, 0:TT],
                           s=(ec == 0), st=(ec == NEC - 1):
                           e.matmul(out=o, lhsT=l, rhs=r, start=s, stop=st),
                           sig=mm if ec == NEC - 1 else None)
                B.wait("vector", mm, mm.n)
                B.emit("vector", lambda e, a=acc_tiles[m][:, ts], i=pd[:, 0:TT]:
                       e.tensor_tensor(out=a, in0=a, in1=i, op=ALU.add), sig=ev)
        phase_fence(B)
        ar.reset(mark0)
        ar.free_top((NEC * T + 1) // 2)
        emit_moe(B, g, s2, ar, xn_tiles, acc_tiles, mWg_d, mWu_d, mWd_d, mW1_d, mW2_d,
                 cwsc_d, wstage)
        phase_fence(B)
        ost = ar.take_f32(1024)
        for c in range(NFC):
            for h in range(2):
                B.wait("vector", mm, mm.n)
                B.wait("vector", ev, ev.n)
                B.wait("vector", st, st.n)
                B.emit("vector", lambda e, i=acc_tiles[c][:, h * 1024:(h + 1) * 1024], o=ost:
                       e.tensor_copy(out=o, in_=i), sig=ev)
                B.wait("sync", ev, ev.n)
                B.emit("sync", lambda e, o=y_d[c * P:(c + 1) * P, h * 1024:(h + 1) * 1024], i=ost:
                       e.dma_start(out=o, in_=i), sig=st, sigk=16)
        B.wait("sync", st, st.n)
        with nc.Block() as block:
            B.replay(block)
    return nc


def emit_moe(B, g, s2, ar, xn_tiles, acc_tiles, wg_d, wu_d, wdn_d, w1_d, w2_d, cwsc_d, wst):
    mm, ev, sc = B.sems["mm"], B.sems["ev"], B.sems["sc"]
    cast = B.sems["wcast"]
    st = B.sems["st"]
    cb = B.sems["cwdma"]
    pb = s2["pb"]
    amark = ar.mark()
    s2x = alloc_ln_scratch(B, ar)
    wgt = ar.take_bf16(NFC * 8)
    cwst = ar.take_f32(P, parts=8)
    emit_load_bf16(B, wg_d, wgt, D, 8, wst, None, cast)
    B.wait("tensor", cast, cast.n)
    w = s2x["bc_a"]
    for sti in range(T // P):
        tsl = slice(sti * P, (sti + 1) * P)
        B.wait("tensor", ev, ev.n)
        for c in range(NFC):
            B.emit("tensor", lambda e, l=xn_tiles[c][:, tsl], r=wgt[:, c * 8:(c + 1) * 8],
                   o=pb[0][:, 0:8], s=(c == 0), st=(c == NFC - 1):
                   e.matmul(out=o, lhsT=l, rhs=r, start=s, stop=st),
                   sig=mm if c == NFC - 1 else None)
        B.wait("vector", mm, mm.n)
        p8 = w[:, 0:8]; m1 = w[:, 8:9]; m2 = w[:, 9:10]; ssum = w[:, 10:11]
        p2 = w[:, 16:24]
        def vstep(fn):
            B.emit("vector", fn, sig=ev)
            B.wait("vector", ev, ev.n)
        vstep(lambda e: e.tensor_copy(out=p8, in_=pb[0][:, 0:8]))
        vstep(lambda e: e.tensor_reduce(out=m1, in_=p8, axis=AX.X, op=ALU.max))
        vstep(lambda e: e.tensor_scalar(out=p8, in0=p8, scalar1=m1, scalar2=None, op0=ALU.subtract))
        B.wait("scalar", ev, ev.n)
        B.emit("scalar", lambda e: e.activation(out=p8, in_=p8, func=AF.Exp), sig=sc)
        B.wait("vector", sc, sc.n)
        vstep(lambda e: e.tensor_reduce(out=ssum, in_=p8, axis=AX.X, op=ALU.add))
        vstep(lambda e: e.reciprocal(out=ssum, in_=ssum))
        vstep(lambda e: e.tensor_scalar(out=p8, in0=p8, scalar1=ssum, scalar2=None, op0=ALU.mult))
        vstep(lambda e: e.tensor_reduce(out=m1, in_=p8, axis=AX.X, op=ALU.max))
        vstep(lambda e: e.tensor_scalar(out=p2, in0=p8, scalar1=m1, scalar2=2.0,
                                        op0=ALU.is_ge, op1=ALU.mult))
        vstep(lambda e: e.tensor_tensor(out=p2, in0=p8, in1=p2, op=ALU.subtract))
        vstep(lambda e: e.tensor_reduce(out=m2, in_=p2, axis=AX.X, op=ALU.max))
        vstep(lambda e: e.tensor_scalar(out=p2, in0=p8, scalar1=m2, scalar2=None, op0=ALU.is_ge))
        vstep(lambda e: e.tensor_tensor(out=p8, in0=p8, in1=p2, op=ALU.mult))
        vstep(lambda e: e.tensor_reduce(out=ssum, in_=p8, axis=AX.X, op=ALU.add))
        vstep(lambda e: e.reciprocal(out=ssum, in_=ssum))
        vstep(lambda e: e.tensor_scalar(out=p8, in0=p8, scalar1=ssum, scalar2=None, op0=ALU.mult))
        B.wait("tensor", ev, ev.n)
        B.emit("tensor", lambda e: e.transpose(out=pb[1][0:8, 0:P], in_=p8, identity=g["identf"][:]),
               sig=mm)
        B.wait("vector", mm, mm.n)
        B.wait("vector", st, st.n)
        B.emit("vector", lambda e: e.tensor_copy(out=cwst[0:8, 0:P], in_=pb[1][0:8, 0:P]), sig=ev)
        B.wait("sync", ev, ev.n)
        B.emit("sync", lambda e, o=cwsc_d[0:8, tsl]: e.dma_start(out=o, in_=cwst[0:8, 0:P]),
               sig=st, sigk=16)
    wue = ar.take_bf16(NFC * 1024)
    wde = ar.take_bf16(NEC * 512)
    Ht = ar.take_bf16(NEC * TT)
    gel = ar.take_f32(TT)
    cwb = ar.take_f32(TT)
    B.wait("sync", st, st.n)
    for e8 in range(9):
        shared = (e8 == 8)
        for it in range(NTT):
            ts = slice(it * TT, (it + 1) * TT)
            if not shared:
                B.wait("sync", ev, ev.n)
                B.emit("sync", lambda e, o=cwb,
                       i=cwsc_d[e8:e8 + 1, ts].broadcast_to([P, TT]):
                       e.dma_start(out=o, in_=i), sig=cb, sigk=16)
            for mhalf in range(2):
                B.wait("sync", mm, mm.n)
                if shared:
                    emit_load_bf16(B, w1_d[:, mhalf * 1024:(mhalf + 1) * 1024], wue,
                                   D, 1024, wst, None, cast)
                else:
                    emit_load_bf16(B, wu_d[e8 * D:(e8 + 1) * D, mhalf * 1024:(mhalf + 1) * 1024],
                                   wue, D, 1024, wst, None, cast)
                B.wait("tensor", cast, cast.n)
                for mh8 in range(8):
                    mh = mhalf * 8 + mh8
                    pp = pb[2 + (mh8 % 2)]
                    B.wait("tensor", sc, sc.n)
                    B.wait("tensor", ev, ev.n)
                    for c in range(NFC):
                        B.emit("tensor", lambda e, l=wue[:, c * 1024 + mh8 * P:c * 1024 + (mh8 + 1) * P],
                               r=xn_tiles[c][:, ts], o=pp[:, 0:TT], s=(c == 0), st=(c == NFC - 1):
                               e.matmul(out=o, lhsT=l, rhs=r, start=s, stop=st),
                               sig=mm if c == NFC - 1 else None)
                    B.wait("scalar", mm, mm.n)
                    B.emit("scalar", lambda e, o=gel, i=pp[:, 0:TT]:
                           e.activation(out=o, in_=i, func=AF.Gelu), sig=sc)
                    B.wait("vector", sc, sc.n)
                    if not shared:
                        B.wait("vector", cb, cb.n)
                    if shared:
                        B.emit("vector", lambda e, o=Ht[:, mh * TT:(mh + 1) * TT], i=gel:
                               e.tensor_copy(out=o, in_=i), sig=ev)
                    else:
                        B.emit("vector", lambda e, o=Ht[:, mh * TT:(mh + 1) * TT], a=gel, b=cwb:
                               e.tensor_tensor(out=o, in0=a, in1=b, op=ALU.mult), sig=ev)
            for mhalf in range(2):
                B.wait("sync", mm, mm.n)
                if shared:
                    emit_load_bf16(B, w2_d[:, mhalf * 512:(mhalf + 1) * 512], wde,
                                   DIN, 512, wst, None, cast)
                else:
                    emit_load_bf16(B, wdn_d[e8 * DIN:(e8 + 1) * DIN, mhalf * 512:(mhalf + 1) * 512],
                                   wde, DIN, 512, wst, None, cast)
                B.wait("tensor", cast, cast.n)
                for m4 in range(4):
                    m = mhalf * 4 + m4
                    pd = pb[m4 % 2]
                    B.wait("tensor", ev, ev.n)
                    for ec in range(NEC):
                        B.emit("tensor", lambda e, l=wde[:, ec * 512 + m4 * P:ec * 512 + (m4 + 1) * P],
                               r=Ht[:, ec * TT:(ec + 1) * TT], o=pd[:, 0:TT],
                               s=(ec == 0), st=(ec == NEC - 1):
                               e.matmul(out=o, lhsT=l, rhs=r, start=s, stop=st),
                               sig=mm if ec == NEC - 1 else None)
                    B.wait("vector", mm, mm.n)
                    B.emit("vector", lambda e, a=acc_tiles[m][:, ts], i=pd[:, 0:TT]:
                           e.tensor_tensor(out=a, in0=a, in1=i, op=ALU.add), sig=ev)
    ar.reset(amark)



def _t(a):
    return np.ascontiguousarray(np.asarray(a).T)



def build_qffn_kernel(dump_gact=False):
    nc = bass.Bass("TRN2", target_bir_lowering=False, debug=False)
    x_d = nc.dram_tensor("x", [D, T], F32, kind="ExternalInput").ap()
    wg_d = nc.dram_tensor("wgT", [D, HQ], I32, kind="ExternalInput").ap()
    wu_d = nc.dram_tensor("wuT", [D, HQ], I32, kind="ExternalInput").ap()
    wd_d = nc.dram_tensor("wdT", [HQ, D], I32, kind="ExternalInput").ap()
    y_d = nc.dram_tensor("y", [D, T], F32, kind="ExternalOutput").ap()
    yg_d = nc.dram_tensor("yg", [4 * P, (HQ // P) * TT], BF16, kind="ExternalOutput").ap() if dump_gact else None

    ctx = ExitStack()
    with ctx:
        B = Builder(nc, ctx)
        for s in ("mm", "ev", "sc", "wld", "wcast", "consts", "st"):
            B.sem(s)
        g = alloc_consts(B)
        ar = Arena(B, 24576)
        base = ar.mark()
        s2 = alloc_ln_scratch(B, ar)
        s2["pbS"] = ptensor(B, "pbS", [P, TT])
        s2["pb"] = [ptensor(B, f"pb{i}", [P, TT]) for i in range(6)]
        s2["pb"].append(s2["pbS"])
        s2["pbT"] = ptensor(B, "pbT", [P, 1024], BF16)
        s2["pbB1"] = s2["pb"][1]
        s2["pbB2"] = s2["pb"][2]
        bufA = sbuf(B, "bufA", [P, NFC * T], BF16)
        bufB = sbuf(B, "bufB", [P, NFC * T], BF16)
        x_tiles = [bufA[:, c * T:(c + 1) * T] for c in range(NFC)]
        xn_tiles = [bufB[:, c * T:(c + 1) * T] for c in range(NFC)]
        acc_tiles = x_tiles
        xstage = Stage(B, ar.take_f32(2 * 1024), B.sems['wcast'], 'xst')

        emit_consts(B, g, B.sems["consts"])
        ld, cast = B.sems["wld"], B.sems["wcast"]
        for c in range(NFC):
            for h in range(2):
                par, ssl = xstage.slot(1024)
                if xstage.lastc[par] > 0:
                    B.wait("sync", cast, xstage.lastc[par])
                sld = xstage.ld[par]
                B.emit("sync", lambda e, s=ssl, d=x_d[c * P:(c + 1) * P, h * 1024:(h + 1) * 1024]:
                       e.dma_start(out=s, in_=d), sig=sld, sigk=16)
                B.wait("gpsimd", sld, sld.n)
                B.emit("gpsimd", lambda e, s=ssl, o=x_tiles[c][:, h * 1024:(h + 1) * 1024]:
                       e.tensor_copy(out=o, in_=s), sig=cast)
                xstage.lastc[par] = cast.n
        for e in ("tensor", "vector", "scalar"):
            B.wait(e, cast, cast.n)
            B.wait(e, B.sems["consts"], B.sems["consts"].n)
        emit_ln(B, g, s2, x_tiles, xn_tiles, T)
        B.wait("gpsimd", B.sems["ev"], B.sems["ev"].n)
        for c in range(NFC):
            B.emit("gpsimd", lambda e, a=acc_tiles[c]: e.memset(a, 0.0),
                   sig=cast if c == NFC - 1 else None)
        for e in ("tensor", "vector"):
            B.wait(e, cast, cast.n)
        emit_qffn(B, g, s2, ar, xn_tiles, acc_tiles, wg_d, wu_d, wd_d, T, yg_d=yg_d)
        mm, ev, st = B.sems["mm"], B.sems["ev"], B.sems["st"]
        ost = ar.take_f32(1024)
        for c in range(NFC):
            for h in range(2):
                B.wait("vector", mm, mm.n)
                if (c, h) != (0, 0):
                    B.wait("vector", st, st.n)
                B.emit("vector", lambda e, i=acc_tiles[c][:, h * 1024:(h + 1) * 1024], o=ost:
                       e.tensor_copy(out=o, in_=i), sig=ev)
                B.wait("sync", ev, ev.n)
                B.emit("sync", lambda e, o=y_d[c * P:(c + 1) * P, h * 1024:(h + 1) * 1024], i=ost:
                       e.dma_start(out=o, in_=i), sig=st, sigk=16)
        B.wait("sync", st, st.n)

        with nc.Block() as block:
            B.replay(block)
    return nc


def _silu(x):
    return x / (1.0 + np.exp(-x))


def _np_ref_qffn(x, wg, wu, wd):
    xl = np.asarray(x, np.float32)
    m = xl.mean(-1, keepdims=True)
    v = xl.var(-1, keepdims=True)
    xn = (xl - m) / np.sqrt(v + EPS_LN)
    s = 1.0 / 127.0
    gate = xn @ (wg.astype(np.float32) * s).T
    up = xn @ (wu.astype(np.float32) * s).T
    h = _silu(gate) * up
    return h @ (wd.astype(np.float32) * s).T


def build_gla_kernel():
    nc = bass.Bass("TRN2", target_bir_lowering=False, debug=False)
    xn_d = nc.dram_tensor("xn", [D, T], F32, kind="ExternalInput").ap()
    wq_d = nc.dram_tensor("wq", [D, 1024], F32, kind="ExternalInput").ap()
    wk_d = nc.dram_tensor("wk", [D, 1024], F32, kind="ExternalInput").ap()
    wv_d = nc.dram_tensor("wv", [D, 1024], F32, kind="ExternalInput").ap()
    wg_d = nc.dram_tensor("wg", [D, 1024], F32, kind="ExternalInput").ap()
    gat_d = nc.dram_tensor("gattn", [D, 1], F32, kind="ExternalInput").ap()
    sinit_d = nc.dram_tensor("sinit", [P, 512], F32, kind="ExternalInput").ap()
    y_d = nc.dram_tensor("y", [D, T], F32, kind="ExternalOutput").ap()
    ys_d = nc.dram_tensor("ys", [P, 512], F32, kind="ExternalOutput").ap()

    ctx = ExitStack()
    with ctx:
        B = Builder(nc, ctx)
        for s in ("mm", "ev", "sc", "wld", "wcast", "consts", "st"):
            B.sem(s)
        g = alloc_consts(B)
        ar = Arena(B, 39500)
        s2 = alloc_ln_scratch(B, ar)
        s2["pbS"] = ptensor(B, "pbS", [P, TT])
        s2["pb"] = [ptensor(B, f"pb{i}", [P, TT]) for i in range(6)]
        s2["pb"].append(s2["pbS"])
        s2["pbT"] = ptensor(B, "pbT", [P, 1024], BF16)
        bufB = sbuf(B, "bufB", [P, NFC * T], BF16)
        xn_tiles = [bufB[:, c * T:(c + 1) * T] for c in range(NFC)]
        o_buf = ar.take_bf16(NFC * T)
        wt = ar.take_bf16(NFC * 1024)
        S32 = ar.take_f32(512)
        Sbf = ar.take_bf16(512)
        ga = alloc_gla(B, ar, T, o_buf, wt, S32, Sbf)
        ga["wstage"] = Stage(B, ar.take_f32(2 * 1024), B.sems["wcast"], "wst")
        ga["wq_d"], ga["wk_d"], ga["wv_d"], ga["wg_d"] = wq_d, wk_d, wv_d, wg_d
        gcols = sbuf(B, "gcols", [P, NFC], F32)
        xstage = Stage(B, ar.take_f32(2 * 1024), B.sems["wcast"], "xst")
        emit_consts(B, g, B.sems["consts"])
        ld, cast = B.sems["wld"], B.sems["wcast"]
        mm, ev, sc = B.sems["mm"], B.sems["ev"], B.sems["sc"]
        for c in range(NFC):
            for h in range(2):
                par, ssl = xstage.slot(1024)
                if xstage.lastc[par] > 0:
                    B.wait("sync", cast, xstage.lastc[par])
                sld = xstage.ld[par]
                B.emit("sync", lambda e, s=ssl, d=xn_d[c * P:(c + 1) * P, h * 1024:(h + 1) * 1024]:
                       e.dma_start(out=s, in_=d), sig=sld, sigk=16)
                B.wait("gpsimd", sld, sld.n)
                B.emit("gpsimd", lambda e, s=ssl, o=xn_tiles[c][:, h * 1024:(h + 1) * 1024]:
                       e.tensor_copy(out=o, in_=s), sig=cast)
                xstage.lastc[par] = cast.n
        gld = B.sem("gld")
        B.emit("sync", lambda e, o=gcols[:], i=gat_d.rearrange("(c p) w -> p (c w)", p=P):
               e.dma_start(out=o, in_=i, allow_slow_non_contiguous=True), sig=gld, sigk=16)
        B.emit("sync", lambda e, o=ga["S32"], i=sinit_d: e.dma_start(out=o, in_=i), sig=gld, sigk=16)
        B.wait("vector", gld, gld.n)
        B.emit("vector", lambda e, o=ga["Sbf"][:, 0:512], i=ga["S32"][:, 0:512]:
               e.tensor_copy(out=o, in_=i), sig=ev)
        for e in ("tensor", "vector", "scalar"):
            B.wait(e, cast, cast.n)
            B.wait(e, B.sems["consts"], B.sems["consts"].n)
            B.wait(e, gld, gld.n)
        gattn_cols = [gcols[:, c:c + 1] for c in range(NFC)]
        dump_names = []

        def dbg(name, ap, after_sc=False):
            dump_names.append(name)
            slot = len(dump_names) - 1
            if after_sc:
                B.wait("vector", B.sems["sc"], B.sems["sc"].n)
            B.wait("vector", B.sems["ev"], B.sems["ev"].n)
            B.wait("vector", B.sems["mm"], B.sems["mm"].n)
            w = ap.shape[1]
            B.emit("vector", lambda e, o=dmp_s[0:ap.shape[0], 0:w], i=ap:
                   e.tensor_copy(out=o, in_=i), sig=B.sems["ev"])
            B.wait("sync", B.sems["ev"], B.sems["ev"].n)
            B.emit("sync", lambda e, o=dmp_d[slot * P:slot * P + ap.shape[0], 0:w],
                   i=dmp_s[0:ap.shape[0], 0:w]:
                   e.dma_start(out=o, in_=i), sig=B.sems["st"], sigk=16)
            for en in ("vector", "tensor", "scalar"):
                B.wait(en, B.sems["st"], B.sems["st"].n)

        import os as _os
        if _os.environ.get("GDBG", "0") == "1":
            dmp_d = nc.dram_tensor("dmp", [24 * P, 1024], F32, kind="ExternalOutput").ap()
            dmp_s = ar.take_f32(1024)
            emit_gla(B, g, s2, ga, xn_tiles, gattn_cols, T, dbg=dbg)
            build_gla_kernel._dump_names = dump_names
        else:
            emit_gla(B, g, s2, ga, xn_tiles, gattn_cols, T)
        st = B.sems["st"]
        ost = ar.take_f32(1024)
        ob = o_buf
        for c in range(NFC):
            for h in range(2):
                B.wait("vector", ev, ev.n)
                if (c, h) != (0, 0):
                    B.wait("vector", st, st.n)
                B.emit("vector", lambda e, i=ob[:, c * T + h * 1024:c * T + (h + 1) * 1024],
                       o=ost: e.tensor_copy(out=o, in_=i), sig=ev)
                B.wait("sync", ev, ev.n)
                B.emit("sync", lambda e, o=y_d[c * P:(c + 1) * P, h * 1024:(h + 1) * 1024], i=ost:
                       e.dma_start(out=o, in_=i), sig=st, sigk=16)
        B.wait("sync", ev, ev.n)
        B.emit("sync", lambda e, o=ys_d, i=ga["S32"]: e.dma_start(out=o, in_=i), sig=st, sigk=16)
        B.wait("sync", st, st.n)
        with nc.Block() as block:
            B.replay(block)
    return nc


def _np_ref_gla(xn, gattn, Wq, Wk, Wv, Wg, S0=None):
    x = xn.astype(np.float64)
    h = x * (1.0 / np.sqrt((x * x).mean(-1, keepdims=True) + EPS_RMS0)) * gattn
    q = (h @ Wq).reshape(-1, 16, 64)
    k = (h @ Wk).reshape(-1, 16, 64)
    v = (h @ Wv).reshape(-1, 16, 64)
    al = 1.0 / (1.0 + np.exp(-(h @ Wg))).reshape(-1, 16, 64)
    qn = q / np.maximum(np.sqrt((q * q).sum(-1, keepdims=True)), 1e-12)
    kn = k / np.maximum(np.sqrt((k * k).sum(-1, keepdims=True)), 1e-12)
    Tn = x.shape[0]
    S = np.zeros((16, 64, 64)) if S0 is None else S0.astype(np.float64).copy()
    o = np.empty((Tn, 16, 64))
    for t in range(Tn):
        S = al[t][:, :, None] * S + kn[t][:, :, None] * v[t][:, None, :]
        o[t] = np.einsum("hkv,hk->hv", S, qn[t])
    return o.reshape(Tn, 1024), S


def _pack_S(S):
    out = np.zeros((P, 512), np.float32)
    for h in range(16):
        out[(h % 2) * 64:(h % 2) * 64 + 64, (h // 2) * 64:(h // 2) * 64 + 64] = S[h]
    return out


def build_l0_kernel():
    nc = bass.Bass("TRN2", target_bir_lowering=False, debug=False)
    x_d = nc.dram_tensor("x", [D, T], F32, kind="ExternalInput").ap()
    wq_d = nc.dram_tensor("wq", [D, 1024], F32, kind="ExternalInput").ap()
    wk_d = nc.dram_tensor("wk", [D, 1024], F32, kind="ExternalInput").ap()
    wv_d = nc.dram_tensor("wv", [D, 1024], F32, kind="ExternalInput").ap()
    wgg_d = nc.dram_tensor("wgg", [D, 1024], F32, kind="ExternalInput").ap()
    wo_d = nc.dram_tensor("wo", [1024, D], F32, kind="ExternalInput").ap()
    w1_d = nc.dram_tensor("w1", [D, DFF0], F32, kind="ExternalInput").ap()
    w2_d = nc.dram_tensor("w2", [D, DFF0], F32, kind="ExternalInput").ap()
    w3_d = nc.dram_tensor("w3", [DFF0, D], F32, kind="ExternalInput").ap()
    gat_d = nc.dram_tensor("gattn", [D, 1], F32, kind="ExternalInput").ap()
    gff_d = nc.dram_tensor("gffn", [D, 1], F32, kind="ExternalInput").ap()
    wgq_d = nc.dram_tensor("wgT", [D, HQ], I32, kind="ExternalInput").ap()
    wuq_d = nc.dram_tensor("wuT", [D, HQ], I32, kind="ExternalInput").ap()
    wdq_d = nc.dram_tensor("wdT", [HQ, D], I32, kind="ExternalInput").ap()
    sin_d = nc.dram_tensor("sinit", [P, 512], F32, kind="ExternalInput").ap()
    y_d = nc.dram_tensor("y", [D, T], F32, kind="ExternalOutput").ap()
    ys_d = nc.dram_tensor("ys", [P, 512], F32, kind="ExternalOutput").ap()

    ctx = ExitStack()
    with ctx:
        B = Builder(nc, ctx)
        for s in ("mm", "ev", "sc", "wld", "wcast", "consts", "st", "gld"):
            B.sem(s)
        g = alloc_consts(B)
        ar = Arena(B, 35820)
        s2 = alloc_ln_scratch(B, ar)
        s2["pbS"] = ptensor(B, "pbS", [P, TT])
        s2["pb"] = [ptensor(B, f"pb{i}", [P, TT]) for i in range(6)]
        s2["pb"].append(s2["pbS"])
        s2["pbT"] = ptensor(B, "pbT", [P, 1024], BF16)
        bufA = sbuf(B, "bufA", [P, NFC * T], BF16)
        bufB = sbuf(B, "bufB", [P, NFC * T], BF16)
        acc_tiles = [bufA[:, c * T:(c + 1) * T] for c in range(NFC)]
        xn_tiles = [bufB[:, c * T:(c + 1) * T] for c in range(NFC)]
        o_buf = ar.take_bf16(NFC * T)
        wt = ar.take_bf16(NFC * 1024)
        S32 = ar.take_f32(512)
        Sbf = ar.take_bf16(512)
        gcols = sbuf(B, "gcols", [P, 2 * NFC], F32)
        wstage = Stage(B, ar.take_f32(2 * 1024), B.sems["wcast"], "wst")
        emit_consts(B, g, B.sems["consts"])
        ld, cast, gld = B.sems["wld"], B.sems["wcast"], B.sems["gld"]
        mm, ev, sc = B.sems["mm"], B.sems["ev"], B.sems["sc"]
        st = B.sems["st"]
        mark0 = ar.mark()
        xstage = Stage(B, ar.take_f32(2 * 1024), B.sems["wcast"], "xst")
        for c in range(NFC):
            for h in range(2):
                par, ssl = xstage.slot(1024)
                if xstage.lastc[par] > 0:
                    B.wait("sync", cast, xstage.lastc[par])
                sld = xstage.ld[par]
                B.emit("sync", lambda e, s=ssl, d=x_d[c * P:(c + 1) * P, h * 1024:(h + 1) * 1024]:
                       e.dma_start(out=s, in_=d), sig=sld, sigk=16)
                B.wait("gpsimd", sld, sld.n)
                B.emit("gpsimd", lambda e, s=ssl, o=acc_tiles[c][:, h * 1024:(h + 1) * 1024]:
                       e.tensor_copy(out=o, in_=s), sig=cast)
                xstage.lastc[par] = cast.n
        B.emit("sync", lambda e, o=gcols[:, 0:NFC], i=gat_d.rearrange("(c p) w -> p (c w)", p=P):
               e.dma_start(out=o, in_=i, allow_slow_non_contiguous=True), sig=gld, sigk=16)
        B.emit("sync", lambda e, o=gcols[:, NFC:2 * NFC], i=gff_d.rearrange("(c p) w -> p (c w)", p=P):
               e.dma_start(out=o, in_=i, allow_slow_non_contiguous=True), sig=gld, sigk=16)
        B.emit("sync", lambda e, o=S32, i=sin_d: e.dma_start(out=o, in_=i), sig=gld, sigk=16)
        B.wait("vector", gld, gld.n)
        B.emit("vector", lambda e, o=Sbf[:, 0:512], i=S32[:, 0:512]:
               e.tensor_copy(out=o, in_=i), sig=ev)
        for e in ("tensor", "vector", "scalar"):
            B.wait(e, cast, cast.n)
            B.wait(e, B.sems["consts"], B.sems["consts"].n)
            B.wait(e, gld, gld.n)
        emit_ln(B, g, s2, acc_tiles, xn_tiles, T)
        ga = alloc_gla(B, ar, T, o_buf, wt, S32, Sbf)
        ga["wstage"] = wstage
        ga["wq_d"], ga["wk_d"], ga["wv_d"], ga["wg_d"] = wq_d, wk_d, wv_d, wgg_d
        gattn_cols = [gcols[:, c:c + 1] for c in range(NFC)]
        gffn_cols = [gcols[:, NFC + c:NFC + c + 1] for c in range(NFC)]
        emit_gla(B, g, s2, ga, xn_tiles, gattn_cols, T)
        ar.reset(mark0)
        emit_wo_swiglu(B, g, s2, ar, xn_tiles, acc_tiles, o_buf, wo_d, w1_d, w2_d, w3_d,
                       gffn_cols, T, wt, wstage)
        import os as _os
        if _os.environ.get("L0DBG", "0") == "1":
            ymid_d = nc.dram_tensor("ymid", [D, T], F32, kind="ExternalOutput").ap()
            omark = ar.mark()
            ost0 = ar.take_f32(1024)
            for c in range(NFC):
                for h in range(2):
                    B.wait("vector", mm, mm.n)
                    B.wait("vector", ev, ev.n)
                    B.wait("vector", st, st.n)
                    B.emit("vector", lambda e, i=acc_tiles[c][:, h * 1024:(h + 1) * 1024], o=ost0:
                           e.tensor_copy(out=o, in_=i), sig=ev)
                    B.wait("sync", ev, ev.n)
                    B.emit("sync", lambda e, o=ymid_d[c * P:(c + 1) * P, h * 1024:(h + 1) * 1024],
                           i=ost0: e.dma_start(out=o, in_=i), sig=st, sigk=16)
            for en in ("tensor", "vector", "scalar", "gpsimd"):
                B.wait(en, st, st.n)
            ar.reset(omark)
        emit_qffn(B, g, s2, ar, xn_tiles, acc_tiles, wgq_d, wuq_d, wdq_d, T)
        amark2 = ar.mark()
        ost = ar.take_f32(1024)
        for c in range(NFC):
            for h in range(2):
                B.wait("vector", mm, mm.n)
                B.wait("vector", ev, ev.n)
                if (c, h) != (0, 0):
                    B.wait("vector", st, st.n)
                B.emit("vector", lambda e, i=acc_tiles[c][:, h * 1024:(h + 1) * 1024], o=ost:
                       e.tensor_copy(out=o, in_=i), sig=ev)
                B.wait("sync", ev, ev.n)
                B.emit("sync", lambda e, o=y_d[c * P:(c + 1) * P, h * 1024:(h + 1) * 1024], i=ost:
                       e.dma_start(out=o, in_=i), sig=st, sigk=16)
        B.wait("sync", ev, ev.n)
        B.emit("sync", lambda e, o=ys_d, i=S32: e.dma_start(out=o, in_=i), sig=st, sigk=16)
        B.wait("sync", st, st.n)
        ar.reset(amark2)
        with nc.Block() as block:
            B.replay(block)
    return nc


def _np_ref_l0(x, W, S0=None):
    xl = x.astype(np.float64)
    m = xl.mean(-1, keepdims=True); v = xl.var(-1, keepdims=True)
    xn = (xl - m) / np.sqrt(v + EPS_LN)
    o, Sf = _np_ref_gla(xn.astype(np.float32), W["g_attn"], W["Wq"], W["Wk"], W["Wv"], W["Wg"], S0)
    att = o @ W["Wo"]
    xm = xn + att
    h2 = xm * (1.0 / np.sqrt((xm * xm).mean(-1, keepdims=True) + EPS_RMS0)) * W["g_ffn"]
    ffn = (_silu(h2 @ W["w1"]) * (h2 @ W["w2"])) @ W["w3"]
    attn_out = xm + ffn
    s = 1.0 / 127.0
    gate = xn @ (W["w_gate"].astype(np.float64) * s).T
    up = xn @ (W["w_up"].astype(np.float64) * s).T
    qf = (_silu(gate) * up) @ (W["w_down"].astype(np.float64) * s).T
    return xl + attn_out + qf, Sf

def _np_ref_l1(resid, W, halo=None, h0=None):
    x = resid.astype(np.float64)
    m = x.mean(-1, keepdims=True); v = x.var(-1, keepdims=True)
    xn = (x - m) / np.sqrt(v + EPS_LN)
    xr = xn * (1.0 / np.sqrt((xn * xn).mean(-1, keepdims=True) + EPS_RMS1)) * W["norm_w"]
    skip = xr @ W["skip_w"].T
    x1 = xr @ W["in_w"].T
    if halo is None:
        halo_x1 = np.zeros((3, DIN))
    else:
        hh = halo.astype(np.float64)
        mh = hh.mean(-1, keepdims=True); vh = hh.var(-1, keepdims=True)
        hn = (hh - mh) / np.sqrt(vh + EPS_LN)
        hr = hn * (1.0 / np.sqrt((hn * hn).mean(-1, keepdims=True) + EPS_RMS1)) * W["norm_w"]
        halo_x1 = hr @ W["in_w"].T
    xp = np.concatenate([halo_x1, x1], 0)
    conv = W["conv_b"] + sum(xp[k:k + x1.shape[0]] * W["conv_w"][:, k] for k in range(4))
    xs = conv / (1.0 + np.exp(-conv))
    A = -np.exp(W["A_log"].astype(np.float64))
    dcore = xs @ W["wd1"].T
    dl = dcore @ W["wd2"].T + W["wd2_b"]
    delta = np.maximum(dl, 0) + np.log1p(np.exp(-np.abs(dl)))
    Bm = xs @ W["wb"].T
    Cm = xs @ W["wc"].T
    h = np.zeros((DIN, NS)) if h0 is None else h0.astype(np.float64).copy()
    y = np.empty((x.shape[0], DIN))
    for t in range(x.shape[0]):
        h = np.exp(delta[t][:, None] * A) * h + (delta[t] * xs[t])[:, None] * Bm[t][None, :]
        y[t] = h @ Cm[t]
    y = y + xs * W["W_D"]
    y = y * (skip / (1.0 + np.exp(-skip)))
    att = y @ W["out_w"].T
    probs = np.exp(xn @ W["Wg"] - (xn @ W["Wg"]).max(-1, keepdims=True))
    probs = probs / probs.sum(-1, keepdims=True)
    top2 = np.argsort(-probs, axis=-1)[:, :2]
    cw = np.zeros_like(probs)
    for t in range(probs.shape[0]):
        tv = probs[t, top2[t]]
        cw[t, top2[t]] = tv / tv.sum()
    def gelu(z):
        return 0.5 * z * (1.0 + _erf_np(z / np.sqrt(2.0)))
    moe = np.zeros((x.shape[0], D))
    for e in range(8):
        He = gelu(xn @ W["Wu"][e])
        moe += cw[:, e:e + 1] * (He @ W["Wd"][e])
    shared = gelu(xn @ W["W1"]) @ W["W2"]
    return x + att + moe + shared, h


def _erf_np(z):
    import math
    return np.vectorize(math.erf)(z)






def emit_bf16_prologue(B, ar, wstage, pairs, osem):
    cast = wstage.cast
    obuf = [ar.take_bf16(2048), ar.take_bf16(2048)]
    lastw = [0, 0]
    i = [0]
    for (srcw, dstw) in pairs:
        K, M = srcw.shape
        for c in range((K + P - 1) // P):
            pk = min(P, K - c * P)
            for m0 in range(0, M, 1024):
                mw = min(1024, M - m0)
                par, ssl = wstage.slot(mw, pk)
                if srcw.dtype == I32:
                    ssl = ssl.bitcast(I32)
                if wstage.lastc[par] > 0:
                    B.wait("sync", cast, wstage.lastc[par])
                sld = wstage.ld[par]
                B.emit("sync", lambda e, s=ssl, d=srcw[c * P:c * P + pk, m0:m0 + mw]:
                       e.dma_start(out=s, in_=d), sig=sld, sigk=16)
                po = i[0] % 2
                i[0] += 1
                B.wait("gpsimd", sld, sld.n)
                if lastw[po] > 0:
                    B.wait("gpsimd", osem[po], lastw[po])
                B.emit("gpsimd", lambda e, s=ssl, o=obuf[po][0:pk, 0:mw]:
                       e.tensor_copy(out=o, in_=s), sig=cast)
                wstage.lastc[par] = cast.n
                B.wait("sync", cast, cast.n)
                B.emit("sync", lambda e, o=dstw[c * P:c * P + pk, m0:m0 + mw],
                       i_=obuf[po][0:pk, 0:mw]: e.dma_start(out=o, in_=i_),
                       sig=osem[po], sigk=16)
                lastw[po] = osem[po].n
    B.wait("sync", osem[0], osem[0].n)
    B.wait("sync", osem[1], osem[1].n)


def build_full_kernel():
    nc = bass.Bass("TRN2", target_bir_lowering=False, debug=False)
    xf_d = nc.dram_tensor("xfirst", [D, T], F32, kind="ExternalInput").ap()
    xo_d = nc.dram_tensor("xown", [D, T], F32, kind="ExternalInput").ap()
    sel_d = nc.dram_tensor("sel", [P, 1], F32, kind="ExternalInput").ap()
    wq_d = nc.dram_tensor("wq", [D, 1024], F32, kind="ExternalInput").ap()
    wk_d = nc.dram_tensor("wk", [D, 1024], F32, kind="ExternalInput").ap()
    wv_d = nc.dram_tensor("wv", [D, 1024], F32, kind="ExternalInput").ap()
    wgg_d = nc.dram_tensor("wgg", [D, 1024], F32, kind="ExternalInput").ap()
    wo_d = nc.dram_tensor("wo", [1024, D], F32, kind="ExternalInput").ap()
    w1_d = nc.dram_tensor("w1", [D, DFF0], F32, kind="ExternalInput").ap()
    w2_d = nc.dram_tensor("w2", [D, DFF0], F32, kind="ExternalInput").ap()
    w3_d = nc.dram_tensor("w3", [DFF0, D], F32, kind="ExternalInput").ap()
    gat_d = nc.dram_tensor("gattn", [D, 1], F32, kind="ExternalInput").ap()
    gff_d = nc.dram_tensor("gffn", [D, 1], F32, kind="ExternalInput").ap()
    wgq_d = nc.dram_tensor("wgT", [D, HQ], I32, kind="ExternalInput").ap()
    wuq_d = nc.dram_tensor("wuT", [D, HQ], I32, kind="ExternalInput").ap()
    wdq_d = nc.dram_tensor("wdT", [HQ, D], I32, kind="ExternalInput").ap()
    nw_d = nc.dram_tensor("normw", [D, 1], F32, kind="ExternalInput").ap()
    skT_d = nc.dram_tensor("skipT", [D, DIN], F32, kind="ExternalInput").ap()
    inT_d = nc.dram_tensor("inT", [D, DIN], F32, kind="ExternalInput").ap()
    cw_d = nc.dram_tensor("convw", [DIN, 4], F32, kind="ExternalInput").ap()
    cb_d = nc.dram_tensor("convb", [DIN, 1], F32, kind="ExternalInput").ap()
    wd1T_d = nc.dram_tensor("wd1T", [DIN, 64], F32, kind="ExternalInput").ap()
    wd2T_d = nc.dram_tensor("wd2T", [64, DIN], F32, kind="ExternalInput").ap()
    wd2b_d = nc.dram_tensor("wd2b", [DIN, 1], F32, kind="ExternalInput").ap()
    wbT_d = nc.dram_tensor("wbT", [DIN, NS], F32, kind="ExternalInput").ap()
    wcT_d = nc.dram_tensor("wcT", [DIN, NS], F32, kind="ExternalInput").ap()
    al_d = nc.dram_tensor("alog", [DIN, NS], F32, kind="ExternalInput").ap()
    wdd_d = nc.dram_tensor("wD", [DIN, 1], F32, kind="ExternalInput").ap()
    outT_d = nc.dram_tensor("outT", [DIN, D], F32, kind="ExternalInput").ap()
    mWg_d = nc.dram_tensor("mWg", [D, 8], F32, kind="ExternalInput").ap()
    mWu_d = nc.dram_tensor("mWu", [8 * D, DIN], F32, kind="ExternalInput").ap()
    mWd_d = nc.dram_tensor("mWd", [8 * DIN, D], F32, kind="ExternalInput").ap()
    mW1_d = nc.dram_tensor("mW1", [D, DIN], F32, kind="ExternalInput").ap()
    mW2_d = nc.dram_tensor("mW2", [DIN, D], F32, kind="ExternalInput").ap()
    y_d = nc.dram_tensor("y", [D, T], F32, kind="ExternalOutput").ap()
    wqb = nc.dram_tensor("wqb", [D, 1024], BF16, kind="Internal").ap()
    wkb = nc.dram_tensor("wkb", [D, 1024], BF16, kind="Internal").ap()
    wvb = nc.dram_tensor("wvb", [D, 1024], BF16, kind="Internal").ap()
    wggb = nc.dram_tensor("wggb", [D, 1024], BF16, kind="Internal").ap()
    wob = nc.dram_tensor("wob", [1024, D], BF16, kind="Internal").ap()
    w1b = nc.dram_tensor("w1b", [D, DFF0], BF16, kind="Internal").ap()
    w2b = nc.dram_tensor("w2b", [D, DFF0], BF16, kind="Internal").ap()
    w3b = nc.dram_tensor("w3b", [DFF0, D], BF16, kind="Internal").ap()
    wgqb = nc.dram_tensor("wgqb", [D, HQ], BF16, kind="Internal").ap()
    wuqb = nc.dram_tensor("wuqb", [D, HQ], BF16, kind="Internal").ap()
    wdqb = nc.dram_tensor("wdqb", [HQ, D], BF16, kind="Internal").ap()
    bsc_d = nc.dram_tensor("bsc", [NS, T], F32, kind="Internal").ap()
    csc_d = nc.dram_tensor("csc", [NS, T], F32, kind="Internal").ap()
    cwsc_d = nc.dram_tensor("cwsc", [8, T], F32, kind="Internal").ap()

    SLL = 256
    ctx = ExitStack()
    with ctx:
        B = Builder(nc, ctx)
        for s in ("mm", "ev", "sc", "wld", "wcast", "consts", "st", "gld", "cwdma"):
            B.sem(s)
        bdma = [B.sem("bdma0"), B.sem("bdma1")]
        g = alloc_consts(B)
        ar = Arena(B, 33700)
        s2 = {}
        s2["pbS"] = ptensor(B, "pbS", [P, TT])
        s2["pb"] = [ptensor(B, f"pb{i}", [P, TT]) for i in range(6)]
        s2["pb"].append(s2["pbS"])
        s2["pbT"] = ptensor(B, "pbT", [P, 1024], BF16)
        pb = s2["pb"]
        bufA = sbuf(B, "bufA", [P, NFC * T], BF16)
        bufB = sbuf(B, "bufB", [P, NFC * T], BF16)
        acc_tiles = [bufA[:, c * T:(c + 1) * T] for c in range(NFC)]
        xn_tiles = [bufB[:, c * T:(c + 1) * T] for c in range(NFC)]
        smalls = sbuf(B, "smalls", [P, 480], F32)
        off = [0]
        def sm(n):
            a = smalls[:, off[0]:off[0] + n]
            off[0] += n
            return a
        nwcols = sm(8); convw = sm(64); convb = sm(16); wd2b = sm(16)
        wDc = sm(16); Acols = sm(256); x1h = sm(48); x1hk = sm(48); selc = sm(1)
        hbuf = sbuf(B, "hbuf", [P, NEC * NS], F32)
        S32d = sbuf(B, "S32d", [P, 512], F32)
        Sbfd = sbuf(B, "Sbfd", [P, 512], BF16)
        gcols = sbuf(B, "gcols", [P, 2 * NFC], F32)
        gattn_cols = [gcols[:, c:c + 1] for c in range(NFC)]
        gffn_cols = [gcols[:, NFC + c:NFC + c + 1] for c in range(NFC)]

        ld, cast, gld = B.sems["wld"], B.sems["wcast"], B.sems["gld"]
        mm, ev, sc = B.sems["mm"], B.sems["ev"], B.sems["sc"]
        st = B.sems["st"]
        wstage = Stage(B, ar.take_f32(2 * 1024), cast, "wst")
        mark0 = ar.mark()

        emit_consts(B, g, B.sems["consts"])
        B.emit("gpsimd", lambda e: e.memset(S32d[:, :], 0.0))
        B.emit("gpsimd", lambda e: e.memset(Sbfd[:, :], 0.0))
        B.emit("gpsimd", lambda e: e.memset(hbuf[:, :], 0.0))
        B.emit("gpsimd", lambda e: e.memset(x1h[:, :], 0.0), sig=B.sems["consts"])

        def ldsmall(dst, dram, rearr=True):
            if rearr:
                w = dram.shape[1]
                src_ap = dram.rearrange("(c p) w -> p c w", p=P)
                dstv = dst.rearrange("p (c w) -> p c w", w=w)
            else:
                src_ap, dstv = dram, dst
            B.emit("sync", lambda e, o=dstv, i=src_ap:
                   e.dma_start(out=o, in_=i, allow_slow_non_contiguous=True), sig=gld, sigk=16)
        ldsmall(nwcols, nw_d); ldsmall(convw, cw_d); ldsmall(convb, cb_d)
        ldsmall(wd2b, wd2b_d); ldsmall(wDc, wdd_d); ldsmall(Acols, al_d)
        ldsmall(selc, sel_d, rearr=False)
        B.emit("sync", lambda e, o=gcols[:, 0:NFC], i=gat_d.rearrange("(c p) w -> p (c w)", p=P):
               e.dma_start(out=o, in_=i, allow_slow_non_contiguous=True), sig=gld, sigk=16)
        B.emit("sync", lambda e, o=gcols[:, NFC:2 * NFC], i=gff_d.rearrange("(c p) w -> p (c w)", p=P):
               e.dma_start(out=o, in_=i, allow_slow_non_contiguous=True), sig=gld, sigk=16)
        B.wait("scalar", gld, gld.n)
        B.emit("scalar", lambda e, o=Acols, i=Acols:
               e.activation(out=o, in_=i, func=AF.Exp), sig=sc)
        B.wait("vector", sc, sc.n)
        B.emit("vector", lambda e, o=Acols:
               e.tensor_scalar(out=o, in0=Acols, scalar1=-1.0, scalar2=None, op0=ALU.mult), sig=ev)

        def l0_pass(x_d):
            barrier_all(B)
            ar.reset(mark0)
            s2.update(alloc_ln_scratch(B, ar))
            o_buf = ar.take_bf16(NFC * T)
            wt = ar.take_bf16(NFC * 1024)
            markL0 = ar.mark()
            for c in range(NFC):
                for h in range(2):
                    par, ssl = wstage.slot(1024)
                    if wstage.lastc[par] > 0:
                        B.wait("sync", cast, wstage.lastc[par])
                    sld = wstage.ld[par]
                    B.emit("sync", lambda e, s=ssl, d=x_d[c * P:(c + 1) * P, h * 1024:(h + 1) * 1024]:
                           e.dma_start(out=s, in_=d), sig=sld, sigk=16)
                    B.wait("gpsimd", sld, sld.n)
                    B.emit("gpsimd", lambda e, s=ssl, o=acc_tiles[c][:, h * 1024:(h + 1) * 1024]:
                           e.tensor_copy(out=o, in_=s), sig=cast)
                    wstage.lastc[par] = cast.n
            for e in ("tensor", "vector", "scalar"):
                B.wait(e, cast, cast.n)
                B.wait(e, B.sems["consts"], B.sems["consts"].n)
                B.wait(e, gld, gld.n)
                B.wait(e, ev, ev.n)
            emit_ln(B, g, s2, acc_tiles, xn_tiles, T)
            ga = alloc_gla(B, ar, T, o_buf, wt, S32d, Sbfd)
            ga["wstage"] = wstage
            ga["wq_d"], ga["wk_d"], ga["wv_d"], ga["wg_d"] = wqb, wkb, wvb, wggb
            emit_gla(B, g, s2, ga, xn_tiles, gattn_cols, T)
            ar.reset(markL0)
            emit_wo_swiglu(B, g, s2, ar, xn_tiles, acc_tiles, o_buf, wob, w1b, w2b, w3b,
                           gffn_cols, T, wt, wstage)
            emit_qffn(B, g, s2, ar, xn_tiles, acc_tiles, wgqb, wuqb, wdqb, T)

        def l1_pass(stateonly):
            barrier_all(B)
            ar.reset(mark0)
            dcore = ar.take_bf16(T)
            bcstg = ar.take_f32(TT, parts=NS)
            rms_bc = ar.take_f32(T)
            xs = ar.take_bf16_top(NEC * T)
            xs_ec = [xs[:, e * T:(e + 1) * T] for e in range(NEC)]
            markL1 = ar.mark()
            s2.update(alloc_ln_scratch(B, ar))
            emit_ln(B, g, s2, acc_tiles, xn_tiles, T)
            for it in range(NTT):
                ts = slice(it * TT, (it + 1) * TT)
                B.wait("scalar", mm, mm.n)
                for c in range(NFC):
                    B.emit("scalar", lambda e, i=xn_tiles[c][:, ts], o=s2["sq8"][:, c * TT:(c + 1) * TT]:
                           e.activation(out=o, in_=i, func=AF.Square), sig=sc if c == NFC - 1 else None)
                B.wait("tensor", sc, sc.n)
                B.wait("tensor", ev, ev.n)
                for c in range(NFC):
                    B.emit("tensor", lambda e, r=s2["sq8"][:, c * TT:(c + 1) * TT], o=pb[6][0:1, 0:TT],
                           l=g["ones_col"][:], s=(c == 0), st=(c == NFC - 1):
                           e.matmul(out=o, lhsT=l, rhs=r, start=s, stop=st),
                           sig=mm if c == NFC - 1 else None)
                B.wait("scalar", mm, mm.n)
                B.emit("scalar", lambda e, o=s2["arow"][0:1, 0:TT], i=pb[6][0:1, 0:TT]:
                       e.activation(out=o, in_=i, func=AF.Sqrt, scale=1.0 / D,
                                    bias=cval1(g, float(EPS_RMS1))), sig=sc)
                B.wait("vector", sc, sc.n)
                B.emit("vector", lambda e, o=s2["arow"][0:1, 0:TT]:
                       e.reciprocal(out=o, in_=s2["arow"][0:1, 0:TT]), sig=ev)
                B.wait("tensor", ev, ev.n)
                B.emit("tensor", lambda e, r=s2["arow"][0:1, 0:TT], o=pb[4][:, 0:TT],
                       l=g["ones_row"][:]:
                       e.matmul(out=o, lhsT=l, rhs=r, start=True, stop=True), sig=mm)
                B.wait("vector", mm, mm.n)
                B.emit("vector", lambda e, o=rms_bc[:, ts], i=pb[4][:, 0:TT]:
                       e.tensor_copy(out=o, in_=i), sig=ev)
            wcol = [ar.take_bf16(NFC * P), ar.take_bf16(NFC * P)]
            x1e = ar.take_bf16(2056)
            cvt = ar.take_f32(T)
            for ec in range(NEC):
                par = ec % 2
                B.wait("sync", mm, mm.n)
                emit_load_bf16(B, inT_d[:, ec * P:(ec + 1) * P], wcol[par], D, P, wstage, None, cast,
                               mul_cols=nwcols)
                B.wait("tensor", cast, cast.n)
                B.wait("tensor", sc, sc.n)
                B.wait("tensor", ev, ev.n)
                B.emit("vector", lambda e, o=x1e[:, 0:3], i=x1h[:, ec * 3:(ec + 1) * 3]:
                       e.tensor_copy(out=o, in_=i), sig=ev)
                for it in range(NTT):
                    pp = pb[it % 2]
                    ts = slice(it * TT, (it + 1) * TT)
                    for c in range(NFC):
                        B.emit("tensor", lambda e, l=wcol[par][:, c * P:(c + 1) * P],
                               r=xn_tiles[c][:, ts], o=pp[:, 0:TT], s=(c == 0), st=(c == NFC - 1):
                               e.matmul(out=o, lhsT=l, rhs=r, start=s, stop=st),
                               sig=mm if c == NFC - 1 else None)
                    B.wait("vector", mm, mm.n)
                    B.emit("vector", lambda e, o=x1e[:, 3 + it * TT:3 + (it + 1) * TT], a=pp[:, 0:TT],
                           b=rms_bc[:, ts]: e.tensor_tensor(out=o, in0=a, in1=b, op=ALU.mult), sig=ev)
                B.emit("vector", lambda e, o=x1hk[:, ec * 3:(ec + 1) * 3],
                       i=x1e[:, 3 + T - 3:3 + T]: e.tensor_copy(out=o, in_=i), sig=ev)
                B.emit("vector", lambda e, o=cvt, i=x1e[:, 0:T], w0=convw[:, ec * 4:ec * 4 + 1],
                       b=convb[:, ec:ec + 1]:
                       e.tensor_scalar(out=o, in0=i, scalar1=w0, scalar2=b,
                                       op0=ALU.mult, op1=ALU.add))
                for k in (1, 2, 3):
                    B.emit("vector", lambda e, o=cvt, i=x1e[:, k:k + T],
                           w=convw[:, ec * 4 + k:ec * 4 + k + 1]:
                           e.scalar_tensor_tensor(out=o, in0=i, scalar=w, in1=o,
                                                  op0=ALU.mult, op1=ALU.add),
                           sig=ev if k == 3 else None)
                B.wait("scalar", ev, ev.n)
                B.emit("scalar", lambda e, o=xs_ec[ec][:, 0:T], i=cvt:
                       e.activation(out=o, in_=i, func=AF.Silu), sig=sc)
            phase_fence(B)
            ar.reset(markL1)
            wd1t = ar.take_bf16(NEC * 64)
            wbct = ar.take_bf16(2 * NEC * NS)
            emit_load_bf16(B, wd1T_d, wd1t, DIN, 64, wstage, None, cast)
            emit_load_bf16(B, wbT_d, wbct, DIN, NS, wstage, None, cast, dst_stride=2 * NS, dst_off=0)
            emit_load_bf16(B, wcT_d, wbct, DIN, NS, wstage, None, cast, dst_stride=2 * NS, dst_off=NS)
            B.wait("tensor", cast, cast.n)
            B.wait("tensor", sc, sc.n)
            for it in range(NTT):
                ts = slice(it * TT, (it + 1) * TT)
                B.wait("tensor", ev, ev.n)
                for ec in range(NEC):
                    B.emit("tensor", lambda e, l=wd1t[:, ec * 64:(ec + 1) * 64],
                           r=xs_ec[ec][:, ts], o=pb[0][0:64, 0:TT],
                           s=(ec == 0), st=(ec == NEC - 1):
                           e.matmul(out=o, lhsT=l, rhs=r, start=s, stop=st))
                for ec in range(NEC):
                    B.emit("tensor", lambda e, l=wbct[:, ec * 2 * NS:ec * 2 * NS + NS],
                           r=xs_ec[ec][:, ts], o=pb[1][0:NS, 0:TT],
                           s=(ec == 0), st=(ec == NEC - 1):
                           e.matmul(out=o, lhsT=l, rhs=r, start=s, stop=st))
                for ec in range(NEC):
                    B.emit("tensor", lambda e, l=wbct[:, ec * 2 * NS + NS:(ec + 1) * 2 * NS],
                           r=xs_ec[ec][:, ts], o=pb[2][0:NS, 0:TT],
                           s=(ec == 0), st=(ec == NEC - 1):
                           e.matmul(out=o, lhsT=l, rhs=r, start=s, stop=st),
                           sig=mm if ec == NEC - 1 else None)
                B.wait("vector", mm, mm.n)
                B.emit("vector", lambda e, o=dcore[0:64, ts], i=pb[0][0:64, 0:TT]:
                       e.tensor_copy(out=o, in_=i), sig=ev)
                B.wait("vector", st, st.n)
                B.emit("vector", lambda e, o=bcstg[0:NS, 0:TT], i=pb[1][0:NS, 0:TT]:
                       e.tensor_copy(out=o, in_=i), sig=ev)
                B.wait("sync", ev, ev.n)
                B.emit("sync", lambda e, o=bsc_d[0:NS, ts], i=bcstg[0:NS, 0:TT]:
                       e.dma_start(out=o, in_=i), sig=st, sigk=16)
                if not stateonly:
                    B.wait("vector", st, st.n)
                    B.emit("vector", lambda e, o=bcstg[0:NS, 0:TT], i=pb[2][0:NS, 0:TT]:
                           e.tensor_copy(out=o, in_=i), sig=ev)
                    B.wait("sync", ev, ev.n)
                    B.emit("sync", lambda e, o=csc_d[0:NS, ts], i=bcstg[0:NS, 0:TT]:
                           e.dma_start(out=o, in_=i), sig=st, sigk=16)
            phase_fence(B)
            ar.reset(markL1)
            SCL = 128
            wd2call = ar.take_bf16(NEC * P)
            ab = ar.take_f32(NS * SCL)
            b_in = ar.take_bf16(NS * SCL)
            Bbc = ar.take_f32(NS * SCL)
            Cbc = ar.take_f32(NS * SCL)
            delta = ar.take_f32(SCL)
            useg = ar.take_f32(SCL)
            ysl = ar.take_f32(SCL)
            softs = ar.take_f32(2 * SCL)
            for ec in range(NEC):
                emit_load_bf16(B, wd2T_d[:, ec * P:(ec + 1) * P], wd2call, 64, P, wstage,
                               None, cast, dst_stride=0, dst_off=ec * P)
            B.wait("sync", st, st.n)
            B.wait("tensor", cast, cast.n)
            for isl in range(T // SCL):
                ts = slice(isl * SCL, (isl + 1) * SCL)
                B.wait("sync", ev, ev.n)
                B.emit("sync", lambda e, o=Bbc[:, 0:NS * SCL].rearrange("p (n t) -> p n t", n=NS),
                       i=bsc_d.rearrange("(o n) t -> o n t", o=1)[0:1, 0:NS, ts].broadcast_to([P, NS, SCL]):
                       e.dma_start(out=o, in_=i), sig=bdma[0], sigk=16)
                if not stateonly:
                    B.emit("sync", lambda e, o=Cbc[:, 0:NS * SCL].rearrange("p (n t) -> p n t", n=NS),
                           i=csc_d.rearrange("(o n) t -> o n t", o=1)[0:1, 0:NS, ts].broadcast_to([P, NS, SCL]):
                           e.dma_start(out=o, in_=i), sig=bdma[1], sigk=16)
                B.wait("vector", bdma[0], bdma[0].n)
                if not stateonly:
                    B.wait("vector", bdma[1], bdma[1].n)
                for ec in range(NEC):
                    B.wait("tensor", sc, sc.n)
                    B.wait("tensor", ev, ev.n)
                    B.emit("tensor", lambda e, l=wd2call[0:64, ec * P:(ec + 1) * P],
                           r=dcore[0:64, ts], o=pb[5][:, 0:SCL]:
                           e.matmul(out=o, lhsT=l, rhs=r, start=True, stop=True), sig=mm)
                    B.wait("vector", mm, mm.n)
                    B.emit("vector", lambda e, o=delta, i=pb[5][:, 0:SCL], b=wd2b[:, ec:ec + 1]:
                           e.tensor_scalar(out=o, in0=i, scalar1=b, scalar2=None, op0=ALU.add), sig=ev)
                    B.wait("scalar", ev, ev.n)
                    B.emit("scalar", lambda e, o=softs[:, 0:SCL], i=delta:
                           e.activation(out=o, in_=i, func=AF.Abs))
                    B.emit("scalar", lambda e, o=softs[:, SCL:2 * SCL], i=softs[:, 0:SCL]:
                           e.activation(out=o, in_=i, func=AF.Exp, scale=-1.0))
                    B.emit("scalar", lambda e, o=softs[:, 0:SCL], i=softs[:, SCL:2 * SCL]:
                           e.activation(out=o, in_=i, func=AF.Ln, bias=cval(g, 1.0)), sig=sc)
                    B.wait("vector", sc, sc.n)
                    B.emit("vector", lambda e, o=delta:
                           e.tensor_scalar(out=o, in0=delta, scalar1=0.0, scalar2=None, op0=ALU.max), sig=ev)
                    B.wait("vector", ev, ev.n)
                    B.emit("vector", lambda e, o=delta, t=softs[:, 0:SCL]:
                           e.tensor_tensor(out=o, in0=o, in1=t, op=ALU.add), sig=ev)
                    B.wait("vector", ev, ev.n)
                    B.emit("vector", lambda e, o=useg, a=delta, b=xs_ec[ec][:, ts]:
                           e.tensor_tensor(out=o, in0=a, in1=b, op=ALU.mult), sig=ev)
                    B.wait("scalar", ev, ev.n)
                    for n in range(NS):
                        B.emit("scalar", lambda e, o=ab[:, n * SCL:(n + 1) * SCL], i=delta,
                               a=Acols[:, ec * NS + n:ec * NS + n + 1]:
                               e.activation(out=o, in_=i, func=AF.Exp, scale=a),
                               sig=sc if n == NS - 1 else None)
                    B.wait("vector", sc, sc.n)
                    for n in range(NS):
                        B.emit("vector", lambda e,
                               o=b_in[:, n * SCL:(n + 1) * SCL],
                               a=useg[:, 0:SCL], bb=Bbc[:, n * SCL:(n + 1) * SCL]:
                               e.tensor_tensor(out=o, in0=a, in1=bb, op=ALU.mult),
                               sig=ev if n == NS - 1 else None)
                    B.wait("vector", ev, ev.n)
                    for n in range(NS):
                        B.emit("vector", lambda e, o=ab[:, n * SCL:(n + 1) * SCL],
                               d0=ab[:, n * SCL:(n + 1) * SCL], d1=b_in[:, n * SCL:(n + 1) * SCL],
                               ini=hbuf[:, ec * NS + n:ec * NS + n + 1]:
                               e.tensor_tensor_scan(out=o, data0=d0, data1=d1, initial=ini,
                                                    op0=ALU.mult, op1=ALU.add),
                               sig=ev if n == NS - 1 else None)
                    B.wait("vector", ev, ev.n)
                    B.emit("vector", lambda e, o=hbuf[:, ec * NS:(ec + 1) * NS],
                           i=ab.rearrange("p (n t) -> p n t", n=NS)[:, :, SCL - 1]:
                           e.tensor_copy(out=o, in_=i), sig=ev)
                    if stateonly:
                        continue
                    B.wait("vector", ev, ev.n)
                    for n in range(NS):
                        B.emit("vector", lambda e, o=ab[:, n * SCL:(n + 1) * SCL],
                               cc=Cbc[:, n * SCL:(n + 1) * SCL]:
                               e.tensor_tensor(out=o, in0=o, in1=cc, op=ALU.mult),
                               sig=ev if n == NS - 1 else None)
                    B.wait("vector", ev, ev.n)
                    nblk = NS
                    while nblk > 1:
                        half = nblk // 2
                        B.emit("vector", lambda e, o=ab[:, 0:half * SCL],
                               i=ab[:, half * SCL:nblk * SCL]:
                               e.tensor_tensor(out=o, in0=o, in1=i, op=ALU.add), sig=ev)
                        B.wait("vector", ev, ev.n)
                        nblk = half
                    B.emit("vector", lambda e, o=xs_ec[ec][:, ts], i=xs_ec[ec][:, ts],
                           w=wDc[:, ec:ec + 1], y=ab[:, 0:SCL]:
                           e.scalar_tensor_tensor(out=o, in0=i, scalar=w, in1=y,
                                                  op0=ALU.mult, op1=ALU.add), sig=ev)
            if stateonly:
                ar.free_top((NEC * T + 1) // 2)
                return
            phase_fence(B)
            ar.reset(markL1)
            skcol = [ar.take_bf16(NFC * P), ar.take_bf16(NFC * P)]
            ssk = ar.take_f32(TT)
            for ec in range(NEC):
                par = ec % 2
                B.wait("sync", mm, mm.n)
                emit_load_bf16(B, skT_d[:, ec * P:(ec + 1) * P], skcol[par], D, P, wstage,
                               None, cast, mul_cols=nwcols)
                skdone = cast.n
                for it in range(NTT):
                    ts = slice(it * TT, (it + 1) * TT)
                    B.wait("tensor", cast, skdone)
                    B.wait("tensor", ev, ev.n)
                    for c in range(NFC):
                        B.emit("tensor", lambda e, l=skcol[par][:, c * P:(c + 1) * P],
                               r=xn_tiles[c][:, ts], o=pb[4][:, 0:TT],
                               s=(c == 0), st=(c == NFC - 1):
                               e.matmul(out=o, lhsT=l, rhs=r, start=s, stop=st),
                               sig=mm if c == NFC - 1 else None)
                    B.wait("vector", mm, mm.n)
                    B.emit("vector", lambda e, o=ssk[:, 0:TT], a=pb[4][:, 0:TT], b=rms_bc[:, ts]:
                           e.tensor_tensor(out=o, in0=a, in1=b, op=ALU.mult), sig=ev)
                    B.wait("scalar", ev, ev.n)
                    B.emit("scalar", lambda e, o=ssk[:, 0:TT], i=ssk[:, 0:TT]:
                           e.activation(out=o, in_=i, func=AF.Silu), sig=sc)
                    B.wait("vector", sc, sc.n)
                    B.emit("vector", lambda e, o=xs_ec[ec][:, ts], a=xs_ec[ec][:, ts],
                           b=ssk[:, 0:TT]:
                           e.tensor_tensor(out=o, in0=a, in1=b, op=ALU.mult), sig=ev)
            phase_fence(B)
            ar.reset(markL1)
            om = ar.take_bf16(NEC * P)
            for m in range(NFC):
                B.wait("sync", mm, mm.n)
                emit_load_bf16(B, outT_d[:, m * P:(m + 1) * P], om, DIN, P, wstage, None, cast)
                B.wait("tensor", cast, cast.n)
                for it in range(NTT):
                    ts = slice(it * TT, (it + 1) * TT)
                    pd = pb[it % 2]
                    B.wait("tensor", ev, ev.n)
                    for ec in range(NEC):
                        B.emit("tensor", lambda e, l=om[:, ec * P:(ec + 1) * P],
                               r=xs_ec[ec][:, ts], o=pd[:, 0:TT],
                               s=(ec == 0), st=(ec == NEC - 1):
                               e.matmul(out=o, lhsT=l, rhs=r, start=s, stop=st),
                               sig=mm if ec == NEC - 1 else None)
                    B.wait("vector", mm, mm.n)
                    B.emit("vector", lambda e, a=acc_tiles[m][:, ts], i=pd[:, 0:TT]:
                           e.tensor_tensor(out=a, in0=a, in1=i, op=ALU.add), sig=ev)
            phase_fence(B)
            ar.reset(markL1)
            ar.free_top((NEC * T + 1) // 2)
            emit_moe(B, g, s2, ar, xn_tiles, acc_tiles, mWg_d, mWu_d, mWd_d, mW1_d, mW2_d,
                     cwsc_d, wstage)
            phase_fence(B)
            ost = ar.take_f32(1024)
            for c in range(NFC):
                for h in range(2):
                    B.wait("vector", mm, mm.n)
                    B.wait("vector", ev, ev.n)
                    B.wait("vector", st, st.n)
                    B.emit("vector", lambda e, i=acc_tiles[c][:, h * 1024:(h + 1) * 1024], o=ost:
                           e.tensor_copy(out=o, in_=i), sig=ev)
                    B.wait("sync", ev, ev.n)
                    B.emit("sync", lambda e, o=y_d[c * P:(c + 1) * P, h * 1024:(h + 1) * 1024], i=ost:
                           e.dma_start(out=o, in_=i), sig=st, sigk=16)
            B.wait("sync", st, st.n)

        osem = [B.sem("pw0"), B.sem("pw1")]
        pmark = ar.mark()
        emit_bf16_prologue(B, ar, wstage,
                           [(wq_d, wqb), (wk_d, wkb), (wv_d, wvb), (wgg_d, wggb),
                            (wo_d, wob), (w1_d, w1b), (w2_d, w2b), (w3_d, w3b),
                            (wgq_d, wgqb), (wuq_d, wuqb), (wdq_d, wdqb)], osem)
        ar.reset(pmark)
        l0_pass(xf_d)
        l1_pass(stateonly=True)
        barrier_all(B)
        B.emit("vector", lambda e: e.tensor_scalar(out=S32d[:, :], in0=S32d[:, :],
               scalar1=selc, scalar2=None, op0=ALU.mult))
        B.emit("vector", lambda e: e.tensor_copy(out=Sbfd[:, :], in_=S32d[:, :]))
        B.emit("vector", lambda e: e.tensor_scalar(out=hbuf[:, :], in0=hbuf[:, :],
               scalar1=selc, scalar2=None, op0=ALU.mult))
        B.emit("vector", lambda e: e.tensor_scalar(out=x1h[:, :], in0=x1hk[:, :],
               scalar1=selc, scalar2=None, op0=ALU.mult), sig=ev)
        l0_pass(xo_d)
        l1_pass(stateonly=False)
        with nc.Block() as block:
            B.replay(block)
    return nc


_BUILT = {}


def _get_built():
    if "l0" not in _BUILT:
        _BUILT["l0"] = build_l0_kernel()
        _BUILT["l1"] = build_l1_kernel()
    return _BUILT["l0"], _BUILT["l1"]


def kernel_multi(L0_attn_x, L0_attn_Wq, L0_attn_Wk, L0_attn_Wv, L0_attn_Wg, L0_attn_Wo,
           L0_attn_w1, L0_attn_w2, L0_attn_w3, L0_attn_g_attn, L0_attn_g_ffn,
           L0_ffn_w_gate, L0_ffn_w_up, L0_ffn_w_down,
           L1_attn_norm_w, L1_attn_skip_w, L1_attn_in_w, L1_attn_conv_w, L1_attn_conv_b,
           L1_attn_wd1, L1_attn_wd2, L1_attn_wd2_b, L1_attn_wb, L1_attn_wc,
           L1_attn_A_log, L1_attn_W_D, L1_attn_out_w,
           L1_ffn_Wg, L1_ffn_Wu, L1_ffn_Wd, L1_ffn_W1, L1_ffn_W2):
    x = np.asarray(L0_attn_x, np.float32)
    nc0, nc1 = _get_built()
    f32 = np.float32
    l0_common = {
        "wq": np.asarray(L0_attn_Wq, f32), "wk": np.asarray(L0_attn_Wk, f32),
        "wv": np.asarray(L0_attn_Wv, f32), "wgg": np.asarray(L0_attn_Wg, f32),
        "wo": np.asarray(L0_attn_Wo, f32),
        "w1": np.asarray(L0_attn_w1, f32), "w2": np.asarray(L0_attn_w2, f32),
        "w3": np.asarray(L0_attn_w3, f32),
        "gattn": np.asarray(L0_attn_g_attn, f32).reshape(D, 1),
        "gffn": np.asarray(L0_attn_g_ffn, f32).reshape(D, 1),
        "wgT": _t(np.asarray(L0_ffn_w_gate, np.int32)),
        "wuT": _t(np.asarray(L0_ffn_w_up, np.int32)),
        "wdT": _t(np.asarray(L0_ffn_w_down, np.int32)),
    }
    l1_common = {
        "normw": np.asarray(L1_attn_norm_w, f32).reshape(D, 1),
        "skipT": _t(np.asarray(L1_attn_skip_w, f32)),
        "inT": _t(np.asarray(L1_attn_in_w, f32)),
        "convw": np.asarray(L1_attn_conv_w, f32),
        "convb": np.asarray(L1_attn_conv_b, f32).reshape(DIN, 1),
        "wd1T": _t(np.asarray(L1_attn_wd1, f32)),
        "wd2T": _t(np.asarray(L1_attn_wd2, f32)),
        "wd2b": np.asarray(L1_attn_wd2_b, f32).reshape(DIN, 1),
        "wbT": _t(np.asarray(L1_attn_wb, f32)),
        "wcT": _t(np.asarray(L1_attn_wc, f32)),
        "alog": np.asarray(L1_attn_A_log, f32),
        "wD": np.asarray(L1_attn_W_D, f32).reshape(DIN, 1),
        "outT": _t(np.asarray(L1_attn_out_w, f32)),
        "mWg": np.asarray(L1_ffn_Wg, f32),
        "mWu": np.ascontiguousarray(np.asarray(L1_ffn_Wu, f32).reshape(8 * D, DIN)),
        "mWd": np.ascontiguousarray(np.asarray(L1_ffn_Wd, f32).reshape(8 * DIN, D)),
        "mW1": np.asarray(L1_ffn_W1, f32), "mW2": np.asarray(L1_ffn_W2, f32),
    }
    zS = np.zeros((P, 512), f32)
    zh = np.zeros((P, NEC * NS), f32)
    zx1h = np.zeros((P, NEC * 3), f32)
    x_first = [np.ascontiguousarray(x[c // 2, 0:T, :].T) for c in range(8)]
    x_own = [np.ascontiguousarray(x[c // 2, (c % 2) * T:(c % 2 + 1) * T, :].T) for c in range(8)]
    cores = list(range(8))
    r1 = run_bass_kernel_spmd(nc0, [dict(l0_common, x=x_first[c], sinit=zS) for c in cores],
                              core_ids=cores)
    resid1_first = [r["y"] for r in r1.results]
    S_sum = [np.ascontiguousarray(r["ys"]) for r in r1.results]
    r2 = run_bass_kernel_spmd(nc0, [dict(l0_common, x=x_own[c],
                                         sinit=(S_sum[c] if c % 2 == 1 else zS)) for c in cores],
                              core_ids=cores)
    resid1_own = [np.ascontiguousarray(r["y"]) for r in r2.results]
    r3 = run_bass_kernel_spmd(nc1, [dict(l1_common, resid=np.ascontiguousarray(resid1_first[c]),
                                         x1halo=zx1h, hinit=zh) for c in cores],
                              core_ids=cores)
    h_sum = [np.ascontiguousarray(r["yh"]) for r in r3.results]
    x1h_sum = [np.ascontiguousarray(r["yx1h"]) for r in r3.results]
    in4 = []
    for c in cores:
        in4.append(dict(l1_common, resid=resid1_own[c],
                        x1halo=(x1h_sum[c] if c % 2 == 1 else zx1h),
                        hinit=(h_sum[c] if c % 2 == 1 else zh)))
    r4 = run_bass_kernel_spmd(nc1, in4, core_ids=cores)
    out = np.empty((4, 2 * T, D), np.float32)
    for c in cores:
        out[c // 2, (c % 2) * T:(c % 2 + 1) * T, :] = r4.results[c]["y"].T
    return out




def _get_built_full():
    if "full" not in _BUILT:
        _BUILT["full"] = build_full_kernel()
    return _BUILT["full"]


def kernel(L0_attn_x, L0_attn_Wq, L0_attn_Wk, L0_attn_Wv, L0_attn_Wg, L0_attn_Wo,
           L0_attn_w1, L0_attn_w2, L0_attn_w3, L0_attn_g_attn, L0_attn_g_ffn,
           L0_ffn_w_gate, L0_ffn_w_up, L0_ffn_w_down,
           L1_attn_norm_w, L1_attn_skip_w, L1_attn_in_w, L1_attn_conv_w, L1_attn_conv_b,
           L1_attn_wd1, L1_attn_wd2, L1_attn_wd2_b, L1_attn_wb, L1_attn_wc,
           L1_attn_A_log, L1_attn_W_D, L1_attn_out_w,
           L1_ffn_Wg, L1_ffn_Wu, L1_ffn_Wd, L1_ffn_W1, L1_ffn_W2):
    x = np.asarray(L0_attn_x, np.float32)
    nc = _get_built_full()
    f32 = np.float32
    common = {
        "wq": np.asarray(L0_attn_Wq, f32), "wk": np.asarray(L0_attn_Wk, f32),
        "wv": np.asarray(L0_attn_Wv, f32), "wgg": np.asarray(L0_attn_Wg, f32),
        "wo": np.asarray(L0_attn_Wo, f32),
        "w1": np.asarray(L0_attn_w1, f32), "w2": np.asarray(L0_attn_w2, f32),
        "w3": np.asarray(L0_attn_w3, f32),
        "gattn": np.asarray(L0_attn_g_attn, f32).reshape(D, 1),
        "gffn": np.asarray(L0_attn_g_ffn, f32).reshape(D, 1),
        "wgT": _t(np.asarray(L0_ffn_w_gate, np.int32)),
        "wuT": _t(np.asarray(L0_ffn_w_up, np.int32)),
        "wdT": _t(np.asarray(L0_ffn_w_down, np.int32)),
        "normw": np.asarray(L1_attn_norm_w, f32).reshape(D, 1),
        "skipT": _t(np.asarray(L1_attn_skip_w, f32)),
        "inT": _t(np.asarray(L1_attn_in_w, f32)),
        "convw": np.asarray(L1_attn_conv_w, f32),
        "convb": np.asarray(L1_attn_conv_b, f32).reshape(DIN, 1),
        "wd1T": _t(np.asarray(L1_attn_wd1, f32)),
        "wd2T": _t(np.asarray(L1_attn_wd2, f32)),
        "wd2b": np.asarray(L1_attn_wd2_b, f32).reshape(DIN, 1),
        "wbT": _t(np.asarray(L1_attn_wb, f32)),
        "wcT": _t(np.asarray(L1_attn_wc, f32)),
        "alog": np.asarray(L1_attn_A_log, f32),
        "wD": np.asarray(L1_attn_W_D, f32).reshape(DIN, 1),
        "outT": _t(np.asarray(L1_attn_out_w, f32)),
        "mWg": np.asarray(L1_ffn_Wg, f32),
        "mWu": np.ascontiguousarray(np.asarray(L1_ffn_Wu, f32).reshape(8 * D, DIN)),
        "mWd": np.ascontiguousarray(np.asarray(L1_ffn_Wd, f32).reshape(8 * DIN, D)),
        "mW1": np.asarray(L1_ffn_W1, f32), "mW2": np.asarray(L1_ffn_W2, f32),
    }
    sel1 = np.ones((P, 1), f32)
    sel0 = np.zeros((P, 1), f32)
    cores = list(range(8))
    ins = []
    for c in cores:
        ins.append(dict(common,
                        xfirst=np.ascontiguousarray(x[c // 2, 0:T, :].T),
                        xown=np.ascontiguousarray(x[c // 2, (c % 2) * T:(c % 2 + 1) * T, :].T),
                        sel=(sel1 if c % 2 == 1 else sel0)))
    r = run_bass_kernel_spmd(nc, ins, core_ids=cores)
    out = np.empty((4, 2 * T, D), np.float32)
    for c in cores:
        out[c // 2, (c % 2) * T:(c % 2 + 1) * T, :] = r.results[c]["y"].T
    return out


if __name__ == "__main__":
    which = sys.argv[1] if len(sys.argv) > 1 else "qffn"
    rng = np.random.default_rng(0)
    if which == "qffn":
        Xfull = rng.standard_normal((T, D)).astype(np.float32)
        wg = rng.integers(-127, 128, (HQ, D)).astype(np.int32)
        wu = rng.integers(-127, 128, (HQ, D)).astype(np.int32)
        wd = rng.integers(-127, 128, (D, HQ)).astype(np.int32)
        nc = build_qffn_kernel()
        in_map = {"x": _t(Xfull), "wgT": _t(wg), "wuT": _t(wu), "wdT": _t(wd)}
        res = run_bass_kernel_spmd(nc, [in_map], core_ids=[0])
        out = res.results[0]["y"].T
        ref = _np_ref_qffn(Xfull, wg, wu, wd)
        print("qffn rel-var:", ((out - ref) ** 2).mean() / (ref ** 2).mean())
    elif which == "l0":
        s_in = 0.1 / np.sqrt(D)
        X = rng.standard_normal((T, D)).astype(np.float32)
        W = {}
        for nmw in ("Wq", "Wk", "Wv", "Wg"):
            W[nmw] = (rng.standard_normal((D, 1024)) * s_in).astype(np.float32)
        W["Wo"] = (rng.standard_normal((1024, D)) * s_in).astype(np.float32)
        W["w1"] = (rng.standard_normal((D, DFF0)) * s_in).astype(np.float32)
        W["w2"] = (rng.standard_normal((D, DFF0)) * s_in).astype(np.float32)
        W["w3"] = (rng.standard_normal((DFF0, D)) * (0.1 / np.sqrt(DFF0))).astype(np.float32)
        W["g_attn"] = (1.0 + 0.1 * rng.standard_normal(D)).astype(np.float32)
        W["g_ffn"] = (1.0 + 0.1 * rng.standard_normal(D)).astype(np.float32)
        W["w_gate"] = rng.integers(-127, 128, (HQ, D)).astype(np.int32)
        W["w_up"] = rng.integers(-127, 128, (HQ, D)).astype(np.int32)
        W["w_down"] = rng.integers(-127, 128, (D, HQ)).astype(np.int32)
        S0 = (rng.standard_normal((16, 64, 64)) * 0.02).astype(np.float32)
        nc = build_l0_kernel()
        im = {"x": _t(X), "wq": W["Wq"], "wk": W["Wk"], "wv": W["Wv"], "wgg": W["Wg"],
              "wo": W["Wo"], "w1": W["w1"], "w2": W["w2"], "w3": W["w3"],
              "gattn": W["g_attn"].reshape(D, 1), "gffn": W["g_ffn"].reshape(D, 1),
              "wgT": _t(W["w_gate"]), "wuT": _t(W["w_up"]), "wdT": _t(W["w_down"]),
              "sinit": _pack_S(S0)}
        ref, refS = _np_ref_l0(X, W, S0)
        res = run_bass_kernel_spmd(nc, [im], core_ids=[0])
        out = res.results[0]["y"].T
        Sf = res.results[0]["ys"]
        print("l0 rel-var:", ((out - ref) ** 2).mean() / (ref ** 2).mean(),
              "S rel-var:", ((Sf - _pack_S(refS)) ** 2).mean() / (_pack_S(refS) ** 2).mean())
    elif which == "l1":
        s_ = lambda shp, fan: (rng.standard_normal(shp) / np.sqrt(fan)).astype(np.float32)
        X = rng.standard_normal((T, D)).astype(np.float32)
        halo = rng.standard_normal((3, D)).astype(np.float32)
        W = {}
        W["norm_w"] = (1.0 + 0.1 * rng.standard_normal(D)).astype(np.float32)
        W["skip_w"] = s_((DIN, D), D); W["in_w"] = s_((DIN, D), D)
        W["conv_w"] = s_((DIN, 4), 4); W["conv_b"] = (0.1 * rng.standard_normal(DIN)).astype(np.float32)
        W["wd1"] = s_((64, DIN), DIN); W["wd2"] = s_((DIN, 64), 64)
        W["wd2_b"] = (0.1 * rng.standard_normal(DIN)).astype(np.float32)
        W["wb"] = s_((NS, DIN), DIN); W["wc"] = s_((NS, DIN), DIN)
        W["A_log"] = rng.standard_normal((DIN, NS)).astype(np.float32)
        W["W_D"] = (1.0 + 0.1 * rng.standard_normal(DIN)).astype(np.float32)
        W["out_w"] = s_((D, DIN), DIN)
        lim = np.sqrt(6.0 / (D + 8))
        W["Wg"] = rng.uniform(-lim, lim, (D, 8)).astype(np.float32)
        lim = np.sqrt(6.0 / (D + DIN))
        W["Wu"] = rng.uniform(-lim, lim, (8, D, DIN)).astype(np.float32)
        W["Wd"] = rng.uniform(-lim, lim, (8, DIN, D)).astype(np.float32)
        W["W1"] = rng.uniform(-lim, lim, (D, DIN)).astype(np.float32)
        W["W2"] = rng.uniform(-lim, lim, (DIN, D)).astype(np.float32)
        h0 = (0.05 * rng.standard_normal((DIN, NS))).astype(np.float32)
        nc = build_l1_kernel()
        im = {"resid": _t(X), "x1halo": np.zeros((P, NEC * 3), np.float32),
              "hinit": np.ascontiguousarray(h0.reshape(NEC, P, NS).transpose(1, 0, 2).reshape(P, NEC * NS)),
              "normw": W["norm_w"].reshape(D, 1), "skipT": _t(W["skip_w"]), "inT": _t(W["in_w"]),
              "convw": W["conv_w"], "convb": W["conv_b"].reshape(DIN, 1),
              "wd1T": _t(W["wd1"]), "wd2T": _t(W["wd2"]), "wd2b": W["wd2_b"].reshape(DIN, 1),
              "wbT": _t(W["wb"]), "wcT": _t(W["wc"]), "alog": W["A_log"],
              "wD": W["W_D"].reshape(DIN, 1), "outT": _t(W["out_w"]),
              "mWg": W["Wg"], "mWu": np.ascontiguousarray(W["Wu"].reshape(8 * D, DIN)),
              "mWd": np.ascontiguousarray(W["Wd"].reshape(8 * DIN, D)),
              "mW1": W["W1"], "mW2": W["W2"]}
        ref, refh = _np_ref_l1(X, W, None, h0)
        res = run_bass_kernel_spmd(nc, [im], core_ids=[0])
        out = res.results[0]["y"].T
        hf = res.results[0]["yh"]
        refh_dev = np.ascontiguousarray(refh.reshape(NEC, P, NS).transpose(1, 0, 2).reshape(P, NEC * NS))
        print("l1 rel-var:", ((out - ref) ** 2).mean() / (ref ** 2).mean(),
              " nan:", np.isnan(out).sum())
        print("l1 h rel-var:", ((hf - refh_dev) ** 2).mean() / (refh_dev ** 2).mean())
    elif which == "gla":
        s_in = 0.1 / np.sqrt(D)
        Xn = rng.standard_normal((T, D)).astype(np.float32)
        Wq = (rng.standard_normal((D, 1024)) * s_in).astype(np.float32)
        Wk = (rng.standard_normal((D, 1024)) * s_in).astype(np.float32)
        Wv = (rng.standard_normal((D, 1024)) * s_in).astype(np.float32)
        Wg = (rng.standard_normal((D, 1024)) * s_in).astype(np.float32)
        gattn = (1.0 + 0.1 * rng.standard_normal(D)).astype(np.float32)
        S0 = (rng.standard_normal((16, 64, 64)) * 0.05).astype(np.float32)
        nc = build_gla_kernel()
        im = {"xn": _t(Xn), "wq": Wq, "wk": Wk, "wv": Wv, "wg": Wg,
              "gattn": gattn.reshape(D, 1), "sinit": _pack_S(S0)}
        ref_o, ref_S = _np_ref_gla(Xn, gattn, Wq, Wk, Wv, Wg, S0)
        res = run_bass_kernel_spmd(nc, [im], core_ids=[0])
        o = res.results[0]["y"].T
        Sf = res.results[0]["ys"]
        print("gla o rel-var:", ((o - ref_o) ** 2).mean() / (ref_o ** 2).mean(),
              "maxabs:", np.abs(o - ref_o).max(), "refscale:", np.abs(ref_o).max())
        print("gla S rel-var:", ((Sf - _pack_S(ref_S)) ** 2).mean() / (_pack_S(ref_S) ** 2 + 1e-12).mean())
